# version 05
# baseline (speedup 1.0000x reference)
import numpy as np
from contextlib import ExitStack

import concourse.bass as bass
import concourse.mybir as mybir
from concourse.bass_utils import run_bass_kernel_spmd

F32 = mybir.dt.float32
BF16 = mybir.dt.bfloat16
AF = mybir.ActivationFunctionType
ALU = mybir.AluOpType
AX = mybir.AxisListType

D = 1024
NSEQ = 2
LSEQ = 2048
TOK = NSEQ * LSEQ
NH = 16
DH = 64
E0, H0 = 8, 2736
E1, H1 = 15, 1024
DIN = 2048
DST = 64
DCONV = 4

ENGS = ("sync", "vector", "scalar", "tensor", "gpsimd")


class Prog:

    def __init__(self, nc):
        self.nc = nc
        self.ops = {e: [] for e in ENGS}
        self.icount = {e: 0 for e in ENGS}
        self.dcount = {e: 0 for e in ENGS}
        self.dslot = {}
        self.last_w = {}
        self.readers = {}
        self.hiw = {e: {} for e in ENGS}
        self.sems = {}
        self._es = None

    NDMA_SEMS = 8

    def make_sems(self, es):
        for e in ENGS:
            self.sems[("c", e)] = es.enter_context(self.nc.semaphore(f"sc_{e}"))
        for e in ("sync", "scalar", "gpsimd"):
            for s in range(self.NDMA_SEMS):
                self.sems[("d", e, s)] = es.enter_context(self.nc.semaphore(f"sd_{e}{s}"))

    def _dep_waits(self, eng, reads, writes):
        need = {}
        def add(dep):
            semkey, cnt = dep
            if need.get(semkey, 0) < cnt:
                need[semkey] = cnt
        for k in reads:
            d = self.last_w.get(k)
            if d:
                add(d)
        for k in writes:
            d = self.last_w.get(k)
            if d:
                add(d)
            for semkey, cnt in self.readers.get(k, {}).items():
                add((semkey, cnt))
        waits = []
        hw = self.hiw[eng]
        for semkey, cnt in sorted(need.items(), key=str):
            if semkey == ("c", eng) and cnt > self.icount[eng]:
                continue
            if semkey[0] == "d":
                tgt = 16 * self.dslot[(semkey[1], semkey[2])]
            else:
                tgt = cnt
            if hw.get(semkey, 0) < tgt:
                hw[semkey] = tgt
                waits.append((self.sems[semkey], tgt))
        return waits

    def _record_access(self, semkey, cnt, reads, writes):
        for k in reads:
            self.readers.setdefault(k, {})[semkey] = cnt
        for k in writes:
            self.last_w[k] = (semkey, cnt)
            self.readers[k] = {}

    def op(self, eng, fn, reads=(), writes=(), inc=True, tag=None):
        waits = self._dep_waits(eng, reads, writes)
        cnt = self.icount[eng] + 1
        if inc:
            self.icount[eng] = cnt
        sem = self.sems[("c", eng)]
        def emit(h, waits=waits, fn=fn, inc=inc, sem=sem, tag=tag):
            for s, v in waits:
                h.wait_ge(s, v)
            ins = fn(h)
            if inc:
                ins.then_inc(sem, 1)
            if tag:
                ins.annotate(tag)
        self.ops[eng].append(emit)
        self._record_access(("c", eng), cnt, reads, writes)

    def dma(self, eng, out, in_, reads=(), writes=(), tag=None, transpose=False,
            slow=False):
        waits = self._dep_waits(eng, reads, writes)
        slot = self.dcount[eng] % self.NDMA_SEMS
        self.dcount[eng] += 1
        key = ("d", eng, slot)
        prev = self.dslot.setdefault((eng, slot), 0)
        cnt = prev + 1
        self.dslot[(eng, slot)] = cnt
        sem = self.sems[key]
        if prev:
            hw = self.hiw[eng]
            if hw.get(key, 0) < 16 * prev:
                hw[key] = 16 * prev
                waits.append((sem, 16 * prev))
        def emit(h, waits=waits, out=out, in_=in_, sem=sem, tag=tag,
                 transpose=transpose, slow=slow):
            for s, v in waits:
                h.wait_ge(s, v)
            kw = {}
            if transpose:
                kw["transpose"] = True
            if slow:
                kw["allow_slow_non_contiguous"] = True
            ins = h.dma_start(out=out, in_=in_, **kw)
            ins.then_inc(sem, 16)
            if tag:
                ins.annotate(tag)
        self.ops[eng].append(emit)
        self._record_access(key, cnt, reads, writes)

    def barrier_all(self, eng):
        waits = []
        hw = self.hiw[eng]
        for e in ENGS:
            if self.icount[e]:
                key = ("c", e)
                tgt = self.icount[e]
                if hw.get(key, 0) < tgt:
                    hw[key] = tgt
                    waits.append((self.sems[key], tgt))
        for (e, s), cnt in self.dslot.items():
            key = ("d", e, s)
            tgt = cnt * 16
            if hw.get(key, 0) < tgt:
                hw[key] = tgt
                waits.append((self.sems[key], tgt))
        def emit(h, waits=waits):
            for s, v in waits:
                h.wait_ge(s, v)
        self.ops[eng].append(emit)

    def finish(self, block):
        handles = {}
        def run(eng):
            def section(h):
                for fn in self.ops[eng]:
                    fn(h)
            return section
        block.sync(run("sync"))
        block.vector(run("vector"))
        block.scalar(run("scalar"))
        block.tensor(run("tensor"))
        block.gpsimd(run("gpsimd"))



def ts(i, n):
    return slice(i * n, (i + 1) * n)


def ceil_div(a, b):
    return -(-a // b)


class Alloc:

    def __init__(self, nc, es):
        self.nc = nc
        self.es = es
        self.n = 0

    def sb(self, name, shape, dtype):
        return self.es.enter_context(self.nc.sbuf_tensor(name, shape, dtype))

    def ps(self, name, shape, dtype=F32):
        return self.es.enter_context(self.nc.psum_tensor(name, shape, dtype))


def load_weight_T(p, al, hbm_ap, name, out_dim, in_dim, stage):
    nI, nO = in_dim // 128, out_dim // 128
    wT = [al.sb(f"{name}_T{i}", [128, out_dim], BF16) for i in range(nI)]
    stage_f, stage_b = stage
    for o in range(nO):
        kf = "wstage_f"
        kb = "wstage_b"
        p.dma("sync", out=stage_f[:], in_=hbm_ap[ts(o, 128), :],
              reads=(), writes=(kf,))
        p.op("vector", lambda h, sf=stage_f, sbf=stage_b: h.tensor_copy(out=sbf[:], in_=sf[:]),
             reads=(kf,), writes=(kb,))
        for i in range(nI):
            p.dma("scalar", out=wT[i][:, ts(o, 128)], in_=stage_b[:, ts(i, 128)],
                  reads=(kb,), writes=(f"{name}_T{i}",), transpose=True)
    return wT


def load_weight_direct(p, al, hbm_ap, name, in_dim, out_dim):
    nI = in_dim // 128
    wT = []
    for i in range(nI):
        stage = al.sb(f"{name}_st{i}", [128, out_dim], F32)
        t = al.sb(f"{name}_d{i}", [128, out_dim], BF16)
        p.dma("sync", out=stage[:], in_=hbm_ap[ts(i, 128), :],
              reads=(), writes=(f"{name}_st{i}",))
        p.op("vector", lambda h, s=stage, t=t: h.tensor_copy(out=t[:], in_=s[:]),
             reads=(f"{name}_st{i}",), writes=(f"{name}_d{i}",))
        wT.append(t)
    return wT


def load_bias(p, al, hbm_ap, name, dim):
    nT = dim // 128
    col = al.sb(f"{name}_col", [128, nT], F32)
    p.dma("sync", out=col[:], in_=hbm_ap.rearrange("(t p) -> p t", p=128),
          reads=(), writes=(name,))
    return col


def load_bias_cols(p, al, hbm_ap, name, dim):
    nT = dim // 128
    col = al.sb(f"{name}_col", [128, nT], F32)
    for j in range(nT):
        p.dma("sync", out=col[:, j:j + 1], in_=hbm_ap[ts(j, 128)].unsqueeze(1),
              reads=(), writes=(name,))
    return col


def make_ones_and_masks(p, al, lseq, qc):
    ones_f = al.sb("ones_f", [128, 1], F32)
    ones_b = al.sb("ones_b", [128, 1], BF16)
    p.op("gpsimd", lambda h: h.memset(ones_f[:], 1.0), writes=("ones_f",))
    p.op("gpsimd", lambda h: h.memset(ones_b[:], 1.0), writes=("ones_b",))
    masks = {}
    for bi in range(qc // 128):
        delta = bi * 128
        m = al.sb(f"mask{bi}", [128, qc], BF16)
        p.op("gpsimd", lambda h, m=m: h.memset(m[:], 1.0), writes=(f"mask{bi}",))
        p.op("gpsimd", lambda h, m=m, delta=delta: h.affine_select(
            out=m[:], in_=m[:], compare_op=ALU.is_ge, fill=0.0,
            base=delta, pattern=[[1, qc]], channel_multiplier=-1),
            reads=(f"mask{bi}",), writes=(f"mask{bi}",))
        masks[delta] = m
    return ones_f, ones_b, masks


def emit_layernorm(p, al, r_tiles, rkeys, out_tiles, okeys, tok, tag, C,
                   cast_to=None, ckeys=None):
    nD = len(r_tiles)
    CH = 512
    ps_s0 = al.ps(f"{tag}_pss0", [1, 512], F32)
    ps_s1 = al.ps(f"{tag}_pss1", [1, 512], F32)
    row_m = al.sb(f"{tag}_rowm", [1, CH], F32)
    row_r = al.sb(f"{tag}_rowr", [1, CH], F32)
    r2 = al.sb(f"{tag}_r2", [128, CH], F32)
    bc_m = al.ps(f"{tag}_bcm", [128, 512], F32)
    bc_r = al.ps(f"{tag}_bcr", [128, 512], F32)
    inv = 1.0 / (nD * 128)
    for c in range(tok // CH):
        cs = ts(c, CH)
        for j in range(nD):
            p.op("tensor", lambda h, j=j, cs=cs: h.matmul(
                out=ps_s0[:, :], lhsT=C["ones_col"], rhs=r_tiles[j][:, cs],
                start=(j == 0), stop=(j == nD - 1)),
                reads=(rkeys[j], "ones_col"), writes=(f"{tag}_pss0",))
        for j in range(nD):
            p.op("scalar", lambda h, j=j, cs=cs: h.activation(
                out=r2[:], in_=r_tiles[j][:, cs], func=AF.Square),
                reads=(rkeys[j],), writes=(f"{tag}_r2",))
            p.op("tensor", lambda h, j=j: h.matmul(
                out=ps_s1[:, :], lhsT=C["ones_col"], rhs=r2[:],
                start=(j == 0), stop=(j == nD - 1)),
                reads=(f"{tag}_r2", "ones_col"), writes=(f"{tag}_pss1",))
        p.op("scalar", lambda h: h.activation(
            out=row_m[:], in_=ps_s0[:, :], func=AF.Copy, scale=inv),
            reads=(f"{tag}_pss0",), writes=(f"{tag}_rowm",))
        p.op("scalar", lambda h: h.activation(
            out=row_r[:], in_=row_m[:], func=AF.Square),
            reads=(f"{tag}_rowm",), writes=(f"{tag}_rowr",))
        p.op("vector", lambda h: h.scalar_tensor_tensor(
            out=row_r[:], in0=ps_s1[:, :], scalar=inv, in1=row_r[:],
            op0=ALU.mult, op1=ALU.subtract),
            reads=(f"{tag}_pss1", f"{tag}_rowr"), writes=(f"{tag}_rowr",))
        p.op("scalar", lambda h: h.activation(
            out=row_r[:], in_=row_r[:], func=AF.Ln, bias=1e-5),
            reads=(f"{tag}_rowr",), writes=(f"{tag}_rowr",))
        p.op("scalar", lambda h: h.activation(
            out=row_r[:], in_=row_r[:], func=AF.Exp, scale=-0.5),
            reads=(f"{tag}_rowr",), writes=(f"{tag}_rowr",))
        p.op("tensor", lambda h: h.matmul(
            out=bc_m[:, :CH], lhsT=C["ones_row"], rhs=row_m[:],
            start=True, stop=True),
            reads=(f"{tag}_rowm", "ones_row"), writes=(f"{tag}_bcm",))
        p.op("tensor", lambda h: h.matmul(
            out=bc_r[:, :CH], lhsT=C["ones_row"], rhs=row_r[:],
            start=True, stop=True),
            reads=(f"{tag}_rowr", "ones_row"), writes=(f"{tag}_bcr",))
        for j in range(nD):
            p.op("vector", lambda h, j=j, cs=cs: h.tensor_sub(
                out=out_tiles[j][:, cs], in0=r_tiles[j][:, cs], in1=bc_m[:, :CH]),
                reads=(rkeys[j], f"{tag}_bcm"), writes=(okeys[j],))
            p.op("vector", lambda h, j=j, cs=cs: h.tensor_mul(
                out=out_tiles[j][:, cs], in0=out_tiles[j][:, cs], in1=bc_r[:, :CH]),
                reads=(okeys[j], f"{tag}_bcr"), writes=(okeys[j],))
            if cast_to is not None:
                p.op("vector", lambda h, j=j, cs=cs: h.tensor_copy(
                    out=cast_to[j][:, cs], in_=out_tiles[j][:, cs]),
                    reads=(okeys[j],), writes=(ckeys[j],))


CONST_VALS = [1e-5, 1.0]


def make_consts(p, al):
    C = {}
    cpool = al.sb("cpool", [128, len(CONST_VALS)], F32)
    for i, v in enumerate(CONST_VALS):
        p.op("gpsimd", lambda h, i=i, v=v: h.memset(cpool[:, i:i + 1], v),
             writes=("cpool",))
        p.nc.const_aps.register(F32, v, cpool[:, i:i + 1])
    ones_col = al.sb("ones_col", [128, 1], F32)
    ones_row = al.sb("ones_row", [1, 128], F32)
    ones_row_b = al.sb("ones_row_b", [1, 128], BF16)
    p.op("gpsimd", lambda h: h.memset(ones_col[:], 1.0), writes=("ones_col",))
    p.op("gpsimd", lambda h: h.memset(ones_row[:], 1.0), writes=("ones_row",))
    p.op("gpsimd", lambda h: h.memset(ones_row_b[:], 1.0), writes=("ones_row_b",))
    C["ones_col"] = ones_col[:]
    C["ones_row"] = ones_row[:]
    C["ones_row_b"] = ones_row_b[:]
    return C


def load_xT(p, al, x_hbm, name, tok, dim=D, dtype=F32, cast_bf=True):
    nD = dim // 128
    xf = [al.sb(f"{name}_f{j}", [128, tok], F32) for j in range(nD)]
    xb = [al.sb(f"{name}_b{j}", [128, tok], BF16) for j in range(nD)] if cast_bf else None
    for j in range(nD):
        p.dma("sync", out=xf[j][:], in_=x_hbm[ts(j, 128), :], writes=(f"{name}_f{j}",))
        if cast_bf:
            p.op("vector", lambda h, j=j: h.tensor_copy(out=xb[j][:], in_=xf[j][:]),
                 reads=(f"{name}_f{j}",), writes=(f"{name}_b{j}",))
    return xf, xb



def build_l0_attn(nc, lseq=LSEQ):
    tok = NSEQ * lseq
    xT = nc.dram_tensor("xT", [D, tok], F32, kind="ExternalInput").ap()
    ws = [nc.dram_tensor(n, [D, D], F32, kind="ExternalInput").ap()
          for n in ("wq", "wk", "wv", "wo")]
    bs = [nc.dram_tensor(n, [D], F32, kind="ExternalInput").ap()
          for n in ("bq", "bk", "bv", "bo")]
    outT = nc.dram_tensor("outT", [D, tok], F32, kind="ExternalOutput").ap()
    es = ExitStack()
    with es:
        al = Alloc(nc, es)
        p = Prog(nc)
        p.make_sems(es)
        emit_l0_attn(nc, p, al, xT, ws, bs, outT, lseq)
        block = es.enter_context(nc.Block())
        p.finish(block)
    return nc


def emit_l0_attn(nc, p, al, xT_hbm, w_hbm, b_hbm, outT_hbm, lseq):
    nD = D // 128
    QC = 512
    C = make_consts(p, al)

    P = [al.ps(f"P{i}", [128, 512], F32) for i in range(6)]

    masks = {}
    for bi in range(QC // 128):
        delta = -bi * 128
        m = al.sb(f"mask{bi}", [128, QC], BF16)
        p.op("gpsimd", lambda h, m=m: h.memset(m[:], 1.0), writes=(f"mask{bi}",))
        p.op("gpsimd", lambda h, m=m, delta=delta: h.affine_select(
            out=m[:], in_=m[:], compare_op=ALU.is_ge, fill=0.0,
            base=delta, pattern=[[1, QC]], channel_multiplier=-1),
            reads=(f"mask{bi}",), writes=(f"mask{bi}",))
        masks[delta] = m

    stage_f = al.sb("wstage_f", [128, D], F32)
    stage_b = al.sb("wstage_b", [128, D], BF16)
    wqT = load_weight_T(p, al, w_hbm[0], "wq", D, D, (stage_f, stage_b))
    wkT = load_weight_T(p, al, w_hbm[1], "wk", D, D, (stage_f, stage_b))
    wvT = load_weight_T(p, al, w_hbm[2], "wv", D, D, (stage_f, stage_b))
    woT = load_weight_T(p, al, w_hbm[3], "wo", D, D, (stage_f, stage_b))
    bq = load_bias_cols(p, al, b_hbm[0], "bq", D)
    bk = load_bias_cols(p, al, b_hbm[1], "bk", D)
    bo = load_bias_cols(p, al, b_hbm[3], "bo", D)
    bv_row_f = al.sb("bv_rowf", [1, D], F32)
    bv_row = al.sb("bv_row", [1, D], BF16)
    p.dma("sync", out=bv_row_f[:], in_=b_hbm[2].unsqueeze(0), writes=("bv_rowf",))
    p.op("vector", lambda h: h.tensor_copy(out=bv_row[:], in_=bv_row_f[:]),
         reads=("bv_rowf",), writes=("bv_row",))

    xbc = [al.sb(f"xbc{j}", [128, 512], BF16) for j in range(nD)]
    KT = [al.sb(f"KT{j}", [128, lseq], BF16) for j in range(nD)]
    Vt = [al.sb(f"Vt{i}", [128, D], BF16) for i in range(lseq // 128)]
    QTc = [al.sb(f"QTc{j}", [128, QC], BF16) for j in range(nD)]
    ATc = [al.sb(f"ATc{j}", [128, QC], BF16) for j in range(nD)]
    rTc = [al.sb(f"rTc{j}", [128, QC], F32) for j in range(nD)]
    Sb = [al.sb(f"Sbb{i}", [128, QC], BF16) for i in range(2)]

    ln_bufs = dict(
        ps_s0=P[4][0:1, :], ps_s1=P[5][0:1, :],
        bc_m=P[4], bc_r=P[5],
        row_m=al.sb("ln_rowm", [1, QC], F32),
        row_r=al.sb("ln_rowr", [1, QC], F32),
        r2=al.sb("ln_r2", [128, QC], F32),
    )

    for seq in range(NSEQ):
        s_lo = seq * lseq
        for c in range(lseq // 512):
            cs = ts(c, 512)
            for j in range(nD):
                p.dma("gpsimd", out=xbc[j][:],
                      in_=xT_hbm[ts(j, 128), s_lo + c * 512:s_lo + (c + 1) * 512],
                      writes=(f"xbc{j}",))
            for e in range(nD):
                for j in range(nD):
                    p.op("tensor", lambda h, e=e, j=j: h.matmul(
                        out=P[0][:, :], lhsT=wkT[j][:, ts(e, 128)], rhs=xbc[j][:],
                        start=(j == 0), stop=(j == nD - 1)),
                        reads=(f"wk_T{j}", f"xbc{j}"), writes=("P0",))
                p.op("vector", lambda h, e=e, cs=cs: h.tensor_scalar(
                    out=KT[e][:, cs], in0=P[0][:, :], scalar1=bk[:, e:e + 1],
                    scalar2=None, op0=ALU.add),
                    reads=("P0", "bk"), writes=(f"KT{e}",))
            for i4 in range(4):
                i = c * 4 + i4
                for hhalf in range(D // 512):
                    hs = ts(hhalf, 512)
                    for j in range(nD):
                        p.op("tensor", lambda h, i4=i4, j=j, hs=hs: h.matmul(
                            out=P[1][:, :], lhsT=xbc[j][:, ts(i4, 128)], rhs=wvT[j][:, hs],
                            start=(j == 0), stop=False),
                            reads=(f"wv_T{j}", f"xbc{j}"), writes=("P1",))
                    p.op("tensor", lambda h, hs=hs: h.matmul(
                        out=P[1][:, :], lhsT=C["ones_row_b"], rhs=bv_row[:, hs],
                        start=False, stop=True),
                        reads=("bv_row", "ones_row_b"), writes=("P1",))
                    p.op("scalar", lambda h, i=i, hs=hs: h.copy(
                        out=Vt[i][:, hs], in_=P[1][:, :]),
                        reads=("P1",), writes=(f"Vt{i}",))
        for qc in range(lseq // QC):
            q0 = qc * QC
            for j in range(nD):
                p.dma("gpsimd", out=xbc[j][:],
                      in_=xT_hbm[ts(j, 128), s_lo + q0:s_lo + q0 + QC],
                      writes=(f"xbc{j}",))
            for e in range(nD):
                for j in range(nD):
                    p.op("tensor", lambda h, e=e, j=j: h.matmul(
                        out=P[0][:, :], lhsT=wqT[j][:, ts(e, 128)],
                        rhs=xbc[j][:],
                        start=(j == 0), stop=(j == nD - 1)),
                        reads=(f"wq_T{j}", f"xbc{j}"), writes=("P0",))
                p.op("vector", lambda h, e=e: h.tensor_scalar(
                    out=QTc[e][:], in0=P[0][:, :], scalar1=bq[:, e:e + 1],
                    scalar2=None, op0=ALU.add),
                    reads=("P0", "bq"), writes=(f"QTc{e}",))
            for head in range(NH):
                et, ro = head // 2, (head % 2) * 64
                n_sblk = (q0 + QC) // 128
                for sb_ in range(n_sblk):
                    s0 = sb_ * 128
                    pp, pk = P[2 + (sb_ % 2)], f"P{2 + (sb_ % 2)}"
                    sbuf, sk = Sb[sb_ % 2], f"Sbb{sb_ % 2}"
                    p.op("tensor", lambda h, et=et, ro=ro, s0=s0, q0=q0, pp=pp: h.matmul(
                        out=pp[:, :QC], lhsT=KT[et][ro:ro + 64, s0:s0 + 128],
                        rhs=QTc[et][ro:ro + 64, :],
                        start=True, stop=True),
                        reads=(f"KT{et}", f"QTc{et}"), writes=(pk,))
                    delta = q0 - s0
                    if delta <= 0:
                        p.op("vector", lambda h, pp=pp, sbuf=sbuf, delta=delta: h.tensor_mul(
                            out=sbuf[:], in0=pp[:, :QC], in1=masks[delta][:]),
                            reads=(pk, f"mask{-delta // 128}"), writes=(sk,))
                    else:
                        p.op("scalar", lambda h, pp=pp, sbuf=sbuf: h.copy(
                            out=sbuf[:], in_=pp[:, :QC]),
                            reads=(pk,), writes=(sk,))
                    p.op("tensor", lambda h, s0=s0, ro=ro, et=et, sbuf=sbuf,
                         st=(sb_ == 0), sp=(sb_ == n_sblk - 1): h.matmul(
                        out=P[1][0:64, :QC], lhsT=Vt[s0 // 128][:, et * 128 + ro: et * 128 + ro + 64],
                        rhs=sbuf[:], start=st, stop=sp, skip_group_check=True),
                        reads=(sk, f"Vt{s0 // 128}"), writes=("P1",))
                p.op("scalar", lambda h, et=et, ro=ro: h.copy(
                    out=ATc[et][ro:ro + 64, :], in_=P[1][0:64, :QC]),
                    reads=("P1",), writes=(f"ATc{et}",))
            for e in range(nD):
                for j in range(nD):
                    p.op("tensor", lambda h, e=e, j=j: h.matmul(
                        out=P[0][:, :], lhsT=woT[j][:, ts(e, 128)], rhs=ATc[j][:],
                        start=(j == 0), stop=(j == nD - 1)),
                        reads=(f"wo_T{j}", f"ATc{j}"), writes=("P0",))
                p.op("vector", lambda h, e=e: h.scalar_tensor_tensor(
                    out=rTc[e][:], in0=P[0][:, :], scalar=bo[:, e:e + 1],
                    in1=xbc[e][:], op0=ALU.add, op1=ALU.add),
                    reads=("P0", "bo", f"xbc{e}"), writes=(f"rTc{e}",))
            emit_ln_chunk(p, rTc, [f"rTc{j}" for j in range(nD)],
                          rTc, [f"rTc{j}" for j in range(nD)], "P4", "P5",
                          ln_bufs, C, QC)
            for j in range(nD):
                p.dma("sync", out=outT_hbm[ts(j, 128), s_lo + q0:s_lo + q0 + QC],
                      in_=rTc[j][:], reads=(f"rTc{j}",), writes=(f"outT{j}",))
    p.barrier_all("sync")


    p.barrier_all("sync")


def emit_ln_chunk(p, r_tiles, rkeys, out_tiles, okeys, pk0, pk1, B, C, CH,
                  cast_to=None, ckeys=None, out_slice=None):
    nD = len(r_tiles)
    inv = 1.0 / (nD * 128)
    osl = out_slice if out_slice is not None else slice(0, CH)
    for j in range(nD):
        p.op("tensor", lambda h, j=j: h.matmul(
            out=B["ps_s0"][:, :CH], lhsT=C["ones_col"], rhs=r_tiles[j][:],
            start=(j == 0), stop=(j == nD - 1)),
            reads=(rkeys[j], "ones_col"), writes=(pk0,))
    for j in range(nD):
        p.op("scalar", lambda h, j=j: h.activation(
            out=B["r2"][:], in_=r_tiles[j][:], func=AF.Square),
            reads=(rkeys[j],), writes=("ln_r2",))
        p.op("tensor", lambda h, j=j: h.matmul(
            out=B["ps_s1"][:, :CH], lhsT=C["ones_col"], rhs=B["r2"][:],
            start=(j == 0), stop=(j == nD - 1)),
            reads=("ln_r2", "ones_col"), writes=(pk1,))
    p.op("scalar", lambda h: h.activation(
        out=B["row_m"][:], in_=B["ps_s0"][:, :CH], func=AF.Copy, scale=inv),
        reads=(pk0,), writes=("ln_rowm",))
    p.op("scalar", lambda h: h.activation(
        out=B["row_r"][:], in_=B["row_m"][:], func=AF.Square),
        reads=("ln_rowm",), writes=("ln_rowr",))
    p.op("vector", lambda h: h.scalar_tensor_tensor(
        out=B["row_r"][:], in0=B["ps_s1"][:, :CH], scalar=inv, in1=B["row_r"][:],
        op0=ALU.mult, op1=ALU.subtract),
        reads=(pk1, "ln_rowr"), writes=("ln_rowr",))
    p.op("scalar", lambda h: h.activation(
        out=B["row_r"][:], in_=B["row_r"][:], func=AF.Ln, bias=1e-5),
        reads=("ln_rowr", "cpool"), writes=("ln_rowr",))
    p.op("scalar", lambda h: h.activation(
        out=B["row_r"][:], in_=B["row_r"][:], func=AF.Exp, scale=-0.5),
        reads=("ln_rowr",), writes=("ln_rowr",))
    p.op("tensor", lambda h: h.matmul(
        out=B["bc_m"][:, :CH], lhsT=C["ones_row"], rhs=B["row_m"][:],
        start=True, stop=True),
        reads=("ln_rowm", "ones_row"), writes=(pk0,))
    p.op("tensor", lambda h: h.matmul(
        out=B["bc_r"][:, :CH], lhsT=C["ones_row"], rhs=B["row_r"][:],
        start=True, stop=True),
        reads=("ln_rowr", "ones_row"), writes=(pk1,))
    for j in range(nD):
        p.op("vector", lambda h, j=j: h.tensor_sub(
            out=out_tiles[j][:, osl], in0=r_tiles[j][:], in1=B["bc_m"][:, :CH]),
            reads=(rkeys[j], pk0), writes=(okeys[j],))
        p.op("vector", lambda h, j=j: h.tensor_mul(
            out=out_tiles[j][:, osl], in0=out_tiles[j][:, osl], in1=B["bc_r"][:, :CH]),
            reads=(okeys[j], pk1), writes=(okeys[j],))
        if cast_to is not None:
            p.op("vector", lambda h, j=j: h.tensor_copy(
                out=cast_to[j][:, osl], in_=out_tiles[j][:, osl]),
                reads=(okeys[j],), writes=(ckeys[j],))



def stream_wT_slice(p, al, hbm_ap, in_dim, o0, o1, dest, dkeys, col0, stage_f,
                    stage_b, skey):
    ow = o1 - o0
    nI = in_dim // 128
    if ow < 128:
        p.op("vector", lambda h: h.memset(stage_b[0:128, 0:in_dim], 0.0),
             writes=(skey + "b",))
    p.dma("sync", out=stage_f[0:ow, 0:in_dim], in_=hbm_ap[o0:o1, :],
          writes=(skey + "f",))
    p.op("vector", lambda h, ow=ow: h.tensor_copy(
        out=stage_b[0:ow, 0:in_dim], in_=stage_f[0:ow, 0:in_dim]),
        reads=(skey + "f",), writes=(skey + "b",))
    for i in range(nI):
        p.dma("scalar", out=dest[i][:, col0:col0 + 128],
              in_=stage_b[0:128, ts(i, 128)],
              reads=(skey + "b",), writes=(dkeys[i],), transpose=True)



def build_l0_ffn(nc, lseq=LSEQ):
    tok = NSEQ * lseq
    xT = nc.dram_tensor("xT", [D, tok], F32, kind="ExternalInput").ap()
    gate_w = nc.dram_tensor("gate_w", [E0, D], F32, kind="ExternalInput").ap()
    Wg = nc.dram_tensor("Wg", [E0, H0, D], F32, kind="ExternalInput").ap()
    W1 = nc.dram_tensor("W1", [E0, H0, D], F32, kind="ExternalInput").ap()
    W2 = nc.dram_tensor("W2", [E0, D, H0], F32, kind="ExternalInput").ap()
    outT = nc.dram_tensor("outT", [D, tok], F32, kind="ExternalOutput").ap()
    es = ExitStack()
    with es:
        al = Alloc(nc, es)
        p = Prog(nc)
        p.make_sems(es)
        emit_l0_ffn(nc, p, al, xT, gate_w, Wg, W1, W2, outT, lseq)
        block = es.enter_context(nc.Block())
        p.finish(block)
    return nc


def emit_l0_ffn(nc, p, al, xT_hbm, gate_hbm, Wg_hbm, W1_hbm, W2_hbm, outT_hbm, lseq):
    tok = NSEQ * lseq
    nD = D // 128
    HT = ceil_div(H0, 128)
    TH = tok // 2
    C = make_consts(p, al)

    P = [al.ps(f"P{i}", [128, 512], F32) for i in range(6)]

    ident_f = al.sb("ident_f", [128, 128], F32)
    p.op("gpsimd", lambda h: h.memset(ident_f[:], 0.0), writes=("ident_f",))
    p.op("gpsimd", lambda h: h.affine_select(
        out=ident_f[:], in_=ident_f[:], compare_op=ALU.not_equal, fill=1.0,
        base=0, pattern=[[-1, 128]], channel_multiplier=1),
        reads=("ident_f",), writes=("ident_f",))
    sel = [al.sb(f"sel{e}", [8, 128], BF16) for e in range(E0)]
    for e in range(E0):
        p.op("gpsimd", lambda h, e=e: h.memset(sel[e][:], 1.0), writes=(f"sel{e}",))
        p.op("gpsimd", lambda h, e=e: h.affine_select(
            out=sel[e][:], in_=sel[e][:], compare_op=ALU.is_equal, fill=0.0,
            base=-e, pattern=[[0, 128]], channel_multiplier=1),
            reads=(f"sel{e}",), writes=(f"sel{e}",))

    stage_f = al.sb("wstage_f", [128, 1408], F32)
    stage_b = al.sb("wstage_b", [128, 1408], BF16)
    gwT = [al.sb(f"gwT{j}", [128, 16], BF16) for j in range(nD)]
    p.op("vector", lambda h: h.memset(stage_b[0:16, 0:D], 0.0), writes=("wstage_b",))
    p.dma("sync", out=stage_f[0:8, 0:D], in_=gate_hbm, writes=("wstage_f",))
    p.op("vector", lambda h: h.tensor_copy(out=stage_b[0:8, 0:D], in_=stage_f[0:8, 0:D]),
         reads=("wstage_f",), writes=("wstage_b",))
    for j in range(nD):
        p.dma("scalar", out=gwT[j][:], in_=stage_b[0:16, ts(j, 128)],
              reads=("wstage_b",), writes=(f"gwT{j}",), transpose=True)

    xb = [al.sb(f"xb{j}", [128, TH], BF16) for j in range(nD)]
    nTT = TH // 128
    Ltm = al.sb("Ltm", [128, nTT * E0], F32)
    comb = al.sb("comb", [128, nTT * E0], F32)
    combE = al.sb("combE", [8, TH], BF16)
    cbc = al.sb("cbc", [128, TH], BF16)
    r_m1 = al.sb("r_m1", [128, nTT], F32)
    r_m2 = al.sb("r_m2", [128, nTT], F32)
    r_eb = al.sb("r_eb", [128, nTT * E0], F32)
    r_sc = al.sb("r_sc", [128, nTT], F32)
    r_sc2 = al.sb("r_sc2", [128, nTT], F32)

    WgTs = [al.sb(f"WgTs{j}", [128, 128], BF16) for j in range(nD)]
    W1Ts = [al.sb(f"W1Ts{j}", [128, 128], BF16) for j in range(nD)]
    HHT = 8
    w2T = [al.sb(f"w2T{j}", [128, D], BF16) for j in range(HHT)]
    gu = [al.sb(f"gu{j}", [128, TH], BF16) for j in range(HHT)]
    gs = al.sb("gs", [128, 512], BF16)
    out_acc = [al.sb(f"oacc{j}", [128, TH], BF16) for j in range(nD)]
    tmp_y = al.sb("tmp_y", [128, 512], BF16)
    rTc = [al.sb(f"rTc{j}", [128, 512], F32) for j in range(nD)]
    ln_bufs = dict(
        ps_s0=P[4][0:1, :], ps_s1=P[5][0:1, :], bc_m=P[4], bc_r=P[5],
        row_m=al.sb("ln_rowm", [1, 512], F32),
        row_r=al.sb("ln_rowr", [1, 512], F32),
        r2=al.sb("ln_r2", [128, 512], F32),
    )

    for half in range(2):
        t0 = half * TH
        for j in range(nD):
            p.dma("gpsimd", out=xb[j][:], in_=xT_hbm[ts(j, 128), t0:t0 + TH],
                  writes=(f"xb{j}",))
        for i in range(nTT):
            for j in range(nD):
                p.op("tensor", lambda h, i=i, j=j: h.matmul(
                    out=P[3][:, 0:E0], lhsT=xb[j][:, ts(i, 128)], rhs=gwT[j][:, 0:E0],
                    start=(j == 0), stop=(j == nD - 1)),
                    reads=(f"xb{j}", f"gwT{j}"), writes=("P3",))
            p.op("scalar", lambda h, i=i: h.copy(
                out=Ltm[:, i * E0:(i + 1) * E0], in_=P[3][:, 0:E0]),
                reads=("P3",), writes=("Ltm",))
        L3 = Ltm[:].rearrange("p (t e) -> p t e", e=E0)
        p.op("vector", lambda h: h.tensor_reduce(
            out=r_m1[:], in_=L3, axis=AX.X, op=ALU.max),
            reads=("Ltm",), writes=("r_m1",))
        m1b = r_m1[:].unsqueeze(2).broadcast_to([128, nTT, E0])
        p.op("vector", lambda h: h.tensor_tensor(
            out=r_eb[:].rearrange("p (t e) -> p t e", e=E0), in0=L3, in1=m1b,
            op=ALU.subtract),
            reads=("Ltm", "r_m1"), writes=("r_eb",))
        p.op("scalar", lambda h: h.activation(
            out=r_eb[:], in_=r_eb[:], func=AF.Exp),
            reads=("r_eb",), writes=("r_eb",))
        eb3 = r_eb[:].rearrange("p (t e) -> p t e", e=E0)
        p.op("vector", lambda h: h.tensor_reduce(
            out=r_sc[:], in_=eb3, axis=AX.X, op=ALU.add),
            reads=("r_eb",), writes=("r_sc",))
        p.op("vector", lambda h: h.tensor_tensor(
            out=comb[:].rearrange("p (t e) -> p t e", e=E0), in0=L3, in1=m1b,
            op=ALU.is_ge),
            reads=("Ltm", "r_m1"), writes=("comb",))
        p.op("vector", lambda h: h.tensor_scalar(
            out=comb[:], in0=comb[:], scalar1=-1e30, scalar2=None, op0=ALU.mult),
            reads=("comb",), writes=("comb",))
        p.op("vector", lambda h: h.tensor_add(
            out=comb[:], in0=comb[:], in1=Ltm[:]),
            reads=("comb", "Ltm"), writes=("comb",))
        p.op("vector", lambda h: h.tensor_reduce(
            out=r_m2[:], in_=comb[:].rearrange("p (t e) -> p t e", e=E0),
            axis=AX.X, op=ALU.max),
            reads=("comb",), writes=("r_m2",))
        m2b = r_m2[:].unsqueeze(2).broadcast_to([128, nTT, E0])
        p.op("vector", lambda h: h.tensor_tensor(
            out=comb[:].rearrange("p (t e) -> p t e", e=E0), in0=L3, in1=m2b,
            op=ALU.is_ge),
            reads=("Ltm", "r_m2"), writes=("comb",))
        p.op("vector", lambda h: h.tensor_mul(
            out=comb[:], in0=comb[:], in1=r_eb[:]),
            reads=("comb", "r_eb"), writes=("comb",))
        p.op("vector", lambda h: h.tensor_reduce(
            out=r_sc2[:], in_=comb[:].rearrange("p (t e) -> p t e", e=E0),
            axis=AX.X, op=ALU.add),
            reads=("comb",), writes=("r_sc2",))
        p.op("vector", lambda h: h.scalar_tensor_tensor(
            out=r_sc2[:], in0=r_sc[:], scalar=1e-8, in1=r_sc2[:],
            op0=ALU.mult, op1=ALU.add),
            reads=("r_sc", "r_sc2"), writes=("r_sc2",))
        p.op("vector", lambda h: h.reciprocal(out=r_sc2[:], in_=r_sc2[:]),
             reads=("r_sc2",), writes=("r_sc2",))
        recb = r_sc2[:].unsqueeze(2).broadcast_to([128, nTT, E0])
        p.op("vector", lambda h: h.tensor_tensor(
            out=comb[:].rearrange("p (t e) -> p t e", e=E0),
            in0=comb[:].rearrange("p (t e) -> p t e", e=E0), in1=recb,
            op=ALU.mult),
            reads=("comb", "r_sc2"), writes=("comb",))
        for i in range(nTT):
            p.op("tensor", lambda h, i=i: h.transpose(
                out=P[3][0:8, 0:128], in_=comb[:, i * E0:(i + 1) * E0],
                identity=ident_f[:]),
                reads=("comb", "ident_f"), writes=("P3",))
            p.op("scalar", lambda h, i=i: h.copy(
                out=combE[:, ts(i, 128)], in_=P[3][0:8, 0:128]),
                reads=("P3",), writes=("combE",))
        for j in range(nD):
            p.op("vector", lambda h, j=j: h.memset(out_acc[j][:], 0.0),
                 writes=(f"oacc{j}",))
        for e in range(E0):
            for cc in range(TH // 512):
                p.op("tensor", lambda h, e=e, cc=cc: h.matmul(
                    out=P[3][:, :], lhsT=sel[e][:], rhs=combE[:, ts(cc, 512)],
                    start=True, stop=True),
                    reads=(f"sel{e}", "combE"), writes=("P3",))
                p.op("scalar", lambda h, cc=cc: h.copy(
                    out=cbc[:, ts(cc, 512)], in_=P[3][:, :]),
                    reads=("P3",), writes=("cbc",))
            for hh in range(3):
                jtiles = list(range(hh * HHT, min(HT, (hh + 1) * HHT)))
                hlo = jtiles[0] * 128
                hhi = min(H0, (jtiles[-1] + 1) * 128)
                wpad = len(jtiles) * 128
                for o in range(nD):
                    if hhi - hlo < wpad:
                        p.op("vector", lambda h, w0=hhi - hlo, w1=wpad: h.memset(
                            stage_b[:, w0:w1], 0.0),
                            writes=("wstage_b",))
                    p.dma("sync", out=stage_f[:, 0:hhi - hlo],
                          in_=W2_hbm[e, ts(o, 128), hlo:hhi], writes=("wstage_f",))
                    p.op("vector", lambda h, w=hhi - hlo: h.tensor_copy(
                        out=stage_b[:, 0:w], in_=stage_f[:, 0:w]),
                        reads=("wstage_f",), writes=("wstage_b",))
                    for ji, j in enumerate(jtiles):
                        p.dma("scalar", out=w2T[ji][:, ts(o, 128)],
                              in_=stage_b[:, ts(ji, 128)],
                              reads=("wstage_b",), writes=(f"w2T{ji}",), transpose=True)
                for ji, j in enumerate(jtiles):
                    jhi = min(H0, j * 128 + 128)
                    stream_wT_slice(p, al, Wg_hbm[e], D, j * 128, jhi,
                                    WgTs, [f"WgTs{q}" for q in range(nD)], 0,
                                    stage_f, stage_b, "wstage_")
                    stream_wT_slice(p, al, W1_hbm[e], D, j * 128, jhi,
                                    W1Ts, [f"W1Ts{q}" for q in range(nD)], 0,
                                    stage_f, stage_b, "wstage_")
                    for cc in range(TH // 512):
                        cs = ts(cc, 512)
                        for q in range(nD):
                            p.op("tensor", lambda h, q=q, cs=cs: h.matmul(
                                out=P[0][:, :], lhsT=WgTs[q][:],
                                rhs=xb[q][:, cs], start=(q == 0), stop=(q == nD - 1)),
                                reads=(f"WgTs{q}", f"xb{q}"), writes=("P0",))
                        for q in range(nD):
                            p.op("tensor", lambda h, q=q, cs=cs: h.matmul(
                                out=P[1][:, :], lhsT=W1Ts[q][:],
                                rhs=xb[q][:, cs], start=(q == 0), stop=(q == nD - 1)),
                                reads=(f"W1Ts{q}", f"xb{q}"), writes=("P1",))
                        p.op("scalar", lambda h: h.activation(
                            out=gs[:, :], in_=P[0][:, :], func=AF.Silu),
                            reads=("P0",), writes=("gs",))
                        p.op("vector", lambda h, ji=ji, cs=cs: h.tensor_mul(
                            out=gu[ji][:, cs], in0=gs[:, :], in1=P[1][:, :]),
                            reads=("gs", "P1"), writes=(f"gu{ji}",))
                for o in range(nD):
                    for cc in range(TH // 512):
                        cs = ts(cc, 512)
                        for ji, j in enumerate(jtiles):
                            p.op("tensor", lambda h, ji=ji, o=o, cs=cs,
                                 st=(ji == 0), sp=(ji == len(jtiles) - 1): h.matmul(
                                out=P[2][:, :], lhsT=w2T[ji][:, ts(o, 128)],
                                rhs=gu[ji][:, cs], start=st, stop=sp,
                                skip_group_check=True),
                                reads=(f"w2T{ji}", f"gu{ji}"), writes=("P2",))
                        p.op("vector", lambda h, o=o, cs=cs: h.tensor_mul(
                            out=tmp_y[:], in0=P[2][:, :], in1=cbc[:, cs]),
                            reads=("P2", "cbc"), writes=("tmp_y",))
                        p.op("vector", lambda h, o=o, cs=cs: h.tensor_add(
                            out=out_acc[o][:, cs], in0=out_acc[o][:, cs], in1=tmp_y[:]),
                            reads=(f"oacc{o}", "tmp_y"), writes=(f"oacc{o}",))
        for cc in range(TH // 512):
            cs = ts(cc, 512)
            for j in range(nD):
                p.op("vector", lambda h, j=j, cs=cs: h.tensor_add(
                    out=rTc[j][:], in0=xb[j][:, cs], in1=out_acc[j][:, cs]),
                    reads=(f"xb{j}", f"oacc{j}"), writes=(f"rTc{j}",))
            emit_ln_chunk(p, rTc, [f"rTc{j}" for j in range(nD)],
                          rTc, [f"rTc{j}" for j in range(nD)], "P4", "P5",
                          ln_bufs, C, 512)
            for j in range(nD):
                p.dma("sync", out=outT_hbm[ts(j, 128), t0 + cc * 512:t0 + (cc + 1) * 512],
                      in_=rTc[j][:], reads=(f"rTc{j}",), writes=(f"outT{j}",))
    p.barrier_all("sync")



def build_l1_ffn(nc, lseq=LSEQ):
    tok = NSEQ * lseq
    xT = nc.dram_tensor("xT", [D, tok], F32, kind="ExternalInput").ap()
    rW = nc.dram_tensor("router_W", [D, E1], F32, kind="ExternalInput").ap()
    rb = nc.dram_tensor("router_b", [E1], F32, kind="ExternalInput").ap()
    sW1 = nc.dram_tensor("sW1", [D, H1], F32, kind="ExternalInput").ap()
    sb1 = nc.dram_tensor("sb1", [H1], F32, kind="ExternalInput").ap()
    sW2 = nc.dram_tensor("sW2", [H1, D], F32, kind="ExternalInput").ap()
    sb2 = nc.dram_tensor("sb2", [D], F32, kind="ExternalInput").ap()
    eW1 = nc.dram_tensor("eW1", [E1, D, H1], F32, kind="ExternalInput").ap()
    eb1 = nc.dram_tensor("eb1", [E1, H1], F32, kind="ExternalInput").ap()
    eW2 = nc.dram_tensor("eW2", [E1, H1, D], F32, kind="ExternalInput").ap()
    eb2 = nc.dram_tensor("eb2", [E1, D], F32, kind="ExternalInput").ap()
    outT = nc.dram_tensor("outT", [D, tok], F32, kind="ExternalOutput").ap()
    es = ExitStack()
    with es:
        al = Alloc(nc, es)
        p = Prog(nc)
        p.make_sems(es)
        emit_l1_ffn(nc, p, al, xT, (rW, rb, sW1, sb1, sW2, sb2, eW1, eb1, eW2, eb2),
                    outT, lseq)
        block = es.enter_context(nc.Block())
        p.finish(block)
    return nc


def emit_l1_ffn(nc, p, al, xT_hbm, W, outT_hbm, lseq):
    rW_h, rb_h, sW1_h, sb1_h, sW2_h, sb2_h, eW1_h, eb1_h, eW2_h, eb2_h = W
    tok = NSEQ * lseq
    nD = D // 128
    nH = H1 // 128
    TH = tok // 2
    nTT = TH // 128
    C = make_consts(p, al)
    P = [al.ps(f"P{i}", [128, 512], F32) for i in range(6)]

    ident_f = al.sb("ident_f", [128, 128], F32)
    p.op("gpsimd", lambda h: h.memset(ident_f[:], 0.0), writes=("ident_f",))
    p.op("gpsimd", lambda h: h.affine_select(
        out=ident_f[:], in_=ident_f[:], compare_op=ALU.not_equal, fill=1.0,
        base=0, pattern=[[-1, 128]], channel_multiplier=1),
        reads=("ident_f",), writes=("ident_f",))
    sel = [al.sb(f"sel{e}", [E1, 128], BF16) for e in range(E1)]
    for e in range(E1):
        p.op("gpsimd", lambda h, e=e: h.memset(sel[e][:], 1.0), writes=(f"sel{e}",))
        p.op("gpsimd", lambda h, e=e: h.affine_select(
            out=sel[e][:], in_=sel[e][:], compare_op=ALU.is_equal, fill=0.0,
            base=-e, pattern=[[0, 128]], channel_multiplier=1),
            reads=(f"sel{e}",), writes=(f"sel{e}",))

    rwT = [al.sb(f"rwT{j}", [128, E1], BF16) for j in range(nD)]
    stage_f = al.sb("wstage_f", [128, max(D, H1)], F32)
    rb_rowf = al.sb("rb_rowf", [1, E1], F32)
    rb_row = al.sb("rb_row", [1, E1], BF16)
    p.dma("sync", out=rb_rowf[:], in_=rb_h.unsqueeze(0), writes=("rb_rowf",))
    p.op("vector", lambda h: h.tensor_copy(out=rb_row[:], in_=rb_rowf[:]),
         reads=("rb_rowf",), writes=("rb_row",))
    for j in range(nD):
        p.dma("sync", out=stage_f[:, 0:E1], in_=rW_h[ts(j, 128), :], writes=("wstage_f",))
        p.op("vector", lambda h, j=j: h.tensor_copy(out=rwT[j][:], in_=stage_f[:, 0:E1]),
             reads=("wstage_f",), writes=(f"rwT{j}",))

    w1d = [al.sb(f"w1d{j}", [128, H1], BF16) for j in range(nD)]
    w2d = [al.sb(f"w2d{j}", [128, D], BF16) for j in range(nH)]
    b1c = al.sb("b1c", [128, nH], F32)
    b2c = al.sb("b2c", [128, nD], F32)
    hb = [al.sb(f"hb{j}", [128, TH], BF16) for j in range(nH)]
    xb = [al.sb(f"xb{j}", [128, TH], BF16) for j in range(nD)]
    out_acc = [al.sb(f"oacc{j}", [128, TH], BF16) for j in range(nD)]
    tmp_y = al.sb("tmp_y", [128, 512], BF16)
    cbc = al.sb("cbc", [128, TH], BF16)
    Ltm = al.sb("Ltm", [128, nTT * E1], F32)
    comb = al.sb("comb", [128, nTT * E1], F32)
    combE = al.sb("combE", [E1, TH], BF16)
    r_m1 = al.sb("r_m1", [128, nTT], F32)
    r_eb = al.sb("r_eb", [128, nTT * E1], F32)
    mskb = al.sb("mskb", [128, nTT * E1], F32)
    r_sc = al.sb("r_sc", [128, nTT], F32)
    rTc = [al.sb(f"rTc{j}", [128, 512], F32) for j in range(nD)]
    ln_bufs = dict(
        ps_s0=P[4][0:1, :], ps_s1=P[5][0:1, :], bc_m=P[4], bc_r=P[5],
        row_m=al.sb("ln_rowm", [1, 512], F32),
        row_r=al.sb("ln_rowr", [1, 512], F32),
        r2=al.sb("ln_r2", [128, 512], F32),
    )

    def load_direct(hbm, dest, dkeys, in_dim, width):
        for j in range(in_dim // 128):
            p.dma("sync", out=stage_f[:, 0:width], in_=hbm[ts(j, 128), :],
                  writes=("wstage_f",))
            p.op("vector", lambda h, j=j, w=width, d=dest: h.tensor_copy(
                out=d[j][:, 0:w], in_=stage_f[:, 0:w]),
                reads=("wstage_f",), writes=(dkeys[j],))

    def load_bias_cols_2d(hbm_row, name, dim, dest):
        for j in range(dim // 128):
            p.dma("sync", out=dest[:, j:j + 1], in_=hbm_row[ts(j, 128)].unsqueeze(1),
                  writes=(name,))

    for half in range(2):
        t0 = half * TH
        for j in range(nD):
            p.dma("gpsimd", out=xb[j][:], in_=xT_hbm[ts(j, 128), t0:t0 + TH],
                  writes=(f"xb{j}",))
        for i in range(nTT):
            for j in range(nD):
                p.op("tensor", lambda h, i=i, j=j: h.matmul(
                    out=P[3][:, 0:E1], lhsT=xb[j][:, ts(i, 128)], rhs=rwT[j][:],
                    start=(j == 0), stop=False),
                    reads=(f"xb{j}", f"rwT{j}"), writes=("P3",))
            p.op("tensor", lambda h: h.matmul(
                out=P[3][:, 0:E1], lhsT=C["ones_row_b"], rhs=rb_row[:],
                start=False, stop=True),
                reads=("rb_row", "ones_row_b"), writes=("P3",))
            p.op("scalar", lambda h, i=i: h.copy(
                out=Ltm[:, i * E1:(i + 1) * E1], in_=P[3][:, 0:E1]),
                reads=("P3",), writes=("Ltm",))
        L3 = Ltm[:].rearrange("p (t e) -> p t e", e=E1)
        p.op("vector", lambda h: h.tensor_reduce(
            out=r_m1[:], in_=L3, axis=AX.X, op=ALU.max),
            reads=("Ltm",), writes=("r_m1",))
        m1b = r_m1[:].unsqueeze(2).broadcast_to([128, nTT, E1])
        p.op("vector", lambda h: h.tensor_tensor(
            out=r_eb[:].rearrange("p (t e) -> p t e", e=E1), in0=L3, in1=m1b,
            op=ALU.subtract),
            reads=("Ltm", "r_m1"), writes=("r_eb",))
        p.op("scalar", lambda h: h.activation(
            out=r_eb[:], in_=r_eb[:], func=AF.Exp),
            reads=("r_eb",), writes=("r_eb",))
        eb3 = r_eb[:].rearrange("p (t e) -> p t e", e=E1)
        p.op("vector", lambda h: h.tensor_reduce(
            out=r_sc[:], in_=eb3, axis=AX.X, op=ALU.add),
            reads=("r_eb",), writes=("r_sc",))
        p.op("vector", lambda h: h.reciprocal(out=r_sc[:], in_=r_sc[:]),
             reads=("r_sc",), writes=("r_sc",))
        p.op("scalar", lambda h: h.copy(out=comb[:], in_=Ltm[:]),
             reads=("Ltm",), writes=("comb",))
        for rnd in range(2):
            p.op("vector", lambda h: h.tensor_reduce(
                out=r_m1[:], in_=comb[:].rearrange("p (t e) -> p t e", e=E1),
                axis=AX.X, op=ALU.max),
                reads=("comb",), writes=("r_m1",))
            p.op("vector", lambda h: h.tensor_tensor(
                out=mskb[:].rearrange("p (t e) -> p t e", e=E1),
                in0=comb[:].rearrange("p (t e) -> p t e", e=E1), in1=m1b,
                op=ALU.is_ge),
                reads=("comb", "r_m1"), writes=("mskb",))
            p.op("vector", lambda h: h.tensor_scalar(
                out=mskb[:], in0=mskb[:], scalar1=-1e30, scalar2=None, op0=ALU.mult),
                reads=("mskb",), writes=("mskb",))
            p.op("vector", lambda h: h.tensor_add(
                out=comb[:], in0=comb[:], in1=mskb[:]),
                reads=("comb", "mskb"), writes=("comb",))
        p.op("vector", lambda h: h.tensor_reduce(
            out=r_m1[:], in_=comb[:].rearrange("p (t e) -> p t e", e=E1),
            axis=AX.X, op=ALU.max),
            reads=("comb",), writes=("r_m1",))
        p.op("vector", lambda h: h.tensor_tensor(
            out=comb[:].rearrange("p (t e) -> p t e", e=E1), in0=L3, in1=m1b,
            op=ALU.is_ge),
            reads=("Ltm", "r_m1"), writes=("comb",))
        p.op("vector", lambda h: h.tensor_mul(
            out=comb[:], in0=comb[:], in1=r_eb[:]),
            reads=("comb", "r_eb"), writes=("comb",))
        recb = r_sc[:].unsqueeze(2).broadcast_to([128, nTT, E1])
        p.op("vector", lambda h: h.tensor_tensor(
            out=comb[:].rearrange("p (t e) -> p t e", e=E1),
            in0=comb[:].rearrange("p (t e) -> p t e", e=E1), in1=recb,
            op=ALU.mult),
            reads=("comb", "r_sc"), writes=("comb",))
        for i in range(nTT):
            p.op("tensor", lambda h, i=i: h.transpose(
                out=P[3][0:E1, 0:128], in_=comb[:, i * E1:(i + 1) * E1],
                identity=ident_f[:]),
                reads=("comb", "ident_f"), writes=("P3",))
            p.op("scalar", lambda h, i=i: h.copy(
                out=combE[:, ts(i, 128)], in_=P[3][0:E1, 0:128]),
                reads=("P3",), writes=("combE",))
        load_direct(sW1_h, w1d, [f"w1d{j}" for j in range(nD)], D, H1)
        load_direct(sW2_h, w2d, [f"w2d{j}" for j in range(nH)], H1, D)
        load_bias_cols_2d(sb1_h, "b1c", H1, b1c)
        load_bias_cols_2d(sb2_h, "b2c", D, b2c)
        for ht in range(nH):
            for cc in range(TH // 512):
                cs = ts(cc, 512)
                for q in range(nD):
                    p.op("tensor", lambda h, q=q, ht=ht, cs=cs: h.matmul(
                        out=P[0][:, :], lhsT=w1d[q][:, ts(ht, 128)], rhs=xb[q][:, cs],
                        start=(q == 0), stop=(q == nD - 1)),
                        reads=(f"w1d{q}", f"xb{q}"), writes=("P0",))
                p.op("scalar", lambda h, ht=ht, cs=cs: h.activation(
                    out=hb[ht][:, cs], in_=P[0][:, :], func=AF.Relu,
                    bias=b1c[:, ht:ht + 1]),
                    reads=("P0", "b1c"), writes=(f"hb{ht}",))
        for o in range(nD):
            for cc in range(TH // 512):
                cs = ts(cc, 512)
                for ht in range(nH):
                    p.op("tensor", lambda h, ht=ht, o=o, cs=cs: h.matmul(
                        out=P[2][:, :], lhsT=w2d[ht][:, ts(o, 128)], rhs=hb[ht][:, cs],
                        start=(ht == 0), stop=(ht == nH - 1)),
                        reads=(f"w2d{ht}", f"hb{ht}"), writes=("P2",))
                p.op("vector", lambda h, o=o, cs=cs: h.tensor_scalar(
                    out=out_acc[o][:, cs], in0=P[2][:, :], scalar1=b2c[:, o:o + 1],
                    scalar2=None, op0=ALU.add),
                    reads=("P2", "b2c"), writes=(f"oacc{o}",))
        for e in range(E1):
            for cc in range(TH // 512):
                p.op("tensor", lambda h, e=e, cc=cc: h.matmul(
                    out=P[3][:, :], lhsT=sel[e][:], rhs=combE[:, ts(cc, 512)],
                    start=True, stop=True),
                    reads=(f"sel{e}", "combE"), writes=("P3",))
                p.op("scalar", lambda h, cc=cc: h.copy(
                    out=cbc[:, ts(cc, 512)], in_=P[3][:, :]),
                    reads=("P3",), writes=("cbc",))
            load_direct(eW1_h[e], w1d, [f"w1d{j}" for j in range(nD)], D, H1)
            load_direct(eW2_h[e], w2d, [f"w2d{j}" for j in range(nH)], H1, D)
            load_bias_cols_2d(eb1_h[e], "b1c", H1, b1c)
            load_bias_cols_2d(eb2_h[e], "b2c", D, b2c)
            for ht in range(nH):
                for cc in range(TH // 512):
                    cs = ts(cc, 512)
                    for q in range(nD):
                        p.op("tensor", lambda h, q=q, ht=ht, cs=cs: h.matmul(
                            out=P[0][:, :], lhsT=w1d[q][:, ts(ht, 128)], rhs=xb[q][:, cs],
                            start=(q == 0), stop=(q == nD - 1)),
                            reads=(f"w1d{q}", f"xb{q}"), writes=("P0",))
                    p.op("scalar", lambda h, ht=ht, cs=cs: h.activation(
                        out=hb[ht][:, cs], in_=P[0][:, :], func=AF.Relu,
                        bias=b1c[:, ht:ht + 1]),
                        reads=("P0", "b1c"), writes=(f"hb{ht}",))
            for o in range(nD):
                for cc in range(TH // 512):
                    cs = ts(cc, 512)
                    for ht in range(nH):
                        p.op("tensor", lambda h, ht=ht, o=o, cs=cs: h.matmul(
                            out=P[2][:, :], lhsT=w2d[ht][:, ts(o, 128)], rhs=hb[ht][:, cs],
                            start=(ht == 0), stop=(ht == nH - 1)),
                            reads=(f"w2d{ht}", f"hb{ht}"), writes=("P2",))
                    p.op("vector", lambda h, o=o, cs=cs: h.scalar_tensor_tensor(
                        out=tmp_y[:], in0=P[2][:, :], scalar=b2c[:, o:o + 1],
                        in1=cbc[:, cs], op0=ALU.add, op1=ALU.mult),
                        reads=("P2", "b2c", "cbc"), writes=("tmp_y",))
                    p.op("vector", lambda h, o=o, cs=cs: h.tensor_add(
                        out=out_acc[o][:, cs], in0=out_acc[o][:, cs], in1=tmp_y[:]),
                        reads=(f"oacc{o}", "tmp_y"), writes=(f"oacc{o}",))
        for cc in range(TH // 512):
            cs = ts(cc, 512)
            for j in range(nD):
                p.op("vector", lambda h, j=j, cs=cs: h.tensor_add(
                    out=rTc[j][:], in0=xb[j][:, cs], in1=out_acc[j][:, cs]),
                    reads=(f"xb{j}", f"oacc{j}"), writes=(f"rTc{j}",))
            emit_ln_chunk(p, rTc, [f"rTc{j}" for j in range(nD)],
                          rTc, [f"rTc{j}" for j in range(nD)], "P4", "P5",
                          ln_bufs, C, 512)
            for j in range(nD):
                p.dma("sync", out=outT_hbm[ts(j, 128), t0 + cc * 512:t0 + (cc + 1) * 512],
                      in_=rTc[j][:], reads=(f"rTc{j}",), writes=(f"outT{j}",))
    p.barrier_all("sync")



def build_l1_attn(nc, lseq=LSEQ):
    tok = NSEQ * lseq
    xT = nc.dram_tensor("xT", [D, tok], F32, kind="ExternalInput").ap()
    W_in = nc.dram_tensor("W_in", [D, 2 * DIN], F32, kind="ExternalInput").ap()
    conv_w = nc.dram_tensor("conv_w", [DIN, DCONV], F32, kind="ExternalInput").ap()
    conv_b = nc.dram_tensor("conv_b", [DIN], F32, kind="ExternalInput").ap()
    W_x = nc.dram_tensor("W_x", [DIN, 2 * DST + 1], F32, kind="ExternalInput").ap()
    A_log = nc.dram_tensor("A_log", [DIN, DST], F32, kind="ExternalInput").ap()
    dt_bias = nc.dram_tensor("dt_bias", [DIN], F32, kind="ExternalInput").ap()
    D_skip = nc.dram_tensor("D_skip", [DIN], F32, kind="ExternalInput").ap()
    W_out = nc.dram_tensor("W_out", [DIN, D], F32, kind="ExternalInput").ap()
    outT = nc.dram_tensor("outT", [D, tok], F32, kind="ExternalOutput").ap()
    es = ExitStack()
    with es:
        al = Alloc(nc, es)
        p = Prog(nc)
        p.make_sems(es)
        emit_l1_attn(nc, p, al, xT,
                     (W_in, conv_w, conv_b, W_x, A_log, dt_bias, D_skip, W_out),
                     outT, lseq)
        block = es.enter_context(nc.Block())
        p.finish(block)
    return nc


def emit_l1_attn(nc, p, al, xT_hbm, W, outT_hbm, lseq):
    (Win_h, convw_h, convb_h, Wx_h, Alog_h, dtb_h, Dskip_h, Wout_h) = W
    nD = D // 128
    nI = DIN // 128
    TC = 128
    TP = 2
    NCH = lseq // TC
    C = make_consts(p, al)
    P = [al.ps(f"P{i}", [128, 512], F32) for i in range(2)]
    ps_y = al.ps("ps_y", [128, 2048], F32)
    ps_tf = al.ps("ps_tf", [128, 512], F32)
    ps_tb = al.ps("ps_tb", [128, 1024], BF16)

    ident_f = al.sb("ident_f", [128, 128], F32)
    p.op("gpsimd", lambda h: h.memset(ident_f[:], 0.0), writes=("ident_f",))
    p.op("gpsimd", lambda h: h.affine_select(
        out=ident_f[:], in_=ident_f[:], compare_op=ALU.not_equal, fill=1.0,
        base=0, pattern=[[-1, 128]], channel_multiplier=1),
        reads=("ident_f",), writes=("ident_f",))
    ident_b = al.sb("ident_b", [128, 128], BF16)
    p.op("gpsimd", lambda h: h.memset(ident_b[:], 0.0), writes=("ident_b",))
    p.op("gpsimd", lambda h: h.affine_select(
        out=ident_b[:], in_=ident_b[:], compare_op=ALU.not_equal, fill=1.0,
        base=0, pattern=[[-1, 128]], channel_multiplier=1),
        reads=("ident_b",), writes=("ident_b",))

    stage_f = al.sb("stg_f", [128, 2048], F32)

    cwc = al.sb("cwc", [128, nI * DCONV], F32)
    for j in range(nI):
        for k in range(DCONV):
            p.dma("sync", out=cwc[:, j * DCONV + k:j * DCONV + k + 1],
                  in_=convw_h[ts(j, 128), k].unsqueeze(1), writes=("cwc",), slow=True)
    cbc_ = al.sb("cbc_", [128, nI], F32)
    dbc = al.sb("dbc", [128, nI], F32)
    skc = al.sb("skc", [128, nI], F32)
    for j in range(nI):
        p.dma("sync", out=cbc_[:, j:j + 1], in_=convb_h[ts(j, 128)].unsqueeze(1),
              writes=("cbc_",))
        p.dma("sync", out=dbc[:, j:j + 1], in_=dtb_h[ts(j, 128)].unsqueeze(1),
              writes=("dbc",))
        p.dma("sync", out=skc[:, j:j + 1], in_=Dskip_h[ts(j, 128)].unsqueeze(1),
              writes=("skc",))

    A_bc = al.sb("A_bc", [128, DIN], BF16)
    atile = al.sb("atile", [128, DST], F32)
    for j in range(nI):
        p.dma("sync", out=stage_f[:, 0:DST], in_=Alog_h[ts(j, 128), :],
              writes=("stg_f",))
        p.op("scalar", lambda h: h.activation(
            out=atile[:], in_=stage_f[:, 0:DST], func=AF.Exp),
            reads=("stg_f",), writes=("atile",))
        p.op("vector", lambda h: h.tensor_scalar(
            out=atile[:], in0=atile[:], scalar1=-1.0, scalar2=None, op0=ALU.mult),
            reads=("atile",), writes=("atile",))
        p.op("tensor", lambda h, j=j: h.transpose(
            out=ps_tf[0:DST, 0:128], in_=atile[:], identity=ident_f[:]),
            reads=("atile", "ident_f"), writes=("ps_tf",))
        p.op("vector", lambda h, j=j: h.tensor_copy(
            out=A_bc[0:DST, ts(j, 128)], in_=ps_tf[0:DST, 0:128]),
            reads=("ps_tf",), writes=("A_bc",))
    p.op("vector", lambda h: h.tensor_copy(
        out=A_bc[64:128, :], in_=A_bc[0:64, :]),
        reads=("A_bc",), writes=("A_bc",))

    wxT = [al.sb(f"wxT{j}", [128, 2 * DST + 1], BF16) for j in range(nI)]
    for j in range(nI):
        p.dma("sync", out=stage_f[:, 0:2 * DST + 1], in_=Wx_h[ts(j, 128), :],
              writes=("stg_f",))
        p.op("vector", lambda h, j=j: h.tensor_copy(
            out=wxT[j][:], in_=stage_f[:, 0:2 * DST + 1]),
            reads=("stg_f",), writes=(f"wxT{j}",))

    T2 = 2 * TC
    xb = [al.sb(f"xb{j}", [128, T2], BF16) for j in range(nD)]
    xs = [al.sb(f"xs{j}", [128, 2 * (TC + 3)], BF16) for j in range(nI)]
    zt = [al.sb(f"zt{j}", [128, T2], BF16) for j in range(nI)]
    xc = [al.sb(f"xc{j}", [128, T2], BF16) for j in range(nI)]
    dt16 = [al.sb(f"dt16{j}", [128, T2], BF16) for j in range(nI)]
    dtx16 = [al.sb(f"dtx16{j}", [128, T2], BF16) for j in range(nI)]
    dtT = [al.sb(f"dtT{r}", [128, DIN], BF16) for r in range(2)]
    dtxT = [al.sb(f"dtxT{r}", [128, DIN], BF16) for r in range(2)]
    dtF = al.sb("dtF", [2, 2 * DIN], BF16)
    dtxF = al.sb("dtxF", [2, 2 * DIN], BF16)
    B_stk = al.sb("B_stk", [128, TC], F32)
    C_stk = al.sb("C_stk", [128, TC], F32)
    Csel = al.sb("Csel", [128, 128], F32)
    r_bc = al.sb("r_bc", [128, T2], F32)
    hst = al.sb("hst", [128, DIN], F32)
    dt_rep = al.sb("dt_rep", [128, TP * DIN], BF16)
    dtx_rep = al.sb("dtx_rep", [128, TP * DIN], BF16)
    g_rep = al.sb("g_rep", [128, TP * DIN], BF16)
    yT = [al.sb(f"yT{j}", [128, T2], BF16) for j in range(nI)]
    yv = al.sb("yv", [128, DIN], BF16)
    scr1 = al.sb("scr1", [128, T2], F32)
    scr2 = al.sb("scr2", [128, T2], F32)
    stage_b2 = al.sb("stage_b2", [128, 2048], BF16)
    zs = al.sb("zs", [128, T2], BF16)
    rTc = [al.sb(f"rTc{j}", [128, T2], F32) for j in range(nD)]
    ln_bufs = dict(
        ps_s0=P[0][0:1, :], ps_s1=P[1][0:1, :], bc_m=P[0], bc_r=P[1],
        row_m=al.sb("ln_rowm", [1, T2], F32),
        row_r=al.sb("ln_rowr", [1, T2], F32),
        r2=al.sb("ln_r2", [128, T2], F32),
    )

    p.op("vector", lambda h: h.memset(Csel[:], 0.0), writes=("Csel",))

    for j in range(nI):
        p.op("vector", lambda h, j=j: h.memset(xs[j][:], 0.0), writes=(f"xs{j}",))

    for ch in range(NCH):
        t0 = ch * TC
        if ch == 0:
            p.op("vector", lambda h: h.memset(hst[:], 0.0), writes=("hst",))
        for j in range(nD):
            for s in range(NSEQ):
                p.dma("gpsimd", out=xb[j][:, s * TC:(s + 1) * TC],
                      in_=xT_hbm[ts(j, 128), s * lseq + t0:s * lseq + t0 + TC],
                      writes=(f"xb{j}",))
        emit_l1_pre(p, al, nc, Win_h, xb, xs, zt, xc, cwc, cbc_, wxT, dt16,
                    dtx16, dtT, dtxT, B_stk, C_stk, r_bc, dbc, stage_f, stage_b2,
                    ident_b, ident_f, ps_tf, ps_tb, P, C, scr1, scr2, ch, nI, nD, TC)
        emit_l1_scan(p, al, nc, A_bc, dt_rep, dtx_rep, g_rep, dtT, dtxT, dtF, dtxF,
                     B_stk, C_stk, Csel, hst, ps_y, ps_tb, yT, yv, ident_b, ch, TC, TP, nI)
        emit_l1_post(p, al, nc, Wout_h, yT, xc, zt, zs, skc, xb, rTc, stage_f,
                     stage_b2, P, C, ln_bufs, outT_hbm, lseq, t0, nI, nD, TC)
    p.barrier_all("sync")


def emit_l1_pre(p, al, nc, Win_h, xb, xs, zt, xc, cwc, cbc_, wxT, dt16,
                dtx16, dtT, dtxT, B_stk, C_stk, r_bc, dbc, stage_f, stage_b2,
                ident_b, ident_f, ps_tf, ps_tb, P, C, scr1, scr2, ch, nI, nD, TC):
    T2 = 2 * TC
    if ch > 0:
        for j in range(nI):
            p.op("vector", lambda h, j=j: h.tensor_copy(
                out=xs[j][:].rearrange("p (s c) -> p s c", s=2)[:, :, 0:3],
                in_=xs[j][:].rearrange("p (s c) -> p s c", s=2)[:, :, TC:TC + 3]),
                reads=(f"xs{j}",), writes=(f"xs{j}",))
    for it in range(2 * nI):
        src = Win_h[:, ts(it, 128)].rearrange("(q p) c -> p q c", p=128)
        p.dma("sync", out=stage_f[:, 0:nD * 128].rearrange("p (q c) -> p q c", q=nD),
              in_=src, writes=("stg_f",))
        p.op("vector", lambda h: h.tensor_copy(
            out=stage_b2[:, 0:nD * 128], in_=stage_f[:, 0:nD * 128]),
            reads=("stg_f",), writes=("stage_b2",))
        for q in range(nD):
            p.op("tensor", lambda h, q=q: h.matmul(
                out=P[0][:, 0:T2], lhsT=stage_b2[:, ts(q, 128)], rhs=xb[q][:, 0:T2],
                start=(q == 0), stop=(q == nD - 1)),
                reads=("stage_b2", f"xb{q}"), writes=("P0",))
        if it < nI:
            xsv = xs[it][:].rearrange("p (s c) -> p s c", s=2)[:, :, 3:3 + TC]
            psv = P[0][:, 0:T2].rearrange("p (s c) -> p s c", s=2)
            p.op("scalar", lambda h, it=it, xsv=xsv, psv=psv: h.copy(
                out=xsv, in_=psv),
                reads=("P0",), writes=(f"xs{it}",))
        else:
            p.op("scalar", lambda h, it=it: h.copy(
                out=zt[it - nI][:], in_=P[0][:, 0:T2]),
                reads=("P0",), writes=(f"zt{it - nI}",))
    for j in range(nI):
        x3 = xs[j][:].rearrange("p (s c) -> p s c", s=2)
        p.op("vector", lambda h, j=j, x3=x3: h.tensor_scalar(
            out=scr1[:].rearrange("p (s t) -> p s t", s=2),
            in0=x3[:, :, 0:TC], scalar1=cwc[:, j * DCONV:j * DCONV + 1],
            scalar2=None, op0=ALU.mult),
            reads=(f"xs{j}", "cwc"), writes=("scr1",))
        for k in range(1, DCONV):
            p.op("vector", lambda h, j=j, k=k, x3=x3: h.scalar_tensor_tensor(
                out=scr1[:].rearrange("p (s t) -> p s t", s=2),
                in0=x3[:, :, k:k + TC], scalar=cwc[:, j * DCONV + k:j * DCONV + k + 1],
                in1=scr1[:].rearrange("p (s t) -> p s t", s=2),
                op0=ALU.mult, op1=ALU.add),
                reads=(f"xs{j}", "cwc", "scr1"), writes=("scr1",))
        p.op("scalar", lambda h, j=j: h.activation(
            out=xc[j][:], in_=scr1[:], func=AF.Silu, bias=cbc_[:, j:j + 1]),
            reads=("scr1", "cbc_"), writes=(f"xc{j}",))
    for q in range(nI):
        p.op("tensor", lambda h, q=q: h.matmul(
            out=P[1][:, 0:T2], lhsT=wxT[q][:, 0:2 * DST], rhs=xc[q][:, 0:T2],
            start=(q == 0), stop=(q == nI - 1)),
            reads=(f"wxT{q}", f"xc{q}"), writes=("P1",))
    for q in range(nI):
        p.op("tensor", lambda h, q=q: h.matmul(
            out=ps_tf[0:1, 0:T2], lhsT=wxT[q][:, 2 * DST:2 * DST + 1],
            rhs=xc[q][:, 0:T2],
            start=(q == 0), stop=(q == nI - 1)),
            reads=(f"wxT{q}", f"xc{q}"), writes=("ps_tf",))
    for s in range(2):
        p.op("vector", lambda h, s=s: h.tensor_copy(
            out=B_stk[s * DST:(s + 1) * DST, :], in_=P[1][0:DST, s * TC:(s + 1) * TC]),
            reads=("P1",), writes=("B_stk",))
        p.op("vector", lambda h, s=s: h.tensor_copy(
            out=C_stk[s * DST:(s + 1) * DST, :], in_=P[1][DST:2 * DST, s * TC:(s + 1) * TC]),
            reads=("P1",), writes=("C_stk",))
    rrow = scr2
    p.op("scalar", lambda h: h.copy(out=scr2[0:1, 0:T2], in_=ps_tf[0:1, 0:T2]),
         reads=("ps_tf",), writes=("scr2",))
    p.op("tensor", lambda h: h.matmul(
        out=ps_tf[:, 0:T2], lhsT=C["ones_row"], rhs=scr2[0:1, 0:T2],
        start=True, stop=True),
        reads=("scr2", "ones_row"), writes=("ps_tf",))
    p.op("vector", lambda h: h.tensor_copy(out=r_bc[:, 0:T2], in_=ps_tf[:, 0:T2]),
         reads=("ps_tf",), writes=("r_bc",))
    for j in range(nI):
        p.op("scalar", lambda h, j=j: h.activation(
            out=scr1[:], in_=r_bc[:, 0:T2], func=AF.Abs, bias=dbc[:, j:j + 1]),
            reads=("r_bc", "dbc"), writes=("scr1",))
        p.op("scalar", lambda h: h.activation(
            out=scr1[:], in_=scr1[:], func=AF.Exp, scale=-1.0),
            reads=("scr1",), writes=("scr1",))
        p.op("scalar", lambda h: h.activation(
            out=scr1[:], in_=scr1[:], func=AF.Ln, bias=1.0),
            reads=("scr1", "cpool"), writes=("scr1",))
        p.op("scalar", lambda h, j=j: h.activation(
            out=scr2[:], in_=r_bc[:, 0:T2], func=AF.Relu, bias=dbc[:, j:j + 1]),
            reads=("r_bc", "dbc"), writes=("scr2",))
        p.op("vector", lambda h, j=j: h.tensor_add(
            out=dt16[j][:], in0=scr1[:], in1=scr2[:]),
            reads=("scr1", "scr2"), writes=(f"dt16{j}",))
        p.op("vector", lambda h, j=j: h.tensor_mul(
            out=dtx16[j][:], in0=dt16[j][:], in1=xc[j][:]),
            reads=(f"dt16{j}", f"xc{j}"), writes=(f"dtx16{j}",))
    for j in range(nI):
        for cb in range(2):
            p.op("tensor", lambda h, j=j, cb=cb: h.transpose(
                out=ps_tb[:, 0:128], in_=dt16[j][:, ts(cb, 128)], identity=ident_b[:]),
                reads=(f"dt16{j}", "ident_b"), writes=("ps_tb",))
            p.op("vector", lambda h, j=j, cb=cb: h.tensor_copy(
                out=dtT[cb][:, ts(j, 128)], in_=ps_tb[:, 0:128]),
                reads=("ps_tb",), writes=(f"dtT{cb}",))
            p.op("tensor", lambda h, j=j, cb=cb: h.transpose(
                out=ps_tb[:, 0:128], in_=dtx16[j][:, ts(cb, 128)], identity=ident_b[:]),
                reads=(f"dtx16{j}", "ident_b"), writes=("ps_tb",))
            p.op("vector", lambda h, j=j, cb=cb: h.tensor_copy(
                out=dtxT[cb][:, ts(j, 128)], in_=ps_tb[:, 0:128]),
                reads=("ps_tb",), writes=(f"dtxT{cb}",))


def emit_l1_scan(p, al, nc, A_bc, dt_rep, dtx_rep, g_rep, dtT, dtxT, dtF, dtxF,
                 B_stk, C_stk, Csel, hst, ps_y, ps_tb, yT, yv, ident_b, ch, TC, TP, nI):
    DINl = DIN
    for grp in range(TC // 64):
        for sub in range(64 // TP):
            tb = grp * 64 + sub * TP
            for s in range(2):
                p.dma("gpsimd", out=dtF[s:s + 1, 0:TP * DINl],
                      in_=dtT[s][tb:tb + TP, :],
                      reads=(f"dtT{s}",), writes=("dtF",))
                p.dma("gpsimd", out=dtxF[s:s + 1, 0:TP * DINl],
                      in_=dtxT[s][tb:tb + TP, :],
                      reads=(f"dtxT{s}",), writes=("dtxF",))
            p.dma("gpsimd", out=dt_rep[:],
                  in_=dtF[:, 0:TP * DINl].unsqueeze(1).broadcast_to([2, 64, TP * DINl]),
                  reads=("dtF",), writes=("dt_rep",))
            p.dma("gpsimd", out=dtx_rep[:],
                  in_=dtxF[:, 0:TP * DINl].unsqueeze(1).broadcast_to([2, 64, TP * DINl]),
                  reads=("dtxF",), writes=("dtx_rep",))
            p.op("vector", lambda h: h.tensor_tensor(
                out=g_rep[:].rearrange("p (t d) -> p t d", t=TP),
                in0=A_bc[:].unsqueeze(1).broadcast_to([128, TP, DINl]),
                in1=dt_rep[:].rearrange("p (t d) -> p t d", t=TP),
                op=ALU.mult),
                reads=("A_bc", "dt_rep"), writes=("g_rep",))
            p.op("scalar", lambda h: h.activation(
                out=g_rep[:], in_=g_rep[:], func=AF.Exp),
                reads=("g_rep",), writes=("g_rep",))
            for tau in range(TP):
                t = tb + tau
                tg = t - grp * 64
                pz = (tg - 1) % 64
                p.op("gpsimd", lambda h, pz=pz: h.memset(Csel[:, pz:pz + 1], 0.0),
                     reads=(), writes=("Csel",))
                p.op("gpsimd", lambda h, pz=pz: h.memset(Csel[:, 64 + pz:64 + pz + 1], 0.0),
                     reads=(), writes=("Csel",))
                p.op("gpsimd", lambda h, t=t, tg=tg: h.tensor_copy(
                    out=Csel[0:64, tg:tg + 1], in_=C_stk[0:64, t:t + 1]),
                    reads=("C_stk",), writes=("Csel",))
                p.op("gpsimd", lambda h, t=t, tg=tg: h.tensor_copy(
                    out=Csel[64:128, 64 + tg:64 + tg + 1], in_=C_stk[64:128, t:t + 1]),
                    reads=("C_stk",), writes=("Csel",))
                p.op("vector", lambda h, tau=tau: h.tensor_mul(
                    out=hst[:], in0=g_rep[:, tau * DINl:(tau + 1) * DINl], in1=hst[:]),
                    reads=("g_rep", "hst"), writes=("hst",))
                p.op("vector", lambda h, tau=tau, t=t: h.scalar_tensor_tensor(
                    out=hst[:], in0=dtx_rep[:, tau * DINl:(tau + 1) * DINl],
                    scalar=B_stk[:, t:t + 1], in1=hst[:],
                    op0=ALU.mult, op1=ALU.add),
                    reads=("dtx_rep", "B_stk", "hst"), writes=("hst",))
                for q4 in range(4):
                    p.op("tensor", lambda h, q4=q4, tg=tg: h.matmul(
                        out=ps_y[:, ts(q4, 512)], lhsT=Csel[:],
                        rhs=hst[:, ts(q4, 512)],
                        start=(tg == 0), stop=(tg == 63), skip_group_check=True),
                        reads=("Csel", "hst"), writes=("ps_y",))
        p.op("scalar", lambda h: h.copy(out=yv[:], in_=ps_y[:, :]),
             reads=("ps_y",), writes=("yv",))
        for jb in range(nI):
            p.op("tensor", lambda h, jb=jb: h.transpose(
                out=ps_tb[:, 0:128], in_=yv[:, ts(jb, 128)], identity=ident_b[:]),
                reads=("yv", "ident_b"), writes=("ps_tb",))
            ov = yT[jb][:].rearrange("p (s t) -> p s t", s=2)[:, :, grp * 64:grp * 64 + 64]
            iv = ps_tb[:, 0:128].rearrange("p (s t) -> p s t", s=2)
            p.op("vector", lambda h, ov=ov, iv=iv: h.tensor_copy(out=ov, in_=iv),
                 reads=("ps_tb",), writes=(f"yT{jb}",))


def emit_l1_post(p, al, nc, Wout_h, yT, xc, zt, zs, skc, xb, rTc, stage_f,
                 stage_b2, P, C, ln_bufs, outT_hbm, lseq, t0, nI, nD, TC):
    T2 = 2 * TC
    for j in range(nI):
        p.op("vector", lambda h, j=j: h.scalar_tensor_tensor(
            out=yT[j][:], in0=xc[j][:], scalar=skc[:, j:j + 1], in1=yT[j][:],
            op0=ALU.mult, op1=ALU.add),
            reads=(f"xc{j}", "skc", f"yT{j}"), writes=(f"yT{j}",))
        p.op("scalar", lambda h, j=j: h.activation(
            out=zs[:], in_=zt[j][:], func=AF.Silu),
            reads=(f"zt{j}",), writes=("zs",))
        p.op("vector", lambda h, j=j: h.tensor_mul(
            out=yT[j][:], in0=yT[j][:], in1=zs[:]),
            reads=(f"yT{j}", "zs"), writes=(f"yT{j}",))
    for o in range(nD):
        src = Wout_h[:, ts(o, 128)].rearrange("(q p) c -> p q c", p=128)
        p.dma("sync", out=stage_f[:, 0:nI * 128].rearrange("p (q c) -> p q c", q=nI),
              in_=src, writes=("stg_f",))
        p.op("vector", lambda h: h.tensor_copy(
            out=stage_b2[:, 0:nI * 128], in_=stage_f[:, 0:nI * 128]),
            reads=("stg_f",), writes=("stage_b2",))
        for q in range(nI):
            p.op("tensor", lambda h, q=q, o=o: h.matmul(
                out=P[1][:, 0:T2], lhsT=stage_b2[:, ts(q, 128)], rhs=yT[q][:],
                start=(q == 0), stop=(q == nI - 1)),
                reads=("stage_b2", f"yT{q}"), writes=("P1",))
        p.op("vector", lambda h, o=o: h.tensor_add(
            out=rTc[o][:], in0=P[1][:, 0:T2], in1=xb[o][:, 0:T2]),
            reads=("P1", f"xb{o}"), writes=(f"rTc{o}",))
    emit_ln_chunk(p, rTc, [f"rTc{j}" for j in range(nD)],
                  rTc, [f"rTc{j}" for j in range(nD)], "P0", "P1",
                  ln_bufs, C, T2)
    for j in range(nD):
        for s in range(2):
            p.dma("sync", out=outT_hbm[ts(j, 128), s * lseq + t0:s * lseq + t0 + TC],
                  in_=rTc[j][:, s * TC:(s + 1) * TC],
                  reads=(f"rTc{j}",), writes=(f"outT{j}",))



_NC_CACHE = {}


def _get_nc(name, builder):
    if name not in _NC_CACHE:
        nc = bass.Bass("TRN2", target_bir_lowering=False, debug=False)
        builder(nc)
        _NC_CACHE[name] = nc
    return _NC_CACHE[name]


def kernel(**inp):
    n_cores = 8
    x = np.asarray(inp["L0_attn_x"])
    B_all = x.shape[0]
    per = B_all // n_cores

    xTs = [np.ascontiguousarray(
        x[c * per:(c + 1) * per].transpose(2, 0, 1).reshape(D, per * LSEQ))
        for c in range(n_cores)]

    ncA = _get_nc("A", build_l0_attn)
    w = {k: np.ascontiguousarray(np.asarray(inp[v]))
         for k, v in [("wq", "L0_attn_wq_w"), ("wk", "L0_attn_wk_w"),
                      ("wv", "L0_attn_wv_w"), ("wo", "L0_attn_wo_w"),
                      ("bq", "L0_attn_wq_b"), ("bk", "L0_attn_wk_b"),
                      ("bv", "L0_attn_wv_b"), ("bo", "L0_attn_wo_b")]}
    maps = [dict(xT=xTs[c], **w) for c in range(n_cores)]
    res = run_bass_kernel_spmd(ncA, maps, core_ids=list(range(n_cores)))
    x1 = [res.results[c]["outT"] for c in range(n_cores)]

    ncB = _get_nc("B", build_l0_ffn)
    w = {"gate_w": np.asarray(inp["L0_ffn_gate_w"]),
         "Wg": np.asarray(inp["L0_ffn_Wg"]),
         "W1": np.asarray(inp["L0_ffn_W1"]),
         "W2": np.asarray(inp["L0_ffn_W2"])}
    maps = [dict(xT=x1[c], **w) for c in range(n_cores)]
    res = run_bass_kernel_spmd(ncB, maps, core_ids=list(range(n_cores)))
    x2 = [res.results[c]["outT"] for c in range(n_cores)]

    ncC = _get_nc("C", build_l1_attn)
    w = {"W_in": np.asarray(inp["L1_attn_W_in"]),
         "conv_w": np.asarray(inp["L1_attn_conv_w"]),
         "conv_b": np.asarray(inp["L1_attn_conv_b"]),
         "W_x": np.asarray(inp["L1_attn_W_x"]),
         "A_log": np.asarray(inp["L1_attn_A_log"]),
         "dt_bias": np.asarray(inp["L1_attn_dt_bias"]),
         "D_skip": np.asarray(inp["L1_attn_D"]),
         "W_out": np.asarray(inp["L1_attn_W_out"])}
    maps = [dict(xT=x2[c], **w) for c in range(n_cores)]
    res = run_bass_kernel_spmd(ncC, maps, core_ids=list(range(n_cores)))
    x3 = [res.results[c]["outT"] for c in range(n_cores)]

    ncD = _get_nc("D", build_l1_ffn)
    w = {"router_W": np.asarray(inp["L1_ffn_router_W"]),
         "router_b": np.asarray(inp["L1_ffn_router_b"]),
         "sW1": np.asarray(inp["L1_ffn_sW1"]),
         "sb1": np.asarray(inp["L1_ffn_sb1"]),
         "sW2": np.asarray(inp["L1_ffn_sW2"]),
         "sb2": np.asarray(inp["L1_ffn_sb2"]),
         "eW1": np.asarray(inp["L1_ffn_eW1"]),
         "eb1": np.asarray(inp["L1_ffn_eb1"]),
         "eW2": np.asarray(inp["L1_ffn_eW2"]),
         "eb2": np.asarray(inp["L1_ffn_eb2"])}
    maps = [dict(xT=x3[c], **w) for c in range(n_cores)]
    res = run_bass_kernel_spmd(ncD, maps, core_ids=list(range(n_cores)))
    x4 = [res.results[c]["outT"] for c in range(n_cores)]

    out = np.concatenate(
        [x4[c].reshape(D, per, LSEQ).transpose(1, 2, 0) for c in range(n_cores)],
        axis=0)
    return out.astype(np.float32)


# version 07
# speedup vs baseline: 1.6948x; 1.6948x over previous
import numpy as np
from contextlib import ExitStack

import concourse.bass as bass
import concourse.mybir as mybir
from concourse.bass_utils import run_bass_kernel_spmd

F32 = mybir.dt.float32
BF16 = mybir.dt.bfloat16
AF = mybir.ActivationFunctionType
ALU = mybir.AluOpType
AX = mybir.AxisListType

D = 1024
NSEQ = 2
LSEQ = 2048
TOK = NSEQ * LSEQ
NH = 16
DH = 64
E0, H0 = 8, 2736
E1, H1 = 15, 1024
DIN = 2048
DST = 64
DCONV = 4

ENGS = ("sync", "vector", "scalar", "tensor", "gpsimd")


class Prog:

    def __init__(self, nc):
        self.nc = nc
        self.ops = {e: [] for e in ENGS}
        self.icount = {e: 0 for e in ENGS}
        self.dcount = {e: 0 for e in ENGS}
        self.dslot = {}
        self.writers = {}
        self.readers = {}
        self._gid = 0
        self.hiw = {e: {} for e in ENGS}
        self.sems = {}
        self._es = None

    NDMA_SEMS = 8

    def make_sems(self, es):
        for e in ENGS:
            self.sems[("c", e)] = es.enter_context(self.nc.semaphore(f"sc_{e}"))
        for e in ("sync", "scalar", "gpsimd"):
            for s in range(self.NDMA_SEMS):
                self.sems[("d", e, s)] = es.enter_context(self.nc.semaphore(f"sd_{e}{s}"))

    def new_gid(self):
        self._gid += 1
        return self._gid

    def _dep_waits(self, eng, reads, writes, wgroup=None):
        need = {}
        def add(semkey, cnt):
            if need.get(semkey, 0) < cnt:
                need[semkey] = cnt
        for k in reads:
            w = self.writers.get(k)
            if w:
                for semkey, cnt in w["members"].items():
                    add(semkey, cnt)
        for k in writes:
            w = self.writers.get(k)
            if w and (wgroup is None or w["gid"] != wgroup):
                for semkey, cnt in w["members"].items():
                    add(semkey, cnt)
            if not (w and wgroup is not None and w["gid"] == wgroup):
                for semkey, cnt in self.readers.get(k, {}).items():
                    add(semkey, cnt)
        waits = []
        hw = self.hiw[eng]
        for semkey, cnt in sorted(need.items(), key=str):
            if semkey == ("c", eng) and cnt > self.icount[eng]:
                continue
            if semkey[0] == "d":
                tgt = 16 * self.dslot[(semkey[1], semkey[2])]
            else:
                tgt = cnt
            if hw.get(semkey, 0) < tgt:
                hw[semkey] = tgt
                waits.append((self.sems[semkey], tgt))
        return waits

    def _record_access(self, semkey, cnt, reads, writes, wgroup=None):
        for k in reads:
            self.readers.setdefault(k, {})[semkey] = cnt
        for k in writes:
            w = self.writers.get(k)
            if w and wgroup is not None and w["gid"] == wgroup:
                m = w["members"]
                if m.get(semkey, 0) < cnt:
                    m[semkey] = cnt
            else:
                self.writers[k] = {"gid": wgroup if wgroup is not None else self.new_gid(),
                                   "members": {semkey: cnt}}
                self.readers[k] = {}

    def op(self, eng, fn, reads=(), writes=(), inc=True, tag=None, wgroup=None):
        waits = self._dep_waits(eng, reads, writes, wgroup)
        cnt = self.icount[eng] + 1
        if inc:
            self.icount[eng] = cnt
        sem = self.sems[("c", eng)]
        def emit(h, waits=waits, fn=fn, inc=inc, sem=sem, tag=tag):
            for s, v in waits:
                h.wait_ge(s, v)
            ins = fn(h)
            if inc:
                ins.then_inc(sem, 1)
            if tag:
                ins.annotate(tag)
        self.ops[eng].append(emit)
        self._record_access(("c", eng), cnt, reads, writes, wgroup)

    def dma(self, eng, out, in_, reads=(), writes=(), tag=None, transpose=False,
            slow=False, wgroup=None):
        waits = self._dep_waits(eng, reads, writes, wgroup)
        slot = self.dcount[eng] % self.NDMA_SEMS
        self.dcount[eng] += 1
        key = ("d", eng, slot)
        prev = self.dslot.setdefault((eng, slot), 0)
        cnt = prev + 1
        self.dslot[(eng, slot)] = cnt
        sem = self.sems[key]
        if prev:
            hw = self.hiw[eng]
            if hw.get(key, 0) < 16 * prev:
                hw[key] = 16 * prev
                waits.append((sem, 16 * prev))
        def emit(h, waits=waits, out=out, in_=in_, sem=sem, tag=tag,
                 transpose=transpose, slow=slow):
            for s, v in waits:
                h.wait_ge(s, v)
            kw = {}
            if transpose:
                kw["transpose"] = True
            if slow:
                kw["allow_slow_non_contiguous"] = True
            ins = h.dma_start(out=out, in_=in_, **kw)
            ins.then_inc(sem, 16)
            if tag:
                ins.annotate(tag)
        self.ops[eng].append(emit)
        self._record_access(key, cnt, reads, writes, wgroup)

    def barrier_all(self, eng):
        waits = []
        hw = self.hiw[eng]
        for e in ENGS:
            if self.icount[e]:
                key = ("c", e)
                tgt = self.icount[e]
                if hw.get(key, 0) < tgt:
                    hw[key] = tgt
                    waits.append((self.sems[key], tgt))
        for (e, s), cnt in self.dslot.items():
            key = ("d", e, s)
            tgt = cnt * 16
            if hw.get(key, 0) < tgt:
                hw[key] = tgt
                waits.append((self.sems[key], tgt))
        def emit(h, waits=waits):
            for s, v in waits:
                h.wait_ge(s, v)
        self.ops[eng].append(emit)

    def finish(self, block):
        handles = {}
        def run(eng):
            def section(h):
                for fn in self.ops[eng]:
                    fn(h)
            return section
        block.sync(run("sync"))
        block.vector(run("vector"))
        block.scalar(run("scalar"))
        block.tensor(run("tensor"))
        block.gpsimd(run("gpsimd"))



def ts(i, n):
    return slice(i * n, (i + 1) * n)


def ceil_div(a, b):
    return -(-a // b)


class Alloc:

    def __init__(self, nc, es):
        self.nc = nc
        self.es = es
        self.n = 0

    def sb(self, name, shape, dtype):
        return self.es.enter_context(self.nc.sbuf_tensor(name, shape, dtype))

    def ps(self, name, shape, dtype=F32):
        return self.es.enter_context(self.nc.psum_tensor(name, shape, dtype))


def load_weight_T(p, al, hbm_ap, name, out_dim, in_dim, stage):
    nI, nO = in_dim // 128, out_dim // 128
    wT = [al.sb(f"{name}_T{i}", [128, out_dim], BF16) for i in range(nI)]
    stage_f, stage_b = stage
    gids = [p.new_gid() for _ in range(nI)]
    for o in range(nO):
        kf = "wstage_f"
        kb = "wstage_b"
        p.dma("sync", out=stage_f[:], in_=hbm_ap[ts(o, 128), :],
              reads=(), writes=(kf,))
        p.op("vector", lambda h, sf=stage_f, sbf=stage_b: h.tensor_copy(out=sbf[:], in_=sf[:]),
             reads=(kf,), writes=(kb,))
        for i in range(nI):
            p.dma("scalar", out=wT[i][:, ts(o, 128)], in_=stage_b[:, ts(i, 128)],
                  reads=(kb,), writes=(f"{name}_T{i}",), transpose=True, wgroup=gids[i])
    return wT


def load_weight_direct(p, al, hbm_ap, name, in_dim, out_dim):
    nI = in_dim // 128
    wT = []
    for i in range(nI):
        stage = al.sb(f"{name}_st{i}", [128, out_dim], F32)
        t = al.sb(f"{name}_d{i}", [128, out_dim], BF16)
        p.dma("sync", out=stage[:], in_=hbm_ap[ts(i, 128), :],
              reads=(), writes=(f"{name}_st{i}",))
        p.op("vector", lambda h, s=stage, t=t: h.tensor_copy(out=t[:], in_=s[:]),
             reads=(f"{name}_st{i}",), writes=(f"{name}_d{i}",))
        wT.append(t)
    return wT


def load_bias(p, al, hbm_ap, name, dim):
    nT = dim // 128
    col = al.sb(f"{name}_col", [128, nT], F32)
    p.dma("sync", out=col[:], in_=hbm_ap.rearrange("(t p) -> p t", p=128),
          reads=(), writes=(name,))
    return col


def load_bias_cols(p, al, hbm_ap, name, dim):
    nT = dim // 128
    col = al.sb(f"{name}_col", [128, nT], F32)
    g = p.new_gid()
    for j in range(nT):
        p.dma("sync", out=col[:, j:j + 1], in_=hbm_ap[ts(j, 128)].unsqueeze(1),
              reads=(), writes=(name,), wgroup=g)
    return col


def make_ones_and_masks(p, al, lseq, qc):
    ones_f = al.sb("ones_f", [128, 1], F32)
    ones_b = al.sb("ones_b", [128, 1], BF16)
    p.op("gpsimd", lambda h: h.memset(ones_f[:], 1.0), writes=("ones_f",))
    p.op("gpsimd", lambda h: h.memset(ones_b[:], 1.0), writes=("ones_b",))
    masks = {}
    for bi in range(qc // 128):
        delta = bi * 128
        m = al.sb(f"mask{bi}", [128, qc], BF16)
        p.op("gpsimd", lambda h, m=m: h.memset(m[:], 1.0), writes=(f"mask{bi}",))
        p.op("gpsimd", lambda h, m=m, delta=delta: h.affine_select(
            out=m[:], in_=m[:], compare_op=ALU.is_ge, fill=0.0,
            base=delta, pattern=[[1, qc]], channel_multiplier=-1),
            reads=(f"mask{bi}",), writes=(f"mask{bi}",))
        masks[delta] = m
    return ones_f, ones_b, masks


def emit_layernorm(p, al, r_tiles, rkeys, out_tiles, okeys, tok, tag, C,
                   cast_to=None, ckeys=None):
    nD = len(r_tiles)
    CH = 512
    ps_s0 = al.ps(f"{tag}_pss0", [1, 512], F32)
    ps_s1 = al.ps(f"{tag}_pss1", [1, 512], F32)
    row_m = al.sb(f"{tag}_rowm", [1, CH], F32)
    row_r = al.sb(f"{tag}_rowr", [1, CH], F32)
    r2 = al.sb(f"{tag}_r2", [128, CH], F32)
    bc_m = al.ps(f"{tag}_bcm", [128, 512], F32)
    bc_r = al.ps(f"{tag}_bcr", [128, 512], F32)
    inv = 1.0 / (nD * 128)
    for c in range(tok // CH):
        cs = ts(c, CH)
        for j in range(nD):
            p.op("tensor", lambda h, j=j, cs=cs: h.matmul(
                out=ps_s0[:, :], lhsT=C["ones_col"], rhs=r_tiles[j][:, cs],
                start=(j == 0), stop=(j == nD - 1)),
                reads=(rkeys[j], "ones_col"), writes=(f"{tag}_pss0",))
        for j in range(nD):
            p.op("scalar", lambda h, j=j, cs=cs: h.activation(
                out=r2[:], in_=r_tiles[j][:, cs], func=AF.Square),
                reads=(rkeys[j],), writes=(f"{tag}_r2",))
            p.op("tensor", lambda h, j=j: h.matmul(
                out=ps_s1[:, :], lhsT=C["ones_col"], rhs=r2[:],
                start=(j == 0), stop=(j == nD - 1)),
                reads=(f"{tag}_r2", "ones_col"), writes=(f"{tag}_pss1",))
        p.op("scalar", lambda h: h.activation(
            out=row_m[:], in_=ps_s0[:, :], func=AF.Copy, scale=inv),
            reads=(f"{tag}_pss0",), writes=(f"{tag}_rowm",))
        p.op("scalar", lambda h: h.activation(
            out=row_r[:], in_=row_m[:], func=AF.Square),
            reads=(f"{tag}_rowm",), writes=(f"{tag}_rowr",))
        p.op("vector", lambda h: h.scalar_tensor_tensor(
            out=row_r[:], in0=ps_s1[:, :], scalar=inv, in1=row_r[:],
            op0=ALU.mult, op1=ALU.subtract),
            reads=(f"{tag}_pss1", f"{tag}_rowr"), writes=(f"{tag}_rowr",))
        p.op("scalar", lambda h: h.activation(
            out=row_r[:], in_=row_r[:], func=AF.Ln, bias=1e-5),
            reads=(f"{tag}_rowr",), writes=(f"{tag}_rowr",))
        p.op("scalar", lambda h: h.activation(
            out=row_r[:], in_=row_r[:], func=AF.Exp, scale=-0.5),
            reads=(f"{tag}_rowr",), writes=(f"{tag}_rowr",))
        p.op("tensor", lambda h: h.matmul(
            out=bc_m[:, :CH], lhsT=C["ones_row"], rhs=row_m[:],
            start=True, stop=True),
            reads=(f"{tag}_rowm", "ones_row"), writes=(f"{tag}_bcm",))
        p.op("tensor", lambda h: h.matmul(
            out=bc_r[:, :CH], lhsT=C["ones_row"], rhs=row_r[:],
            start=True, stop=True),
            reads=(f"{tag}_rowr", "ones_row"), writes=(f"{tag}_bcr",))
        for j in range(nD):
            p.op("vector", lambda h, j=j, cs=cs: h.tensor_sub(
                out=out_tiles[j][:, cs], in0=r_tiles[j][:, cs], in1=bc_m[:, :CH]),
                reads=(rkeys[j], f"{tag}_bcm"), writes=(okeys[j],))
            p.op("vector", lambda h, j=j, cs=cs: h.tensor_mul(
                out=out_tiles[j][:, cs], in0=out_tiles[j][:, cs], in1=bc_r[:, :CH]),
                reads=(okeys[j], f"{tag}_bcr"), writes=(okeys[j],))
            if cast_to is not None:
                p.op("vector", lambda h, j=j, cs=cs: h.tensor_copy(
                    out=cast_to[j][:, cs], in_=out_tiles[j][:, cs]),
                    reads=(okeys[j],), writes=(ckeys[j],))


CONST_VALS = [1e-5, 1.0]


def make_consts(p, al):
    C = {}
    cpool = al.sb("cpool", [128, len(CONST_VALS)], F32)
    for i, v in enumerate(CONST_VALS):
        p.op("gpsimd", lambda h, i=i, v=v: h.memset(cpool[:, i:i + 1], v),
             writes=("cpool",))
        p.nc.const_aps.register(F32, v, cpool[:, i:i + 1])
    ones_col = al.sb("ones_col", [128, 1], F32)
    ones_row = al.sb("ones_row", [1, 128], F32)
    ones_row_b = al.sb("ones_row_b", [1, 128], BF16)
    p.op("gpsimd", lambda h: h.memset(ones_col[:], 1.0), writes=("ones_col",))
    p.op("gpsimd", lambda h: h.memset(ones_row[:], 1.0), writes=("ones_row",))
    p.op("gpsimd", lambda h: h.memset(ones_row_b[:], 1.0), writes=("ones_row_b",))
    C["ones_col"] = ones_col[:]
    C["ones_row"] = ones_row[:]
    C["ones_row_b"] = ones_row_b[:]
    return C


def load_xT(p, al, x_hbm, name, tok, dim=D, dtype=F32, cast_bf=True):
    nD = dim // 128
    xf = [al.sb(f"{name}_f{j}", [128, tok], F32) for j in range(nD)]
    xb = [al.sb(f"{name}_b{j}", [128, tok], BF16) for j in range(nD)] if cast_bf else None
    for j in range(nD):
        p.dma("sync", out=xf[j][:], in_=x_hbm[ts(j, 128), :], writes=(f"{name}_f{j}",))
        if cast_bf:
            p.op("vector", lambda h, j=j: h.tensor_copy(out=xb[j][:], in_=xf[j][:]),
                 reads=(f"{name}_f{j}",), writes=(f"{name}_b{j}",))
    return xf, xb



def build_l0_attn(nc, lseq=LSEQ):
    tok = NSEQ * lseq
    xT = nc.dram_tensor("xT", [D, tok], F32, kind="ExternalInput").ap()
    ws = [nc.dram_tensor(n, [D, D], F32, kind="ExternalInput").ap()
          for n in ("wq", "wk", "wv", "wo")]
    bs = [nc.dram_tensor(n, [D], F32, kind="ExternalInput").ap()
          for n in ("bq", "bk", "bv", "bo")]
    outT = nc.dram_tensor("outT", [D, tok], F32, kind="ExternalOutput").ap()
    es = ExitStack()
    with es:
        al = Alloc(nc, es)
        p = Prog(nc)
        p.make_sems(es)
        emit_l0_attn(nc, p, al, xT, ws, bs, outT, lseq)
        block = es.enter_context(nc.Block())
        p.finish(block)
    return nc


def emit_l0_attn(nc, p, al, xT_hbm, w_hbm, b_hbm, outT_hbm, lseq):
    nD = D // 128
    QC = 512
    C = make_consts(p, al)

    P = [al.ps(f"P{i}", [128, 512], F32) for i in range(6)]

    masks = {}
    for bi in range(QC // 128):
        delta = -bi * 128
        m = al.sb(f"mask{bi}", [128, QC], BF16)
        p.op("gpsimd", lambda h, m=m: h.memset(m[:], 1.0), writes=(f"mask{bi}",))
        p.op("gpsimd", lambda h, m=m, delta=delta: h.affine_select(
            out=m[:], in_=m[:], compare_op=ALU.is_ge, fill=0.0,
            base=delta, pattern=[[1, QC]], channel_multiplier=-1),
            reads=(f"mask{bi}",), writes=(f"mask{bi}",))
        masks[delta] = m

    stage_f = al.sb("wstage_f", [128, D], F32)
    stage_b = al.sb("wstage_b", [128, D], BF16)
    wqT = load_weight_T(p, al, w_hbm[0], "wq", D, D, (stage_f, stage_b))
    wkT = load_weight_T(p, al, w_hbm[1], "wk", D, D, (stage_f, stage_b))
    wvT = load_weight_T(p, al, w_hbm[2], "wv", D, D, (stage_f, stage_b))
    woT = load_weight_T(p, al, w_hbm[3], "wo", D, D, (stage_f, stage_b))
    bq = load_bias_cols(p, al, b_hbm[0], "bq", D)
    bk = load_bias_cols(p, al, b_hbm[1], "bk", D)
    bo = load_bias_cols(p, al, b_hbm[3], "bo", D)
    bv_row_f = al.sb("bv_rowf", [1, D], F32)
    bv_row = al.sb("bv_row", [1, D], BF16)
    p.dma("sync", out=bv_row_f[:], in_=b_hbm[2].unsqueeze(0), writes=("bv_rowf",))
    p.op("vector", lambda h: h.tensor_copy(out=bv_row[:], in_=bv_row_f[:]),
         reads=("bv_rowf",), writes=("bv_row",))

    xbc = [al.sb(f"xbc{j}", [128, 512], BF16) for j in range(nD)]
    KT = [al.sb(f"KT{j}", [128, lseq], BF16) for j in range(nD)]
    Vt = [al.sb(f"Vt{i}", [128, D], BF16) for i in range(lseq // 128)]
    QTc = [al.sb(f"QTc{j}", [128, QC], BF16) for j in range(nD)]
    ATc = [al.sb(f"ATc{j}", [128, QC], BF16) for j in range(nD)]
    rTc = [al.sb(f"rTc{j}", [128, QC], F32) for j in range(nD)]
    Sb = [al.sb(f"Sbb{i}", [128, QC], BF16) for i in range(2)]

    ln_bufs = dict(
        ps_s0=P[4][0:1, :], ps_s1=P[5][0:1, :],
        bc_m=P[4], bc_r=P[5],
        row_m=al.sb("ln_rowm", [1, QC], F32),
        row_r=al.sb("ln_rowr", [1, QC], F32),
        r2=al.sb("ln_r2", [128, QC], F32),
    )

    for seq in range(NSEQ):
        s_lo = seq * lseq
        for c in range(lseq // 512):
            cs = ts(c, 512)
            for j in range(nD):
                p.dma("gpsimd", out=xbc[j][:],
                      in_=xT_hbm[ts(j, 128), s_lo + c * 512:s_lo + (c + 1) * 512],
                      writes=(f"xbc{j}",))
            for e in range(nD):
                for j in range(nD):
                    p.op("tensor", lambda h, e=e, j=j: h.matmul(
                        out=P[0][:, :], lhsT=wkT[j][:, ts(e, 128)], rhs=xbc[j][:],
                        start=(j == 0), stop=(j == nD - 1)),
                        reads=(f"wk_T{j}", f"xbc{j}"), writes=("P0",))
                p.op("vector", lambda h, e=e, cs=cs: h.tensor_scalar(
                    out=KT[e][:, cs], in0=P[0][:, :], scalar1=bk[:, e:e + 1],
                    scalar2=None, op0=ALU.add),
                    reads=("P0", "bk"), writes=(f"KT{e}",))
            for i4 in range(4):
                i = c * 4 + i4
                for hhalf in range(D // 512):
                    hs = ts(hhalf, 512)
                    for j in range(nD):
                        p.op("tensor", lambda h, i4=i4, j=j, hs=hs: h.matmul(
                            out=P[1][:, :], lhsT=xbc[j][:, ts(i4, 128)], rhs=wvT[j][:, hs],
                            start=(j == 0), stop=False),
                            reads=(f"wv_T{j}", f"xbc{j}"), writes=("P1",))
                    p.op("tensor", lambda h, hs=hs: h.matmul(
                        out=P[1][:, :], lhsT=C["ones_row_b"], rhs=bv_row[:, hs],
                        start=False, stop=True),
                        reads=("bv_row", "ones_row_b"), writes=("P1",))
                    p.op("scalar", lambda h, i=i, hs=hs: h.copy(
                        out=Vt[i][:, hs], in_=P[1][:, :]),
                        reads=("P1",), writes=(f"Vt{i}",))
        for qc in range(lseq // QC):
            q0 = qc * QC
            for j in range(nD):
                p.dma("gpsimd", out=xbc[j][:],
                      in_=xT_hbm[ts(j, 128), s_lo + q0:s_lo + q0 + QC],
                      writes=(f"xbc{j}",))
            for e in range(nD):
                for j in range(nD):
                    p.op("tensor", lambda h, e=e, j=j: h.matmul(
                        out=P[0][:, :], lhsT=wqT[j][:, ts(e, 128)],
                        rhs=xbc[j][:],
                        start=(j == 0), stop=(j == nD - 1)),
                        reads=(f"wq_T{j}", f"xbc{j}"), writes=("P0",))
                p.op("vector", lambda h, e=e: h.tensor_scalar(
                    out=QTc[e][:], in0=P[0][:, :], scalar1=bq[:, e:e + 1],
                    scalar2=None, op0=ALU.add),
                    reads=("P0", "bq"), writes=(f"QTc{e}",))
            for head in range(NH):
                et, ro = head // 2, (head % 2) * 64
                n_sblk = (q0 + QC) // 128
                for sb_ in range(n_sblk):
                    s0 = sb_ * 128
                    pp, pk = P[2 + (sb_ % 2)], f"P{2 + (sb_ % 2)}"
                    sbuf, sk = Sb[sb_ % 2], f"Sbb{sb_ % 2}"
                    p.op("tensor", lambda h, et=et, ro=ro, s0=s0, q0=q0, pp=pp: h.matmul(
                        out=pp[:, :QC], lhsT=KT[et][ro:ro + 64, s0:s0 + 128],
                        rhs=QTc[et][ro:ro + 64, :],
                        start=True, stop=True),
                        reads=(f"KT{et}", f"QTc{et}"), writes=(pk,))
                    delta = q0 - s0
                    if delta <= 0:
                        p.op("vector", lambda h, pp=pp, sbuf=sbuf, delta=delta: h.tensor_mul(
                            out=sbuf[:], in0=pp[:, :QC], in1=masks[delta][:]),
                            reads=(pk, f"mask{-delta // 128}"), writes=(sk,))
                    else:
                        p.op("scalar", lambda h, pp=pp, sbuf=sbuf: h.copy(
                            out=sbuf[:], in_=pp[:, :QC]),
                            reads=(pk,), writes=(sk,))
                    p.op("tensor", lambda h, s0=s0, ro=ro, et=et, sbuf=sbuf,
                         st=(sb_ == 0), sp=(sb_ == n_sblk - 1): h.matmul(
                        out=P[1][0:64, :QC], lhsT=Vt[s0 // 128][:, et * 128 + ro: et * 128 + ro + 64],
                        rhs=sbuf[:], start=st, stop=sp, skip_group_check=True),
                        reads=(sk, f"Vt{s0 // 128}"), writes=("P1",))
                p.op("scalar", lambda h, et=et, ro=ro: h.copy(
                    out=ATc[et][ro:ro + 64, :], in_=P[1][0:64, :QC]),
                    reads=("P1",), writes=(f"ATc{et}",))
            for e in range(nD):
                for j in range(nD):
                    p.op("tensor", lambda h, e=e, j=j: h.matmul(
                        out=P[0][:, :], lhsT=woT[j][:, ts(e, 128)], rhs=ATc[j][:],
                        start=(j == 0), stop=(j == nD - 1)),
                        reads=(f"wo_T{j}", f"ATc{j}"), writes=("P0",))
                p.op("vector", lambda h, e=e: h.scalar_tensor_tensor(
                    out=rTc[e][:], in0=P[0][:, :], scalar=bo[:, e:e + 1],
                    in1=xbc[e][:], op0=ALU.add, op1=ALU.add),
                    reads=("P0", "bo", f"xbc{e}"), writes=(f"rTc{e}",))
            emit_ln_chunk(p, rTc, [f"rTc{j}" for j in range(nD)],
                          rTc, [f"rTc{j}" for j in range(nD)], "P4", "P5",
                          ln_bufs, C, QC)
            for j in range(nD):
                p.dma("sync", out=outT_hbm[ts(j, 128), s_lo + q0:s_lo + q0 + QC],
                      in_=rTc[j][:], reads=(f"rTc{j}",), writes=(f"outT{j}",))
    p.barrier_all("sync")


    p.barrier_all("sync")


def emit_ln_chunk(p, r_tiles, rkeys, out_tiles, okeys, pk0, pk1, B, C, CH,
                  cast_to=None, ckeys=None, out_slice=None):
    nD = len(r_tiles)
    inv = 1.0 / (nD * 128)
    osl = out_slice if out_slice is not None else slice(0, CH)
    for j in range(nD):
        p.op("tensor", lambda h, j=j: h.matmul(
            out=B["ps_s0"][:, :CH], lhsT=C["ones_col"], rhs=r_tiles[j][:],
            start=(j == 0), stop=(j == nD - 1)),
            reads=(rkeys[j], "ones_col"), writes=(pk0,))
    for j in range(nD):
        p.op("scalar", lambda h, j=j: h.activation(
            out=B["r2"][:], in_=r_tiles[j][:], func=AF.Square),
            reads=(rkeys[j],), writes=("ln_r2",))
        p.op("tensor", lambda h, j=j: h.matmul(
            out=B["ps_s1"][:, :CH], lhsT=C["ones_col"], rhs=B["r2"][:],
            start=(j == 0), stop=(j == nD - 1)),
            reads=("ln_r2", "ones_col"), writes=(pk1,))
    p.op("scalar", lambda h: h.activation(
        out=B["row_m"][:], in_=B["ps_s0"][:, :CH], func=AF.Copy, scale=inv),
        reads=(pk0,), writes=("ln_rowm",))
    p.op("scalar", lambda h: h.activation(
        out=B["row_r"][:], in_=B["row_m"][:], func=AF.Square),
        reads=("ln_rowm",), writes=("ln_rowr",))
    p.op("vector", lambda h: h.scalar_tensor_tensor(
        out=B["row_r"][:], in0=B["ps_s1"][:, :CH], scalar=inv, in1=B["row_r"][:],
        op0=ALU.mult, op1=ALU.subtract),
        reads=(pk1, "ln_rowr"), writes=("ln_rowr",))
    p.op("scalar", lambda h: h.activation(
        out=B["row_r"][:], in_=B["row_r"][:], func=AF.Ln, bias=1e-5),
        reads=("ln_rowr", "cpool"), writes=("ln_rowr",))
    p.op("scalar", lambda h: h.activation(
        out=B["row_r"][:], in_=B["row_r"][:], func=AF.Exp, scale=-0.5),
        reads=("ln_rowr",), writes=("ln_rowr",))
    p.op("tensor", lambda h: h.matmul(
        out=B["bc_m"][:, :CH], lhsT=C["ones_row"], rhs=B["row_m"][:],
        start=True, stop=True),
        reads=("ln_rowm", "ones_row"), writes=(pk0,))
    p.op("tensor", lambda h: h.matmul(
        out=B["bc_r"][:, :CH], lhsT=C["ones_row"], rhs=B["row_r"][:],
        start=True, stop=True),
        reads=("ln_rowr", "ones_row"), writes=(pk1,))
    for j in range(nD):
        p.op("vector", lambda h, j=j: h.tensor_sub(
            out=out_tiles[j][:, osl], in0=r_tiles[j][:], in1=B["bc_m"][:, :CH]),
            reads=(rkeys[j], pk0), writes=(okeys[j],))
        p.op("vector", lambda h, j=j: h.tensor_mul(
            out=out_tiles[j][:, osl], in0=out_tiles[j][:, osl], in1=B["bc_r"][:, :CH]),
            reads=(okeys[j], pk1), writes=(okeys[j],))
        if cast_to is not None:
            p.op("vector", lambda h, j=j: h.tensor_copy(
                out=cast_to[j][:, osl], in_=out_tiles[j][:, osl]),
                reads=(okeys[j],), writes=(ckeys[j],))



def stream_wT_slice(p, al, hbm_ap, in_dim, o0, o1, dest, dkeys, col0,
                    stage_b, skey, tq="scalar"):
    ow = o1 - o0
    nI = in_dim // 128
    if ow < 128:
        ow0 = (ow // 32) * 32
        segs = [(a, b) for a, b in [(32, 64), (64, 128)] if b > ow0]
        for a, b in segs:
            a = max(a, ow0)
            p.op("vector", lambda h, a=a, b=b: h.memset(stage_b[a:b, 0:in_dim], 0.0),
                 reads=(skey,), writes=(skey,))
    p.dma("gpsimd", out=stage_b[0:ow, 0:in_dim], in_=hbm_ap[o0:o1, :],
          reads=(skey,), writes=(skey,))
    for i in range(nI):
        g = p.new_gid()
        p.dma(tq, out=dest[i][:, col0:col0 + 128],
              in_=stage_b[0:128, ts(i, 128)],
              reads=(skey,), writes=(dkeys[i],), transpose=True, wgroup=g)



def build_l0_ffn(nc, lseq=LSEQ):
    tok = NSEQ * lseq
    xT = nc.dram_tensor("xT", [D, tok], F32, kind="ExternalInput").ap()
    gate_w = nc.dram_tensor("gate_w", [E0, D], F32, kind="ExternalInput").ap()
    Wg = nc.dram_tensor("Wg", [E0, H0, D], F32, kind="ExternalInput").ap()
    W1 = nc.dram_tensor("W1", [E0, H0, D], F32, kind="ExternalInput").ap()
    W2 = nc.dram_tensor("W2", [E0, D, H0], F32, kind="ExternalInput").ap()
    outT = nc.dram_tensor("outT", [D, tok], F32, kind="ExternalOutput").ap()
    es = ExitStack()
    with es:
        al = Alloc(nc, es)
        p = Prog(nc)
        p.make_sems(es)
        emit_l0_ffn(nc, p, al, xT, gate_w, Wg, W1, W2, outT, lseq)
        block = es.enter_context(nc.Block())
        p.finish(block)
    return nc


def emit_l0_ffn(nc, p, al, xT_hbm, gate_hbm, Wg_hbm, W1_hbm, W2_hbm, outT_hbm, lseq):
    tok = NSEQ * lseq
    nD = D // 128
    HT = ceil_div(H0, 128)
    TH = tok // 2
    C = make_consts(p, al)

    P = [al.ps(f"P{i}", [128, 512], F32) for i in range(6)]

    ident_f = al.sb("ident_f", [128, 128], F32)
    p.op("gpsimd", lambda h: h.memset(ident_f[:], 0.0), writes=("ident_f",))
    p.op("gpsimd", lambda h: h.affine_select(
        out=ident_f[:], in_=ident_f[:], compare_op=ALU.not_equal, fill=1.0,
        base=0, pattern=[[-1, 128]], channel_multiplier=1),
        reads=("ident_f",), writes=("ident_f",))
    sel = [al.sb(f"sel{e}", [8, 128], BF16) for e in range(E0)]
    for e in range(E0):
        p.op("gpsimd", lambda h, e=e: h.memset(sel[e][:], 1.0), writes=(f"sel{e}",))
        p.op("gpsimd", lambda h, e=e: h.affine_select(
            out=sel[e][:], in_=sel[e][:], compare_op=ALU.is_equal, fill=0.0,
            base=-e, pattern=[[0, 128]], channel_multiplier=1),
            reads=(f"sel{e}",), writes=(f"sel{e}",))

    stg2 = [al.sb(f"stg2_{b}", [128, 1408], BF16) for b in range(2)]
    gwT = [al.sb(f"gwT{j}", [128, 16], BF16) for j in range(nD)]
    p.op("vector", lambda h: h.memset(stg2[0][0:16, 0:D], 0.0), writes=("stg2_0",))
    p.dma("gpsimd", out=stg2[0][0:8, 0:D], in_=gate_hbm,
          reads=("stg2_0",), writes=("stg2_0",))
    for j in range(nD):
        p.dma("scalar", out=gwT[j][:], in_=stg2[0][0:16, ts(j, 128)],
              reads=("stg2_0",), writes=(f"gwT{j}",), transpose=True)

    xb = [al.sb(f"xb{j}", [128, TH], BF16) for j in range(nD)]
    nTT = TH // 128
    Ltm = al.sb("Ltm", [128, nTT * E0], F32)
    comb = al.sb("comb", [128, nTT * E0], F32)
    combE = al.sb("combE", [8, TH], BF16)
    cbc = al.sb("cbc", [128, TH], BF16)
    r_m1 = al.sb("r_m1", [128, nTT], F32)
    r_m2 = al.sb("r_m2", [128, nTT], F32)
    r_eb = al.sb("r_eb", [128, nTT * E0], F32)
    r_sc = al.sb("r_sc", [128, nTT], F32)
    r_sc2 = al.sb("r_sc2", [128, nTT], F32)

    WgTs = [[al.sb(f"WgTs{b}_{j}", [128, 128], BF16) for j in range(nD)]
            for b in range(2)]
    W1Ts = [[al.sb(f"W1Ts{b}_{j}", [128, 128], BF16) for j in range(nD)]
            for b in range(2)]
    HHT = 8
    w2T = [al.sb(f"w2T{j}", [128, D], BF16) for j in range(HHT)]
    gu = [al.sb(f"gu{j}", [128, TH], BF16) for j in range(HHT)]
    gs = al.sb("gs", [128, 512], BF16)
    out_acc = [al.sb(f"oacc{j}", [128, TH], BF16) for j in range(nD)]
    tmp_y = al.sb("tmp_y", [128, 512], BF16)
    rTc = [al.sb(f"rTc{j}", [128, 512], F32) for j in range(nD)]
    ln_bufs = dict(
        ps_s0=P[4][0:1, :], ps_s1=P[5][0:1, :], bc_m=P[4], bc_r=P[5],
        row_m=al.sb("ln_rowm", [1, 512], F32),
        row_r=al.sb("ln_rowr", [1, 512], F32),
        r2=al.sb("ln_r2", [128, 512], F32),
    )

    for half in range(2):
        t0 = half * TH
        for j in range(nD):
            p.dma("gpsimd", out=xb[j][:], in_=xT_hbm[ts(j, 128), t0:t0 + TH],
                  writes=(f"xb{j}",))
        for i in range(nTT):
            for j in range(nD):
                p.op("tensor", lambda h, i=i, j=j: h.matmul(
                    out=P[3][:, 0:E0], lhsT=xb[j][:, ts(i, 128)], rhs=gwT[j][:, 0:E0],
                    start=(j == 0), stop=(j == nD - 1)),
                    reads=(f"xb{j}", f"gwT{j}"), writes=("P3",))
            p.op("scalar", lambda h, i=i: h.copy(
                out=Ltm[:, i * E0:(i + 1) * E0], in_=P[3][:, 0:E0]),
                reads=("P3",), writes=("Ltm",))
        L3 = Ltm[:].rearrange("p (t e) -> p t e", e=E0)
        p.op("vector", lambda h: h.tensor_reduce(
            out=r_m1[:], in_=L3, axis=AX.X, op=ALU.max),
            reads=("Ltm",), writes=("r_m1",))
        m1b = r_m1[:].unsqueeze(2).broadcast_to([128, nTT, E0])
        p.op("vector", lambda h: h.tensor_tensor(
            out=r_eb[:].rearrange("p (t e) -> p t e", e=E0), in0=L3, in1=m1b,
            op=ALU.subtract),
            reads=("Ltm", "r_m1"), writes=("r_eb",))
        p.op("scalar", lambda h: h.activation(
            out=r_eb[:], in_=r_eb[:], func=AF.Exp),
            reads=("r_eb",), writes=("r_eb",))
        eb3 = r_eb[:].rearrange("p (t e) -> p t e", e=E0)
        p.op("vector", lambda h: h.tensor_reduce(
            out=r_sc[:], in_=eb3, axis=AX.X, op=ALU.add),
            reads=("r_eb",), writes=("r_sc",))
        p.op("vector", lambda h: h.tensor_tensor(
            out=comb[:].rearrange("p (t e) -> p t e", e=E0), in0=L3, in1=m1b,
            op=ALU.is_ge),
            reads=("Ltm", "r_m1"), writes=("comb",))
        p.op("vector", lambda h: h.tensor_scalar(
            out=comb[:], in0=comb[:], scalar1=-1e30, scalar2=None, op0=ALU.mult),
            reads=("comb",), writes=("comb",))
        p.op("vector", lambda h: h.tensor_add(
            out=comb[:], in0=comb[:], in1=Ltm[:]),
            reads=("comb", "Ltm"), writes=("comb",))
        p.op("vector", lambda h: h.tensor_reduce(
            out=r_m2[:], in_=comb[:].rearrange("p (t e) -> p t e", e=E0),
            axis=AX.X, op=ALU.max),
            reads=("comb",), writes=("r_m2",))
        m2b = r_m2[:].unsqueeze(2).broadcast_to([128, nTT, E0])
        p.op("vector", lambda h: h.tensor_tensor(
            out=comb[:].rearrange("p (t e) -> p t e", e=E0), in0=L3, in1=m2b,
            op=ALU.is_ge),
            reads=("Ltm", "r_m2"), writes=("comb",))
        p.op("vector", lambda h: h.tensor_mul(
            out=comb[:], in0=comb[:], in1=r_eb[:]),
            reads=("comb", "r_eb"), writes=("comb",))
        p.op("vector", lambda h: h.tensor_reduce(
            out=r_sc2[:], in_=comb[:].rearrange("p (t e) -> p t e", e=E0),
            axis=AX.X, op=ALU.add),
            reads=("comb",), writes=("r_sc2",))
        p.op("vector", lambda h: h.scalar_tensor_tensor(
            out=r_sc2[:], in0=r_sc[:], scalar=1e-8, in1=r_sc2[:],
            op0=ALU.mult, op1=ALU.add),
            reads=("r_sc", "r_sc2"), writes=("r_sc2",))
        p.op("vector", lambda h: h.reciprocal(out=r_sc2[:], in_=r_sc2[:]),
             reads=("r_sc2",), writes=("r_sc2",))
        recb = r_sc2[:].unsqueeze(2).broadcast_to([128, nTT, E0])
        p.op("vector", lambda h: h.tensor_tensor(
            out=comb[:].rearrange("p (t e) -> p t e", e=E0),
            in0=comb[:].rearrange("p (t e) -> p t e", e=E0), in1=recb,
            op=ALU.mult),
            reads=("comb", "r_sc2"), writes=("comb",))
        for i in range(nTT):
            p.op("tensor", lambda h, i=i: h.transpose(
                out=P[3][0:8, 0:128], in_=comb[:, i * E0:(i + 1) * E0],
                identity=ident_f[:]),
                reads=("comb", "ident_f"), writes=("P3",))
            p.op("scalar", lambda h, i=i: h.copy(
                out=combE[:, ts(i, 128)], in_=P[3][0:8, 0:128]),
                reads=("P3",), writes=("combE",))
        for j in range(nD):
            p.op("vector", lambda h, j=j: h.memset(out_acc[j][:], 0.0),
                 writes=(f"oacc{j}",))
        for e in range(E0):
            for cc in range(TH // 512):
                p.op("tensor", lambda h, e=e, cc=cc: h.matmul(
                    out=P[3][:, :], lhsT=sel[e][:], rhs=combE[:, ts(cc, 512)],
                    start=True, stop=True),
                    reads=(f"sel{e}", "combE"), writes=("P3",))
                p.op("scalar", lambda h, cc=cc: h.copy(
                    out=cbc[:, ts(cc, 512)], in_=P[3][:, :]),
                    reads=("P3",), writes=("cbc",))
            for hh in range(3):
                jtiles = list(range(hh * HHT, min(HT, (hh + 1) * HHT)))
                hlo = jtiles[0] * 128
                hhi = min(H0, (jtiles[-1] + 1) * 128)
                wpad = len(jtiles) * 128
                w2gids = [p.new_gid() for _ in jtiles]
                for o in range(nD):
                    sb2 = stg2[o % 2]
                    sk2 = f"stg2_{o % 2}"
                    if hhi - hlo < wpad:
                        p.op("vector", lambda h, w0=hhi - hlo, w1=wpad, sb2=sb2: h.memset(
                            sb2[:, w0:w1], 0.0),
                            reads=(sk2,), writes=(sk2,))
                    p.dma("gpsimd", out=sb2[:, 0:hhi - hlo],
                          in_=W2_hbm[e, ts(o, 128), hlo:hhi],
                          reads=(sk2,), writes=(sk2,))
                    for ji, j in enumerate(jtiles):
                        p.dma("scalar",
                              out=w2T[ji][:, ts(o, 128)],
                              in_=sb2[:, ts(ji, 128)],
                              reads=(sk2,), writes=(f"w2T{ji}",), transpose=True,
                              wgroup=w2gids[ji])
                for ji, j in enumerate(jtiles):
                    jhi = min(H0, j * 128 + 128)
                    bb = ji % 2
                    stream_wT_slice(p, al, Wg_hbm[e], D, j * 128, jhi,
                                    WgTs[bb], [f"WgTs{bb}_{q}" for q in range(nD)], 0,
                                    stg2[0], "stg2_0", tq="scalar")
                    stream_wT_slice(p, al, W1_hbm[e], D, j * 128, jhi,
                                    W1Ts[bb], [f"W1Ts{bb}_{q}" for q in range(nD)], 0,
                                    stg2[1], "stg2_1", tq="scalar")
                    for cc in range(TH // 512):
                        cs = ts(cc, 512)
                        for q in range(nD):
                            p.op("tensor", lambda h, q=q, cs=cs, bb=bb: h.matmul(
                                out=P[0][:, :], lhsT=WgTs[bb][q][:],
                                rhs=xb[q][:, cs], start=(q == 0), stop=(q == nD - 1)),
                                reads=(f"WgTs{bb}_{q}", f"xb{q}"), writes=("P0",))
                        for q in range(nD):
                            p.op("tensor", lambda h, q=q, cs=cs, bb=bb: h.matmul(
                                out=P[1][:, :], lhsT=W1Ts[bb][q][:],
                                rhs=xb[q][:, cs], start=(q == 0), stop=(q == nD - 1)),
                                reads=(f"W1Ts{bb}_{q}", f"xb{q}"), writes=("P1",))
                        p.op("scalar", lambda h: h.activation(
                            out=gs[:, :], in_=P[0][:, :], func=AF.Silu),
                            reads=("P0",), writes=("gs",))
                        p.op("vector", lambda h, ji=ji, cs=cs: h.tensor_mul(
                            out=gu[ji][:, cs], in0=gs[:, :], in1=P[1][:, :]),
                            reads=("gs", "P1"), writes=(f"gu{ji}",))
                for o in range(nD):
                    for cc in range(TH // 512):
                        cs = ts(cc, 512)
                        for ji, j in enumerate(jtiles):
                            p.op("tensor", lambda h, ji=ji, o=o, cs=cs,
                                 st=(ji == 0), sp=(ji == len(jtiles) - 1): h.matmul(
                                out=P[2][:, :], lhsT=w2T[ji][:, ts(o, 128)],
                                rhs=gu[ji][:, cs], start=st, stop=sp,
                                skip_group_check=True),
                                reads=(f"w2T{ji}", f"gu{ji}"), writes=("P2",))
                        p.op("vector", lambda h, o=o, cs=cs: h.tensor_mul(
                            out=tmp_y[:], in0=P[2][:, :], in1=cbc[:, cs]),
                            reads=("P2", "cbc"), writes=("tmp_y",))
                        p.op("vector", lambda h, o=o, cs=cs: h.tensor_add(
                            out=out_acc[o][:, cs], in0=out_acc[o][:, cs], in1=tmp_y[:]),
                            reads=(f"oacc{o}", "tmp_y"), writes=(f"oacc{o}",))
        for cc in range(TH // 512):
            cs = ts(cc, 512)
            for j in range(nD):
                p.op("vector", lambda h, j=j, cs=cs: h.tensor_add(
                    out=rTc[j][:], in0=xb[j][:, cs], in1=out_acc[j][:, cs]),
                    reads=(f"xb{j}", f"oacc{j}"), writes=(f"rTc{j}",))
            emit_ln_chunk(p, rTc, [f"rTc{j}" for j in range(nD)],
                          rTc, [f"rTc{j}" for j in range(nD)], "P4", "P5",
                          ln_bufs, C, 512)
            for j in range(nD):
                p.dma("sync", out=outT_hbm[ts(j, 128), t0 + cc * 512:t0 + (cc + 1) * 512],
                      in_=rTc[j][:], reads=(f"rTc{j}",), writes=(f"outT{j}",))
    p.barrier_all("sync")



def build_l1_ffn(nc, lseq=LSEQ):
    tok = NSEQ * lseq
    xT = nc.dram_tensor("xT", [D, tok], F32, kind="ExternalInput").ap()
    rW = nc.dram_tensor("router_W", [D, E1], F32, kind="ExternalInput").ap()
    rb = nc.dram_tensor("router_b", [E1], F32, kind="ExternalInput").ap()
    sW1 = nc.dram_tensor("sW1", [D, H1], F32, kind="ExternalInput").ap()
    sb1 = nc.dram_tensor("sb1", [H1], F32, kind="ExternalInput").ap()
    sW2 = nc.dram_tensor("sW2", [H1, D], F32, kind="ExternalInput").ap()
    sb2 = nc.dram_tensor("sb2", [D], F32, kind="ExternalInput").ap()
    eW1 = nc.dram_tensor("eW1", [E1, D, H1], F32, kind="ExternalInput").ap()
    eb1 = nc.dram_tensor("eb1", [E1, H1], F32, kind="ExternalInput").ap()
    eW2 = nc.dram_tensor("eW2", [E1, H1, D], F32, kind="ExternalInput").ap()
    eb2 = nc.dram_tensor("eb2", [E1, D], F32, kind="ExternalInput").ap()
    outT = nc.dram_tensor("outT", [D, tok], F32, kind="ExternalOutput").ap()
    es = ExitStack()
    with es:
        al = Alloc(nc, es)
        p = Prog(nc)
        p.make_sems(es)
        emit_l1_ffn(nc, p, al, xT, (rW, rb, sW1, sb1, sW2, sb2, eW1, eb1, eW2, eb2),
                    outT, lseq)
        block = es.enter_context(nc.Block())
        p.finish(block)
    return nc


def emit_l1_ffn(nc, p, al, xT_hbm, W, outT_hbm, lseq):
    rW_h, rb_h, sW1_h, sb1_h, sW2_h, sb2_h, eW1_h, eb1_h, eW2_h, eb2_h = W
    tok = NSEQ * lseq
    nD = D // 128
    nH = H1 // 128
    TH = tok // 2
    nTT = TH // 128
    C = make_consts(p, al)
    P = [al.ps(f"P{i}", [128, 512], F32) for i in range(6)]

    ident_f = al.sb("ident_f", [128, 128], F32)
    p.op("gpsimd", lambda h: h.memset(ident_f[:], 0.0), writes=("ident_f",))
    p.op("gpsimd", lambda h: h.affine_select(
        out=ident_f[:], in_=ident_f[:], compare_op=ALU.not_equal, fill=1.0,
        base=0, pattern=[[-1, 128]], channel_multiplier=1),
        reads=("ident_f",), writes=("ident_f",))
    sel = [al.sb(f"sel{e}", [E1, 128], BF16) for e in range(E1)]
    for e in range(E1):
        p.op("gpsimd", lambda h, e=e: h.memset(sel[e][:], 1.0), writes=(f"sel{e}",))
        p.op("gpsimd", lambda h, e=e: h.affine_select(
            out=sel[e][:], in_=sel[e][:], compare_op=ALU.is_equal, fill=0.0,
            base=-e, pattern=[[0, 128]], channel_multiplier=1),
            reads=(f"sel{e}",), writes=(f"sel{e}",))

    rwT = [al.sb(f"rwT{j}", [128, E1], BF16) for j in range(nD)]
    stage_f = al.sb("wstage_f", [128, max(D, H1)], F32)
    rb_rowf = al.sb("rb_rowf", [1, E1], F32)
    rb_row = al.sb("rb_row", [1, E1], BF16)
    p.dma("sync", out=rb_rowf[:], in_=rb_h.unsqueeze(0), writes=("rb_rowf",))
    p.op("vector", lambda h: h.tensor_copy(out=rb_row[:], in_=rb_rowf[:]),
         reads=("rb_rowf",), writes=("rb_row",))
    for j in range(nD):
        p.dma("sync", out=stage_f[:, 0:E1], in_=rW_h[ts(j, 128), :], writes=("wstage_f",))
        p.op("vector", lambda h, j=j: h.tensor_copy(out=rwT[j][:], in_=stage_f[:, 0:E1]),
             reads=("wstage_f",), writes=(f"rwT{j}",))

    w1d = [[al.sb(f"w1d{b}_{j}", [128, H1], BF16) for j in range(nD)] for b in range(2)]
    w2d = [[al.sb(f"w2d{b}_{j}", [128, D], BF16) for j in range(nH)] for b in range(2)]
    b1c = [al.sb(f"b1c{b}", [128, nH], F32) for b in range(2)]
    b2c = [al.sb(f"b2c{b}", [128, nD], F32) for b in range(2)]
    hb = [al.sb(f"hb{j}", [128, TH], BF16) for j in range(nH)]
    xb = [al.sb(f"xb{j}", [128, TH], BF16) for j in range(nD)]
    out_acc = [al.sb(f"oacc{j}", [128, TH], BF16) for j in range(nD)]
    tmp_y = al.sb("tmp_y", [128, 512], BF16)
    cbc = al.sb("cbc", [128, TH], BF16)
    Ltm = al.sb("Ltm", [128, nTT * E1], F32)
    comb = al.sb("comb", [128, nTT * E1], F32)
    combE = al.sb("combE", [E1, TH], BF16)
    r_m1 = al.sb("r_m1", [128, nTT], F32)
    r_eb = al.sb("r_eb", [128, nTT * E1], F32)
    mskb = al.sb("mskb", [128, nTT * E1], F32)
    r_sc = al.sb("r_sc", [128, nTT], F32)
    rTc = [al.sb(f"rTc{j}", [128, 512], F32) for j in range(nD)]
    ln_bufs = dict(
        ps_s0=P[4][0:1, :], ps_s1=P[5][0:1, :], bc_m=P[4], bc_r=P[5],
        row_m=al.sb("ln_rowm", [1, 512], F32),
        row_r=al.sb("ln_rowr", [1, 512], F32),
        r2=al.sb("ln_r2", [128, 512], F32),
    )

    def load_direct(hbm, dest, dkeys, in_dim, width):
        for j in range(in_dim // 128):
            p.dma("gpsimd" if j % 2 else "gpsimd", out=dest[j][:, 0:width],
                  in_=hbm[ts(j, 128), :], writes=(dkeys[j],))

    def load_bias_cols_2d(hbm_row, name, dim, dest):
        g = p.new_gid()
        for j in range(dim // 128):
            p.dma("sync", out=dest[:, j:j + 1], in_=hbm_row[ts(j, 128)].unsqueeze(1),
                  writes=(name,), wgroup=g)

    for half in range(2):
        t0 = half * TH
        for j in range(nD):
            p.dma("gpsimd", out=xb[j][:], in_=xT_hbm[ts(j, 128), t0:t0 + TH],
                  writes=(f"xb{j}",))
        for i in range(nTT):
            for j in range(nD):
                p.op("tensor", lambda h, i=i, j=j: h.matmul(
                    out=P[3][:, 0:E1], lhsT=xb[j][:, ts(i, 128)], rhs=rwT[j][:],
                    start=(j == 0), stop=False),
                    reads=(f"xb{j}", f"rwT{j}"), writes=("P3",))
            p.op("tensor", lambda h: h.matmul(
                out=P[3][:, 0:E1], lhsT=C["ones_row_b"], rhs=rb_row[:],
                start=False, stop=True),
                reads=("rb_row", "ones_row_b"), writes=("P3",))
            p.op("scalar", lambda h, i=i: h.copy(
                out=Ltm[:, i * E1:(i + 1) * E1], in_=P[3][:, 0:E1]),
                reads=("P3",), writes=("Ltm",))
        L3 = Ltm[:].rearrange("p (t e) -> p t e", e=E1)
        p.op("vector", lambda h: h.tensor_reduce(
            out=r_m1[:], in_=L3, axis=AX.X, op=ALU.max),
            reads=("Ltm",), writes=("r_m1",))
        m1b = r_m1[:].unsqueeze(2).broadcast_to([128, nTT, E1])
        p.op("vector", lambda h: h.tensor_tensor(
            out=r_eb[:].rearrange("p (t e) -> p t e", e=E1), in0=L3, in1=m1b,
            op=ALU.subtract),
            reads=("Ltm", "r_m1"), writes=("r_eb",))
        p.op("scalar", lambda h: h.activation(
            out=r_eb[:], in_=r_eb[:], func=AF.Exp),
            reads=("r_eb",), writes=("r_eb",))
        eb3 = r_eb[:].rearrange("p (t e) -> p t e", e=E1)
        p.op("vector", lambda h: h.tensor_reduce(
            out=r_sc[:], in_=eb3, axis=AX.X, op=ALU.add),
            reads=("r_eb",), writes=("r_sc",))
        p.op("vector", lambda h: h.reciprocal(out=r_sc[:], in_=r_sc[:]),
             reads=("r_sc",), writes=("r_sc",))
        p.op("scalar", lambda h: h.copy(out=comb[:], in_=Ltm[:]),
             reads=("Ltm",), writes=("comb",))
        for rnd in range(2):
            p.op("vector", lambda h: h.tensor_reduce(
                out=r_m1[:], in_=comb[:].rearrange("p (t e) -> p t e", e=E1),
                axis=AX.X, op=ALU.max),
                reads=("comb",), writes=("r_m1",))
            p.op("vector", lambda h: h.tensor_tensor(
                out=mskb[:].rearrange("p (t e) -> p t e", e=E1),
                in0=comb[:].rearrange("p (t e) -> p t e", e=E1), in1=m1b,
                op=ALU.is_ge),
                reads=("comb", "r_m1"), writes=("mskb",))
            p.op("vector", lambda h: h.tensor_scalar(
                out=mskb[:], in0=mskb[:], scalar1=-1e30, scalar2=None, op0=ALU.mult),
                reads=("mskb",), writes=("mskb",))
            p.op("vector", lambda h: h.tensor_add(
                out=comb[:], in0=comb[:], in1=mskb[:]),
                reads=("comb", "mskb"), writes=("comb",))
        p.op("vector", lambda h: h.tensor_reduce(
            out=r_m1[:], in_=comb[:].rearrange("p (t e) -> p t e", e=E1),
            axis=AX.X, op=ALU.max),
            reads=("comb",), writes=("r_m1",))
        p.op("vector", lambda h: h.tensor_tensor(
            out=comb[:].rearrange("p (t e) -> p t e", e=E1), in0=L3, in1=m1b,
            op=ALU.is_ge),
            reads=("Ltm", "r_m1"), writes=("comb",))
        p.op("vector", lambda h: h.tensor_mul(
            out=comb[:], in0=comb[:], in1=r_eb[:]),
            reads=("comb", "r_eb"), writes=("comb",))
        recb = r_sc[:].unsqueeze(2).broadcast_to([128, nTT, E1])
        p.op("vector", lambda h: h.tensor_tensor(
            out=comb[:].rearrange("p (t e) -> p t e", e=E1),
            in0=comb[:].rearrange("p (t e) -> p t e", e=E1), in1=recb,
            op=ALU.mult),
            reads=("comb", "r_sc"), writes=("comb",))
        for i in range(nTT):
            p.op("tensor", lambda h, i=i: h.transpose(
                out=P[3][0:E1, 0:128], in_=comb[:, i * E1:(i + 1) * E1],
                identity=ident_f[:]),
                reads=("comb", "ident_f"), writes=("P3",))
            p.op("scalar", lambda h, i=i: h.copy(
                out=combE[:, ts(i, 128)], in_=P[3][0:E1, 0:128]),
                reads=("P3",), writes=("combE",))
        load_direct(sW1_h, w1d[0], [f"w1d0_{j}" for j in range(nD)], D, H1)
        load_direct(sW2_h, w2d[0], [f"w2d0_{j}" for j in range(nH)], H1, D)
        load_bias_cols_2d(sb1_h, "b1c0", H1, b1c[0])
        load_bias_cols_2d(sb2_h, "b2c0", D, b2c[0])
        for ht in range(nH):
            for cc in range(TH // 512):
                cs = ts(cc, 512)
                for q in range(nD):
                    p.op("tensor", lambda h, q=q, ht=ht, cs=cs: h.matmul(
                        out=P[0][:, :], lhsT=w1d[0][q][:, ts(ht, 128)], rhs=xb[q][:, cs],
                        start=(q == 0), stop=(q == nD - 1)),
                        reads=(f"w1d0_{q}", f"xb{q}"), writes=("P0",))
                p.op("scalar", lambda h, ht=ht, cs=cs: h.activation(
                    out=hb[ht][:, cs], in_=P[0][:, :], func=AF.Relu,
                    bias=b1c[0][:, ht:ht + 1]),
                    reads=("P0", "b1c0"), writes=(f"hb{ht}",))
        for o in range(nD):
            for cc in range(TH // 512):
                cs = ts(cc, 512)
                for ht in range(nH):
                    p.op("tensor", lambda h, ht=ht, o=o, cs=cs: h.matmul(
                        out=P[2][:, :], lhsT=w2d[0][ht][:, ts(o, 128)], rhs=hb[ht][:, cs],
                        start=(ht == 0), stop=(ht == nH - 1)),
                        reads=(f"w2d0_{ht}", f"hb{ht}"), writes=("P2",))
                p.op("vector", lambda h, o=o, cs=cs: h.tensor_scalar(
                    out=out_acc[o][:, cs], in0=P[2][:, :], scalar1=b2c[0][:, o:o + 1],
                    scalar2=None, op0=ALU.add),
                    reads=("P2", "b2c0"), writes=(f"oacc{o}",))
        for e in range(E1):
            for cc in range(TH // 512):
                p.op("tensor", lambda h, e=e, cc=cc: h.matmul(
                    out=P[3][:, :], lhsT=sel[e][:], rhs=combE[:, ts(cc, 512)],
                    start=True, stop=True),
                    reads=(f"sel{e}", "combE"), writes=("P3",))
                p.op("scalar", lambda h, cc=cc: h.copy(
                    out=cbc[:, ts(cc, 512)], in_=P[3][:, :]),
                    reads=("P3",), writes=("cbc",))
            bb = (e + 1) % 2
            load_direct(eW1_h[e], w1d[bb], [f"w1d{bb}_{j}" for j in range(nD)], D, H1)
            load_direct(eW2_h[e], w2d[bb], [f"w2d{bb}_{j}" for j in range(nH)], H1, D)
            load_bias_cols_2d(eb1_h[e], f"b1c{bb}", H1, b1c[bb])
            load_bias_cols_2d(eb2_h[e], f"b2c{bb}", D, b2c[bb])
            for ht in range(nH):
                for cc in range(TH // 512):
                    cs = ts(cc, 512)
                    for q in range(nD):
                        p.op("tensor", lambda h, q=q, ht=ht, cs=cs, bb=bb: h.matmul(
                            out=P[0][:, :], lhsT=w1d[bb][q][:, ts(ht, 128)], rhs=xb[q][:, cs],
                            start=(q == 0), stop=(q == nD - 1)),
                            reads=(f"w1d{bb}_{q}", f"xb{q}"), writes=("P0",))
                    p.op("scalar", lambda h, ht=ht, cs=cs, bb=bb: h.activation(
                        out=hb[ht][:, cs], in_=P[0][:, :], func=AF.Relu,
                        bias=b1c[bb][:, ht:ht + 1]),
                        reads=("P0", f"b1c{bb}"), writes=(f"hb{ht}",))
            for o in range(nD):
                for cc in range(TH // 512):
                    cs = ts(cc, 512)
                    for ht in range(nH):
                        p.op("tensor", lambda h, ht=ht, o=o, cs=cs, bb=bb: h.matmul(
                            out=P[2][:, :], lhsT=w2d[bb][ht][:, ts(o, 128)], rhs=hb[ht][:, cs],
                            start=(ht == 0), stop=(ht == nH - 1)),
                            reads=(f"w2d{bb}_{ht}", f"hb{ht}"), writes=("P2",))
                    p.op("vector", lambda h, o=o, cs=cs, bb=bb: h.scalar_tensor_tensor(
                        out=tmp_y[:], in0=P[2][:, :], scalar=b2c[bb][:, o:o + 1],
                        in1=cbc[:, cs], op0=ALU.add, op1=ALU.mult),
                        reads=("P2", f"b2c{bb}", "cbc"), writes=("tmp_y",))
                    p.op("vector", lambda h, o=o, cs=cs: h.tensor_add(
                        out=out_acc[o][:, cs], in0=out_acc[o][:, cs], in1=tmp_y[:]),
                        reads=(f"oacc{o}", "tmp_y"), writes=(f"oacc{o}",))
        for cc in range(TH // 512):
            cs = ts(cc, 512)
            for j in range(nD):
                p.op("vector", lambda h, j=j, cs=cs: h.tensor_add(
                    out=rTc[j][:], in0=xb[j][:, cs], in1=out_acc[j][:, cs]),
                    reads=(f"xb{j}", f"oacc{j}"), writes=(f"rTc{j}",))
            emit_ln_chunk(p, rTc, [f"rTc{j}" for j in range(nD)],
                          rTc, [f"rTc{j}" for j in range(nD)], "P4", "P5",
                          ln_bufs, C, 512)
            for j in range(nD):
                p.dma("sync", out=outT_hbm[ts(j, 128), t0 + cc * 512:t0 + (cc + 1) * 512],
                      in_=rTc[j][:], reads=(f"rTc{j}",), writes=(f"outT{j}",))
    p.barrier_all("sync")



def build_l1_attn(nc, lseq=LSEQ):
    tok = NSEQ * lseq
    xT = nc.dram_tensor("xT", [D, tok], F32, kind="ExternalInput").ap()
    W_in = nc.dram_tensor("W_in", [D, 2 * DIN], F32, kind="ExternalInput").ap()
    conv_w = nc.dram_tensor("conv_w", [DIN, DCONV], F32, kind="ExternalInput").ap()
    conv_b = nc.dram_tensor("conv_b", [DIN], F32, kind="ExternalInput").ap()
    W_x = nc.dram_tensor("W_x", [DIN, 2 * DST + 1], F32, kind="ExternalInput").ap()
    A_log = nc.dram_tensor("A_log", [DIN, DST], F32, kind="ExternalInput").ap()
    dt_bias = nc.dram_tensor("dt_bias", [DIN], F32, kind="ExternalInput").ap()
    D_skip = nc.dram_tensor("D_skip", [DIN], F32, kind="ExternalInput").ap()
    W_out = nc.dram_tensor("W_out", [DIN, D], F32, kind="ExternalInput").ap()
    outT = nc.dram_tensor("outT", [D, tok], F32, kind="ExternalOutput").ap()
    es = ExitStack()
    with es:
        al = Alloc(nc, es)
        p = Prog(nc)
        p.make_sems(es)
        emit_l1_attn(nc, p, al, xT,
                     (W_in, conv_w, conv_b, W_x, A_log, dt_bias, D_skip, W_out),
                     outT, lseq)
        block = es.enter_context(nc.Block())
        p.finish(block)
    return nc


def emit_l1_attn(nc, p, al, xT_hbm, W, outT_hbm, lseq):
    (Win_h, convw_h, convb_h, Wx_h, Alog_h, dtb_h, Dskip_h, Wout_h) = W
    nD = D // 128
    nI = DIN // 128
    TC = 128
    TP = 4
    NCH = lseq // TC
    C = make_consts(p, al)
    P = [al.ps(f"P{i}", [128, 512], F32) for i in range(2)]
    ps_y = al.ps("ps_y", [128, 2048], F32)
    ps_tf = al.ps("ps_tf", [128, 512], F32)
    ps_tb = al.ps("ps_tb", [128, 1024], BF16)

    ident_f = al.sb("ident_f", [128, 128], F32)
    p.op("gpsimd", lambda h: h.memset(ident_f[:], 0.0), writes=("ident_f",))
    p.op("gpsimd", lambda h: h.affine_select(
        out=ident_f[:], in_=ident_f[:], compare_op=ALU.not_equal, fill=1.0,
        base=0, pattern=[[-1, 128]], channel_multiplier=1),
        reads=("ident_f",), writes=("ident_f",))
    ident_b = al.sb("ident_b", [128, 128], BF16)
    p.op("gpsimd", lambda h: h.memset(ident_b[:], 0.0), writes=("ident_b",))
    p.op("gpsimd", lambda h: h.affine_select(
        out=ident_b[:], in_=ident_b[:], compare_op=ALU.not_equal, fill=1.0,
        base=0, pattern=[[-1, 128]], channel_multiplier=1),
        reads=("ident_b",), writes=("ident_b",))

    stage_f = al.sb("stg_f", [128, 512], F32)

    cwc = al.sb("cwc", [128, nI * DCONV], F32)
    gw_ = p.new_gid()
    for j in range(nI):
        for k in range(DCONV):
            p.dma("sync", out=cwc[:, j * DCONV + k:j * DCONV + k + 1],
                  in_=convw_h[ts(j, 128), k].unsqueeze(1), writes=("cwc",), slow=True,
                  wgroup=gw_)
    cbc_ = al.sb("cbc_", [128, nI], F32)
    dbc = al.sb("dbc", [128, nI], F32)
    skc = al.sb("skc", [128, nI], F32)
    g1_, g2_, g3_ = p.new_gid(), p.new_gid(), p.new_gid()
    for j in range(nI):
        p.dma("sync", out=cbc_[:, j:j + 1], in_=convb_h[ts(j, 128)].unsqueeze(1),
              writes=("cbc_",), wgroup=g1_)
        p.dma("sync", out=dbc[:, j:j + 1], in_=dtb_h[ts(j, 128)].unsqueeze(1),
              writes=("dbc",), wgroup=g2_)
        p.dma("sync", out=skc[:, j:j + 1], in_=Dskip_h[ts(j, 128)].unsqueeze(1),
              writes=("skc",), wgroup=g3_)

    A_bc = al.sb("A_bc", [128, DIN], BF16)
    atile = al.sb("atile", [128, DST], F32)
    for j in range(nI):
        p.dma("sync", out=stage_f[:, 0:DST], in_=Alog_h[ts(j, 128), :],
              writes=("stg_f",))
        p.op("scalar", lambda h: h.activation(
            out=atile[:], in_=stage_f[:, 0:DST], func=AF.Exp),
            reads=("stg_f",), writes=("atile",))
        p.op("vector", lambda h: h.tensor_scalar(
            out=atile[:], in0=atile[:], scalar1=-1.0, scalar2=None, op0=ALU.mult),
            reads=("atile",), writes=("atile",))
        p.op("tensor", lambda h, j=j: h.transpose(
            out=ps_tf[0:DST, 0:128], in_=atile[:], identity=ident_f[:]),
            reads=("atile", "ident_f"), writes=("ps_tf",))
        p.op("vector", lambda h, j=j: h.tensor_copy(
            out=A_bc[0:DST, ts(j, 128)], in_=ps_tf[0:DST, 0:128]),
            reads=("ps_tf",), writes=("A_bc",))
    p.op("vector", lambda h: h.tensor_copy(
        out=A_bc[64:128, :], in_=A_bc[0:64, :]),
        reads=("A_bc",), writes=("A_bc",))

    wxT = [al.sb(f"wxT{j}", [128, 2 * DST + 1], BF16) for j in range(nI)]
    for j in range(nI):
        p.dma("sync", out=stage_f[:, 0:2 * DST + 1], in_=Wx_h[ts(j, 128), :],
              writes=("stg_f",))
        p.op("vector", lambda h, j=j: h.tensor_copy(
            out=wxT[j][:], in_=stage_f[:, 0:2 * DST + 1]),
            reads=("stg_f",), writes=(f"wxT{j}",))

    T2 = 2 * TC
    xb = [al.sb(f"xb{j}", [128, T2], BF16) for j in range(nD)]
    xs = [al.sb(f"xs{j}", [128, 2 * (TC + 3)], BF16) for j in range(nI)]
    zt = [al.sb(f"zt{j}", [128, T2], BF16) for j in range(nI)]
    xc = [al.sb(f"xc{j}", [128, T2], BF16) for j in range(nI)]
    dt16 = [al.sb(f"dt16{j}", [128, T2], BF16) for j in range(nI)]
    dtx16 = [al.sb(f"dtx16{j}", [128, T2], BF16) for j in range(nI)]
    dtT = [al.sb(f"dtT{r}", [128, DIN], BF16) for r in range(2)]
    dtxT = [al.sb(f"dtxT{r}", [128, DIN], BF16) for r in range(2)]
    dt_dram = nc.dram_tensor("dt_dram", [2, TC * DIN], BF16, kind="Internal").ap()
    dtx_dram = nc.dram_tensor("dtx_dram", [2, TC * DIN], BF16, kind="Internal").ap()
    B_stk = al.sb("B_stk", [128, TC], F32)
    C_stk = al.sb("C_stk", [128, TC], F32)
    Csel = al.sb("Csel", [128, 128], F32)
    r_bc = al.sb("r_bc", [128, T2], F32)
    hst = [al.sb(f"hst{b}", [128, DIN], F32) for b in range(2)]
    dt_rep = [al.sb(f"dt_rep{b}", [128, TP * DIN], BF16) for b in range(2)]
    dtx_rep = [al.sb(f"dtx_rep{b}", [128, TP * DIN], BF16) for b in range(2)]
    g_rep = dt_rep
    yT = [al.sb(f"yT{j}", [128, T2], BF16) for j in range(nI)]
    scr1 = al.sb("scr1", [128, T2], F32)
    scr2 = al.sb("scr2", [128, T2], F32)
    wslab = [al.sb(f"wslab{b}", [128, 2048], BF16) for b in range(2)]
    yv = wslab[0]
    zs = al.sb("zs", [128, T2], BF16)
    rTc = [al.sb(f"rTc{j}", [128, T2], F32) for j in range(nD)]
    ln_bufs = dict(
        ps_s0=P[0][0:1, :], ps_s1=P[1][0:1, :], bc_m=P[0], bc_r=P[1],
        row_m=al.sb("ln_rowm", [1, T2], F32),
        row_r=al.sb("ln_rowr", [1, T2], F32),
        r2=al.sb("ln_r2", [128, T2], F32),
    )

    p.op("vector", lambda h: h.memset(Csel[:], 0.0), writes=("Csel",))

    for j in range(nI):
        p.op("vector", lambda h, j=j: h.memset(xs[j][:], 0.0), writes=(f"xs{j}",))

    for ch in range(NCH):
        t0 = ch * TC
        if ch == 0:
            p.op("vector", lambda h: h.memset(hst[0][:], 0.0), writes=("hst0",))
            p.op("vector", lambda h: h.memset(hst[1][:], 0.0), writes=("hst1",))
        for j in range(nD):
            g = p.new_gid()
            for s in range(NSEQ):
                p.dma("gpsimd", out=xb[j][:, s * TC:(s + 1) * TC],
                      in_=xT_hbm[ts(j, 128), s * lseq + t0:s * lseq + t0 + TC],
                      writes=(f"xb{j}",), wgroup=g)
        emit_l1_pre(p, al, nc, Win_h, xb, xs, zt, xc, cwc, cbc_, wxT, dt16,
                    dtx16, dtT, dtxT, B_stk, C_stk, r_bc, dbc, stage_f, wslab,
                    ident_b, ident_f, ps_tf, ps_tb, P, C, scr1, scr2, ch, nI, nD, TC)
        emit_l1_scan(p, al, nc, A_bc, dt_rep, dtx_rep, g_rep, dtT, dtxT, dt_dram,
                     dtx_dram, B_stk, C_stk, Csel, hst, ps_y, ps_tb, yT, yv,
                     ident_b, ch, TC, TP, nI)
        emit_l1_post(p, al, nc, Wout_h, yT, xc, zt, zs, skc, xb, rTc, stage_f,
                     wslab, P, C, ln_bufs, outT_hbm, lseq, t0, nI, nD, TC)
    p.barrier_all("sync")


def emit_l1_pre(p, al, nc, Win_h, xb, xs, zt, xc, cwc, cbc_, wxT, dt16,
                dtx16, dtT, dtxT, B_stk, C_stk, r_bc, dbc, stage_f, wslab,
                ident_b, ident_f, ps_tf, ps_tb, P, C, scr1, scr2, ch, nI, nD, TC):
    T2 = 2 * TC
    if ch > 0:
        for j in range(nI):
            p.op("vector", lambda h, j=j: h.tensor_copy(
                out=xs[j][:].rearrange("p (s c) -> p s c", s=2)[:, :, 0:3],
                in_=xs[j][:].rearrange("p (s c) -> p s c", s=2)[:, :, TC:TC + 3]),
                reads=(f"xs{j}",), writes=(f"xs{j}",))
    for it in range(2 * nI):
        src = Win_h[:, ts(it, 128)].rearrange("(q p) c -> p q c", p=128)
        sl = wslab[it % 2]
        slk = f"wslab{it % 2}"
        p.dma("gpsimd", out=sl[:, 0:nD * 128].rearrange("p (q c) -> p q c", q=nD),
              in_=src, writes=(slk,))
        for q in range(nD):
            p.op("tensor", lambda h, q=q, sl=sl: h.matmul(
                out=P[0][:, 0:T2], lhsT=sl[:, ts(q, 128)], rhs=xb[q][:, 0:T2],
                start=(q == 0), stop=(q == nD - 1)),
                reads=(slk, f"xb{q}"), writes=("P0",))
        if it < nI:
            xsv = xs[it][:].rearrange("p (s c) -> p s c", s=2)[:, :, 3:3 + TC]
            psv = P[0][:, 0:T2].rearrange("p (s c) -> p s c", s=2)
            p.op("scalar", lambda h, it=it, xsv=xsv, psv=psv: h.copy(
                out=xsv, in_=psv),
                reads=("P0",), writes=(f"xs{it}",))
        else:
            p.op("scalar", lambda h, it=it: h.copy(
                out=zt[it - nI][:], in_=P[0][:, 0:T2]),
                reads=("P0",), writes=(f"zt{it - nI}",))
    for j in range(nI):
        x3 = xs[j][:].rearrange("p (s c) -> p s c", s=2)
        p.op("vector", lambda h, j=j, x3=x3: h.tensor_scalar(
            out=scr1[:].rearrange("p (s t) -> p s t", s=2),
            in0=x3[:, :, 0:TC], scalar1=cwc[:, j * DCONV:j * DCONV + 1],
            scalar2=None, op0=ALU.mult),
            reads=(f"xs{j}", "cwc"), writes=("scr1",))
        for k in range(1, DCONV):
            p.op("vector", lambda h, j=j, k=k, x3=x3: h.scalar_tensor_tensor(
                out=scr1[:].rearrange("p (s t) -> p s t", s=2),
                in0=x3[:, :, k:k + TC], scalar=cwc[:, j * DCONV + k:j * DCONV + k + 1],
                in1=scr1[:].rearrange("p (s t) -> p s t", s=2),
                op0=ALU.mult, op1=ALU.add),
                reads=(f"xs{j}", "cwc", "scr1"), writes=("scr1",))
        p.op("scalar", lambda h, j=j: h.activation(
            out=xc[j][:], in_=scr1[:], func=AF.Silu, bias=cbc_[:, j:j + 1]),
            reads=("scr1", "cbc_"), writes=(f"xc{j}",))
    for q in range(nI):
        p.op("tensor", lambda h, q=q: h.matmul(
            out=P[1][:, 0:T2], lhsT=wxT[q][:, 0:2 * DST], rhs=xc[q][:, 0:T2],
            start=(q == 0), stop=(q == nI - 1)),
            reads=(f"wxT{q}", f"xc{q}"), writes=("P1",))
    for q in range(nI):
        p.op("tensor", lambda h, q=q: h.matmul(
            out=ps_tf[0:1, 0:T2], lhsT=wxT[q][:, 2 * DST:2 * DST + 1],
            rhs=xc[q][:, 0:T2],
            start=(q == 0), stop=(q == nI - 1)),
            reads=(f"wxT{q}", f"xc{q}"), writes=("ps_tf",))
    for s in range(2):
        p.op("vector", lambda h, s=s: h.tensor_copy(
            out=B_stk[s * DST:(s + 1) * DST, :], in_=P[1][0:DST, s * TC:(s + 1) * TC]),
            reads=("P1",), writes=("B_stk",))
        p.op("vector", lambda h, s=s: h.tensor_copy(
            out=C_stk[s * DST:(s + 1) * DST, :], in_=P[1][DST:2 * DST, s * TC:(s + 1) * TC]),
            reads=("P1",), writes=("C_stk",))
    rrow = scr2
    p.op("scalar", lambda h: h.copy(out=scr2[0:1, 0:T2], in_=ps_tf[0:1, 0:T2]),
         reads=("ps_tf",), writes=("scr2",))
    p.op("tensor", lambda h: h.matmul(
        out=ps_tf[:, 0:T2], lhsT=C["ones_row"], rhs=scr2[0:1, 0:T2],
        start=True, stop=True),
        reads=("scr2", "ones_row"), writes=("ps_tf",))
    p.op("vector", lambda h: h.tensor_copy(out=r_bc[:, 0:T2], in_=ps_tf[:, 0:T2]),
         reads=("ps_tf",), writes=("r_bc",))
    for j in range(nI):
        p.op("scalar", lambda h, j=j: h.activation(
            out=scr1[:], in_=r_bc[:, 0:T2], func=AF.Abs, bias=dbc[:, j:j + 1]),
            reads=("r_bc", "dbc"), writes=("scr1",))
        p.op("scalar", lambda h: h.activation(
            out=scr1[:], in_=scr1[:], func=AF.Exp, scale=-1.0),
            reads=("scr1",), writes=("scr1",))
        p.op("scalar", lambda h: h.activation(
            out=scr1[:], in_=scr1[:], func=AF.Ln, bias=1.0),
            reads=("scr1", "cpool"), writes=("scr1",))
        p.op("scalar", lambda h, j=j: h.activation(
            out=scr2[:], in_=r_bc[:, 0:T2], func=AF.Relu, bias=dbc[:, j:j + 1]),
            reads=("r_bc", "dbc"), writes=("scr2",))
        p.op("vector", lambda h, j=j: h.tensor_add(
            out=dt16[j][:], in0=scr1[:], in1=scr2[:]),
            reads=("scr1", "scr2"), writes=(f"dt16{j}",))
        p.op("vector", lambda h, j=j: h.tensor_mul(
            out=dtx16[j][:], in0=dt16[j][:], in1=xc[j][:]),
            reads=(f"dt16{j}", f"xc{j}"), writes=(f"dtx16{j}",))
    for j in range(nI):
        for cb in range(2):
            p.op("tensor", lambda h, j=j, cb=cb: h.transpose(
                out=ps_tb[:, 0:128], in_=dt16[j][:, ts(cb, 128)], identity=ident_b[:]),
                reads=(f"dt16{j}", "ident_b"), writes=("ps_tb",))
            p.op("vector", lambda h, j=j, cb=cb: h.tensor_copy(
                out=dtT[cb][:, ts(j, 128)], in_=ps_tb[:, 0:128]),
                reads=("ps_tb",), writes=(f"dtT{cb}",))
            p.op("tensor", lambda h, j=j, cb=cb: h.transpose(
                out=ps_tb[:, 0:128], in_=dtx16[j][:, ts(cb, 128)], identity=ident_b[:]),
                reads=(f"dtx16{j}", "ident_b"), writes=("ps_tb",))
            p.op("vector", lambda h, j=j, cb=cb: h.tensor_copy(
                out=dtxT[cb][:, ts(j, 128)], in_=ps_tb[:, 0:128]),
                reads=("ps_tb",), writes=(f"dtxT{cb}",))


def emit_l1_scan(p, al, nc, A_bc, dt_rep, dtx_rep, g_rep, dtT, dtxT, dt_dram,
                 dtx_dram, B_stk, C_stk, Csel, hst, ps_y, ps_tb, yT, yv,
                 ident_b, ch, TC, TP, nI):
    DINl = DIN
    nsub = TC // TP
    g1, g2 = p.new_gid(), p.new_gid()
    for s in range(2):
        p.dma("sync", out=dt_dram[s:s + 1, :], in_=dtT[s][:],
              reads=(f"dtT{s}",), writes=("dt_dram",), wgroup=g1)
        p.dma("scalar", out=dtx_dram[s:s + 1, :], in_=dtxT[s][:],
              reads=(f"dtxT{s}",), writes=("dtx_dram",), wgroup=g2)
    for sub in range(nsub):
        tb = sub * TP
        bi = sub % 2
        p.dma("sync", out=dt_rep[bi][:],
              in_=dt_dram[:, tb * DINl:(tb + TP) * DINl]
              .unsqueeze(1).broadcast_to([2, 64, TP * DINl]),
              reads=("dt_dram",), writes=(f"dt_rep{bi}",))
        p.dma("scalar", out=dtx_rep[bi][:],
              in_=dtx_dram[:, tb * DINl:(tb + TP) * DINl]
              .unsqueeze(1).broadcast_to([2, 64, TP * DINl]),
              reads=("dtx_dram",), writes=(f"dtx_rep{bi}",))
        p.op("vector", lambda h, bi=bi: h.tensor_tensor(
            out=dt_rep[bi][:].rearrange("p (t d) -> p t d", t=TP),
            in0=A_bc[:].unsqueeze(1).broadcast_to([128, TP, DINl]),
            in1=dt_rep[bi][:].rearrange("p (t d) -> p t d", t=TP),
            op=ALU.mult),
            reads=("A_bc", f"dt_rep{bi}"), writes=(f"dt_rep{bi}",))
        p.op("scalar", lambda h, bi=bi: h.activation(
            out=dt_rep[bi][:], in_=dt_rep[bi][:], func=AF.Exp),
            reads=(f"dt_rep{bi}",), writes=(f"dt_rep{bi}",))
        for tau in range(TP):
            t = tb + tau
            tg = t % 64
            hi, ho = (ch * TC + t) % 2, (ch * TC + t + 1) % 2
            pz = (tg - 1) % 64
            p.op("gpsimd", lambda h, pz=pz: h.memset(Csel[:, pz:pz + 1], 0.0),
                 reads=(), writes=("Csel",))
            p.op("gpsimd", lambda h, pz=pz: h.memset(Csel[:, 64 + pz:64 + pz + 1], 0.0),
                 reads=(), writes=("Csel",))
            p.op("gpsimd", lambda h, t=t, tg=tg: h.tensor_copy(
                out=Csel[0:64, tg:tg + 1], in_=C_stk[0:64, t:t + 1]),
                reads=("C_stk",), writes=("Csel",))
            p.op("gpsimd", lambda h, t=t, tg=tg: h.tensor_copy(
                out=Csel[64:128, 64 + tg:64 + tg + 1], in_=C_stk[64:128, t:t + 1]),
                reads=("C_stk",), writes=("Csel",))
            p.op("vector", lambda h, tau=tau, bi=bi, hi=hi, ho=ho: h.tensor_mul(
                out=hst[ho][:], in0=g_rep[bi][:, tau * DINl:(tau + 1) * DINl],
                in1=hst[hi][:]),
                reads=(f"dt_rep{bi}", f"hst{hi}"), writes=(f"hst{ho}",))
            p.op("vector", lambda h, tau=tau, t=t, bi=bi, ho=ho: h.scalar_tensor_tensor(
                out=hst[ho][:], in0=dtx_rep[bi][:, tau * DINl:(tau + 1) * DINl],
                scalar=B_stk[:, t:t + 1], in1=hst[ho][:],
                op0=ALU.mult, op1=ALU.add),
                reads=(f"dtx_rep{bi}", "B_stk", f"hst{ho}"), writes=(f"hst{ho}",))
            for q4 in range(4):
                p.op("tensor", lambda h, q4=q4, tg=tg, ho=ho: h.matmul(
                    out=ps_y[:, ts(q4, 512)], lhsT=Csel[:],
                    rhs=hst[ho][:, ts(q4, 512)],
                    start=(tg == 0), stop=(tg == 63), skip_group_check=True),
                    reads=("Csel", f"hst{ho}"), writes=("ps_y",))
        if (sub + 1) % (64 // TP) == 0:
            grp = sub // (64 // TP)
            p.op("scalar", lambda h: h.copy(out=yv[:], in_=ps_y[:, :]),
                 reads=("ps_y",), writes=("wslab0",))
            for jb in range(nI):
                p.op("tensor", lambda h, jb=jb: h.transpose(
                    out=ps_tb[:, 0:128], in_=yv[:, ts(jb, 128)], identity=ident_b[:]),
                    reads=("wslab0", "ident_b"), writes=("ps_tb",))
                ov = yT[jb][:].rearrange("p (s t) -> p s t", s=2)[:, :, grp * 64:grp * 64 + 64]
                iv = ps_tb[:, 0:128].rearrange("p (s t) -> p s t", s=2)
                p.op("vector", lambda h, ov=ov, iv=iv: h.tensor_copy(out=ov, in_=iv),
                     reads=("ps_tb",), writes=(f"yT{jb}",))


def emit_l1_post(p, al, nc, Wout_h, yT, xc, zt, zs, skc, xb, rTc, stage_f,
                 wslab, P, C, ln_bufs, outT_hbm, lseq, t0, nI, nD, TC):
    T2 = 2 * TC
    for j in range(nI):
        p.op("vector", lambda h, j=j: h.scalar_tensor_tensor(
            out=yT[j][:], in0=xc[j][:], scalar=skc[:, j:j + 1], in1=yT[j][:],
            op0=ALU.mult, op1=ALU.add),
            reads=(f"xc{j}", "skc", f"yT{j}"), writes=(f"yT{j}",))
        p.op("scalar", lambda h, j=j: h.activation(
            out=zs[:], in_=zt[j][:], func=AF.Silu),
            reads=(f"zt{j}",), writes=("zs",))
        p.op("vector", lambda h, j=j: h.tensor_mul(
            out=yT[j][:], in0=yT[j][:], in1=zs[:]),
            reads=(f"yT{j}", "zs"), writes=(f"yT{j}",))
    for o in range(nD):
        src = Wout_h[:, ts(o, 128)].rearrange("(q p) c -> p q c", p=128)
        sl = wslab[o % 2]
        slk = f"wslab{o % 2}"
        p.dma("gpsimd", out=sl[:, 0:nI * 128].rearrange("p (q c) -> p q c", q=nI),
              in_=src, writes=(slk,))
        for q in range(nI):
            p.op("tensor", lambda h, q=q, o=o, sl=sl: h.matmul(
                out=P[1][:, 0:T2], lhsT=sl[:, ts(q, 128)], rhs=yT[q][:],
                start=(q == 0), stop=(q == nI - 1)),
                reads=(slk, f"yT{q}"), writes=("P1",))
        p.op("vector", lambda h, o=o: h.tensor_add(
            out=rTc[o][:], in0=P[1][:, 0:T2], in1=xb[o][:, 0:T2]),
            reads=("P1", f"xb{o}"), writes=(f"rTc{o}",))
    emit_ln_chunk(p, rTc, [f"rTc{j}" for j in range(nD)],
                  rTc, [f"rTc{j}" for j in range(nD)], "P0", "P1",
                  ln_bufs, C, T2)
    for j in range(nD):
        for s in range(2):
            p.dma("sync", out=outT_hbm[ts(j, 128), s * lseq + t0:s * lseq + t0 + TC],
                  in_=rTc[j][:, s * TC:(s + 1) * TC],
                  reads=(f"rTc{j}",), writes=(f"outT{j}",))



_NC_CACHE = {}


def _get_nc(name, builder):
    if name not in _NC_CACHE:
        nc = bass.Bass("TRN2", target_bir_lowering=False, debug=False)
        builder(nc)
        _NC_CACHE[name] = nc
    return _NC_CACHE[name]


def kernel(**inp):
    n_cores = 8
    x = np.asarray(inp["L0_attn_x"])
    B_all = x.shape[0]
    per = B_all // n_cores

    xTs = [np.ascontiguousarray(
        x[c * per:(c + 1) * per].transpose(2, 0, 1).reshape(D, per * LSEQ))
        for c in range(n_cores)]

    ncA = _get_nc("A", build_l0_attn)
    w = {k: np.ascontiguousarray(np.asarray(inp[v]))
         for k, v in [("wq", "L0_attn_wq_w"), ("wk", "L0_attn_wk_w"),
                      ("wv", "L0_attn_wv_w"), ("wo", "L0_attn_wo_w"),
                      ("bq", "L0_attn_wq_b"), ("bk", "L0_attn_wk_b"),
                      ("bv", "L0_attn_wv_b"), ("bo", "L0_attn_wo_b")]}
    maps = [dict(xT=xTs[c], **w) for c in range(n_cores)]
    res = run_bass_kernel_spmd(ncA, maps, core_ids=list(range(n_cores)))
    x1 = [res.results[c]["outT"] for c in range(n_cores)]

    ncB = _get_nc("B", build_l0_ffn)
    w = {"gate_w": np.asarray(inp["L0_ffn_gate_w"]),
         "Wg": np.asarray(inp["L0_ffn_Wg"]),
         "W1": np.asarray(inp["L0_ffn_W1"]),
         "W2": np.asarray(inp["L0_ffn_W2"])}
    maps = [dict(xT=x1[c], **w) for c in range(n_cores)]
    res = run_bass_kernel_spmd(ncB, maps, core_ids=list(range(n_cores)))
    x2 = [res.results[c]["outT"] for c in range(n_cores)]

    ncC = _get_nc("C", build_l1_attn)
    w = {"W_in": np.asarray(inp["L1_attn_W_in"]),
         "conv_w": np.asarray(inp["L1_attn_conv_w"]),
         "conv_b": np.asarray(inp["L1_attn_conv_b"]),
         "W_x": np.asarray(inp["L1_attn_W_x"]),
         "A_log": np.asarray(inp["L1_attn_A_log"]),
         "dt_bias": np.asarray(inp["L1_attn_dt_bias"]),
         "D_skip": np.asarray(inp["L1_attn_D"]),
         "W_out": np.asarray(inp["L1_attn_W_out"])}
    maps = [dict(xT=x2[c], **w) for c in range(n_cores)]
    res = run_bass_kernel_spmd(ncC, maps, core_ids=list(range(n_cores)))
    x3 = [res.results[c]["outT"] for c in range(n_cores)]

    ncD = _get_nc("D", build_l1_ffn)
    w = {"router_W": np.asarray(inp["L1_ffn_router_W"]),
         "router_b": np.asarray(inp["L1_ffn_router_b"]),
         "sW1": np.asarray(inp["L1_ffn_sW1"]),
         "sb1": np.asarray(inp["L1_ffn_sb1"]),
         "sW2": np.asarray(inp["L1_ffn_sW2"]),
         "sb2": np.asarray(inp["L1_ffn_sb2"]),
         "eW1": np.asarray(inp["L1_ffn_eW1"]),
         "eb1": np.asarray(inp["L1_ffn_eb1"]),
         "eW2": np.asarray(inp["L1_ffn_eW2"]),
         "eb2": np.asarray(inp["L1_ffn_eb2"])}
    maps = [dict(xT=x3[c], **w) for c in range(n_cores)]
    res = run_bass_kernel_spmd(ncD, maps, core_ids=list(range(n_cores)))
    x4 = [res.results[c]["outT"] for c in range(n_cores)]

    out = np.concatenate(
        [x4[c].reshape(D, per, LSEQ).transpose(1, 2, 0) for c in range(n_cores)],
        axis=0)
    return out.astype(np.float32)


# version 08
# speedup vs baseline: 1.8676x; 1.1019x over previous
import numpy as np
from contextlib import ExitStack

import concourse.bass as bass
import concourse.mybir as mybir
from concourse.bass_utils import run_bass_kernel_spmd

F32 = mybir.dt.float32
BF16 = mybir.dt.bfloat16
AF = mybir.ActivationFunctionType
ALU = mybir.AluOpType
AX = mybir.AxisListType

D = 1024
NSEQ = 2
LSEQ = 2048
TOK = NSEQ * LSEQ
NH = 16
DH = 64
E0, H0 = 8, 2736
E1, H1 = 15, 1024
DIN = 2048
DST = 64
DCONV = 4

ENGS = ("sync", "vector", "scalar", "tensor", "gpsimd")


class Prog:

    def __init__(self, nc):
        self.nc = nc
        self.ops = {e: [] for e in ENGS}
        self.icount = {e: 0 for e in ENGS}
        self.dcount = {e: 0 for e in ENGS}
        self.dslot = {}
        self.writers = {}
        self.readers = {}
        self._gid = 0
        self.hiw = {e: {} for e in ENGS}
        self.sems = {}
        self._es = None

    NDMA_SEMS = 8

    def make_sems(self, es):
        for e in ENGS:
            self.sems[("c", e)] = es.enter_context(self.nc.semaphore(f"sc_{e}"))
        for e in ("sync", "scalar", "gpsimd"):
            for s in range(self.NDMA_SEMS):
                self.sems[("d", e, s)] = es.enter_context(self.nc.semaphore(f"sd_{e}{s}"))

    def new_gid(self):
        self._gid += 1
        return self._gid

    def _dep_waits(self, eng, reads, writes, wgroup=None):
        need = {}
        def add(semkey, cnt):
            if need.get(semkey, 0) < cnt:
                need[semkey] = cnt
        for k in reads:
            w = self.writers.get(k)
            if w:
                for semkey, cnt in w["members"].items():
                    add(semkey, cnt)
        for k in writes:
            w = self.writers.get(k)
            if w and (wgroup is None or w["gid"] != wgroup):
                for semkey, cnt in w["members"].items():
                    add(semkey, cnt)
            if not (w and wgroup is not None and w["gid"] == wgroup):
                for semkey, cnt in self.readers.get(k, {}).items():
                    add(semkey, cnt)
        waits = []
        hw = self.hiw[eng]
        for semkey, cnt in sorted(need.items(), key=str):
            if semkey == ("c", eng) and cnt > self.icount[eng]:
                continue
            if semkey[0] == "d":
                tgt = 16 * self.dslot[(semkey[1], semkey[2])]
            else:
                tgt = cnt
            if hw.get(semkey, 0) < tgt:
                hw[semkey] = tgt
                waits.append((self.sems[semkey], tgt))
        return waits

    def _record_access(self, semkey, cnt, reads, writes, wgroup=None):
        for k in reads:
            self.readers.setdefault(k, {})[semkey] = cnt
        for k in writes:
            w = self.writers.get(k)
            if w and wgroup is not None and w["gid"] == wgroup:
                m = w["members"]
                if m.get(semkey, 0) < cnt:
                    m[semkey] = cnt
            else:
                self.writers[k] = {"gid": wgroup if wgroup is not None else self.new_gid(),
                                   "members": {semkey: cnt}}
                self.readers[k] = {}

    def op(self, eng, fn, reads=(), writes=(), inc=True, tag=None, wgroup=None):
        waits = self._dep_waits(eng, reads, writes, wgroup)
        cnt = self.icount[eng] + 1
        if inc:
            self.icount[eng] = cnt
        sem = self.sems[("c", eng)]
        def emit(h, waits=waits, fn=fn, inc=inc, sem=sem, tag=tag):
            for s, v in waits:
                h.wait_ge(s, v)
            ins = fn(h)
            if inc:
                ins.then_inc(sem, 1)
            if tag:
                ins.annotate(tag)
        self.ops[eng].append(emit)
        self._record_access(("c", eng), cnt, reads, writes, wgroup)

    def dma(self, eng, out, in_, reads=(), writes=(), tag=None, transpose=False,
            slow=False, wgroup=None):
        waits = self._dep_waits(eng, reads, writes, wgroup)
        slot = self.dcount[eng] % self.NDMA_SEMS
        self.dcount[eng] += 1
        key = ("d", eng, slot)
        prev = self.dslot.setdefault((eng, slot), 0)
        cnt = prev + 1
        self.dslot[(eng, slot)] = cnt
        sem = self.sems[key]
        if prev:
            hw = self.hiw[eng]
            if hw.get(key, 0) < 16 * prev:
                hw[key] = 16 * prev
                waits.append((sem, 16 * prev))
        def emit(h, waits=waits, out=out, in_=in_, sem=sem, tag=tag,
                 transpose=transpose, slow=slow):
            for s, v in waits:
                h.wait_ge(s, v)
            kw = {}
            if transpose:
                kw["transpose"] = True
            if slow:
                kw["allow_slow_non_contiguous"] = True
            ins = h.dma_start(out=out, in_=in_, **kw)
            ins.then_inc(sem, 16)
            if tag:
                ins.annotate(tag)
        self.ops[eng].append(emit)
        self._record_access(key, cnt, reads, writes, wgroup)

    def barrier_all(self, eng):
        waits = []
        hw = self.hiw[eng]
        for e in ENGS:
            if self.icount[e]:
                key = ("c", e)
                tgt = self.icount[e]
                if hw.get(key, 0) < tgt:
                    hw[key] = tgt
                    waits.append((self.sems[key], tgt))
        for (e, s), cnt in self.dslot.items():
            key = ("d", e, s)
            tgt = cnt * 16
            if hw.get(key, 0) < tgt:
                hw[key] = tgt
                waits.append((self.sems[key], tgt))
        def emit(h, waits=waits):
            for s, v in waits:
                h.wait_ge(s, v)
        self.ops[eng].append(emit)

    def finish(self, block):
        handles = {}
        def run(eng):
            def section(h):
                for fn in self.ops[eng]:
                    fn(h)
            return section
        block.sync(run("sync"))
        block.vector(run("vector"))
        block.scalar(run("scalar"))
        block.tensor(run("tensor"))
        block.gpsimd(run("gpsimd"))



def ts(i, n):
    return slice(i * n, (i + 1) * n)


def ceil_div(a, b):
    return -(-a // b)


class Alloc:

    def __init__(self, nc, es):
        self.nc = nc
        self.es = es
        self.n = 0

    def sb(self, name, shape, dtype):
        return self.es.enter_context(self.nc.sbuf_tensor(name, shape, dtype))

    def ps(self, name, shape, dtype=F32):
        return self.es.enter_context(self.nc.psum_tensor(name, shape, dtype))


def load_weight_T(p, al, hbm_ap, name, out_dim, in_dim, stage):
    nI, nO = in_dim // 128, out_dim // 128
    wT = [al.sb(f"{name}_T{i}", [128, out_dim], BF16) for i in range(nI)]
    stage_f, stage_b = stage
    gids = [p.new_gid() for _ in range(nI)]
    for o in range(nO):
        kf = "wstage_f"
        kb = "wstage_b"
        p.dma("sync", out=stage_f[:], in_=hbm_ap[ts(o, 128), :],
              reads=(), writes=(kf,))
        p.op("vector", lambda h, sf=stage_f, sbf=stage_b: h.tensor_copy(out=sbf[:], in_=sf[:]),
             reads=(kf,), writes=(kb,))
        for i in range(nI):
            p.dma("scalar", out=wT[i][:, ts(o, 128)], in_=stage_b[:, ts(i, 128)],
                  reads=(kb,), writes=(f"{name}_T{i}",), transpose=True, wgroup=gids[i])
    return wT


def load_weight_direct(p, al, hbm_ap, name, in_dim, out_dim):
    nI = in_dim // 128
    wT = []
    for i in range(nI):
        stage = al.sb(f"{name}_st{i}", [128, out_dim], F32)
        t = al.sb(f"{name}_d{i}", [128, out_dim], BF16)
        p.dma("sync", out=stage[:], in_=hbm_ap[ts(i, 128), :],
              reads=(), writes=(f"{name}_st{i}",))
        p.op("vector", lambda h, s=stage, t=t: h.tensor_copy(out=t[:], in_=s[:]),
             reads=(f"{name}_st{i}",), writes=(f"{name}_d{i}",))
        wT.append(t)
    return wT


def load_bias(p, al, hbm_ap, name, dim):
    nT = dim // 128
    col = al.sb(f"{name}_col", [128, nT], F32)
    p.dma("sync", out=col[:], in_=hbm_ap.rearrange("(t p) -> p t", p=128),
          reads=(), writes=(name,))
    return col


def load_bias_cols(p, al, hbm_ap, name, dim):
    nT = dim // 128
    col = al.sb(f"{name}_col", [128, nT], F32)
    g = p.new_gid()
    for j in range(nT):
        p.dma("sync", out=col[:, j:j + 1], in_=hbm_ap[ts(j, 128)].unsqueeze(1),
              reads=(), writes=(name,), wgroup=g)
    return col


def make_ones_and_masks(p, al, lseq, qc):
    ones_f = al.sb("ones_f", [128, 1], F32)
    ones_b = al.sb("ones_b", [128, 1], BF16)
    p.op("gpsimd", lambda h: h.memset(ones_f[:], 1.0), writes=("ones_f",))
    p.op("gpsimd", lambda h: h.memset(ones_b[:], 1.0), writes=("ones_b",))
    masks = {}
    for bi in range(qc // 128):
        delta = bi * 128
        m = al.sb(f"mask{bi}", [128, qc], BF16)
        p.op("gpsimd", lambda h, m=m: h.memset(m[:], 1.0), writes=(f"mask{bi}",))
        p.op("gpsimd", lambda h, m=m, delta=delta: h.affine_select(
            out=m[:], in_=m[:], compare_op=ALU.is_ge, fill=0.0,
            base=delta, pattern=[[1, qc]], channel_multiplier=-1),
            reads=(f"mask{bi}",), writes=(f"mask{bi}",))
        masks[delta] = m
    return ones_f, ones_b, masks


def emit_layernorm(p, al, r_tiles, rkeys, out_tiles, okeys, tok, tag, C,
                   cast_to=None, ckeys=None):
    nD = len(r_tiles)
    CH = 512
    ps_s0 = al.ps(f"{tag}_pss0", [1, 512], F32)
    ps_s1 = al.ps(f"{tag}_pss1", [1, 512], F32)
    row_m = al.sb(f"{tag}_rowm", [1, CH], F32)
    row_r = al.sb(f"{tag}_rowr", [1, CH], F32)
    r2 = al.sb(f"{tag}_r2", [128, CH], F32)
    bc_m = al.ps(f"{tag}_bcm", [128, 512], F32)
    bc_r = al.ps(f"{tag}_bcr", [128, 512], F32)
    inv = 1.0 / (nD * 128)
    for c in range(tok // CH):
        cs = ts(c, CH)
        for j in range(nD):
            p.op("tensor", lambda h, j=j, cs=cs: h.matmul(
                out=ps_s0[:, :], lhsT=C["ones_col"], rhs=r_tiles[j][:, cs],
                start=(j == 0), stop=(j == nD - 1)),
                reads=(rkeys[j], "ones_col"), writes=(f"{tag}_pss0",))
        for j in range(nD):
            p.op("scalar", lambda h, j=j, cs=cs: h.activation(
                out=r2[:], in_=r_tiles[j][:, cs], func=AF.Square),
                reads=(rkeys[j],), writes=(f"{tag}_r2",))
            p.op("tensor", lambda h, j=j: h.matmul(
                out=ps_s1[:, :], lhsT=C["ones_col"], rhs=r2[:],
                start=(j == 0), stop=(j == nD - 1)),
                reads=(f"{tag}_r2", "ones_col"), writes=(f"{tag}_pss1",))
        p.op("scalar", lambda h: h.activation(
            out=row_m[:], in_=ps_s0[:, :], func=AF.Copy, scale=inv),
            reads=(f"{tag}_pss0",), writes=(f"{tag}_rowm",))
        p.op("scalar", lambda h: h.activation(
            out=row_r[:], in_=row_m[:], func=AF.Square),
            reads=(f"{tag}_rowm",), writes=(f"{tag}_rowr",))
        p.op("vector", lambda h: h.scalar_tensor_tensor(
            out=row_r[:], in0=ps_s1[:, :], scalar=inv, in1=row_r[:],
            op0=ALU.mult, op1=ALU.subtract),
            reads=(f"{tag}_pss1", f"{tag}_rowr"), writes=(f"{tag}_rowr",))
        p.op("scalar", lambda h: h.activation(
            out=row_r[:], in_=row_r[:], func=AF.Ln, bias=1e-5),
            reads=(f"{tag}_rowr",), writes=(f"{tag}_rowr",))
        p.op("scalar", lambda h: h.activation(
            out=row_r[:], in_=row_r[:], func=AF.Exp, scale=-0.5),
            reads=(f"{tag}_rowr",), writes=(f"{tag}_rowr",))
        p.op("tensor", lambda h: h.matmul(
            out=bc_m[:, :CH], lhsT=C["ones_row"], rhs=row_m[:],
            start=True, stop=True),
            reads=(f"{tag}_rowm", "ones_row"), writes=(f"{tag}_bcm",))
        p.op("tensor", lambda h: h.matmul(
            out=bc_r[:, :CH], lhsT=C["ones_row"], rhs=row_r[:],
            start=True, stop=True),
            reads=(f"{tag}_rowr", "ones_row"), writes=(f"{tag}_bcr",))
        for j in range(nD):
            p.op("vector", lambda h, j=j, cs=cs: h.tensor_sub(
                out=out_tiles[j][:, cs], in0=r_tiles[j][:, cs], in1=bc_m[:, :CH]),
                reads=(rkeys[j], f"{tag}_bcm"), writes=(okeys[j],))
            p.op("vector", lambda h, j=j, cs=cs: h.tensor_mul(
                out=out_tiles[j][:, cs], in0=out_tiles[j][:, cs], in1=bc_r[:, :CH]),
                reads=(okeys[j], f"{tag}_bcr"), writes=(okeys[j],))
            if cast_to is not None:
                p.op("vector", lambda h, j=j, cs=cs: h.tensor_copy(
                    out=cast_to[j][:, cs], in_=out_tiles[j][:, cs]),
                    reads=(okeys[j],), writes=(ckeys[j],))


CONST_VALS = [1e-5, 1.0]


def make_consts(p, al):
    C = {}
    cpool = al.sb("cpool", [128, len(CONST_VALS)], F32)
    for i, v in enumerate(CONST_VALS):
        p.op("gpsimd", lambda h, i=i, v=v: h.memset(cpool[:, i:i + 1], v),
             writes=("cpool",))
        p.nc.const_aps.register(F32, v, cpool[:, i:i + 1])
    ones_col = al.sb("ones_col", [128, 1], F32)
    ones_row = al.sb("ones_row", [1, 128], F32)
    ones_row_b = al.sb("ones_row_b", [1, 128], BF16)
    p.op("gpsimd", lambda h: h.memset(ones_col[:], 1.0), writes=("ones_col",))
    p.op("gpsimd", lambda h: h.memset(ones_row[:], 1.0), writes=("ones_row",))
    p.op("gpsimd", lambda h: h.memset(ones_row_b[:], 1.0), writes=("ones_row_b",))
    C["ones_col"] = ones_col[:]
    C["ones_row"] = ones_row[:]
    C["ones_row_b"] = ones_row_b[:]
    return C


def load_xT(p, al, x_hbm, name, tok, dim=D, dtype=F32, cast_bf=True):
    nD = dim // 128
    xf = [al.sb(f"{name}_f{j}", [128, tok], F32) for j in range(nD)]
    xb = [al.sb(f"{name}_b{j}", [128, tok], BF16) for j in range(nD)] if cast_bf else None
    for j in range(nD):
        p.dma("sync", out=xf[j][:], in_=x_hbm[ts(j, 128), :], writes=(f"{name}_f{j}",))
        if cast_bf:
            p.op("vector", lambda h, j=j: h.tensor_copy(out=xb[j][:], in_=xf[j][:]),
                 reads=(f"{name}_f{j}",), writes=(f"{name}_b{j}",))
    return xf, xb



def build_l0_attn(nc, lseq=LSEQ):
    tok = NSEQ * lseq
    xT = nc.dram_tensor("xT", [D, tok], F32, kind="ExternalInput").ap()
    ws = [nc.dram_tensor(n, [D, D], F32, kind="ExternalInput").ap()
          for n in ("wq", "wk", "wv", "wo")]
    bs = [nc.dram_tensor(n, [D], F32, kind="ExternalInput").ap()
          for n in ("bq", "bk", "bv", "bo")]
    outT = nc.dram_tensor("outT", [D, tok], F32, kind="ExternalOutput").ap()
    es = ExitStack()
    with es:
        al = Alloc(nc, es)
        p = Prog(nc)
        p.make_sems(es)
        emit_l0_attn(nc, p, al, xT, ws, bs, outT, lseq)
        block = es.enter_context(nc.Block())
        p.finish(block)
    return nc


def emit_l0_attn(nc, p, al, xT_hbm, w_hbm, b_hbm, outT_hbm, lseq):
    nD = D // 128
    QC = 512
    C = make_consts(p, al)

    P = [al.ps(f"P{i}", [128, 512], F32) for i in range(6)]

    masks = {}
    for bi in range(QC // 128):
        delta = -bi * 128
        m = al.sb(f"mask{bi}", [128, QC], BF16)
        p.op("gpsimd", lambda h, m=m: h.memset(m[:], 1.0), writes=(f"mask{bi}",))
        p.op("gpsimd", lambda h, m=m, delta=delta: h.affine_select(
            out=m[:], in_=m[:], compare_op=ALU.is_ge, fill=0.0,
            base=delta, pattern=[[1, QC]], channel_multiplier=-1),
            reads=(f"mask{bi}",), writes=(f"mask{bi}",))
        masks[delta] = m

    stage_f = al.sb("wstage_f", [128, D], F32)
    stage_b = al.sb("wstage_b", [128, D], BF16)
    wqT = load_weight_T(p, al, w_hbm[0], "wq", D, D, (stage_f, stage_b))
    wkT = load_weight_T(p, al, w_hbm[1], "wk", D, D, (stage_f, stage_b))
    wvT = load_weight_T(p, al, w_hbm[2], "wv", D, D, (stage_f, stage_b))
    woT = load_weight_T(p, al, w_hbm[3], "wo", D, D, (stage_f, stage_b))
    bq = load_bias_cols(p, al, b_hbm[0], "bq", D)
    bk = load_bias_cols(p, al, b_hbm[1], "bk", D)
    bo = load_bias_cols(p, al, b_hbm[3], "bo", D)
    bv_row_f = al.sb("bv_rowf", [1, D], F32)
    bv_row = al.sb("bv_row", [1, D], BF16)
    p.dma("sync", out=bv_row_f[:], in_=b_hbm[2].unsqueeze(0), writes=("bv_rowf",))
    p.op("vector", lambda h: h.tensor_copy(out=bv_row[:], in_=bv_row_f[:]),
         reads=("bv_rowf",), writes=("bv_row",))

    xbc = [al.sb(f"xbc{j}", [128, 512], BF16) for j in range(nD)]
    KT = [al.sb(f"KT{j}", [128, lseq], BF16) for j in range(nD)]
    Vt = [al.sb(f"Vt{i}", [128, D], BF16) for i in range(lseq // 128)]
    QTc = [al.sb(f"QTc{j}", [128, QC], BF16) for j in range(nD)]
    ATc = [al.sb(f"ATc{j}", [128, QC], BF16) for j in range(nD)]
    rTc = [al.sb(f"rTc{j}", [128, QC], F32) for j in range(nD)]
    Sb = [al.sb(f"Sbb{i}", [128, QC], BF16) for i in range(2)]

    ln_bufs = dict(
        ps_s0=P[4][0:1, :], ps_s1=P[5][0:1, :],
        bc_m=P[4], bc_r=P[5],
        row_m=al.sb("ln_rowm", [1, QC], F32),
        row_r=al.sb("ln_rowr", [1, QC], F32),
        r2=al.sb("ln_r2", [128, QC], F32),
    )

    for seq in range(NSEQ):
        s_lo = seq * lseq
        for c in range(lseq // 512):
            cs = ts(c, 512)
            for j in range(nD):
                p.dma("gpsimd", out=xbc[j][:],
                      in_=xT_hbm[ts(j, 128), s_lo + c * 512:s_lo + (c + 1) * 512],
                      writes=(f"xbc{j}",))
            for e in range(nD):
                for j in range(nD):
                    p.op("tensor", lambda h, e=e, j=j: h.matmul(
                        out=P[0][:, :], lhsT=wkT[j][:, ts(e, 128)], rhs=xbc[j][:],
                        start=(j == 0), stop=(j == nD - 1)),
                        reads=(f"wk_T{j}", f"xbc{j}"), writes=("P0",))
                p.op("vector", lambda h, e=e, cs=cs: h.tensor_scalar(
                    out=KT[e][:, cs], in0=P[0][:, :], scalar1=bk[:, e:e + 1],
                    scalar2=None, op0=ALU.add),
                    reads=("P0", "bk"), writes=(f"KT{e}",))
            for i4 in range(4):
                i = c * 4 + i4
                for hhalf in range(D // 512):
                    hs = ts(hhalf, 512)
                    for j in range(nD):
                        p.op("tensor", lambda h, i4=i4, j=j, hs=hs: h.matmul(
                            out=P[1][:, :], lhsT=xbc[j][:, ts(i4, 128)], rhs=wvT[j][:, hs],
                            start=(j == 0), stop=False),
                            reads=(f"wv_T{j}", f"xbc{j}"), writes=("P1",))
                    p.op("tensor", lambda h, hs=hs: h.matmul(
                        out=P[1][:, :], lhsT=C["ones_row_b"], rhs=bv_row[:, hs],
                        start=False, stop=True),
                        reads=("bv_row", "ones_row_b"), writes=("P1",))
                    p.op("scalar", lambda h, i=i, hs=hs: h.copy(
                        out=Vt[i][:, hs], in_=P[1][:, :]),
                        reads=("P1",), writes=(f"Vt{i}",))
        for qc in range(lseq // QC):
            q0 = qc * QC
            for j in range(nD):
                p.dma("gpsimd", out=xbc[j][:],
                      in_=xT_hbm[ts(j, 128), s_lo + q0:s_lo + q0 + QC],
                      writes=(f"xbc{j}",))
            for e in range(nD):
                for j in range(nD):
                    p.op("tensor", lambda h, e=e, j=j: h.matmul(
                        out=P[0][:, :], lhsT=wqT[j][:, ts(e, 128)],
                        rhs=xbc[j][:],
                        start=(j == 0), stop=(j == nD - 1)),
                        reads=(f"wq_T{j}", f"xbc{j}"), writes=("P0",))
                p.op("vector", lambda h, e=e: h.tensor_scalar(
                    out=QTc[e][:], in0=P[0][:, :], scalar1=bq[:, e:e + 1],
                    scalar2=None, op0=ALU.add),
                    reads=("P0", "bq"), writes=(f"QTc{e}",))
            for head in range(NH):
                et, ro = head // 2, (head % 2) * 64
                n_sblk = (q0 + QC) // 128
                for sb_ in range(n_sblk):
                    s0 = sb_ * 128
                    pp, pk = P[2 + (sb_ % 2)], f"P{2 + (sb_ % 2)}"
                    sbuf, sk = Sb[sb_ % 2], f"Sbb{sb_ % 2}"
                    p.op("tensor", lambda h, et=et, ro=ro, s0=s0, q0=q0, pp=pp: h.matmul(
                        out=pp[:, :QC], lhsT=KT[et][ro:ro + 64, s0:s0 + 128],
                        rhs=QTc[et][ro:ro + 64, :],
                        start=True, stop=True),
                        reads=(f"KT{et}", f"QTc{et}"), writes=(pk,))
                    delta = q0 - s0
                    if delta <= 0:
                        p.op("vector", lambda h, pp=pp, sbuf=sbuf, delta=delta: h.tensor_mul(
                            out=sbuf[:], in0=pp[:, :QC], in1=masks[delta][:]),
                            reads=(pk, f"mask{-delta // 128}"), writes=(sk,))
                    else:
                        p.op("scalar", lambda h, pp=pp, sbuf=sbuf: h.copy(
                            out=sbuf[:], in_=pp[:, :QC]),
                            reads=(pk,), writes=(sk,))
                    p.op("tensor", lambda h, s0=s0, ro=ro, et=et, sbuf=sbuf,
                         st=(sb_ == 0), sp=(sb_ == n_sblk - 1): h.matmul(
                        out=P[1][0:64, :QC], lhsT=Vt[s0 // 128][:, et * 128 + ro: et * 128 + ro + 64],
                        rhs=sbuf[:], start=st, stop=sp, skip_group_check=True),
                        reads=(sk, f"Vt{s0 // 128}"), writes=("P1",))
                p.op("scalar", lambda h, et=et, ro=ro: h.copy(
                    out=ATc[et][ro:ro + 64, :], in_=P[1][0:64, :QC]),
                    reads=("P1",), writes=(f"ATc{et}",))
            for e in range(nD):
                for j in range(nD):
                    p.op("tensor", lambda h, e=e, j=j: h.matmul(
                        out=P[0][:, :], lhsT=woT[j][:, ts(e, 128)], rhs=ATc[j][:],
                        start=(j == 0), stop=(j == nD - 1)),
                        reads=(f"wo_T{j}", f"ATc{j}"), writes=("P0",))
                p.op("vector", lambda h, e=e: h.scalar_tensor_tensor(
                    out=rTc[e][:], in0=P[0][:, :], scalar=bo[:, e:e + 1],
                    in1=xbc[e][:], op0=ALU.add, op1=ALU.add),
                    reads=("P0", "bo", f"xbc{e}"), writes=(f"rTc{e}",))
            emit_ln_chunk(p, rTc, [f"rTc{j}" for j in range(nD)],
                          rTc, [f"rTc{j}" for j in range(nD)], "P4", "P5",
                          ln_bufs, C, QC)
            for j in range(nD):
                p.dma("sync", out=outT_hbm[ts(j, 128), s_lo + q0:s_lo + q0 + QC],
                      in_=rTc[j][:], reads=(f"rTc{j}",), writes=(f"outT{j}",))
    p.barrier_all("sync")


    p.barrier_all("sync")


def emit_ln_chunk(p, r_tiles, rkeys, out_tiles, okeys, pk0, pk1, B, C, CH,
                  cast_to=None, ckeys=None, out_slice=None):
    nD = len(r_tiles)
    inv = 1.0 / (nD * 128)
    osl = out_slice if out_slice is not None else slice(0, CH)
    for j in range(nD):
        p.op("tensor", lambda h, j=j: h.matmul(
            out=B["ps_s0"][:, :CH], lhsT=C["ones_col"], rhs=r_tiles[j][:],
            start=(j == 0), stop=(j == nD - 1)),
            reads=(rkeys[j], "ones_col"), writes=(pk0,))
    for j in range(nD):
        p.op("scalar", lambda h, j=j: h.activation(
            out=B["r2"][:], in_=r_tiles[j][:], func=AF.Square),
            reads=(rkeys[j],), writes=("ln_r2",))
        p.op("tensor", lambda h, j=j: h.matmul(
            out=B["ps_s1"][:, :CH], lhsT=C["ones_col"], rhs=B["r2"][:],
            start=(j == 0), stop=(j == nD - 1)),
            reads=("ln_r2", "ones_col"), writes=(pk1,))
    p.op("scalar", lambda h: h.activation(
        out=B["row_m"][:], in_=B["ps_s0"][:, :CH], func=AF.Copy, scale=inv),
        reads=(pk0,), writes=("ln_rowm",))
    p.op("scalar", lambda h: h.activation(
        out=B["row_r"][:], in_=B["row_m"][:], func=AF.Square),
        reads=("ln_rowm",), writes=("ln_rowr",))
    p.op("vector", lambda h: h.scalar_tensor_tensor(
        out=B["row_r"][:], in0=B["ps_s1"][:, :CH], scalar=inv, in1=B["row_r"][:],
        op0=ALU.mult, op1=ALU.subtract),
        reads=(pk1, "ln_rowr"), writes=("ln_rowr",))
    p.op("scalar", lambda h: h.activation(
        out=B["row_r"][:], in_=B["row_r"][:], func=AF.Ln, bias=1e-5),
        reads=("ln_rowr", "cpool"), writes=("ln_rowr",))
    p.op("scalar", lambda h: h.activation(
        out=B["row_r"][:], in_=B["row_r"][:], func=AF.Exp, scale=-0.5),
        reads=("ln_rowr",), writes=("ln_rowr",))
    p.op("tensor", lambda h: h.matmul(
        out=B["bc_m"][:, :CH], lhsT=C["ones_row"], rhs=B["row_m"][:],
        start=True, stop=True),
        reads=("ln_rowm", "ones_row"), writes=(pk0,))
    p.op("tensor", lambda h: h.matmul(
        out=B["bc_r"][:, :CH], lhsT=C["ones_row"], rhs=B["row_r"][:],
        start=True, stop=True),
        reads=("ln_rowr", "ones_row"), writes=(pk1,))
    for j in range(nD):
        p.op("vector", lambda h, j=j: h.tensor_sub(
            out=out_tiles[j][:, osl], in0=r_tiles[j][:], in1=B["bc_m"][:, :CH]),
            reads=(rkeys[j], pk0), writes=(okeys[j],))
        p.op("vector", lambda h, j=j: h.tensor_mul(
            out=out_tiles[j][:, osl], in0=out_tiles[j][:, osl], in1=B["bc_r"][:, :CH]),
            reads=(okeys[j], pk1), writes=(okeys[j],))
        if cast_to is not None:
            p.op("vector", lambda h, j=j: h.tensor_copy(
                out=cast_to[j][:, osl], in_=out_tiles[j][:, osl]),
                reads=(okeys[j],), writes=(ckeys[j],))



def stream_wT_slice_pe(p, hbm_ap, in_dim, o0, o1, dest, dkeys, col0,
                       stage_b, skey, ps_pair, pskeys, ident_b):
    ow = o1 - o0
    nI = in_dim // 128
    if ow < 128:
        ow0 = (ow // 32) * 32
        segs = [(a, b) for a, b in [(32, 64), (64, 128)] if b > ow0]
        for a, b in segs:
            a = max(a, ow0)
            p.op("vector", lambda h, a=a, b=b: h.memset(stage_b[a:b, 0:in_dim], 0.0),
                 reads=(skey,), writes=(skey,))
    p.dma("gpsimd", out=stage_b[0:ow, 0:in_dim], in_=hbm_ap[o0:o1, :],
          reads=(skey,), writes=(skey,))
    for i in range(nI):
        pt, ptk = ps_pair[i % 2], pskeys[i % 2]
        p.op("tensor", lambda h, i=i, pt=pt: h.transpose(
            out=pt[:, 0:128], in_=stage_b[0:128, ts(i, 128)], identity=ident_b),
            reads=(skey, "ident_b"), writes=(ptk,))
        g = p.new_gid()
        p.op("vector", lambda h, i=i, pt=pt: h.tensor_copy(
            out=dest[i][:, col0:col0 + 128], in_=pt[:, 0:128]),
            reads=(ptk,), writes=(dkeys[i],), wgroup=g)


def stream_wT_slice(p, al, hbm_ap, in_dim, o0, o1, dest, dkeys, col0,
                    stage_b, skey, tq="scalar"):
    ow = o1 - o0
    nI = in_dim // 128
    if ow < 128:
        ow0 = (ow // 32) * 32
        segs = [(a, b) for a, b in [(32, 64), (64, 128)] if b > ow0]
        for a, b in segs:
            a = max(a, ow0)
            p.op("vector", lambda h, a=a, b=b: h.memset(stage_b[a:b, 0:in_dim], 0.0),
                 reads=(skey,), writes=(skey,))
    p.dma("gpsimd", out=stage_b[0:ow, 0:in_dim], in_=hbm_ap[o0:o1, :],
          reads=(skey,), writes=(skey,))
    for i in range(nI):
        g = p.new_gid()
        p.dma(tq, out=dest[i][:, col0:col0 + 128],
              in_=stage_b[0:128, ts(i, 128)],
              reads=(skey,), writes=(dkeys[i],), transpose=True, wgroup=g)



def build_l0_ffn(nc, lseq=LSEQ):
    tok = NSEQ * lseq
    xT = nc.dram_tensor("xT", [D, tok], F32, kind="ExternalInput").ap()
    gate_w = nc.dram_tensor("gate_w", [E0, D], F32, kind="ExternalInput").ap()
    Wg = nc.dram_tensor("Wg", [E0, H0, D], F32, kind="ExternalInput").ap()
    W1 = nc.dram_tensor("W1", [E0, H0, D], F32, kind="ExternalInput").ap()
    W2 = nc.dram_tensor("W2", [E0, D, H0], F32, kind="ExternalInput").ap()
    outT = nc.dram_tensor("outT", [D, tok], F32, kind="ExternalOutput").ap()
    es = ExitStack()
    with es:
        al = Alloc(nc, es)
        p = Prog(nc)
        p.make_sems(es)
        emit_l0_ffn(nc, p, al, xT, gate_w, Wg, W1, W2, outT, lseq)
        block = es.enter_context(nc.Block())
        p.finish(block)
    return nc


def emit_l0_ffn(nc, p, al, xT_hbm, gate_hbm, Wg_hbm, W1_hbm, W2_hbm, outT_hbm, lseq):
    tok = NSEQ * lseq
    nD = D // 128
    HT = ceil_div(H0, 128)
    TH = tok // 2
    C = make_consts(p, al)

    P = [al.ps(f"P{i}", [128, 512], F32) for i in range(6)]
    ps_tb = al.ps("ps_tb", [128, 1024], BF16)
    ps_tb2 = al.ps("ps_tb2", [128, 1024], BF16)

    ident_f = al.sb("ident_f", [128, 128], F32)
    p.op("gpsimd", lambda h: h.memset(ident_f[:], 0.0), writes=("ident_f",))
    p.op("gpsimd", lambda h: h.affine_select(
        out=ident_f[:], in_=ident_f[:], compare_op=ALU.not_equal, fill=1.0,
        base=0, pattern=[[-1, 128]], channel_multiplier=1),
        reads=("ident_f",), writes=("ident_f",))
    ident_b = al.sb("ident_b", [128, 128], BF16)
    p.op("gpsimd", lambda h: h.memset(ident_b[:], 0.0), writes=("ident_b",))
    p.op("gpsimd", lambda h: h.affine_select(
        out=ident_b[:], in_=ident_b[:], compare_op=ALU.not_equal, fill=1.0,
        base=0, pattern=[[-1, 128]], channel_multiplier=1),
        reads=("ident_b",), writes=("ident_b",))
    sel = [al.sb(f"sel{e}", [8, 128], BF16) for e in range(E0)]
    for e in range(E0):
        p.op("gpsimd", lambda h, e=e: h.memset(sel[e][:], 1.0), writes=(f"sel{e}",))
        p.op("gpsimd", lambda h, e=e: h.affine_select(
            out=sel[e][:], in_=sel[e][:], compare_op=ALU.is_equal, fill=0.0,
            base=-e, pattern=[[0, 128]], channel_multiplier=1),
            reads=(f"sel{e}",), writes=(f"sel{e}",))

    stg2 = [al.sb(f"stg2_{b}", [128, 1408], BF16) for b in range(2)]
    gwT = [al.sb(f"gwT{j}", [128, 16], BF16) for j in range(nD)]
    p.op("vector", lambda h: h.memset(stg2[0][0:16, 0:D], 0.0), writes=("stg2_0",))
    p.dma("gpsimd", out=stg2[0][0:8, 0:D], in_=gate_hbm,
          reads=("stg2_0",), writes=("stg2_0",))
    for j in range(nD):
        p.dma("scalar", out=gwT[j][:], in_=stg2[0][0:16, ts(j, 128)],
              reads=("stg2_0",), writes=(f"gwT{j}",), transpose=True)

    xb = [al.sb(f"xb{j}", [128, TH], BF16) for j in range(nD)]
    nTT = TH // 128
    Ltm = al.sb("Ltm", [128, nTT * E0], F32)
    comb = al.sb("comb", [128, nTT * E0], F32)
    combE = al.sb("combE", [8, TH], BF16)
    cbc = al.sb("cbc", [128, TH], BF16)
    r_m1 = al.sb("r_m1", [128, nTT], F32)
    r_m2 = al.sb("r_m2", [128, nTT], F32)
    r_eb = al.sb("r_eb", [128, nTT * E0], F32)
    r_sc = al.sb("r_sc", [128, nTT], F32)
    r_sc2 = al.sb("r_sc2", [128, nTT], F32)

    WgTs = [[al.sb(f"WgTs{b}_{j}", [128, 128], BF16) for j in range(nD)]
            for b in range(2)]
    W1Ts = [[al.sb(f"W1Ts{b}_{j}", [128, 128], BF16) for j in range(nD)]
            for b in range(2)]
    HHT = 8
    w2T = [al.sb(f"w2T{j}", [128, D], BF16) for j in range(HHT)]
    gu = [al.sb(f"gu{j}", [128, TH], BF16) for j in range(HHT)]
    gs = al.sb("gs", [128, 512], BF16)
    gs2 = al.sb("gs2", [128, 512], BF16)
    out_acc = [al.sb(f"oacc{j}", [128, TH], BF16) for j in range(nD)]
    tmp_y = al.sb("tmp_y", [128, 512], BF16)
    rTc = [al.sb(f"rTc{j}", [128, 512], F32) for j in range(nD)]
    ln_bufs = dict(
        ps_s0=P[4][0:1, :], ps_s1=P[5][0:1, :], bc_m=P[4], bc_r=P[5],
        row_m=al.sb("ln_rowm", [1, 512], F32),
        row_r=al.sb("ln_rowr", [1, 512], F32),
        r2=al.sb("ln_r2", [128, 512], F32),
    )

    for half in range(2):
        t0 = half * TH
        for j in range(nD):
            p.dma("gpsimd", out=xb[j][:], in_=xT_hbm[ts(j, 128), t0:t0 + TH],
                  writes=(f"xb{j}",))
        for i in range(nTT):
            for j in range(nD):
                p.op("tensor", lambda h, i=i, j=j: h.matmul(
                    out=P[3][:, 0:E0], lhsT=xb[j][:, ts(i, 128)], rhs=gwT[j][:, 0:E0],
                    start=(j == 0), stop=(j == nD - 1)),
                    reads=(f"xb{j}", f"gwT{j}"), writes=("P3",))
            p.op("scalar", lambda h, i=i: h.copy(
                out=Ltm[:, i * E0:(i + 1) * E0], in_=P[3][:, 0:E0]),
                reads=("P3",), writes=("Ltm",))
        L3 = Ltm[:].rearrange("p (t e) -> p t e", e=E0)
        p.op("vector", lambda h: h.tensor_reduce(
            out=r_m1[:], in_=L3, axis=AX.X, op=ALU.max),
            reads=("Ltm",), writes=("r_m1",))
        m1b = r_m1[:].unsqueeze(2).broadcast_to([128, nTT, E0])
        p.op("vector", lambda h: h.tensor_tensor(
            out=r_eb[:].rearrange("p (t e) -> p t e", e=E0), in0=L3, in1=m1b,
            op=ALU.subtract),
            reads=("Ltm", "r_m1"), writes=("r_eb",))
        p.op("scalar", lambda h: h.activation(
            out=r_eb[:], in_=r_eb[:], func=AF.Exp),
            reads=("r_eb",), writes=("r_eb",))
        eb3 = r_eb[:].rearrange("p (t e) -> p t e", e=E0)
        p.op("vector", lambda h: h.tensor_reduce(
            out=r_sc[:], in_=eb3, axis=AX.X, op=ALU.add),
            reads=("r_eb",), writes=("r_sc",))
        p.op("vector", lambda h: h.tensor_tensor(
            out=comb[:].rearrange("p (t e) -> p t e", e=E0), in0=L3, in1=m1b,
            op=ALU.is_ge),
            reads=("Ltm", "r_m1"), writes=("comb",))
        p.op("vector", lambda h: h.tensor_scalar(
            out=comb[:], in0=comb[:], scalar1=-1e30, scalar2=None, op0=ALU.mult),
            reads=("comb",), writes=("comb",))
        p.op("vector", lambda h: h.tensor_add(
            out=comb[:], in0=comb[:], in1=Ltm[:]),
            reads=("comb", "Ltm"), writes=("comb",))
        p.op("vector", lambda h: h.tensor_reduce(
            out=r_m2[:], in_=comb[:].rearrange("p (t e) -> p t e", e=E0),
            axis=AX.X, op=ALU.max),
            reads=("comb",), writes=("r_m2",))
        m2b = r_m2[:].unsqueeze(2).broadcast_to([128, nTT, E0])
        p.op("vector", lambda h: h.tensor_tensor(
            out=comb[:].rearrange("p (t e) -> p t e", e=E0), in0=L3, in1=m2b,
            op=ALU.is_ge),
            reads=("Ltm", "r_m2"), writes=("comb",))
        p.op("vector", lambda h: h.tensor_mul(
            out=comb[:], in0=comb[:], in1=r_eb[:]),
            reads=("comb", "r_eb"), writes=("comb",))
        p.op("vector", lambda h: h.tensor_reduce(
            out=r_sc2[:], in_=comb[:].rearrange("p (t e) -> p t e", e=E0),
            axis=AX.X, op=ALU.add),
            reads=("comb",), writes=("r_sc2",))
        p.op("vector", lambda h: h.scalar_tensor_tensor(
            out=r_sc2[:], in0=r_sc[:], scalar=1e-8, in1=r_sc2[:],
            op0=ALU.mult, op1=ALU.add),
            reads=("r_sc", "r_sc2"), writes=("r_sc2",))
        p.op("vector", lambda h: h.reciprocal(out=r_sc2[:], in_=r_sc2[:]),
             reads=("r_sc2",), writes=("r_sc2",))
        recb = r_sc2[:].unsqueeze(2).broadcast_to([128, nTT, E0])
        p.op("vector", lambda h: h.tensor_tensor(
            out=comb[:].rearrange("p (t e) -> p t e", e=E0),
            in0=comb[:].rearrange("p (t e) -> p t e", e=E0), in1=recb,
            op=ALU.mult),
            reads=("comb", "r_sc2"), writes=("comb",))
        for i in range(nTT):
            p.op("tensor", lambda h, i=i: h.transpose(
                out=P[3][0:8, 0:128], in_=comb[:, i * E0:(i + 1) * E0],
                identity=ident_f[:]),
                reads=("comb", "ident_f"), writes=("P3",))
            p.op("scalar", lambda h, i=i: h.copy(
                out=combE[:, ts(i, 128)], in_=P[3][0:8, 0:128]),
                reads=("P3",), writes=("combE",))
        for j in range(nD):
            p.op("vector", lambda h, j=j: h.memset(out_acc[j][:], 0.0),
                 writes=(f"oacc{j}",))
        for e in range(E0):
            for cc in range(TH // 512):
                p.op("tensor", lambda h, e=e, cc=cc: h.matmul(
                    out=P[3][:, :], lhsT=sel[e][:], rhs=combE[:, ts(cc, 512)],
                    start=True, stop=True),
                    reads=(f"sel{e}", "combE"), writes=("P3",))
                p.op("scalar", lambda h, cc=cc: h.copy(
                    out=cbc[:, ts(cc, 512)], in_=P[3][:, :]),
                    reads=("P3",), writes=("cbc",))
            for hh in range(3):
                jtiles = list(range(hh * HHT, min(HT, (hh + 1) * HHT)))
                hlo = jtiles[0] * 128
                hhi = min(H0, (jtiles[-1] + 1) * 128)
                wpad = len(jtiles) * 128
                w2gids = [p.new_gid() for _ in jtiles]
                for o in range(nD):
                    sb2 = stg2[o % 2]
                    sk2 = f"stg2_{o % 2}"
                    if hhi - hlo < wpad:
                        p.op("vector", lambda h, w0=hhi - hlo, w1=wpad, sb2=sb2: h.memset(
                            sb2[:, w0:w1], 0.0),
                            reads=(sk2,), writes=(sk2,))
                    p.dma("gpsimd", out=sb2[:, 0:hhi - hlo],
                          in_=W2_hbm[e, ts(o, 128), hlo:hhi],
                          reads=(sk2,), writes=(sk2,))
                    for ji, j in enumerate(jtiles):
                        pt, ptk = (ps_tb, "ps_tb") if ji % 2 == 0 else (ps_tb2, "ps_tb2")
                        p.op("tensor", lambda h, ji=ji, sb2=sb2, pt=pt: h.transpose(
                            out=pt[:, 0:128], in_=sb2[:, ts(ji, 128)],
                            identity=ident_b[:]),
                            reads=(sk2, "ident_b"), writes=(ptk,))
                        p.op("vector", lambda h, ji=ji, o=o, pt=pt: h.tensor_copy(
                            out=w2T[ji][:, ts(o, 128)], in_=pt[:, 0:128]),
                            reads=(ptk,), writes=(f"w2T{ji}",), wgroup=w2gids[ji])
                for ji, j in enumerate(jtiles):
                    jhi = min(H0, j * 128 + 128)
                    bb = ji % 2
                    stream_wT_slice_pe(p, Wg_hbm[e], D, j * 128, jhi,
                                       WgTs[bb], [f"WgTs{bb}_{q}" for q in range(nD)], 0,
                                       stg2[0], "stg2_0", (ps_tb, ps_tb2),
                                       ("ps_tb", "ps_tb2"), ident_b[:])
                    stream_wT_slice_pe(p, W1_hbm[e], D, j * 128, jhi,
                                       W1Ts[bb], [f"W1Ts{bb}_{q}" for q in range(nD)], 0,
                                       stg2[1], "stg2_1", (ps_tb, ps_tb2),
                                       ("ps_tb", "ps_tb2"), ident_b[:])
                    pg, pgk = (P[0], "P0") if ji % 2 == 0 else (P[4], "P4")
                    pu, puk = (P[1], "P1") if ji % 2 == 0 else (P[5], "P5")
                    gsb, gsk = (gs, "gs") if ji % 2 == 0 else (gs2, "gs2")
                    for cc in range(TH // 512):
                        cs = ts(cc, 512)
                        for q in range(nD):
                            p.op("tensor", lambda h, q=q, cs=cs, bb=bb, pg=pg: h.matmul(
                                out=pg[:, :], lhsT=WgTs[bb][q][:],
                                rhs=xb[q][:, cs], start=(q == 0), stop=(q == nD - 1)),
                                reads=(f"WgTs{bb}_{q}", f"xb{q}"), writes=(pgk,))
                        for q in range(nD):
                            p.op("tensor", lambda h, q=q, cs=cs, bb=bb, pu=pu: h.matmul(
                                out=pu[:, :], lhsT=W1Ts[bb][q][:],
                                rhs=xb[q][:, cs], start=(q == 0), stop=(q == nD - 1)),
                                reads=(f"W1Ts{bb}_{q}", f"xb{q}"), writes=(puk,))
                        p.op("scalar", lambda h, pg=pg, gsb=gsb: h.activation(
                            out=gsb[:, :], in_=pg[:, :], func=AF.Silu),
                            reads=(pgk,), writes=(gsk,))
                        p.op("vector", lambda h, ji=ji, cs=cs, pu=pu, gsb=gsb: h.tensor_mul(
                            out=gu[ji][:, cs], in0=gsb[:, :], in1=pu[:, :]),
                            reads=(gsk, puk), writes=(f"gu{ji}",))
                for o in range(nD):
                    for cc in range(TH // 512):
                        cs = ts(cc, 512)
                        py, pyk = (P[2], "P2") if (o * 4 + cc) % 2 == 0 else (P[3], "P3")
                        for ji, j in enumerate(jtiles):
                            p.op("tensor", lambda h, ji=ji, o=o, cs=cs, py=py,
                                 st=(ji == 0), sp=(ji == len(jtiles) - 1): h.matmul(
                                out=py[:, :], lhsT=w2T[ji][:, ts(o, 128)],
                                rhs=gu[ji][:, cs], start=st, stop=sp,
                                skip_group_check=True),
                                reads=(f"w2T{ji}", f"gu{ji}"), writes=(pyk,))
                        p.op("vector", lambda h, o=o, cs=cs, py=py: h.tensor_mul(
                            out=tmp_y[:], in0=py[:, :], in1=cbc[:, cs]),
                            reads=(pyk, "cbc"), writes=("tmp_y",))
                        p.op("vector", lambda h, o=o, cs=cs: h.tensor_add(
                            out=out_acc[o][:, cs], in0=out_acc[o][:, cs], in1=tmp_y[:]),
                            reads=(f"oacc{o}", "tmp_y"), writes=(f"oacc{o}",))
        for cc in range(TH // 512):
            cs = ts(cc, 512)
            for j in range(nD):
                p.op("vector", lambda h, j=j, cs=cs: h.tensor_add(
                    out=rTc[j][:], in0=xb[j][:, cs], in1=out_acc[j][:, cs]),
                    reads=(f"xb{j}", f"oacc{j}"), writes=(f"rTc{j}",))
            emit_ln_chunk(p, rTc, [f"rTc{j}" for j in range(nD)],
                          rTc, [f"rTc{j}" for j in range(nD)], "P4", "P5",
                          ln_bufs, C, 512)
            for j in range(nD):
                p.dma("sync", out=outT_hbm[ts(j, 128), t0 + cc * 512:t0 + (cc + 1) * 512],
                      in_=rTc[j][:], reads=(f"rTc{j}",), writes=(f"outT{j}",))
    p.barrier_all("sync")



def build_l1_ffn(nc, lseq=LSEQ):
    tok = NSEQ * lseq
    xT = nc.dram_tensor("xT", [D, tok], F32, kind="ExternalInput").ap()
    rW = nc.dram_tensor("router_W", [D, E1], F32, kind="ExternalInput").ap()
    rb = nc.dram_tensor("router_b", [E1], F32, kind="ExternalInput").ap()
    sW1 = nc.dram_tensor("sW1", [D, H1], F32, kind="ExternalInput").ap()
    sb1 = nc.dram_tensor("sb1", [H1], F32, kind="ExternalInput").ap()
    sW2 = nc.dram_tensor("sW2", [H1, D], F32, kind="ExternalInput").ap()
    sb2 = nc.dram_tensor("sb2", [D], F32, kind="ExternalInput").ap()
    eW1 = nc.dram_tensor("eW1", [E1, D, H1], F32, kind="ExternalInput").ap()
    eb1 = nc.dram_tensor("eb1", [E1, H1], F32, kind="ExternalInput").ap()
    eW2 = nc.dram_tensor("eW2", [E1, H1, D], F32, kind="ExternalInput").ap()
    eb2 = nc.dram_tensor("eb2", [E1, D], F32, kind="ExternalInput").ap()
    outT = nc.dram_tensor("outT", [D, tok], F32, kind="ExternalOutput").ap()
    es = ExitStack()
    with es:
        al = Alloc(nc, es)
        p = Prog(nc)
        p.make_sems(es)
        emit_l1_ffn(nc, p, al, xT, (rW, rb, sW1, sb1, sW2, sb2, eW1, eb1, eW2, eb2),
                    outT, lseq)
        block = es.enter_context(nc.Block())
        p.finish(block)
    return nc


def emit_l1_ffn(nc, p, al, xT_hbm, W, outT_hbm, lseq):
    rW_h, rb_h, sW1_h, sb1_h, sW2_h, sb2_h, eW1_h, eb1_h, eW2_h, eb2_h = W
    tok = NSEQ * lseq
    nD = D // 128
    nH = H1 // 128
    TH = tok // 2
    nTT = TH // 128
    C = make_consts(p, al)
    P = [al.ps(f"P{i}", [128, 512], F32) for i in range(6)]

    ident_f = al.sb("ident_f", [128, 128], F32)
    p.op("gpsimd", lambda h: h.memset(ident_f[:], 0.0), writes=("ident_f",))
    p.op("gpsimd", lambda h: h.affine_select(
        out=ident_f[:], in_=ident_f[:], compare_op=ALU.not_equal, fill=1.0,
        base=0, pattern=[[-1, 128]], channel_multiplier=1),
        reads=("ident_f",), writes=("ident_f",))
    sel = [al.sb(f"sel{e}", [E1, 128], BF16) for e in range(E1)]
    for e in range(E1):
        p.op("gpsimd", lambda h, e=e: h.memset(sel[e][:], 1.0), writes=(f"sel{e}",))
        p.op("gpsimd", lambda h, e=e: h.affine_select(
            out=sel[e][:], in_=sel[e][:], compare_op=ALU.is_equal, fill=0.0,
            base=-e, pattern=[[0, 128]], channel_multiplier=1),
            reads=(f"sel{e}",), writes=(f"sel{e}",))

    rwT = [al.sb(f"rwT{j}", [128, E1], BF16) for j in range(nD)]
    stage_f = al.sb("wstage_f", [128, max(D, H1)], F32)
    rb_rowf = al.sb("rb_rowf", [1, E1], F32)
    rb_row = al.sb("rb_row", [1, E1], BF16)
    p.dma("sync", out=rb_rowf[:], in_=rb_h.unsqueeze(0), writes=("rb_rowf",))
    p.op("vector", lambda h: h.tensor_copy(out=rb_row[:], in_=rb_rowf[:]),
         reads=("rb_rowf",), writes=("rb_row",))
    for j in range(nD):
        p.dma("sync", out=stage_f[:, 0:E1], in_=rW_h[ts(j, 128), :], writes=("wstage_f",))
        p.op("vector", lambda h, j=j: h.tensor_copy(out=rwT[j][:], in_=stage_f[:, 0:E1]),
             reads=("wstage_f",), writes=(f"rwT{j}",))

    w1d = [[al.sb(f"w1d{b}_{j}", [128, H1], BF16) for j in range(nD)] for b in range(2)]
    w2d = [[al.sb(f"w2d{b}_{j}", [128, D], BF16) for j in range(nH)] for b in range(2)]
    b1c = [al.sb(f"b1c{b}", [128, nH], F32) for b in range(2)]
    b2c = [al.sb(f"b2c{b}", [128, nD], F32) for b in range(2)]
    hb = [al.sb(f"hb{j}", [128, TH], BF16) for j in range(nH)]
    xb = [al.sb(f"xb{j}", [128, TH], BF16) for j in range(nD)]
    out_acc = [al.sb(f"oacc{j}", [128, TH], BF16) for j in range(nD)]
    tmp_y = al.sb("tmp_y", [128, 512], BF16)
    cbc = al.sb("cbc", [128, TH], BF16)
    Ltm = al.sb("Ltm", [128, nTT * E1], F32)
    comb = al.sb("comb", [128, nTT * E1], F32)
    combE = al.sb("combE", [E1, TH], BF16)
    r_m1 = al.sb("r_m1", [128, nTT], F32)
    r_eb = al.sb("r_eb", [128, nTT * E1], F32)
    mskb = al.sb("mskb", [128, nTT * E1], F32)
    r_sc = al.sb("r_sc", [128, nTT], F32)
    rTc = [al.sb(f"rTc{j}", [128, 512], F32) for j in range(nD)]
    ln_bufs = dict(
        ps_s0=P[4][0:1, :], ps_s1=P[5][0:1, :], bc_m=P[4], bc_r=P[5],
        row_m=al.sb("ln_rowm", [1, 512], F32),
        row_r=al.sb("ln_rowr", [1, 512], F32),
        r2=al.sb("ln_r2", [128, 512], F32),
    )

    def load_direct(hbm, dest, dkeys, in_dim, width):
        for j in range(in_dim // 128):
            p.dma("gpsimd" if j % 2 else "gpsimd", out=dest[j][:, 0:width],
                  in_=hbm[ts(j, 128), :], writes=(dkeys[j],))

    def load_bias_cols_2d(hbm_row, name, dim, dest):
        g = p.new_gid()
        for j in range(dim // 128):
            p.dma("sync", out=dest[:, j:j + 1], in_=hbm_row[ts(j, 128)].unsqueeze(1),
                  writes=(name,), wgroup=g)

    for half in range(2):
        t0 = half * TH
        for j in range(nD):
            p.dma("gpsimd", out=xb[j][:], in_=xT_hbm[ts(j, 128), t0:t0 + TH],
                  writes=(f"xb{j}",))
        for i in range(nTT):
            for j in range(nD):
                p.op("tensor", lambda h, i=i, j=j: h.matmul(
                    out=P[3][:, 0:E1], lhsT=xb[j][:, ts(i, 128)], rhs=rwT[j][:],
                    start=(j == 0), stop=False),
                    reads=(f"xb{j}", f"rwT{j}"), writes=("P3",))
            p.op("tensor", lambda h: h.matmul(
                out=P[3][:, 0:E1], lhsT=C["ones_row_b"], rhs=rb_row[:],
                start=False, stop=True),
                reads=("rb_row", "ones_row_b"), writes=("P3",))
            p.op("scalar", lambda h, i=i: h.copy(
                out=Ltm[:, i * E1:(i + 1) * E1], in_=P[3][:, 0:E1]),
                reads=("P3",), writes=("Ltm",))
        L3 = Ltm[:].rearrange("p (t e) -> p t e", e=E1)
        p.op("vector", lambda h: h.tensor_reduce(
            out=r_m1[:], in_=L3, axis=AX.X, op=ALU.max),
            reads=("Ltm",), writes=("r_m1",))
        m1b = r_m1[:].unsqueeze(2).broadcast_to([128, nTT, E1])
        p.op("vector", lambda h: h.tensor_tensor(
            out=r_eb[:].rearrange("p (t e) -> p t e", e=E1), in0=L3, in1=m1b,
            op=ALU.subtract),
            reads=("Ltm", "r_m1"), writes=("r_eb",))
        p.op("scalar", lambda h: h.activation(
            out=r_eb[:], in_=r_eb[:], func=AF.Exp),
            reads=("r_eb",), writes=("r_eb",))
        eb3 = r_eb[:].rearrange("p (t e) -> p t e", e=E1)
        p.op("vector", lambda h: h.tensor_reduce(
            out=r_sc[:], in_=eb3, axis=AX.X, op=ALU.add),
            reads=("r_eb",), writes=("r_sc",))
        p.op("vector", lambda h: h.reciprocal(out=r_sc[:], in_=r_sc[:]),
             reads=("r_sc",), writes=("r_sc",))
        p.op("scalar", lambda h: h.copy(out=comb[:], in_=Ltm[:]),
             reads=("Ltm",), writes=("comb",))
        for rnd in range(2):
            p.op("vector", lambda h: h.tensor_reduce(
                out=r_m1[:], in_=comb[:].rearrange("p (t e) -> p t e", e=E1),
                axis=AX.X, op=ALU.max),
                reads=("comb",), writes=("r_m1",))
            p.op("vector", lambda h: h.tensor_tensor(
                out=mskb[:].rearrange("p (t e) -> p t e", e=E1),
                in0=comb[:].rearrange("p (t e) -> p t e", e=E1), in1=m1b,
                op=ALU.is_ge),
                reads=("comb", "r_m1"), writes=("mskb",))
            p.op("vector", lambda h: h.tensor_scalar(
                out=mskb[:], in0=mskb[:], scalar1=-1e30, scalar2=None, op0=ALU.mult),
                reads=("mskb",), writes=("mskb",))
            p.op("vector", lambda h: h.tensor_add(
                out=comb[:], in0=comb[:], in1=mskb[:]),
                reads=("comb", "mskb"), writes=("comb",))
        p.op("vector", lambda h: h.tensor_reduce(
            out=r_m1[:], in_=comb[:].rearrange("p (t e) -> p t e", e=E1),
            axis=AX.X, op=ALU.max),
            reads=("comb",), writes=("r_m1",))
        p.op("vector", lambda h: h.tensor_tensor(
            out=comb[:].rearrange("p (t e) -> p t e", e=E1), in0=L3, in1=m1b,
            op=ALU.is_ge),
            reads=("Ltm", "r_m1"), writes=("comb",))
        p.op("vector", lambda h: h.tensor_mul(
            out=comb[:], in0=comb[:], in1=r_eb[:]),
            reads=("comb", "r_eb"), writes=("comb",))
        recb = r_sc[:].unsqueeze(2).broadcast_to([128, nTT, E1])
        p.op("vector", lambda h: h.tensor_tensor(
            out=comb[:].rearrange("p (t e) -> p t e", e=E1),
            in0=comb[:].rearrange("p (t e) -> p t e", e=E1), in1=recb,
            op=ALU.mult),
            reads=("comb", "r_sc"), writes=("comb",))
        for i in range(nTT):
            p.op("tensor", lambda h, i=i: h.transpose(
                out=P[3][0:E1, 0:128], in_=comb[:, i * E1:(i + 1) * E1],
                identity=ident_f[:]),
                reads=("comb", "ident_f"), writes=("P3",))
            p.op("scalar", lambda h, i=i: h.copy(
                out=combE[:, ts(i, 128)], in_=P[3][0:E1, 0:128]),
                reads=("P3",), writes=("combE",))
        load_direct(sW1_h, w1d[0], [f"w1d0_{j}" for j in range(nD)], D, H1)
        load_direct(sW2_h, w2d[0], [f"w2d0_{j}" for j in range(nH)], H1, D)
        load_bias_cols_2d(sb1_h, "b1c0", H1, b1c[0])
        load_bias_cols_2d(sb2_h, "b2c0", D, b2c[0])
        for ht in range(nH):
            for cc in range(TH // 512):
                cs = ts(cc, 512)
                for q in range(nD):
                    p.op("tensor", lambda h, q=q, ht=ht, cs=cs: h.matmul(
                        out=P[0][:, :], lhsT=w1d[0][q][:, ts(ht, 128)], rhs=xb[q][:, cs],
                        start=(q == 0), stop=(q == nD - 1)),
                        reads=(f"w1d0_{q}", f"xb{q}"), writes=("P0",))
                p.op("scalar", lambda h, ht=ht, cs=cs: h.activation(
                    out=hb[ht][:, cs], in_=P[0][:, :], func=AF.Relu,
                    bias=b1c[0][:, ht:ht + 1]),
                    reads=("P0", "b1c0"), writes=(f"hb{ht}",))
        for o in range(nD):
            for cc in range(TH // 512):
                cs = ts(cc, 512)
                for ht in range(nH):
                    p.op("tensor", lambda h, ht=ht, o=o, cs=cs: h.matmul(
                        out=P[2][:, :], lhsT=w2d[0][ht][:, ts(o, 128)], rhs=hb[ht][:, cs],
                        start=(ht == 0), stop=(ht == nH - 1)),
                        reads=(f"w2d0_{ht}", f"hb{ht}"), writes=("P2",))
                p.op("vector", lambda h, o=o, cs=cs: h.tensor_scalar(
                    out=out_acc[o][:, cs], in0=P[2][:, :], scalar1=b2c[0][:, o:o + 1],
                    scalar2=None, op0=ALU.add),
                    reads=("P2", "b2c0"), writes=(f"oacc{o}",))
        for e in range(E1):
            for cc in range(TH // 512):
                p.op("tensor", lambda h, e=e, cc=cc: h.matmul(
                    out=P[3][:, :], lhsT=sel[e][:], rhs=combE[:, ts(cc, 512)],
                    start=True, stop=True),
                    reads=(f"sel{e}", "combE"), writes=("P3",))
                p.op("scalar", lambda h, cc=cc: h.copy(
                    out=cbc[:, ts(cc, 512)], in_=P[3][:, :]),
                    reads=("P3",), writes=("cbc",))
            bb = (e + 1) % 2
            load_direct(eW1_h[e], w1d[bb], [f"w1d{bb}_{j}" for j in range(nD)], D, H1)
            load_direct(eW2_h[e], w2d[bb], [f"w2d{bb}_{j}" for j in range(nH)], H1, D)
            load_bias_cols_2d(eb1_h[e], f"b1c{bb}", H1, b1c[bb])
            load_bias_cols_2d(eb2_h[e], f"b2c{bb}", D, b2c[bb])
            for ht in range(nH):
                for cc in range(TH // 512):
                    cs = ts(cc, 512)
                    for q in range(nD):
                        p.op("tensor", lambda h, q=q, ht=ht, cs=cs, bb=bb: h.matmul(
                            out=P[0][:, :], lhsT=w1d[bb][q][:, ts(ht, 128)], rhs=xb[q][:, cs],
                            start=(q == 0), stop=(q == nD - 1)),
                            reads=(f"w1d{bb}_{q}", f"xb{q}"), writes=("P0",))
                    p.op("scalar", lambda h, ht=ht, cs=cs, bb=bb: h.activation(
                        out=hb[ht][:, cs], in_=P[0][:, :], func=AF.Relu,
                        bias=b1c[bb][:, ht:ht + 1]),
                        reads=("P0", f"b1c{bb}"), writes=(f"hb{ht}",))
            for o in range(nD):
                for cc in range(TH // 512):
                    cs = ts(cc, 512)
                    for ht in range(nH):
                        p.op("tensor", lambda h, ht=ht, o=o, cs=cs, bb=bb: h.matmul(
                            out=P[2][:, :], lhsT=w2d[bb][ht][:, ts(o, 128)], rhs=hb[ht][:, cs],
                            start=(ht == 0), stop=(ht == nH - 1)),
                            reads=(f"w2d{bb}_{ht}", f"hb{ht}"), writes=("P2",))
                    p.op("vector", lambda h, o=o, cs=cs, bb=bb: h.scalar_tensor_tensor(
                        out=tmp_y[:], in0=P[2][:, :], scalar=b2c[bb][:, o:o + 1],
                        in1=cbc[:, cs], op0=ALU.add, op1=ALU.mult),
                        reads=("P2", f"b2c{bb}", "cbc"), writes=("tmp_y",))
                    p.op("vector", lambda h, o=o, cs=cs: h.tensor_add(
                        out=out_acc[o][:, cs], in0=out_acc[o][:, cs], in1=tmp_y[:]),
                        reads=(f"oacc{o}", "tmp_y"), writes=(f"oacc{o}",))
        for cc in range(TH // 512):
            cs = ts(cc, 512)
            for j in range(nD):
                p.op("vector", lambda h, j=j, cs=cs: h.tensor_add(
                    out=rTc[j][:], in0=xb[j][:, cs], in1=out_acc[j][:, cs]),
                    reads=(f"xb{j}", f"oacc{j}"), writes=(f"rTc{j}",))
            emit_ln_chunk(p, rTc, [f"rTc{j}" for j in range(nD)],
                          rTc, [f"rTc{j}" for j in range(nD)], "P4", "P5",
                          ln_bufs, C, 512)
            for j in range(nD):
                p.dma("sync", out=outT_hbm[ts(j, 128), t0 + cc * 512:t0 + (cc + 1) * 512],
                      in_=rTc[j][:], reads=(f"rTc{j}",), writes=(f"outT{j}",))
    p.barrier_all("sync")



def build_l1_attn(nc, lseq=LSEQ):
    tok = NSEQ * lseq
    xT = nc.dram_tensor("xT", [D, tok], F32, kind="ExternalInput").ap()
    W_in = nc.dram_tensor("W_in", [D, 2 * DIN], F32, kind="ExternalInput").ap()
    conv_w = nc.dram_tensor("conv_w", [DIN, DCONV], F32, kind="ExternalInput").ap()
    conv_b = nc.dram_tensor("conv_b", [DIN], F32, kind="ExternalInput").ap()
    W_x = nc.dram_tensor("W_x", [DIN, 2 * DST + 1], F32, kind="ExternalInput").ap()
    A_log = nc.dram_tensor("A_log", [DIN, DST], F32, kind="ExternalInput").ap()
    dt_bias = nc.dram_tensor("dt_bias", [DIN], F32, kind="ExternalInput").ap()
    D_skip = nc.dram_tensor("D_skip", [DIN], F32, kind="ExternalInput").ap()
    W_out = nc.dram_tensor("W_out", [DIN, D], F32, kind="ExternalInput").ap()
    outT = nc.dram_tensor("outT", [D, tok], F32, kind="ExternalOutput").ap()
    es = ExitStack()
    with es:
        al = Alloc(nc, es)
        p = Prog(nc)
        p.make_sems(es)
        emit_l1_attn(nc, p, al, xT,
                     (W_in, conv_w, conv_b, W_x, A_log, dt_bias, D_skip, W_out),
                     outT, lseq)
        block = es.enter_context(nc.Block())
        p.finish(block)
    return nc


def emit_l1_attn(nc, p, al, xT_hbm, W, outT_hbm, lseq):
    (Win_h, convw_h, convb_h, Wx_h, Alog_h, dtb_h, Dskip_h, Wout_h) = W
    nD = D // 128
    nI = DIN // 128
    TC = 128
    TP = 4
    NCH = lseq // TC
    C = make_consts(p, al)
    P = [al.ps(f"P{i}", [128, 512], F32) for i in range(2)]
    ps_y = al.ps("ps_y", [128, 2048], F32)
    ps_tf = al.ps("ps_tf", [128, 512], F32)
    ps_tb = al.ps("ps_tb", [128, 1024], BF16)

    ident_f = al.sb("ident_f", [128, 128], F32)
    p.op("gpsimd", lambda h: h.memset(ident_f[:], 0.0), writes=("ident_f",))
    p.op("gpsimd", lambda h: h.affine_select(
        out=ident_f[:], in_=ident_f[:], compare_op=ALU.not_equal, fill=1.0,
        base=0, pattern=[[-1, 128]], channel_multiplier=1),
        reads=("ident_f",), writes=("ident_f",))
    ident_b = al.sb("ident_b", [128, 128], BF16)
    p.op("gpsimd", lambda h: h.memset(ident_b[:], 0.0), writes=("ident_b",))
    p.op("gpsimd", lambda h: h.affine_select(
        out=ident_b[:], in_=ident_b[:], compare_op=ALU.not_equal, fill=1.0,
        base=0, pattern=[[-1, 128]], channel_multiplier=1),
        reads=("ident_b",), writes=("ident_b",))

    stage_f = al.sb("stg_f", [128, 512], F32)

    cwc = al.sb("cwc", [128, nI * DCONV], F32)
    gw_ = p.new_gid()
    for j in range(nI):
        for k in range(DCONV):
            p.dma("sync", out=cwc[:, j * DCONV + k:j * DCONV + k + 1],
                  in_=convw_h[ts(j, 128), k].unsqueeze(1), writes=("cwc",), slow=True,
                  wgroup=gw_)
    cbc_ = al.sb("cbc_", [128, nI], F32)
    dbc = al.sb("dbc", [128, nI], F32)
    skc = al.sb("skc", [128, nI], F32)
    g1_, g2_, g3_ = p.new_gid(), p.new_gid(), p.new_gid()
    for j in range(nI):
        p.dma("sync", out=cbc_[:, j:j + 1], in_=convb_h[ts(j, 128)].unsqueeze(1),
              writes=("cbc_",), wgroup=g1_)
        p.dma("sync", out=dbc[:, j:j + 1], in_=dtb_h[ts(j, 128)].unsqueeze(1),
              writes=("dbc",), wgroup=g2_)
        p.dma("sync", out=skc[:, j:j + 1], in_=Dskip_h[ts(j, 128)].unsqueeze(1),
              writes=("skc",), wgroup=g3_)

    A_bc = al.sb("A_bc", [128, DIN], BF16)
    atile = al.sb("atile", [128, DST], F32)
    for j in range(nI):
        p.dma("sync", out=stage_f[:, 0:DST], in_=Alog_h[ts(j, 128), :],
              writes=("stg_f",))
        p.op("scalar", lambda h: h.activation(
            out=atile[:], in_=stage_f[:, 0:DST], func=AF.Exp),
            reads=("stg_f",), writes=("atile",))
        p.op("vector", lambda h: h.tensor_scalar(
            out=atile[:], in0=atile[:], scalar1=-1.0, scalar2=None, op0=ALU.mult),
            reads=("atile",), writes=("atile",))
        p.op("tensor", lambda h, j=j: h.transpose(
            out=ps_tf[0:DST, 0:128], in_=atile[:], identity=ident_f[:]),
            reads=("atile", "ident_f"), writes=("ps_tf",))
        p.op("vector", lambda h, j=j: h.tensor_copy(
            out=A_bc[0:DST, ts(j, 128)], in_=ps_tf[0:DST, 0:128]),
            reads=("ps_tf",), writes=("A_bc",))
    p.op("vector", lambda h: h.tensor_copy(
        out=A_bc[64:128, :], in_=A_bc[0:64, :]),
        reads=("A_bc",), writes=("A_bc",))

    wxT = [al.sb(f"wxT{j}", [128, 2 * DST + 1], BF16) for j in range(nI)]
    for j in range(nI):
        p.dma("sync", out=stage_f[:, 0:2 * DST + 1], in_=Wx_h[ts(j, 128), :],
              writes=("stg_f",))
        p.op("vector", lambda h, j=j: h.tensor_copy(
            out=wxT[j][:], in_=stage_f[:, 0:2 * DST + 1]),
            reads=("stg_f",), writes=(f"wxT{j}",))

    T2 = 2 * TC
    xb = [al.sb(f"xb{j}", [128, T2], BF16) for j in range(nD)]
    xs = [al.sb(f"xs{j}", [128, 2 * (TC + 3)], BF16) for j in range(nI)]
    zt = [al.sb(f"zt{j}", [128, T2], BF16) for j in range(nI)]
    xc = [al.sb(f"xc{j}", [128, T2], BF16) for j in range(nI)]
    dt16 = [al.sb(f"dt16{j}", [128, T2], BF16) for j in range(nI)]
    dtx16 = [al.sb(f"dtx16{j}", [128, T2], BF16) for j in range(nI)]
    dtT = [al.sb(f"dtT{r}", [128, DIN], BF16) for r in range(2)]
    dtxT = [al.sb(f"dtxT{r}", [128, DIN], BF16) for r in range(2)]
    dt_dram = nc.dram_tensor("dt_dram", [2, TC * DIN], BF16, kind="Internal").ap()
    dtx_dram = nc.dram_tensor("dtx_dram", [2, TC * DIN], BF16, kind="Internal").ap()
    B_stk = al.sb("B_stk", [128, TC], F32)
    C_stk = al.sb("C_stk", [128, TC], F32)
    Csel = al.sb("Csel", [128, 128], F32)
    r_bc = al.sb("r_bc", [128, T2], F32)
    hst = [al.sb(f"hst{b}", [128, DIN], F32) for b in range(2)]
    dt_rep = [al.sb(f"dt_rep{b}", [128, TP * DIN], BF16) for b in range(2)]
    dtx_rep = [al.sb(f"dtx_rep{b}", [128, TP * DIN], BF16) for b in range(2)]
    g_rep = dt_rep
    yT = [al.sb(f"yT{j}", [128, T2], BF16) for j in range(nI)]
    scr1 = al.sb("scr1", [128, T2], F32)
    scr2 = al.sb("scr2", [128, T2], F32)
    wslab = [al.sb(f"wslab{b}", [128, 2048], BF16) for b in range(2)]
    yv = wslab[0]
    zs = al.sb("zs", [128, T2], BF16)
    rTc = [al.sb(f"rTc{j}", [128, T2], F32) for j in range(nD)]
    ln_bufs = dict(
        ps_s0=P[0][0:1, :], ps_s1=P[1][0:1, :], bc_m=P[0], bc_r=P[1],
        row_m=al.sb("ln_rowm", [1, T2], F32),
        row_r=al.sb("ln_rowr", [1, T2], F32),
        r2=al.sb("ln_r2", [128, T2], F32),
    )

    p.op("vector", lambda h: h.memset(Csel[:], 0.0), writes=("Csel",))

    for j in range(nI):
        p.op("vector", lambda h, j=j: h.memset(xs[j][:], 0.0), writes=(f"xs{j}",))

    for ch in range(NCH):
        t0 = ch * TC
        if ch == 0:
            p.op("vector", lambda h: h.memset(hst[0][:], 0.0), writes=("hst0",))
            p.op("vector", lambda h: h.memset(hst[1][:], 0.0), writes=("hst1",))
        for j in range(nD):
            g = p.new_gid()
            for s in range(NSEQ):
                p.dma("gpsimd", out=xb[j][:, s * TC:(s + 1) * TC],
                      in_=xT_hbm[ts(j, 128), s * lseq + t0:s * lseq + t0 + TC],
                      writes=(f"xb{j}",), wgroup=g)
        emit_l1_pre(p, al, nc, Win_h, xb, xs, zt, xc, cwc, cbc_, wxT, dt16,
                    dtx16, dtT, dtxT, B_stk, C_stk, r_bc, dbc, stage_f, wslab,
                    ident_b, ident_f, ps_tf, ps_tb, P, C, scr1, scr2, ch, nI, nD, TC)
        emit_l1_scan(p, al, nc, A_bc, dt_rep, dtx_rep, g_rep, dtT, dtxT, dt_dram,
                     dtx_dram, B_stk, C_stk, Csel, hst, ps_y, ps_tb, yT, yv,
                     ident_b, ch, TC, TP, nI)
        emit_l1_post(p, al, nc, Wout_h, yT, xc, zt, zs, skc, xb, rTc, stage_f,
                     wslab, P, C, ln_bufs, outT_hbm, lseq, t0, nI, nD, TC)
    p.barrier_all("sync")


def emit_l1_pre(p, al, nc, Win_h, xb, xs, zt, xc, cwc, cbc_, wxT, dt16,
                dtx16, dtT, dtxT, B_stk, C_stk, r_bc, dbc, stage_f, wslab,
                ident_b, ident_f, ps_tf, ps_tb, P, C, scr1, scr2, ch, nI, nD, TC):
    T2 = 2 * TC
    if ch > 0:
        for j in range(nI):
            p.op("vector", lambda h, j=j: h.tensor_copy(
                out=xs[j][:].rearrange("p (s c) -> p s c", s=2)[:, :, 0:3],
                in_=xs[j][:].rearrange("p (s c) -> p s c", s=2)[:, :, TC:TC + 3]),
                reads=(f"xs{j}",), writes=(f"xs{j}",))
    for it in range(2 * nI):
        src = Win_h[:, ts(it, 128)].rearrange("(q p) c -> p q c", p=128)
        sl = wslab[it % 2]
        slk = f"wslab{it % 2}"
        p.dma("gpsimd", out=sl[:, 0:nD * 128].rearrange("p (q c) -> p q c", q=nD),
              in_=src, writes=(slk,))
        for q in range(nD):
            p.op("tensor", lambda h, q=q, sl=sl: h.matmul(
                out=P[0][:, 0:T2], lhsT=sl[:, ts(q, 128)], rhs=xb[q][:, 0:T2],
                start=(q == 0), stop=(q == nD - 1)),
                reads=(slk, f"xb{q}"), writes=("P0",))
        if it < nI:
            xsv = xs[it][:].rearrange("p (s c) -> p s c", s=2)[:, :, 3:3 + TC]
            psv = P[0][:, 0:T2].rearrange("p (s c) -> p s c", s=2)
            p.op("scalar", lambda h, it=it, xsv=xsv, psv=psv: h.copy(
                out=xsv, in_=psv),
                reads=("P0",), writes=(f"xs{it}",))
        else:
            p.op("scalar", lambda h, it=it: h.copy(
                out=zt[it - nI][:], in_=P[0][:, 0:T2]),
                reads=("P0",), writes=(f"zt{it - nI}",))
    for j in range(nI):
        x3 = xs[j][:].rearrange("p (s c) -> p s c", s=2)
        p.op("vector", lambda h, j=j, x3=x3: h.tensor_scalar(
            out=scr1[:].rearrange("p (s t) -> p s t", s=2),
            in0=x3[:, :, 0:TC], scalar1=cwc[:, j * DCONV:j * DCONV + 1],
            scalar2=None, op0=ALU.mult),
            reads=(f"xs{j}", "cwc"), writes=("scr1",))
        for k in range(1, DCONV):
            p.op("vector", lambda h, j=j, k=k, x3=x3: h.scalar_tensor_tensor(
                out=scr1[:].rearrange("p (s t) -> p s t", s=2),
                in0=x3[:, :, k:k + TC], scalar=cwc[:, j * DCONV + k:j * DCONV + k + 1],
                in1=scr1[:].rearrange("p (s t) -> p s t", s=2),
                op0=ALU.mult, op1=ALU.add),
                reads=(f"xs{j}", "cwc", "scr1"), writes=("scr1",))
        p.op("scalar", lambda h, j=j: h.activation(
            out=xc[j][:], in_=scr1[:], func=AF.Silu, bias=cbc_[:, j:j + 1]),
            reads=("scr1", "cbc_"), writes=(f"xc{j}",))
    for q in range(nI):
        p.op("tensor", lambda h, q=q: h.matmul(
            out=P[1][:, 0:T2], lhsT=wxT[q][:, 0:2 * DST], rhs=xc[q][:, 0:T2],
            start=(q == 0), stop=(q == nI - 1)),
            reads=(f"wxT{q}", f"xc{q}"), writes=("P1",))
    for q in range(nI):
        p.op("tensor", lambda h, q=q: h.matmul(
            out=ps_tf[0:1, 0:T2], lhsT=wxT[q][:, 2 * DST:2 * DST + 1],
            rhs=xc[q][:, 0:T2],
            start=(q == 0), stop=(q == nI - 1)),
            reads=(f"wxT{q}", f"xc{q}"), writes=("ps_tf",))
    for s in range(2):
        p.op("vector", lambda h, s=s: h.tensor_copy(
            out=B_stk[s * DST:(s + 1) * DST, :], in_=P[1][0:DST, s * TC:(s + 1) * TC]),
            reads=("P1",), writes=("B_stk",))
        p.op("vector", lambda h, s=s: h.tensor_copy(
            out=C_stk[s * DST:(s + 1) * DST, :], in_=P[1][DST:2 * DST, s * TC:(s + 1) * TC]),
            reads=("P1",), writes=("C_stk",))
    rrow = scr2
    p.op("scalar", lambda h: h.copy(out=scr2[0:1, 0:T2], in_=ps_tf[0:1, 0:T2]),
         reads=("ps_tf",), writes=("scr2",))
    p.op("tensor", lambda h: h.matmul(
        out=ps_tf[:, 0:T2], lhsT=C["ones_row"], rhs=scr2[0:1, 0:T2],
        start=True, stop=True),
        reads=("scr2", "ones_row"), writes=("ps_tf",))
    p.op("vector", lambda h: h.tensor_copy(out=r_bc[:, 0:T2], in_=ps_tf[:, 0:T2]),
         reads=("ps_tf",), writes=("r_bc",))
    for j in range(nI):
        p.op("scalar", lambda h, j=j: h.activation(
            out=scr1[:], in_=r_bc[:, 0:T2], func=AF.Abs, bias=dbc[:, j:j + 1]),
            reads=("r_bc", "dbc"), writes=("scr1",))
        p.op("scalar", lambda h: h.activation(
            out=scr1[:], in_=scr1[:], func=AF.Exp, scale=-1.0),
            reads=("scr1",), writes=("scr1",))
        p.op("scalar", lambda h: h.activation(
            out=scr1[:], in_=scr1[:], func=AF.Ln, bias=1.0),
            reads=("scr1", "cpool"), writes=("scr1",))
        p.op("scalar", lambda h, j=j: h.activation(
            out=scr2[:], in_=r_bc[:, 0:T2], func=AF.Relu, bias=dbc[:, j:j + 1]),
            reads=("r_bc", "dbc"), writes=("scr2",))
        p.op("vector", lambda h, j=j: h.tensor_add(
            out=dt16[j][:], in0=scr1[:], in1=scr2[:]),
            reads=("scr1", "scr2"), writes=(f"dt16{j}",))
        p.op("vector", lambda h, j=j: h.tensor_mul(
            out=dtx16[j][:], in0=dt16[j][:], in1=xc[j][:]),
            reads=(f"dt16{j}", f"xc{j}"), writes=(f"dtx16{j}",))
    for j in range(nI):
        for cb in range(2):
            p.op("tensor", lambda h, j=j, cb=cb: h.transpose(
                out=ps_tb[:, 0:128], in_=dt16[j][:, ts(cb, 128)], identity=ident_b[:]),
                reads=(f"dt16{j}", "ident_b"), writes=("ps_tb",))
            p.op("vector", lambda h, j=j, cb=cb: h.tensor_copy(
                out=dtT[cb][:, ts(j, 128)], in_=ps_tb[:, 0:128]),
                reads=("ps_tb",), writes=(f"dtT{cb}",))
            p.op("tensor", lambda h, j=j, cb=cb: h.transpose(
                out=ps_tb[:, 0:128], in_=dtx16[j][:, ts(cb, 128)], identity=ident_b[:]),
                reads=(f"dtx16{j}", "ident_b"), writes=("ps_tb",))
            p.op("vector", lambda h, j=j, cb=cb: h.tensor_copy(
                out=dtxT[cb][:, ts(j, 128)], in_=ps_tb[:, 0:128]),
                reads=("ps_tb",), writes=(f"dtxT{cb}",))


def emit_l1_scan(p, al, nc, A_bc, dt_rep, dtx_rep, g_rep, dtT, dtxT, dt_dram,
                 dtx_dram, B_stk, C_stk, Csel, hst, ps_y, ps_tb, yT, yv,
                 ident_b, ch, TC, TP, nI):
    DINl = DIN
    nsub = TC // TP
    g1, g2 = p.new_gid(), p.new_gid()
    for s in range(2):
        p.dma("sync", out=dt_dram[s:s + 1, :], in_=dtT[s][:],
              reads=(f"dtT{s}",), writes=("dt_dram",), wgroup=g1)
        p.dma("scalar", out=dtx_dram[s:s + 1, :], in_=dtxT[s][:],
              reads=(f"dtxT{s}",), writes=("dtx_dram",), wgroup=g2)
    for sub in range(nsub):
        tb = sub * TP
        bi = sub % 2
        p.dma("sync", out=dt_rep[bi][:],
              in_=dt_dram[:, tb * DINl:(tb + TP) * DINl]
              .unsqueeze(1).broadcast_to([2, 64, TP * DINl]),
              reads=("dt_dram",), writes=(f"dt_rep{bi}",))
        p.dma("scalar", out=dtx_rep[bi][:],
              in_=dtx_dram[:, tb * DINl:(tb + TP) * DINl]
              .unsqueeze(1).broadcast_to([2, 64, TP * DINl]),
              reads=("dtx_dram",), writes=(f"dtx_rep{bi}",))
        p.op("vector", lambda h, bi=bi: h.tensor_tensor(
            out=dt_rep[bi][:].rearrange("p (t d) -> p t d", t=TP),
            in0=A_bc[:].unsqueeze(1).broadcast_to([128, TP, DINl]),
            in1=dt_rep[bi][:].rearrange("p (t d) -> p t d", t=TP),
            op=ALU.mult),
            reads=("A_bc", f"dt_rep{bi}"), writes=(f"dt_rep{bi}",))
        p.op("scalar", lambda h, bi=bi: h.activation(
            out=dt_rep[bi][:], in_=dt_rep[bi][:], func=AF.Exp),
            reads=(f"dt_rep{bi}",), writes=(f"dt_rep{bi}",))
        for tau in range(TP):
            t = tb + tau
            tg = t % 64
            hi, ho = (ch * TC + t) % 2, (ch * TC + t + 1) % 2
            pz = (tg - 1) % 64
            p.op("gpsimd", lambda h, pz=pz: h.memset(Csel[:, pz:pz + 1], 0.0),
                 reads=(), writes=("Csel",))
            p.op("gpsimd", lambda h, pz=pz: h.memset(Csel[:, 64 + pz:64 + pz + 1], 0.0),
                 reads=(), writes=("Csel",))
            p.op("gpsimd", lambda h, t=t, tg=tg: h.tensor_copy(
                out=Csel[0:64, tg:tg + 1], in_=C_stk[0:64, t:t + 1]),
                reads=("C_stk",), writes=("Csel",))
            p.op("gpsimd", lambda h, t=t, tg=tg: h.tensor_copy(
                out=Csel[64:128, 64 + tg:64 + tg + 1], in_=C_stk[64:128, t:t + 1]),
                reads=("C_stk",), writes=("Csel",))
            p.op("vector", lambda h, tau=tau, bi=bi, hi=hi, ho=ho: h.tensor_mul(
                out=hst[ho][:], in0=g_rep[bi][:, tau * DINl:(tau + 1) * DINl],
                in1=hst[hi][:]),
                reads=(f"dt_rep{bi}", f"hst{hi}"), writes=(f"hst{ho}",))
            p.op("vector", lambda h, tau=tau, t=t, bi=bi, ho=ho: h.scalar_tensor_tensor(
                out=hst[ho][:], in0=dtx_rep[bi][:, tau * DINl:(tau + 1) * DINl],
                scalar=B_stk[:, t:t + 1], in1=hst[ho][:],
                op0=ALU.mult, op1=ALU.add),
                reads=(f"dtx_rep{bi}", "B_stk", f"hst{ho}"), writes=(f"hst{ho}",))
            for q4 in range(4):
                p.op("tensor", lambda h, q4=q4, tg=tg, ho=ho: h.matmul(
                    out=ps_y[:, ts(q4, 512)], lhsT=Csel[:],
                    rhs=hst[ho][:, ts(q4, 512)],
                    start=(tg == 0), stop=(tg == 63), skip_group_check=True),
                    reads=("Csel", f"hst{ho}"), writes=("ps_y",))
        if (sub + 1) % (64 // TP) == 0:
            grp = sub // (64 // TP)
            p.op("scalar", lambda h: h.copy(out=yv[:], in_=ps_y[:, :]),
                 reads=("ps_y",), writes=("wslab0",))
            for jb in range(nI):
                p.op("tensor", lambda h, jb=jb: h.transpose(
                    out=ps_tb[:, 0:128], in_=yv[:, ts(jb, 128)], identity=ident_b[:]),
                    reads=("wslab0", "ident_b"), writes=("ps_tb",))
                ov = yT[jb][:].rearrange("p (s t) -> p s t", s=2)[:, :, grp * 64:grp * 64 + 64]
                iv = ps_tb[:, 0:128].rearrange("p (s t) -> p s t", s=2)
                p.op("vector", lambda h, ov=ov, iv=iv: h.tensor_copy(out=ov, in_=iv),
                     reads=("ps_tb",), writes=(f"yT{jb}",))


def emit_l1_post(p, al, nc, Wout_h, yT, xc, zt, zs, skc, xb, rTc, stage_f,
                 wslab, P, C, ln_bufs, outT_hbm, lseq, t0, nI, nD, TC):
    T2 = 2 * TC
    for j in range(nI):
        p.op("vector", lambda h, j=j: h.scalar_tensor_tensor(
            out=yT[j][:], in0=xc[j][:], scalar=skc[:, j:j + 1], in1=yT[j][:],
            op0=ALU.mult, op1=ALU.add),
            reads=(f"xc{j}", "skc", f"yT{j}"), writes=(f"yT{j}",))
        p.op("scalar", lambda h, j=j: h.activation(
            out=zs[:], in_=zt[j][:], func=AF.Silu),
            reads=(f"zt{j}",), writes=("zs",))
        p.op("vector", lambda h, j=j: h.tensor_mul(
            out=yT[j][:], in0=yT[j][:], in1=zs[:]),
            reads=(f"yT{j}", "zs"), writes=(f"yT{j}",))
    for o in range(nD):
        src = Wout_h[:, ts(o, 128)].rearrange("(q p) c -> p q c", p=128)
        sl = wslab[o % 2]
        slk = f"wslab{o % 2}"
        p.dma("gpsimd", out=sl[:, 0:nI * 128].rearrange("p (q c) -> p q c", q=nI),
              in_=src, writes=(slk,))
        for q in range(nI):
            p.op("tensor", lambda h, q=q, o=o, sl=sl: h.matmul(
                out=P[1][:, 0:T2], lhsT=sl[:, ts(q, 128)], rhs=yT[q][:],
                start=(q == 0), stop=(q == nI - 1)),
                reads=(slk, f"yT{q}"), writes=("P1",))
        p.op("vector", lambda h, o=o: h.tensor_add(
            out=rTc[o][:], in0=P[1][:, 0:T2], in1=xb[o][:, 0:T2]),
            reads=("P1", f"xb{o}"), writes=(f"rTc{o}",))
    emit_ln_chunk(p, rTc, [f"rTc{j}" for j in range(nD)],
                  rTc, [f"rTc{j}" for j in range(nD)], "P0", "P1",
                  ln_bufs, C, T2)
    for j in range(nD):
        for s in range(2):
            p.dma("sync", out=outT_hbm[ts(j, 128), s * lseq + t0:s * lseq + t0 + TC],
                  in_=rTc[j][:, s * TC:(s + 1) * TC],
                  reads=(f"rTc{j}",), writes=(f"outT{j}",))



_NC_CACHE = {}


def _get_nc(name, builder):
    if name not in _NC_CACHE:
        nc = bass.Bass("TRN2", target_bir_lowering=False, debug=False)
        builder(nc)
        _NC_CACHE[name] = nc
    return _NC_CACHE[name]


def kernel(**inp):
    n_cores = 8
    x = np.asarray(inp["L0_attn_x"])
    B_all = x.shape[0]
    per = B_all // n_cores

    xTs = [np.ascontiguousarray(
        x[c * per:(c + 1) * per].transpose(2, 0, 1).reshape(D, per * LSEQ))
        for c in range(n_cores)]

    ncA = _get_nc("A", build_l0_attn)
    w = {k: np.ascontiguousarray(np.asarray(inp[v]))
         for k, v in [("wq", "L0_attn_wq_w"), ("wk", "L0_attn_wk_w"),
                      ("wv", "L0_attn_wv_w"), ("wo", "L0_attn_wo_w"),
                      ("bq", "L0_attn_wq_b"), ("bk", "L0_attn_wk_b"),
                      ("bv", "L0_attn_wv_b"), ("bo", "L0_attn_wo_b")]}
    maps = [dict(xT=xTs[c], **w) for c in range(n_cores)]
    res = run_bass_kernel_spmd(ncA, maps, core_ids=list(range(n_cores)))
    x1 = [res.results[c]["outT"] for c in range(n_cores)]

    ncB = _get_nc("B", build_l0_ffn)
    w = {"gate_w": np.asarray(inp["L0_ffn_gate_w"]),
         "Wg": np.asarray(inp["L0_ffn_Wg"]),
         "W1": np.asarray(inp["L0_ffn_W1"]),
         "W2": np.asarray(inp["L0_ffn_W2"])}
    maps = [dict(xT=x1[c], **w) for c in range(n_cores)]
    res = run_bass_kernel_spmd(ncB, maps, core_ids=list(range(n_cores)))
    x2 = [res.results[c]["outT"] for c in range(n_cores)]

    ncC = _get_nc("C", build_l1_attn)
    w = {"W_in": np.asarray(inp["L1_attn_W_in"]),
         "conv_w": np.asarray(inp["L1_attn_conv_w"]),
         "conv_b": np.asarray(inp["L1_attn_conv_b"]),
         "W_x": np.asarray(inp["L1_attn_W_x"]),
         "A_log": np.asarray(inp["L1_attn_A_log"]),
         "dt_bias": np.asarray(inp["L1_attn_dt_bias"]),
         "D_skip": np.asarray(inp["L1_attn_D"]),
         "W_out": np.asarray(inp["L1_attn_W_out"])}
    maps = [dict(xT=x2[c], **w) for c in range(n_cores)]
    res = run_bass_kernel_spmd(ncC, maps, core_ids=list(range(n_cores)))
    x3 = [res.results[c]["outT"] for c in range(n_cores)]

    ncD = _get_nc("D", build_l1_ffn)
    w = {"router_W": np.asarray(inp["L1_ffn_router_W"]),
         "router_b": np.asarray(inp["L1_ffn_router_b"]),
         "sW1": np.asarray(inp["L1_ffn_sW1"]),
         "sb1": np.asarray(inp["L1_ffn_sb1"]),
         "sW2": np.asarray(inp["L1_ffn_sW2"]),
         "sb2": np.asarray(inp["L1_ffn_sb2"]),
         "eW1": np.asarray(inp["L1_ffn_eW1"]),
         "eb1": np.asarray(inp["L1_ffn_eb1"]),
         "eW2": np.asarray(inp["L1_ffn_eW2"]),
         "eb2": np.asarray(inp["L1_ffn_eb2"])}
    maps = [dict(xT=x3[c], **w) for c in range(n_cores)]
    res = run_bass_kernel_spmd(ncD, maps, core_ids=list(range(n_cores)))
    x4 = [res.results[c]["outT"] for c in range(n_cores)]

    out = np.concatenate(
        [x4[c].reshape(D, per, LSEQ).transpose(1, 2, 0) for c in range(n_cores)],
        axis=0)
    return out.astype(np.float32)


# version 09
# speedup vs baseline: 1.8700x; 1.0013x over previous
import numpy as np
from contextlib import ExitStack

import concourse.bass as bass
import concourse.mybir as mybir
from concourse.bass_utils import run_bass_kernel_spmd

F32 = mybir.dt.float32
BF16 = mybir.dt.bfloat16
AF = mybir.ActivationFunctionType
ALU = mybir.AluOpType
AX = mybir.AxisListType

D = 1024
NSEQ = 2
LSEQ = 2048
TOK = NSEQ * LSEQ
NH = 16
DH = 64
E0, H0 = 8, 2736
E1, H1 = 15, 1024
DIN = 2048
DST = 64
DCONV = 4

ENGS = ("sync", "vector", "scalar", "tensor", "gpsimd")


class Prog:

    def __init__(self, nc):
        self.nc = nc
        self.ops = {e: [] for e in ENGS}
        self.icount = {e: 0 for e in ENGS}
        self.dcount = {e: 0 for e in ENGS}
        self.dslot = {}
        self.writers = {}
        self.readers = {}
        self._gid = 0
        self.hiw = {e: {} for e in ENGS}
        self.sems = {}
        self._es = None

    NDMA_SEMS = 8

    def make_sems(self, es):
        for e in ENGS:
            self.sems[("c", e)] = es.enter_context(self.nc.semaphore(f"sc_{e}"))
        for e in ("sync", "scalar", "gpsimd"):
            for s in range(self.NDMA_SEMS):
                self.sems[("d", e, s)] = es.enter_context(self.nc.semaphore(f"sd_{e}{s}"))

    def new_gid(self):
        self._gid += 1
        return self._gid

    def _dep_waits(self, eng, reads, writes, wgroup=None):
        need = {}
        def add(semkey, cnt):
            if need.get(semkey, 0) < cnt:
                need[semkey] = cnt
        for k in reads:
            w = self.writers.get(k)
            if w:
                for semkey, cnt in w["members"].items():
                    add(semkey, cnt)
        for k in writes:
            w = self.writers.get(k)
            if w and (wgroup is None or w["gid"] != wgroup):
                for semkey, cnt in w["members"].items():
                    add(semkey, cnt)
            if not (w and wgroup is not None and w["gid"] == wgroup):
                for semkey, cnt in self.readers.get(k, {}).items():
                    add(semkey, cnt)
        waits = []
        hw = self.hiw[eng]
        for semkey, cnt in sorted(need.items(), key=str):
            if semkey == ("c", eng) and cnt > self.icount[eng]:
                continue
            if semkey[0] == "d":
                tgt = 16 * self.dslot[(semkey[1], semkey[2])]
            else:
                tgt = cnt
            if hw.get(semkey, 0) < tgt:
                hw[semkey] = tgt
                waits.append((self.sems[semkey], tgt))
        return waits

    def _record_access(self, semkey, cnt, reads, writes, wgroup=None):
        for k in reads:
            self.readers.setdefault(k, {})[semkey] = cnt
        for k in writes:
            w = self.writers.get(k)
            if w and wgroup is not None and w["gid"] == wgroup:
                m = w["members"]
                if m.get(semkey, 0) < cnt:
                    m[semkey] = cnt
            else:
                self.writers[k] = {"gid": wgroup if wgroup is not None else self.new_gid(),
                                   "members": {semkey: cnt}}
                self.readers[k] = {}

    def op(self, eng, fn, reads=(), writes=(), inc=True, tag=None, wgroup=None):
        waits = self._dep_waits(eng, reads, writes, wgroup)
        cnt = self.icount[eng] + 1
        if inc:
            self.icount[eng] = cnt
        sem = self.sems[("c", eng)]
        def emit(h, waits=waits, fn=fn, inc=inc, sem=sem, tag=tag):
            for s, v in waits:
                h.wait_ge(s, v)
            ins = fn(h)
            if inc:
                ins.then_inc(sem, 1)
            if tag:
                ins.annotate(tag)
        self.ops[eng].append(emit)
        self._record_access(("c", eng), cnt, reads, writes, wgroup)

    def dma(self, eng, out, in_, reads=(), writes=(), tag=None, transpose=False,
            slow=False, wgroup=None):
        waits = self._dep_waits(eng, reads, writes, wgroup)
        slot = self.dcount[eng] % self.NDMA_SEMS
        self.dcount[eng] += 1
        key = ("d", eng, slot)
        prev = self.dslot.setdefault((eng, slot), 0)
        cnt = prev + 1
        self.dslot[(eng, slot)] = cnt
        sem = self.sems[key]
        if prev:
            hw = self.hiw[eng]
            if hw.get(key, 0) < 16 * prev:
                hw[key] = 16 * prev
                waits.append((sem, 16 * prev))
        def emit(h, waits=waits, out=out, in_=in_, sem=sem, tag=tag,
                 transpose=transpose, slow=slow):
            for s, v in waits:
                h.wait_ge(s, v)
            kw = {}
            if transpose:
                kw["transpose"] = True
            if slow:
                kw["allow_slow_non_contiguous"] = True
            ins = h.dma_start(out=out, in_=in_, **kw)
            ins.then_inc(sem, 16)
            if tag:
                ins.annotate(tag)
        self.ops[eng].append(emit)
        self._record_access(key, cnt, reads, writes, wgroup)

    def barrier_all(self, eng):
        waits = []
        hw = self.hiw[eng]
        for e in ENGS:
            if self.icount[e]:
                key = ("c", e)
                tgt = self.icount[e]
                if hw.get(key, 0) < tgt:
                    hw[key] = tgt
                    waits.append((self.sems[key], tgt))
        for (e, s), cnt in self.dslot.items():
            key = ("d", e, s)
            tgt = cnt * 16
            if hw.get(key, 0) < tgt:
                hw[key] = tgt
                waits.append((self.sems[key], tgt))
        def emit(h, waits=waits):
            for s, v in waits:
                h.wait_ge(s, v)
        self.ops[eng].append(emit)

    def finish(self, block):
        handles = {}
        def run(eng):
            def section(h):
                for fn in self.ops[eng]:
                    fn(h)
            return section
        block.sync(run("sync"))
        block.vector(run("vector"))
        block.scalar(run("scalar"))
        block.tensor(run("tensor"))
        block.gpsimd(run("gpsimd"))



def ts(i, n):
    return slice(i * n, (i + 1) * n)


def ceil_div(a, b):
    return -(-a // b)


class Alloc:

    def __init__(self, nc, es):
        self.nc = nc
        self.es = es
        self.n = 0

    def sb(self, name, shape, dtype):
        return self.es.enter_context(self.nc.sbuf_tensor(name, shape, dtype))

    def ps(self, name, shape, dtype=F32):
        return self.es.enter_context(self.nc.psum_tensor(name, shape, dtype))


def load_weight_T(p, al, hbm_ap, name, out_dim, in_dim, stage):
    nI, nO = in_dim // 128, out_dim // 128
    wT = [al.sb(f"{name}_T{i}", [128, out_dim], BF16) for i in range(nI)]
    stage_f, stage_b = stage
    gids = [p.new_gid() for _ in range(nI)]
    for o in range(nO):
        kf = "wstage_f"
        kb = "wstage_b"
        p.dma("sync", out=stage_f[:], in_=hbm_ap[ts(o, 128), :],
              reads=(), writes=(kf,))
        p.op("vector", lambda h, sf=stage_f, sbf=stage_b: h.tensor_copy(out=sbf[:], in_=sf[:]),
             reads=(kf,), writes=(kb,))
        for i in range(nI):
            p.dma("scalar", out=wT[i][:, ts(o, 128)], in_=stage_b[:, ts(i, 128)],
                  reads=(kb,), writes=(f"{name}_T{i}",), transpose=True, wgroup=gids[i])
    return wT


def load_weight_direct(p, al, hbm_ap, name, in_dim, out_dim):
    nI = in_dim // 128
    wT = []
    for i in range(nI):
        stage = al.sb(f"{name}_st{i}", [128, out_dim], F32)
        t = al.sb(f"{name}_d{i}", [128, out_dim], BF16)
        p.dma("sync", out=stage[:], in_=hbm_ap[ts(i, 128), :],
              reads=(), writes=(f"{name}_st{i}",))
        p.op("vector", lambda h, s=stage, t=t: h.tensor_copy(out=t[:], in_=s[:]),
             reads=(f"{name}_st{i}",), writes=(f"{name}_d{i}",))
        wT.append(t)
    return wT


def load_bias(p, al, hbm_ap, name, dim):
    nT = dim // 128
    col = al.sb(f"{name}_col", [128, nT], F32)
    p.dma("sync", out=col[:], in_=hbm_ap.rearrange("(t p) -> p t", p=128),
          reads=(), writes=(name,))
    return col


def load_bias_cols(p, al, hbm_ap, name, dim):
    nT = dim // 128
    col = al.sb(f"{name}_col", [128, nT], F32)
    g = p.new_gid()
    for j in range(nT):
        p.dma("sync", out=col[:, j:j + 1], in_=hbm_ap[ts(j, 128)].unsqueeze(1),
              reads=(), writes=(name,), wgroup=g)
    return col


def make_ones_and_masks(p, al, lseq, qc):
    ones_f = al.sb("ones_f", [128, 1], F32)
    ones_b = al.sb("ones_b", [128, 1], BF16)
    p.op("gpsimd", lambda h: h.memset(ones_f[:], 1.0), writes=("ones_f",))
    p.op("gpsimd", lambda h: h.memset(ones_b[:], 1.0), writes=("ones_b",))
    masks = {}
    for bi in range(qc // 128):
        delta = bi * 128
        m = al.sb(f"mask{bi}", [128, qc], BF16)
        p.op("gpsimd", lambda h, m=m: h.memset(m[:], 1.0), writes=(f"mask{bi}",))
        p.op("gpsimd", lambda h, m=m, delta=delta: h.affine_select(
            out=m[:], in_=m[:], compare_op=ALU.is_ge, fill=0.0,
            base=delta, pattern=[[1, qc]], channel_multiplier=-1),
            reads=(f"mask{bi}",), writes=(f"mask{bi}",))
        masks[delta] = m
    return ones_f, ones_b, masks


def emit_layernorm(p, al, r_tiles, rkeys, out_tiles, okeys, tok, tag, C,
                   cast_to=None, ckeys=None):
    nD = len(r_tiles)
    CH = 512
    ps_s0 = al.ps(f"{tag}_pss0", [1, 512], F32)
    ps_s1 = al.ps(f"{tag}_pss1", [1, 512], F32)
    row_m = al.sb(f"{tag}_rowm", [1, CH], F32)
    row_r = al.sb(f"{tag}_rowr", [1, CH], F32)
    r2 = al.sb(f"{tag}_r2", [128, CH], F32)
    bc_m = al.ps(f"{tag}_bcm", [128, 512], F32)
    bc_r = al.ps(f"{tag}_bcr", [128, 512], F32)
    inv = 1.0 / (nD * 128)
    for c in range(tok // CH):
        cs = ts(c, CH)
        for j in range(nD):
            p.op("tensor", lambda h, j=j, cs=cs: h.matmul(
                out=ps_s0[:, :], lhsT=C["ones_col"], rhs=r_tiles[j][:, cs],
                start=(j == 0), stop=(j == nD - 1)),
                reads=(rkeys[j], "ones_col"), writes=(f"{tag}_pss0",))
        for j in range(nD):
            p.op("scalar", lambda h, j=j, cs=cs: h.activation(
                out=r2[:], in_=r_tiles[j][:, cs], func=AF.Square),
                reads=(rkeys[j],), writes=(f"{tag}_r2",))
            p.op("tensor", lambda h, j=j: h.matmul(
                out=ps_s1[:, :], lhsT=C["ones_col"], rhs=r2[:],
                start=(j == 0), stop=(j == nD - 1)),
                reads=(f"{tag}_r2", "ones_col"), writes=(f"{tag}_pss1",))
        p.op("scalar", lambda h: h.activation(
            out=row_m[:], in_=ps_s0[:, :], func=AF.Copy, scale=inv),
            reads=(f"{tag}_pss0",), writes=(f"{tag}_rowm",))
        p.op("scalar", lambda h: h.activation(
            out=row_r[:], in_=row_m[:], func=AF.Square),
            reads=(f"{tag}_rowm",), writes=(f"{tag}_rowr",))
        p.op("vector", lambda h: h.scalar_tensor_tensor(
            out=row_r[:], in0=ps_s1[:, :], scalar=inv, in1=row_r[:],
            op0=ALU.mult, op1=ALU.subtract),
            reads=(f"{tag}_pss1", f"{tag}_rowr"), writes=(f"{tag}_rowr",))
        p.op("scalar", lambda h: h.activation(
            out=row_r[:], in_=row_r[:], func=AF.Ln, bias=1e-5),
            reads=(f"{tag}_rowr",), writes=(f"{tag}_rowr",))
        p.op("scalar", lambda h: h.activation(
            out=row_r[:], in_=row_r[:], func=AF.Exp, scale=-0.5),
            reads=(f"{tag}_rowr",), writes=(f"{tag}_rowr",))
        p.op("tensor", lambda h: h.matmul(
            out=bc_m[:, :CH], lhsT=C["ones_row"], rhs=row_m[:],
            start=True, stop=True),
            reads=(f"{tag}_rowm", "ones_row"), writes=(f"{tag}_bcm",))
        p.op("tensor", lambda h: h.matmul(
            out=bc_r[:, :CH], lhsT=C["ones_row"], rhs=row_r[:],
            start=True, stop=True),
            reads=(f"{tag}_rowr", "ones_row"), writes=(f"{tag}_bcr",))
        for j in range(nD):
            p.op("vector", lambda h, j=j, cs=cs: h.tensor_sub(
                out=out_tiles[j][:, cs], in0=r_tiles[j][:, cs], in1=bc_m[:, :CH]),
                reads=(rkeys[j], f"{tag}_bcm"), writes=(okeys[j],))
            p.op("vector", lambda h, j=j, cs=cs: h.tensor_mul(
                out=out_tiles[j][:, cs], in0=out_tiles[j][:, cs], in1=bc_r[:, :CH]),
                reads=(okeys[j], f"{tag}_bcr"), writes=(okeys[j],))
            if cast_to is not None:
                p.op("vector", lambda h, j=j, cs=cs: h.tensor_copy(
                    out=cast_to[j][:, cs], in_=out_tiles[j][:, cs]),
                    reads=(okeys[j],), writes=(ckeys[j],))


CONST_VALS = [1e-5, 1.0]


def make_consts(p, al):
    C = {}
    cpool = al.sb("cpool", [128, len(CONST_VALS)], F32)
    for i, v in enumerate(CONST_VALS):
        p.op("gpsimd", lambda h, i=i, v=v: h.memset(cpool[:, i:i + 1], v),
             writes=("cpool",))
        p.nc.const_aps.register(F32, v, cpool[:, i:i + 1])
    ones_col = al.sb("ones_col", [128, 1], F32)
    ones_row = al.sb("ones_row", [1, 128], F32)
    ones_row_b = al.sb("ones_row_b", [1, 128], BF16)
    p.op("gpsimd", lambda h: h.memset(ones_col[:], 1.0), writes=("ones_col",))
    p.op("gpsimd", lambda h: h.memset(ones_row[:], 1.0), writes=("ones_row",))
    p.op("gpsimd", lambda h: h.memset(ones_row_b[:], 1.0), writes=("ones_row_b",))
    C["ones_col"] = ones_col[:]
    C["ones_row"] = ones_row[:]
    C["ones_row_b"] = ones_row_b[:]
    return C


def load_xT(p, al, x_hbm, name, tok, dim=D, dtype=F32, cast_bf=True):
    nD = dim // 128
    xf = [al.sb(f"{name}_f{j}", [128, tok], F32) for j in range(nD)]
    xb = [al.sb(f"{name}_b{j}", [128, tok], BF16) for j in range(nD)] if cast_bf else None
    for j in range(nD):
        p.dma("sync", out=xf[j][:], in_=x_hbm[ts(j, 128), :], writes=(f"{name}_f{j}",))
        if cast_bf:
            p.op("vector", lambda h, j=j: h.tensor_copy(out=xb[j][:], in_=xf[j][:]),
                 reads=(f"{name}_f{j}",), writes=(f"{name}_b{j}",))
    return xf, xb



def build_l0_attn(nc, lseq=LSEQ):
    tok = NSEQ * lseq
    xT = nc.dram_tensor("xT", [D, tok], F32, kind="ExternalInput").ap()
    ws = [nc.dram_tensor(n, [D, D], F32, kind="ExternalInput").ap()
          for n in ("wq", "wk", "wv", "wo")]
    bs = [nc.dram_tensor(n, [D], F32, kind="ExternalInput").ap()
          for n in ("bq", "bk", "bv", "bo")]
    outT = nc.dram_tensor("outT", [D, tok], F32, kind="ExternalOutput").ap()
    es = ExitStack()
    with es:
        al = Alloc(nc, es)
        p = Prog(nc)
        p.make_sems(es)
        emit_l0_attn(nc, p, al, xT, ws, bs, outT, lseq)
        block = es.enter_context(nc.Block())
        p.finish(block)
    return nc


def emit_l0_attn(nc, p, al, xT_hbm, w_hbm, b_hbm, outT_hbm, lseq):
    nD = D // 128
    QC = 512
    C = make_consts(p, al)

    P = [al.ps(f"P{i}", [128, 512], F32) for i in range(6)]

    masks = {}
    for bi in range(QC // 128):
        delta = -bi * 128
        m = al.sb(f"mask{bi}", [128, QC], BF16)
        p.op("gpsimd", lambda h, m=m: h.memset(m[:], 1.0), writes=(f"mask{bi}",))
        p.op("gpsimd", lambda h, m=m, delta=delta: h.affine_select(
            out=m[:], in_=m[:], compare_op=ALU.is_ge, fill=0.0,
            base=delta, pattern=[[1, QC]], channel_multiplier=-1),
            reads=(f"mask{bi}",), writes=(f"mask{bi}",))
        masks[delta] = m

    stage_f = al.sb("wstage_f", [128, D], F32)
    stage_b = al.sb("wstage_b", [128, D], BF16)
    wqT = load_weight_T(p, al, w_hbm[0], "wq", D, D, (stage_f, stage_b))
    wkT = load_weight_T(p, al, w_hbm[1], "wk", D, D, (stage_f, stage_b))
    wvT = load_weight_T(p, al, w_hbm[2], "wv", D, D, (stage_f, stage_b))
    woT = load_weight_T(p, al, w_hbm[3], "wo", D, D, (stage_f, stage_b))
    bq = load_bias_cols(p, al, b_hbm[0], "bq", D)
    bk = load_bias_cols(p, al, b_hbm[1], "bk", D)
    bo = load_bias_cols(p, al, b_hbm[3], "bo", D)
    bv_row_f = al.sb("bv_rowf", [1, D], F32)
    bv_row = al.sb("bv_row", [1, D], BF16)
    p.dma("sync", out=bv_row_f[:], in_=b_hbm[2].unsqueeze(0), writes=("bv_rowf",))
    p.op("vector", lambda h: h.tensor_copy(out=bv_row[:], in_=bv_row_f[:]),
         reads=("bv_rowf",), writes=("bv_row",))

    xbc = [al.sb(f"xbc{j}", [128, 512], BF16) for j in range(nD)]
    KT = [al.sb(f"KT{j}", [128, lseq], BF16) for j in range(nD)]
    Vt = [al.sb(f"Vt{i}", [128, D], BF16) for i in range(lseq // 128)]
    QTc = [al.sb(f"QTc{j}", [128, QC], BF16) for j in range(nD)]
    ATc = [al.sb(f"ATc{j}", [128, QC], BF16) for j in range(nD)]
    rTc = [al.sb(f"rTc{j}", [128, QC], F32) for j in range(nD)]
    Sb = [al.sb(f"Sbb{i}", [128, QC], BF16) for i in range(2)]

    ln_bufs = dict(
        ps_s0=P[4][0:1, :], ps_s1=P[5][0:1, :],
        bc_m=P[4], bc_r=P[5],
        row_m=al.sb("ln_rowm", [1, QC], F32),
        row_r=al.sb("ln_rowr", [1, QC], F32),
        r2=al.sb("ln_r2", [128, QC], F32),
    )

    for seq in range(NSEQ):
        s_lo = seq * lseq
        for c in range(lseq // 512):
            cs = ts(c, 512)
            for j in range(nD):
                p.dma("gpsimd", out=xbc[j][:],
                      in_=xT_hbm[ts(j, 128), s_lo + c * 512:s_lo + (c + 1) * 512],
                      writes=(f"xbc{j}",))
            for e in range(nD):
                for j in range(nD):
                    p.op("tensor", lambda h, e=e, j=j: h.matmul(
                        out=P[0][:, :], lhsT=wkT[j][:, ts(e, 128)], rhs=xbc[j][:],
                        start=(j == 0), stop=(j == nD - 1)),
                        reads=(f"wk_T{j}", f"xbc{j}"), writes=("P0",))
                p.op("vector", lambda h, e=e, cs=cs: h.tensor_scalar(
                    out=KT[e][:, cs], in0=P[0][:, :], scalar1=bk[:, e:e + 1],
                    scalar2=None, op0=ALU.add),
                    reads=("P0", "bk"), writes=(f"KT{e}",))
            for i4 in range(4):
                i = c * 4 + i4
                for hhalf in range(D // 512):
                    hs = ts(hhalf, 512)
                    for j in range(nD):
                        p.op("tensor", lambda h, i4=i4, j=j, hs=hs: h.matmul(
                            out=P[1][:, :], lhsT=xbc[j][:, ts(i4, 128)], rhs=wvT[j][:, hs],
                            start=(j == 0), stop=False),
                            reads=(f"wv_T{j}", f"xbc{j}"), writes=("P1",))
                    p.op("tensor", lambda h, hs=hs: h.matmul(
                        out=P[1][:, :], lhsT=C["ones_row_b"], rhs=bv_row[:, hs],
                        start=False, stop=True),
                        reads=("bv_row", "ones_row_b"), writes=("P1",))
                    p.op("scalar", lambda h, i=i, hs=hs: h.copy(
                        out=Vt[i][:, hs], in_=P[1][:, :]),
                        reads=("P1",), writes=(f"Vt{i}",))
        for qc in range(lseq // QC):
            q0 = qc * QC
            for j in range(nD):
                p.dma("gpsimd", out=xbc[j][:],
                      in_=xT_hbm[ts(j, 128), s_lo + q0:s_lo + q0 + QC],
                      writes=(f"xbc{j}",))
            for e in range(nD):
                for j in range(nD):
                    p.op("tensor", lambda h, e=e, j=j: h.matmul(
                        out=P[0][:, :], lhsT=wqT[j][:, ts(e, 128)],
                        rhs=xbc[j][:],
                        start=(j == 0), stop=(j == nD - 1)),
                        reads=(f"wq_T{j}", f"xbc{j}"), writes=("P0",))
                p.op("vector", lambda h, e=e: h.tensor_scalar(
                    out=QTc[e][:], in0=P[0][:, :], scalar1=bq[:, e:e + 1],
                    scalar2=None, op0=ALU.add),
                    reads=("P0", "bq"), writes=(f"QTc{e}",))
            for head in range(NH):
                et, ro = head // 2, (head % 2) * 64
                n_sblk = (q0 + QC) // 128
                for sb_ in range(n_sblk):
                    s0 = sb_ * 128
                    pp, pk = P[2 + (sb_ % 2)], f"P{2 + (sb_ % 2)}"
                    sbuf, sk = Sb[sb_ % 2], f"Sbb{sb_ % 2}"
                    p.op("tensor", lambda h, et=et, ro=ro, s0=s0, q0=q0, pp=pp: h.matmul(
                        out=pp[:, :QC], lhsT=KT[et][ro:ro + 64, s0:s0 + 128],
                        rhs=QTc[et][ro:ro + 64, :],
                        start=True, stop=True),
                        reads=(f"KT{et}", f"QTc{et}"), writes=(pk,))
                    delta = q0 - s0
                    if delta <= 0:
                        p.op("vector", lambda h, pp=pp, sbuf=sbuf, delta=delta: h.tensor_mul(
                            out=sbuf[:], in0=pp[:, :QC], in1=masks[delta][:]),
                            reads=(pk, f"mask{-delta // 128}"), writes=(sk,))
                    else:
                        p.op("scalar", lambda h, pp=pp, sbuf=sbuf: h.copy(
                            out=sbuf[:], in_=pp[:, :QC]),
                            reads=(pk,), writes=(sk,))
                    p.op("tensor", lambda h, s0=s0, ro=ro, et=et, sbuf=sbuf,
                         st=(sb_ == 0), sp=(sb_ == n_sblk - 1): h.matmul(
                        out=P[1][0:64, :QC], lhsT=Vt[s0 // 128][:, et * 128 + ro: et * 128 + ro + 64],
                        rhs=sbuf[:], start=st, stop=sp, skip_group_check=True),
                        reads=(sk, f"Vt{s0 // 128}"), writes=("P1",))
                p.op("scalar", lambda h, et=et, ro=ro: h.copy(
                    out=ATc[et][ro:ro + 64, :], in_=P[1][0:64, :QC]),
                    reads=("P1",), writes=(f"ATc{et}",))
            for e in range(nD):
                for j in range(nD):
                    p.op("tensor", lambda h, e=e, j=j: h.matmul(
                        out=P[0][:, :], lhsT=woT[j][:, ts(e, 128)], rhs=ATc[j][:],
                        start=(j == 0), stop=(j == nD - 1)),
                        reads=(f"wo_T{j}", f"ATc{j}"), writes=("P0",))
                p.op("vector", lambda h, e=e: h.scalar_tensor_tensor(
                    out=rTc[e][:], in0=P[0][:, :], scalar=bo[:, e:e + 1],
                    in1=xbc[e][:], op0=ALU.add, op1=ALU.add),
                    reads=("P0", "bo", f"xbc{e}"), writes=(f"rTc{e}",))
            emit_ln_chunk(p, rTc, [f"rTc{j}" for j in range(nD)],
                          rTc, [f"rTc{j}" for j in range(nD)], "P4", "P5",
                          ln_bufs, C, QC)
            for j in range(nD):
                p.dma("sync", out=outT_hbm[ts(j, 128), s_lo + q0:s_lo + q0 + QC],
                      in_=rTc[j][:], reads=(f"rTc{j}",), writes=(f"outT{j}",))
    p.barrier_all("sync")


    p.barrier_all("sync")


def emit_ln_chunk(p, r_tiles, rkeys, out_tiles, okeys, pk0, pk1, B, C, CH,
                  cast_to=None, ckeys=None, out_slice=None):
    nD = len(r_tiles)
    inv = 1.0 / (nD * 128)
    osl = out_slice if out_slice is not None else slice(0, CH)
    for j in range(nD):
        p.op("tensor", lambda h, j=j: h.matmul(
            out=B["ps_s0"][:, :CH], lhsT=C["ones_col"], rhs=r_tiles[j][:],
            start=(j == 0), stop=(j == nD - 1)),
            reads=(rkeys[j], "ones_col"), writes=(pk0,))
    for j in range(nD):
        p.op("scalar", lambda h, j=j: h.activation(
            out=B["r2"][:], in_=r_tiles[j][:], func=AF.Square),
            reads=(rkeys[j],), writes=("ln_r2",))
        p.op("tensor", lambda h, j=j: h.matmul(
            out=B["ps_s1"][:, :CH], lhsT=C["ones_col"], rhs=B["r2"][:],
            start=(j == 0), stop=(j == nD - 1)),
            reads=("ln_r2", "ones_col"), writes=(pk1,))
    p.op("scalar", lambda h: h.activation(
        out=B["row_m"][:], in_=B["ps_s0"][:, :CH], func=AF.Copy, scale=inv),
        reads=(pk0,), writes=("ln_rowm",))
    p.op("scalar", lambda h: h.activation(
        out=B["row_r"][:], in_=B["row_m"][:], func=AF.Square),
        reads=("ln_rowm",), writes=("ln_rowr",))
    p.op("vector", lambda h: h.scalar_tensor_tensor(
        out=B["row_r"][:], in0=B["ps_s1"][:, :CH], scalar=inv, in1=B["row_r"][:],
        op0=ALU.mult, op1=ALU.subtract),
        reads=(pk1, "ln_rowr"), writes=("ln_rowr",))
    p.op("scalar", lambda h: h.activation(
        out=B["row_r"][:], in_=B["row_r"][:], func=AF.Ln, bias=1e-5),
        reads=("ln_rowr", "cpool"), writes=("ln_rowr",))
    p.op("scalar", lambda h: h.activation(
        out=B["row_r"][:], in_=B["row_r"][:], func=AF.Exp, scale=-0.5),
        reads=("ln_rowr",), writes=("ln_rowr",))
    p.op("tensor", lambda h: h.matmul(
        out=B["bc_m"][:, :CH], lhsT=C["ones_row"], rhs=B["row_m"][:],
        start=True, stop=True),
        reads=("ln_rowm", "ones_row"), writes=(pk0,))
    p.op("tensor", lambda h: h.matmul(
        out=B["bc_r"][:, :CH], lhsT=C["ones_row"], rhs=B["row_r"][:],
        start=True, stop=True),
        reads=("ln_rowr", "ones_row"), writes=(pk1,))
    for j in range(nD):
        p.op("vector", lambda h, j=j: h.tensor_sub(
            out=out_tiles[j][:, osl], in0=r_tiles[j][:], in1=B["bc_m"][:, :CH]),
            reads=(rkeys[j], pk0), writes=(okeys[j],))
        p.op("vector", lambda h, j=j: h.tensor_mul(
            out=out_tiles[j][:, osl], in0=out_tiles[j][:, osl], in1=B["bc_r"][:, :CH]),
            reads=(okeys[j], pk1), writes=(okeys[j],))
        if cast_to is not None:
            p.op("vector", lambda h, j=j: h.tensor_copy(
                out=cast_to[j][:, osl], in_=out_tiles[j][:, osl]),
                reads=(okeys[j],), writes=(ckeys[j],))



def stream_wT_slice_pe(p, hbm_ap, in_dim, o0, o1, dest, dkeys, col0,
                       stage_b, skey, ps_pair, pskeys, ident_b):
    ow = o1 - o0
    nI = in_dim // 128
    if ow < 128:
        ow0 = (ow // 32) * 32
        segs = [(a, b) for a, b in [(32, 64), (64, 128)] if b > ow0]
        for a, b in segs:
            a = max(a, ow0)
            p.op("vector", lambda h, a=a, b=b: h.memset(stage_b[a:b, 0:in_dim], 0.0),
                 reads=(skey,), writes=(skey,))
    p.dma("gpsimd", out=stage_b[0:ow, 0:in_dim], in_=hbm_ap[o0:o1, :],
          reads=(skey,), writes=(skey,))
    for i in range(nI):
        pt, ptk = ps_pair[i % 2], pskeys[i % 2]
        p.op("tensor", lambda h, i=i, pt=pt: h.transpose(
            out=pt[:, 0:128], in_=stage_b[0:128, ts(i, 128)], identity=ident_b),
            reads=(skey, "ident_b"), writes=(ptk,))
        g = p.new_gid()
        p.op("vector", lambda h, i=i, pt=pt: h.tensor_copy(
            out=dest[i][:, col0:col0 + 128], in_=pt[:, 0:128]),
            reads=(ptk,), writes=(dkeys[i],), wgroup=g)


def stream_wT_slice(p, al, hbm_ap, in_dim, o0, o1, dest, dkeys, col0,
                    stage_b, skey, tq="scalar"):
    ow = o1 - o0
    nI = in_dim // 128
    if ow < 128:
        ow0 = (ow // 32) * 32
        segs = [(a, b) for a, b in [(32, 64), (64, 128)] if b > ow0]
        for a, b in segs:
            a = max(a, ow0)
            p.op("vector", lambda h, a=a, b=b: h.memset(stage_b[a:b, 0:in_dim], 0.0),
                 reads=(skey,), writes=(skey,))
    p.dma("gpsimd", out=stage_b[0:ow, 0:in_dim], in_=hbm_ap[o0:o1, :],
          reads=(skey,), writes=(skey,))
    for i in range(nI):
        g = p.new_gid()
        p.dma(tq, out=dest[i][:, col0:col0 + 128],
              in_=stage_b[0:128, ts(i, 128)],
              reads=(skey,), writes=(dkeys[i],), transpose=True, wgroup=g)



def build_l0_ffn(nc, lseq=LSEQ):
    tok = NSEQ * lseq
    xT = nc.dram_tensor("xT", [D, tok], F32, kind="ExternalInput").ap()
    gate_w = nc.dram_tensor("gate_w", [E0, D], F32, kind="ExternalInput").ap()
    Wg = nc.dram_tensor("Wg", [E0, H0, D], F32, kind="ExternalInput").ap()
    W1 = nc.dram_tensor("W1", [E0, H0, D], F32, kind="ExternalInput").ap()
    W2 = nc.dram_tensor("W2", [E0, D, H0], F32, kind="ExternalInput").ap()
    outT = nc.dram_tensor("outT", [D, tok], F32, kind="ExternalOutput").ap()
    es = ExitStack()
    with es:
        al = Alloc(nc, es)
        p = Prog(nc)
        p.make_sems(es)
        emit_l0_ffn(nc, p, al, xT, gate_w, Wg, W1, W2, outT, lseq)
        block = es.enter_context(nc.Block())
        p.finish(block)
    return nc


def emit_l0_ffn(nc, p, al, xT_hbm, gate_hbm, Wg_hbm, W1_hbm, W2_hbm, outT_hbm, lseq):
    tok = NSEQ * lseq
    nD = D // 128
    HT = ceil_div(H0, 128)
    TH = tok // 2
    C = make_consts(p, al)

    P = [al.ps(f"P{i}", [128, 512], F32) for i in range(6)]
    ps_tb = al.ps("ps_tb", [128, 1024], BF16)
    ps_tb2 = al.ps("ps_tb2", [128, 1024], BF16)

    ident_f = al.sb("ident_f", [128, 128], F32)
    p.op("gpsimd", lambda h: h.memset(ident_f[:], 0.0), writes=("ident_f",))
    p.op("gpsimd", lambda h: h.affine_select(
        out=ident_f[:], in_=ident_f[:], compare_op=ALU.not_equal, fill=1.0,
        base=0, pattern=[[-1, 128]], channel_multiplier=1),
        reads=("ident_f",), writes=("ident_f",))
    ident_b = al.sb("ident_b", [128, 128], BF16)
    p.op("gpsimd", lambda h: h.memset(ident_b[:], 0.0), writes=("ident_b",))
    p.op("gpsimd", lambda h: h.affine_select(
        out=ident_b[:], in_=ident_b[:], compare_op=ALU.not_equal, fill=1.0,
        base=0, pattern=[[-1, 128]], channel_multiplier=1),
        reads=("ident_b",), writes=("ident_b",))
    sel = [al.sb(f"sel{e}", [8, 128], BF16) for e in range(E0)]
    for e in range(E0):
        p.op("gpsimd", lambda h, e=e: h.memset(sel[e][:], 1.0), writes=(f"sel{e}",))
        p.op("gpsimd", lambda h, e=e: h.affine_select(
            out=sel[e][:], in_=sel[e][:], compare_op=ALU.is_equal, fill=0.0,
            base=-e, pattern=[[0, 128]], channel_multiplier=1),
            reads=(f"sel{e}",), writes=(f"sel{e}",))

    stg2 = [al.sb(f"stg2_{b}", [128, 1408], BF16) for b in range(2)]
    gwT = [al.sb(f"gwT{j}", [128, 16], BF16) for j in range(nD)]
    p.op("vector", lambda h: h.memset(stg2[0][0:16, 0:D], 0.0), writes=("stg2_0",))
    p.dma("gpsimd", out=stg2[0][0:8, 0:D], in_=gate_hbm,
          reads=("stg2_0",), writes=("stg2_0",))
    for j in range(nD):
        p.dma("scalar", out=gwT[j][:], in_=stg2[0][0:16, ts(j, 128)],
              reads=("stg2_0",), writes=(f"gwT{j}",), transpose=True)

    xb = [al.sb(f"xb{j}", [128, TH], BF16) for j in range(nD)]
    nTT = TH // 128
    Ltm = al.sb("Ltm", [128, nTT * E0], F32)
    comb = al.sb("comb", [128, nTT * E0], F32)
    combE = al.sb("combE", [8, TH], BF16)
    cbc = al.sb("cbc", [128, TH], BF16)
    r_m1 = al.sb("r_m1", [128, nTT], F32)
    r_m2 = al.sb("r_m2", [128, nTT], F32)
    r_eb = al.sb("r_eb", [128, nTT * E0], F32)
    r_sc = al.sb("r_sc", [128, nTT], F32)
    r_sc2 = al.sb("r_sc2", [128, nTT], F32)

    WgTs = [[al.sb(f"WgTs{b}_{j}", [128, 128], BF16) for j in range(nD)]
            for b in range(2)]
    W1Ts = [[al.sb(f"W1Ts{b}_{j}", [128, 128], BF16) for j in range(nD)]
            for b in range(2)]
    HHT = 8
    w2T = [al.sb(f"w2T{j}", [128, D], BF16) for j in range(HHT)]
    gu = [al.sb(f"gu{j}", [128, TH], BF16) for j in range(HHT)]
    gs = al.sb("gs", [128, 512], BF16)
    gs2 = al.sb("gs2", [128, 512], BF16)
    out_acc = [al.sb(f"oacc{j}", [128, TH], BF16) for j in range(nD)]
    tmp_y = al.sb("tmp_y", [128, 512], BF16)
    rTc = [al.sb(f"rTc{j}", [128, 512], F32) for j in range(nD)]
    ln_bufs = dict(
        ps_s0=P[4][0:1, :], ps_s1=P[5][0:1, :], bc_m=P[4], bc_r=P[5],
        row_m=al.sb("ln_rowm", [1, 512], F32),
        row_r=al.sb("ln_rowr", [1, 512], F32),
        r2=al.sb("ln_r2", [128, 512], F32),
    )

    for half in range(2):
        t0 = half * TH
        for j in range(nD):
            p.dma("gpsimd", out=xb[j][:], in_=xT_hbm[ts(j, 128), t0:t0 + TH],
                  writes=(f"xb{j}",))
        for i in range(nTT):
            for j in range(nD):
                p.op("tensor", lambda h, i=i, j=j: h.matmul(
                    out=P[3][:, 0:E0], lhsT=xb[j][:, ts(i, 128)], rhs=gwT[j][:, 0:E0],
                    start=(j == 0), stop=(j == nD - 1)),
                    reads=(f"xb{j}", f"gwT{j}"), writes=("P3",))
            p.op("scalar", lambda h, i=i: h.copy(
                out=Ltm[:, i * E0:(i + 1) * E0], in_=P[3][:, 0:E0]),
                reads=("P3",), writes=("Ltm",))
        L3 = Ltm[:].rearrange("p (t e) -> p t e", e=E0)
        p.op("vector", lambda h: h.tensor_reduce(
            out=r_m1[:], in_=L3, axis=AX.X, op=ALU.max),
            reads=("Ltm",), writes=("r_m1",))
        m1b = r_m1[:].unsqueeze(2).broadcast_to([128, nTT, E0])
        p.op("vector", lambda h: h.tensor_tensor(
            out=r_eb[:].rearrange("p (t e) -> p t e", e=E0), in0=L3, in1=m1b,
            op=ALU.subtract),
            reads=("Ltm", "r_m1"), writes=("r_eb",))
        p.op("scalar", lambda h: h.activation(
            out=r_eb[:], in_=r_eb[:], func=AF.Exp),
            reads=("r_eb",), writes=("r_eb",))
        eb3 = r_eb[:].rearrange("p (t e) -> p t e", e=E0)
        p.op("vector", lambda h: h.tensor_reduce(
            out=r_sc[:], in_=eb3, axis=AX.X, op=ALU.add),
            reads=("r_eb",), writes=("r_sc",))
        p.op("vector", lambda h: h.tensor_tensor(
            out=comb[:].rearrange("p (t e) -> p t e", e=E0), in0=L3, in1=m1b,
            op=ALU.is_ge),
            reads=("Ltm", "r_m1"), writes=("comb",))
        p.op("vector", lambda h: h.tensor_scalar(
            out=comb[:], in0=comb[:], scalar1=-1e30, scalar2=None, op0=ALU.mult),
            reads=("comb",), writes=("comb",))
        p.op("vector", lambda h: h.tensor_add(
            out=comb[:], in0=comb[:], in1=Ltm[:]),
            reads=("comb", "Ltm"), writes=("comb",))
        p.op("vector", lambda h: h.tensor_reduce(
            out=r_m2[:], in_=comb[:].rearrange("p (t e) -> p t e", e=E0),
            axis=AX.X, op=ALU.max),
            reads=("comb",), writes=("r_m2",))
        m2b = r_m2[:].unsqueeze(2).broadcast_to([128, nTT, E0])
        p.op("vector", lambda h: h.tensor_tensor(
            out=comb[:].rearrange("p (t e) -> p t e", e=E0), in0=L3, in1=m2b,
            op=ALU.is_ge),
            reads=("Ltm", "r_m2"), writes=("comb",))
        p.op("vector", lambda h: h.tensor_mul(
            out=comb[:], in0=comb[:], in1=r_eb[:]),
            reads=("comb", "r_eb"), writes=("comb",))
        p.op("vector", lambda h: h.tensor_reduce(
            out=r_sc2[:], in_=comb[:].rearrange("p (t e) -> p t e", e=E0),
            axis=AX.X, op=ALU.add),
            reads=("comb",), writes=("r_sc2",))
        p.op("vector", lambda h: h.scalar_tensor_tensor(
            out=r_sc2[:], in0=r_sc[:], scalar=1e-8, in1=r_sc2[:],
            op0=ALU.mult, op1=ALU.add),
            reads=("r_sc", "r_sc2"), writes=("r_sc2",))
        p.op("vector", lambda h: h.reciprocal(out=r_sc2[:], in_=r_sc2[:]),
             reads=("r_sc2",), writes=("r_sc2",))
        recb = r_sc2[:].unsqueeze(2).broadcast_to([128, nTT, E0])
        p.op("vector", lambda h: h.tensor_tensor(
            out=comb[:].rearrange("p (t e) -> p t e", e=E0),
            in0=comb[:].rearrange("p (t e) -> p t e", e=E0), in1=recb,
            op=ALU.mult),
            reads=("comb", "r_sc2"), writes=("comb",))
        for i in range(nTT):
            p.op("tensor", lambda h, i=i: h.transpose(
                out=P[3][0:8, 0:128], in_=comb[:, i * E0:(i + 1) * E0],
                identity=ident_f[:]),
                reads=("comb", "ident_f"), writes=("P3",))
            p.op("scalar", lambda h, i=i: h.copy(
                out=combE[:, ts(i, 128)], in_=P[3][0:8, 0:128]),
                reads=("P3",), writes=("combE",))
        for j in range(nD):
            p.op("vector", lambda h, j=j: h.memset(out_acc[j][:], 0.0),
                 writes=(f"oacc{j}",))
        for e in range(E0):
            for cc in range(TH // 512):
                p.op("tensor", lambda h, e=e, cc=cc: h.matmul(
                    out=P[3][:, :], lhsT=sel[e][:], rhs=combE[:, ts(cc, 512)],
                    start=True, stop=True),
                    reads=(f"sel{e}", "combE"), writes=("P3",))
                p.op("scalar", lambda h, cc=cc: h.copy(
                    out=cbc[:, ts(cc, 512)], in_=P[3][:, :]),
                    reads=("P3",), writes=("cbc",))
            for hh in range(3):
                jtiles = list(range(hh * HHT, min(HT, (hh + 1) * HHT)))
                hlo = jtiles[0] * 128
                hhi = min(H0, (jtiles[-1] + 1) * 128)
                wpad = len(jtiles) * 128
                w2gids = [p.new_gid() for _ in jtiles]
                for o in range(nD):
                    sb2 = stg2[o % 2]
                    sk2 = f"stg2_{o % 2}"
                    if hhi - hlo < wpad:
                        p.op("vector", lambda h, w0=hhi - hlo, w1=wpad, sb2=sb2: h.memset(
                            sb2[:, w0:w1], 0.0),
                            reads=(sk2,), writes=(sk2,))
                    p.dma("gpsimd", out=sb2[:, 0:hhi - hlo],
                          in_=W2_hbm[e, ts(o, 128), hlo:hhi],
                          reads=(sk2,), writes=(sk2,))
                    for ji, j in enumerate(jtiles):
                        pt, ptk = (ps_tb, "ps_tb") if ji % 2 == 0 else (ps_tb2, "ps_tb2")
                        p.op("tensor", lambda h, ji=ji, sb2=sb2, pt=pt: h.transpose(
                            out=pt[:, 0:128], in_=sb2[:, ts(ji, 128)],
                            identity=ident_b[:]),
                            reads=(sk2, "ident_b"), writes=(ptk,))
                        p.op("vector", lambda h, ji=ji, o=o, pt=pt: h.tensor_copy(
                            out=w2T[ji][:, ts(o, 128)], in_=pt[:, 0:128]),
                            reads=(ptk,), writes=(f"w2T{ji}",), wgroup=w2gids[ji])
                for ji, j in enumerate(jtiles):
                    jhi = min(H0, j * 128 + 128)
                    bb = ji % 2
                    stream_wT_slice_pe(p, Wg_hbm[e], D, j * 128, jhi,
                                       WgTs[bb], [f"WgTs{bb}_{q}" for q in range(nD)], 0,
                                       stg2[0], "stg2_0", (ps_tb, ps_tb2),
                                       ("ps_tb", "ps_tb2"), ident_b[:])
                    stream_wT_slice_pe(p, W1_hbm[e], D, j * 128, jhi,
                                       W1Ts[bb], [f"W1Ts{bb}_{q}" for q in range(nD)], 0,
                                       stg2[1], "stg2_1", (ps_tb, ps_tb2),
                                       ("ps_tb", "ps_tb2"), ident_b[:])
                    pg, pgk = (P[0], "P0") if ji % 2 == 0 else (P[4], "P4")
                    pu, puk = (P[1], "P1") if ji % 2 == 0 else (P[5], "P5")
                    gsb, gsk = (gs, "gs") if ji % 2 == 0 else (gs2, "gs2")
                    for cc in range(TH // 512):
                        cs = ts(cc, 512)
                        for q in range(nD):
                            p.op("tensor", lambda h, q=q, cs=cs, bb=bb, pg=pg: h.matmul(
                                out=pg[:, :], lhsT=WgTs[bb][q][:],
                                rhs=xb[q][:, cs], start=(q == 0), stop=(q == nD - 1)),
                                reads=(f"WgTs{bb}_{q}", f"xb{q}"), writes=(pgk,))
                        for q in range(nD):
                            p.op("tensor", lambda h, q=q, cs=cs, bb=bb, pu=pu: h.matmul(
                                out=pu[:, :], lhsT=W1Ts[bb][q][:],
                                rhs=xb[q][:, cs], start=(q == 0), stop=(q == nD - 1)),
                                reads=(f"W1Ts{bb}_{q}", f"xb{q}"), writes=(puk,))
                        p.op("scalar", lambda h, pg=pg, gsb=gsb: h.activation(
                            out=gsb[:, :], in_=pg[:, :], func=AF.Silu),
                            reads=(pgk,), writes=(gsk,))
                        p.op("vector", lambda h, ji=ji, cs=cs, pu=pu, gsb=gsb: h.tensor_mul(
                            out=gu[ji][:, cs], in0=gsb[:, :], in1=pu[:, :]),
                            reads=(gsk, puk), writes=(f"gu{ji}",))
                for o in range(nD):
                    for cc in range(TH // 512):
                        cs = ts(cc, 512)
                        py, pyk = (P[2], "P2") if (o * 4 + cc) % 2 == 0 else (P[3], "P3")
                        for ji, j in enumerate(jtiles):
                            p.op("tensor", lambda h, ji=ji, o=o, cs=cs, py=py,
                                 st=(ji == 0), sp=(ji == len(jtiles) - 1): h.matmul(
                                out=py[:, :], lhsT=w2T[ji][:, ts(o, 128)],
                                rhs=gu[ji][:, cs], start=st, stop=sp,
                                skip_group_check=True),
                                reads=(f"w2T{ji}", f"gu{ji}"), writes=(pyk,))
                        p.op("vector", lambda h, o=o, cs=cs, py=py: h.tensor_mul(
                            out=tmp_y[:], in0=py[:, :], in1=cbc[:, cs]),
                            reads=(pyk, "cbc"), writes=("tmp_y",))
                        p.op("vector", lambda h, o=o, cs=cs: h.tensor_add(
                            out=out_acc[o][:, cs], in0=out_acc[o][:, cs], in1=tmp_y[:]),
                            reads=(f"oacc{o}", "tmp_y"), writes=(f"oacc{o}",))
        for cc in range(TH // 512):
            cs = ts(cc, 512)
            for j in range(nD):
                p.op("vector", lambda h, j=j, cs=cs: h.tensor_add(
                    out=rTc[j][:], in0=xb[j][:, cs], in1=out_acc[j][:, cs]),
                    reads=(f"xb{j}", f"oacc{j}"), writes=(f"rTc{j}",))
            emit_ln_chunk(p, rTc, [f"rTc{j}" for j in range(nD)],
                          rTc, [f"rTc{j}" for j in range(nD)], "P4", "P5",
                          ln_bufs, C, 512)
            for j in range(nD):
                p.dma("sync", out=outT_hbm[ts(j, 128), t0 + cc * 512:t0 + (cc + 1) * 512],
                      in_=rTc[j][:], reads=(f"rTc{j}",), writes=(f"outT{j}",))
    p.barrier_all("sync")



def build_l1_ffn(nc, lseq=LSEQ):
    tok = NSEQ * lseq
    xT = nc.dram_tensor("xT", [D, tok], F32, kind="ExternalInput").ap()
    rW = nc.dram_tensor("router_W", [D, E1], F32, kind="ExternalInput").ap()
    rb = nc.dram_tensor("router_b", [E1], F32, kind="ExternalInput").ap()
    sW1 = nc.dram_tensor("sW1", [D, H1], F32, kind="ExternalInput").ap()
    sb1 = nc.dram_tensor("sb1", [H1], F32, kind="ExternalInput").ap()
    sW2 = nc.dram_tensor("sW2", [H1, D], F32, kind="ExternalInput").ap()
    sb2 = nc.dram_tensor("sb2", [D], F32, kind="ExternalInput").ap()
    eW1 = nc.dram_tensor("eW1", [E1, D, H1], F32, kind="ExternalInput").ap()
    eb1 = nc.dram_tensor("eb1", [E1, H1], F32, kind="ExternalInput").ap()
    eW2 = nc.dram_tensor("eW2", [E1, H1, D], F32, kind="ExternalInput").ap()
    eb2 = nc.dram_tensor("eb2", [E1, D], F32, kind="ExternalInput").ap()
    outT = nc.dram_tensor("outT", [D, tok], F32, kind="ExternalOutput").ap()
    es = ExitStack()
    with es:
        al = Alloc(nc, es)
        p = Prog(nc)
        p.make_sems(es)
        emit_l1_ffn(nc, p, al, xT, (rW, rb, sW1, sb1, sW2, sb2, eW1, eb1, eW2, eb2),
                    outT, lseq)
        block = es.enter_context(nc.Block())
        p.finish(block)
    return nc


def emit_l1_ffn(nc, p, al, xT_hbm, W, outT_hbm, lseq):
    rW_h, rb_h, sW1_h, sb1_h, sW2_h, sb2_h, eW1_h, eb1_h, eW2_h, eb2_h = W
    tok = NSEQ * lseq
    nD = D // 128
    nH = H1 // 128
    TH = tok // 2
    nTT = TH // 128
    C = make_consts(p, al)
    P = [al.ps(f"P{i}", [128, 512], F32) for i in range(6)]

    ident_f = al.sb("ident_f", [128, 128], F32)
    p.op("gpsimd", lambda h: h.memset(ident_f[:], 0.0), writes=("ident_f",))
    p.op("gpsimd", lambda h: h.affine_select(
        out=ident_f[:], in_=ident_f[:], compare_op=ALU.not_equal, fill=1.0,
        base=0, pattern=[[-1, 128]], channel_multiplier=1),
        reads=("ident_f",), writes=("ident_f",))
    sel = [al.sb(f"sel{e}", [E1, 128], BF16) for e in range(E1)]
    for e in range(E1):
        p.op("gpsimd", lambda h, e=e: h.memset(sel[e][:], 1.0), writes=(f"sel{e}",))
        p.op("gpsimd", lambda h, e=e: h.affine_select(
            out=sel[e][:], in_=sel[e][:], compare_op=ALU.is_equal, fill=0.0,
            base=-e, pattern=[[0, 128]], channel_multiplier=1),
            reads=(f"sel{e}",), writes=(f"sel{e}",))

    rwT = [al.sb(f"rwT{j}", [128, E1], BF16) for j in range(nD)]
    stage_f = al.sb("wstage_f", [128, max(D, H1)], F32)
    rb_rowf = al.sb("rb_rowf", [1, E1], F32)
    rb_row = al.sb("rb_row", [1, E1], BF16)
    p.dma("sync", out=rb_rowf[:], in_=rb_h.unsqueeze(0), writes=("rb_rowf",))
    p.op("vector", lambda h: h.tensor_copy(out=rb_row[:], in_=rb_rowf[:]),
         reads=("rb_rowf",), writes=("rb_row",))
    for j in range(nD):
        p.dma("sync", out=stage_f[:, 0:E1], in_=rW_h[ts(j, 128), :], writes=("wstage_f",))
        p.op("vector", lambda h, j=j: h.tensor_copy(out=rwT[j][:], in_=stage_f[:, 0:E1]),
             reads=("wstage_f",), writes=(f"rwT{j}",))

    w1d = [[al.sb(f"w1d{b}_{j}", [128, H1], BF16) for j in range(nD)] for b in range(2)]
    w2d = [[al.sb(f"w2d{b}_{j}", [128, D], BF16) for j in range(nH)] for b in range(2)]
    b1c = [al.sb(f"b1c{b}", [128, nH], F32) for b in range(2)]
    b2c = [al.sb(f"b2c{b}", [128, nD], F32) for b in range(2)]
    hb = [al.sb(f"hb{j}", [128, TH], BF16) for j in range(nH)]
    xb = [al.sb(f"xb{j}", [128, TH], BF16) for j in range(nD)]
    out_acc = [al.sb(f"oacc{j}", [128, TH], BF16) for j in range(nD)]
    tmp_y = al.sb("tmp_y", [128, 512], BF16)
    cbc = al.sb("cbc", [128, TH], BF16)
    Ltm = al.sb("Ltm", [128, nTT * E1], F32)
    comb = al.sb("comb", [128, nTT * E1], F32)
    combE = al.sb("combE", [E1, TH], BF16)
    r_m1 = al.sb("r_m1", [128, nTT], F32)
    r_eb = al.sb("r_eb", [128, nTT * E1], F32)
    mskb = al.sb("mskb", [128, nTT * E1], F32)
    r_sc = al.sb("r_sc", [128, nTT], F32)
    rTc = [al.sb(f"rTc{j}", [128, 512], F32) for j in range(nD)]
    ln_bufs = dict(
        ps_s0=P[4][0:1, :], ps_s1=P[5][0:1, :], bc_m=P[4], bc_r=P[5],
        row_m=al.sb("ln_rowm", [1, 512], F32),
        row_r=al.sb("ln_rowr", [1, 512], F32),
        r2=al.sb("ln_r2", [128, 512], F32),
    )

    def load_direct(hbm, dest, dkeys, in_dim, width):
        for j in range(in_dim // 128):
            p.dma("gpsimd" if j % 2 else "gpsimd", out=dest[j][:, 0:width],
                  in_=hbm[ts(j, 128), :], writes=(dkeys[j],))

    def load_bias_cols_2d(hbm_row, name, dim, dest):
        g = p.new_gid()
        for j in range(dim // 128):
            p.dma("sync", out=dest[:, j:j + 1], in_=hbm_row[ts(j, 128)].unsqueeze(1),
                  writes=(name,), wgroup=g)

    for half in range(2):
        t0 = half * TH
        for j in range(nD):
            p.dma("gpsimd", out=xb[j][:], in_=xT_hbm[ts(j, 128), t0:t0 + TH],
                  writes=(f"xb{j}",))
        for i in range(nTT):
            for j in range(nD):
                p.op("tensor", lambda h, i=i, j=j: h.matmul(
                    out=P[3][:, 0:E1], lhsT=xb[j][:, ts(i, 128)], rhs=rwT[j][:],
                    start=(j == 0), stop=False),
                    reads=(f"xb{j}", f"rwT{j}"), writes=("P3",))
            p.op("tensor", lambda h: h.matmul(
                out=P[3][:, 0:E1], lhsT=C["ones_row_b"], rhs=rb_row[:],
                start=False, stop=True),
                reads=("rb_row", "ones_row_b"), writes=("P3",))
            p.op("scalar", lambda h, i=i: h.copy(
                out=Ltm[:, i * E1:(i + 1) * E1], in_=P[3][:, 0:E1]),
                reads=("P3",), writes=("Ltm",))
        L3 = Ltm[:].rearrange("p (t e) -> p t e", e=E1)
        p.op("vector", lambda h: h.tensor_reduce(
            out=r_m1[:], in_=L3, axis=AX.X, op=ALU.max),
            reads=("Ltm",), writes=("r_m1",))
        m1b = r_m1[:].unsqueeze(2).broadcast_to([128, nTT, E1])
        p.op("vector", lambda h: h.tensor_tensor(
            out=r_eb[:].rearrange("p (t e) -> p t e", e=E1), in0=L3, in1=m1b,
            op=ALU.subtract),
            reads=("Ltm", "r_m1"), writes=("r_eb",))
        p.op("scalar", lambda h: h.activation(
            out=r_eb[:], in_=r_eb[:], func=AF.Exp),
            reads=("r_eb",), writes=("r_eb",))
        eb3 = r_eb[:].rearrange("p (t e) -> p t e", e=E1)
        p.op("vector", lambda h: h.tensor_reduce(
            out=r_sc[:], in_=eb3, axis=AX.X, op=ALU.add),
            reads=("r_eb",), writes=("r_sc",))
        p.op("vector", lambda h: h.reciprocal(out=r_sc[:], in_=r_sc[:]),
             reads=("r_sc",), writes=("r_sc",))
        p.op("scalar", lambda h: h.copy(out=comb[:], in_=Ltm[:]),
             reads=("Ltm",), writes=("comb",))
        for rnd in range(2):
            p.op("vector", lambda h: h.tensor_reduce(
                out=r_m1[:], in_=comb[:].rearrange("p (t e) -> p t e", e=E1),
                axis=AX.X, op=ALU.max),
                reads=("comb",), writes=("r_m1",))
            p.op("vector", lambda h: h.tensor_tensor(
                out=mskb[:].rearrange("p (t e) -> p t e", e=E1),
                in0=comb[:].rearrange("p (t e) -> p t e", e=E1), in1=m1b,
                op=ALU.is_ge),
                reads=("comb", "r_m1"), writes=("mskb",))
            p.op("vector", lambda h: h.tensor_scalar(
                out=mskb[:], in0=mskb[:], scalar1=-1e30, scalar2=None, op0=ALU.mult),
                reads=("mskb",), writes=("mskb",))
            p.op("vector", lambda h: h.tensor_add(
                out=comb[:], in0=comb[:], in1=mskb[:]),
                reads=("comb", "mskb"), writes=("comb",))
        p.op("vector", lambda h: h.tensor_reduce(
            out=r_m1[:], in_=comb[:].rearrange("p (t e) -> p t e", e=E1),
            axis=AX.X, op=ALU.max),
            reads=("comb",), writes=("r_m1",))
        p.op("vector", lambda h: h.tensor_tensor(
            out=comb[:].rearrange("p (t e) -> p t e", e=E1), in0=L3, in1=m1b,
            op=ALU.is_ge),
            reads=("Ltm", "r_m1"), writes=("comb",))
        p.op("vector", lambda h: h.tensor_mul(
            out=comb[:], in0=comb[:], in1=r_eb[:]),
            reads=("comb", "r_eb"), writes=("comb",))
        recb = r_sc[:].unsqueeze(2).broadcast_to([128, nTT, E1])
        p.op("vector", lambda h: h.tensor_tensor(
            out=comb[:].rearrange("p (t e) -> p t e", e=E1),
            in0=comb[:].rearrange("p (t e) -> p t e", e=E1), in1=recb,
            op=ALU.mult),
            reads=("comb", "r_sc"), writes=("comb",))
        for i in range(nTT):
            p.op("tensor", lambda h, i=i: h.transpose(
                out=P[3][0:E1, 0:128], in_=comb[:, i * E1:(i + 1) * E1],
                identity=ident_f[:]),
                reads=("comb", "ident_f"), writes=("P3",))
            p.op("scalar", lambda h, i=i: h.copy(
                out=combE[:, ts(i, 128)], in_=P[3][0:E1, 0:128]),
                reads=("P3",), writes=("combE",))
        load_direct(sW1_h, w1d[0], [f"w1d0_{j}" for j in range(nD)], D, H1)
        load_direct(sW2_h, w2d[0], [f"w2d0_{j}" for j in range(nH)], H1, D)
        load_bias_cols_2d(sb1_h, "b1c0", H1, b1c[0])
        load_bias_cols_2d(sb2_h, "b2c0", D, b2c[0])
        for ht in range(nH):
            for cc in range(TH // 512):
                cs = ts(cc, 512)
                for q in range(nD):
                    p.op("tensor", lambda h, q=q, ht=ht, cs=cs: h.matmul(
                        out=P[0][:, :], lhsT=w1d[0][q][:, ts(ht, 128)], rhs=xb[q][:, cs],
                        start=(q == 0), stop=(q == nD - 1)),
                        reads=(f"w1d0_{q}", f"xb{q}"), writes=("P0",))
                p.op("scalar", lambda h, ht=ht, cs=cs: h.activation(
                    out=hb[ht][:, cs], in_=P[0][:, :], func=AF.Relu,
                    bias=b1c[0][:, ht:ht + 1]),
                    reads=("P0", "b1c0"), writes=(f"hb{ht}",))
        for o in range(nD):
            for cc in range(TH // 512):
                cs = ts(cc, 512)
                for ht in range(nH):
                    p.op("tensor", lambda h, ht=ht, o=o, cs=cs: h.matmul(
                        out=P[2][:, :], lhsT=w2d[0][ht][:, ts(o, 128)], rhs=hb[ht][:, cs],
                        start=(ht == 0), stop=(ht == nH - 1)),
                        reads=(f"w2d0_{ht}", f"hb{ht}"), writes=("P2",))
                p.op("vector", lambda h, o=o, cs=cs: h.tensor_scalar(
                    out=out_acc[o][:, cs], in0=P[2][:, :], scalar1=b2c[0][:, o:o + 1],
                    scalar2=None, op0=ALU.add),
                    reads=("P2", "b2c0"), writes=(f"oacc{o}",))
        for e in range(E1):
            for cc in range(TH // 512):
                p.op("tensor", lambda h, e=e, cc=cc: h.matmul(
                    out=P[3][:, :], lhsT=sel[e][:], rhs=combE[:, ts(cc, 512)],
                    start=True, stop=True),
                    reads=(f"sel{e}", "combE"), writes=("P3",))
                p.op("scalar", lambda h, cc=cc: h.copy(
                    out=cbc[:, ts(cc, 512)], in_=P[3][:, :]),
                    reads=("P3",), writes=("cbc",))
            bb = (e + 1) % 2
            load_direct(eW1_h[e], w1d[bb], [f"w1d{bb}_{j}" for j in range(nD)], D, H1)
            load_direct(eW2_h[e], w2d[bb], [f"w2d{bb}_{j}" for j in range(nH)], H1, D)
            load_bias_cols_2d(eb1_h[e], f"b1c{bb}", H1, b1c[bb])
            load_bias_cols_2d(eb2_h[e], f"b2c{bb}", D, b2c[bb])
            for ht in range(nH):
                for cc in range(TH // 512):
                    cs = ts(cc, 512)
                    for q in range(nD):
                        p.op("tensor", lambda h, q=q, ht=ht, cs=cs, bb=bb: h.matmul(
                            out=P[0][:, :], lhsT=w1d[bb][q][:, ts(ht, 128)], rhs=xb[q][:, cs],
                            start=(q == 0), stop=(q == nD - 1)),
                            reads=(f"w1d{bb}_{q}", f"xb{q}"), writes=("P0",))
                    p.op("scalar", lambda h, ht=ht, cs=cs, bb=bb: h.activation(
                        out=hb[ht][:, cs], in_=P[0][:, :], func=AF.Relu,
                        bias=b1c[bb][:, ht:ht + 1]),
                        reads=("P0", f"b1c{bb}"), writes=(f"hb{ht}",))
            for o in range(nD):
                for cc in range(TH // 512):
                    cs = ts(cc, 512)
                    for ht in range(nH):
                        p.op("tensor", lambda h, ht=ht, o=o, cs=cs, bb=bb: h.matmul(
                            out=P[2][:, :], lhsT=w2d[bb][ht][:, ts(o, 128)], rhs=hb[ht][:, cs],
                            start=(ht == 0), stop=(ht == nH - 1)),
                            reads=(f"w2d{bb}_{ht}", f"hb{ht}"), writes=("P2",))
                    p.op("vector", lambda h, o=o, cs=cs, bb=bb: h.scalar_tensor_tensor(
                        out=tmp_y[:], in0=P[2][:, :], scalar=b2c[bb][:, o:o + 1],
                        in1=cbc[:, cs], op0=ALU.add, op1=ALU.mult),
                        reads=("P2", f"b2c{bb}", "cbc"), writes=("tmp_y",))
                    p.op("vector", lambda h, o=o, cs=cs: h.tensor_add(
                        out=out_acc[o][:, cs], in0=out_acc[o][:, cs], in1=tmp_y[:]),
                        reads=(f"oacc{o}", "tmp_y"), writes=(f"oacc{o}",))
        for cc in range(TH // 512):
            cs = ts(cc, 512)
            for j in range(nD):
                p.op("vector", lambda h, j=j, cs=cs: h.tensor_add(
                    out=rTc[j][:], in0=xb[j][:, cs], in1=out_acc[j][:, cs]),
                    reads=(f"xb{j}", f"oacc{j}"), writes=(f"rTc{j}",))
            emit_ln_chunk(p, rTc, [f"rTc{j}" for j in range(nD)],
                          rTc, [f"rTc{j}" for j in range(nD)], "P4", "P5",
                          ln_bufs, C, 512)
            for j in range(nD):
                p.dma("sync", out=outT_hbm[ts(j, 128), t0 + cc * 512:t0 + (cc + 1) * 512],
                      in_=rTc[j][:], reads=(f"rTc{j}",), writes=(f"outT{j}",))
    p.barrier_all("sync")



def build_l1_attn(nc, lseq=LSEQ):
    tok = NSEQ * lseq
    xT = nc.dram_tensor("xT", [D, tok], F32, kind="ExternalInput").ap()
    W_in = nc.dram_tensor("W_in", [D, 2 * DIN], F32, kind="ExternalInput").ap()
    conv_w = nc.dram_tensor("conv_w", [DIN, DCONV], F32, kind="ExternalInput").ap()
    conv_b = nc.dram_tensor("conv_b", [DIN], F32, kind="ExternalInput").ap()
    W_x = nc.dram_tensor("W_x", [DIN, 2 * DST + 1], F32, kind="ExternalInput").ap()
    A_log = nc.dram_tensor("A_log", [DIN, DST], F32, kind="ExternalInput").ap()
    dt_bias = nc.dram_tensor("dt_bias", [DIN], F32, kind="ExternalInput").ap()
    D_skip = nc.dram_tensor("D_skip", [DIN], F32, kind="ExternalInput").ap()
    W_out = nc.dram_tensor("W_out", [DIN, D], F32, kind="ExternalInput").ap()
    outT = nc.dram_tensor("outT", [D, tok], F32, kind="ExternalOutput").ap()
    es = ExitStack()
    with es:
        al = Alloc(nc, es)
        p = Prog(nc)
        p.make_sems(es)
        emit_l1_attn(nc, p, al, xT,
                     (W_in, conv_w, conv_b, W_x, A_log, dt_bias, D_skip, W_out),
                     outT, lseq)
        block = es.enter_context(nc.Block())
        p.finish(block)
    return nc


def emit_l1_attn(nc, p, al, xT_hbm, W, outT_hbm, lseq):
    (Win_h, convw_h, convb_h, Wx_h, Alog_h, dtb_h, Dskip_h, Wout_h) = W
    nD = D // 128
    nI = DIN // 128
    TC = 128
    TP = 4
    NCH = lseq // TC
    C = make_consts(p, al)
    P = [al.ps(f"P{i}", [128, 512], F32) for i in range(2)]
    ps_y = al.ps("ps_y", [128, 2048], F32)
    ps_tf = al.ps("ps_tf", [128, 512], F32)
    ps_tb = al.ps("ps_tb", [128, 1024], BF16)

    ident_f = al.sb("ident_f", [128, 128], F32)
    p.op("gpsimd", lambda h: h.memset(ident_f[:], 0.0), writes=("ident_f",))
    p.op("gpsimd", lambda h: h.affine_select(
        out=ident_f[:], in_=ident_f[:], compare_op=ALU.not_equal, fill=1.0,
        base=0, pattern=[[-1, 128]], channel_multiplier=1),
        reads=("ident_f",), writes=("ident_f",))
    ident_b = al.sb("ident_b", [128, 128], BF16)
    p.op("gpsimd", lambda h: h.memset(ident_b[:], 0.0), writes=("ident_b",))
    p.op("gpsimd", lambda h: h.affine_select(
        out=ident_b[:], in_=ident_b[:], compare_op=ALU.not_equal, fill=1.0,
        base=0, pattern=[[-1, 128]], channel_multiplier=1),
        reads=("ident_b",), writes=("ident_b",))

    stage_f = al.sb("stg_f", [128, 512], F32)

    cwc = al.sb("cwc", [128, nI * DCONV], F32)
    gw_ = p.new_gid()
    for j in range(nI):
        for k in range(DCONV):
            p.dma("sync", out=cwc[:, j * DCONV + k:j * DCONV + k + 1],
                  in_=convw_h[ts(j, 128), k].unsqueeze(1), writes=("cwc",), slow=True,
                  wgroup=gw_)
    cbc_ = al.sb("cbc_", [128, nI], F32)
    dbc = al.sb("dbc", [128, nI], F32)
    skc = al.sb("skc", [128, nI], F32)
    g1_, g2_, g3_ = p.new_gid(), p.new_gid(), p.new_gid()
    for j in range(nI):
        p.dma("sync", out=cbc_[:, j:j + 1], in_=convb_h[ts(j, 128)].unsqueeze(1),
              writes=("cbc_",), wgroup=g1_)
        p.dma("sync", out=dbc[:, j:j + 1], in_=dtb_h[ts(j, 128)].unsqueeze(1),
              writes=("dbc",), wgroup=g2_)
        p.dma("sync", out=skc[:, j:j + 1], in_=Dskip_h[ts(j, 128)].unsqueeze(1),
              writes=("skc",), wgroup=g3_)

    A_bc = al.sb("A_bc", [128, DIN], BF16)
    atile = al.sb("atile", [128, DST], F32)
    for j in range(nI):
        p.dma("sync", out=stage_f[:, 0:DST], in_=Alog_h[ts(j, 128), :],
              writes=("stg_f",))
        p.op("scalar", lambda h: h.activation(
            out=atile[:], in_=stage_f[:, 0:DST], func=AF.Exp),
            reads=("stg_f",), writes=("atile",))
        p.op("vector", lambda h: h.tensor_scalar(
            out=atile[:], in0=atile[:], scalar1=-1.0, scalar2=None, op0=ALU.mult),
            reads=("atile",), writes=("atile",))
        p.op("tensor", lambda h, j=j: h.transpose(
            out=ps_tf[0:DST, 0:128], in_=atile[:], identity=ident_f[:]),
            reads=("atile", "ident_f"), writes=("ps_tf",))
        p.op("vector", lambda h, j=j: h.tensor_copy(
            out=A_bc[0:DST, ts(j, 128)], in_=ps_tf[0:DST, 0:128]),
            reads=("ps_tf",), writes=("A_bc",))
    p.op("vector", lambda h: h.tensor_copy(
        out=A_bc[64:128, :], in_=A_bc[0:64, :]),
        reads=("A_bc",), writes=("A_bc",))

    wxT = [al.sb(f"wxT{j}", [128, 2 * DST + 1], BF16) for j in range(nI)]
    for j in range(nI):
        p.dma("sync", out=stage_f[:, 0:2 * DST + 1], in_=Wx_h[ts(j, 128), :],
              writes=("stg_f",))
        p.op("vector", lambda h, j=j: h.tensor_copy(
            out=wxT[j][:], in_=stage_f[:, 0:2 * DST + 1]),
            reads=("stg_f",), writes=(f"wxT{j}",))

    T2 = 2 * TC
    xb = [al.sb(f"xb{j}", [128, T2], BF16) for j in range(nD)]
    xs = [al.sb(f"xs{j}", [128, 2 * (TC + 3)], BF16) for j in range(nI)]
    zt = [al.sb(f"zt{j}", [128, T2], BF16) for j in range(nI)]
    xc = [al.sb(f"xc{j}", [128, T2], BF16) for j in range(nI)]
    dt16 = [al.sb(f"dt16{j}", [128, T2], BF16) for j in range(nI)]
    dtx16 = [al.sb(f"dtx16{j}", [128, T2], BF16) for j in range(nI)]
    dtT = [al.sb(f"dtT{r}", [128, DIN], BF16) for r in range(2)]
    dtxT = [al.sb(f"dtxT{r}", [128, DIN], BF16) for r in range(2)]
    dt_dram = nc.dram_tensor("dt_dram", [2, TC * DIN], BF16, kind="Internal").ap()
    dtx_dram = nc.dram_tensor("dtx_dram", [2, TC * DIN], BF16, kind="Internal").ap()
    B_stk = al.sb("B_stk", [128, TC], F32)
    C_stk = al.sb("C_stk", [128, TC], F32)
    Csel = al.sb("Csel", [128, 128], F32)
    r_bc = al.sb("r_bc", [128, T2], F32)
    hst = [al.sb(f"hst{b}", [128, DIN], F32) for b in range(2)]
    dt_rep = [al.sb(f"dt_rep{b}", [128, TP * DIN], BF16) for b in range(2)]
    dtx_rep = [al.sb(f"dtx_rep{b}", [128, TP * DIN], BF16) for b in range(2)]
    g_rep = dt_rep
    yT = [al.sb(f"yT{j}", [128, T2], BF16) for j in range(nI)]
    scr1 = al.sb("scr1", [128, T2], F32)
    scr2 = al.sb("scr2", [128, T2], F32)
    wslab = [al.sb(f"wslab{b}", [128, 2048], BF16) for b in range(2)]
    yv = wslab[0]
    zs = al.sb("zs", [128, T2], BF16)
    rTc = [al.sb(f"rTc{j}", [128, T2], F32) for j in range(nD)]
    ln_bufs = dict(
        ps_s0=P[0][0:1, :], ps_s1=P[1][0:1, :], bc_m=P[0], bc_r=P[1],
        row_m=al.sb("ln_rowm", [1, T2], F32),
        row_r=al.sb("ln_rowr", [1, T2], F32),
        r2=al.sb("ln_r2", [128, T2], F32),
    )

    p.op("vector", lambda h: h.memset(Csel[:], 0.0), writes=("Csel",))

    for j in range(nI):
        p.op("vector", lambda h, j=j: h.memset(xs[j][:], 0.0), writes=(f"xs{j}",))

    for ch in range(NCH):
        t0 = ch * TC
        if ch == 0:
            p.op("vector", lambda h: h.memset(hst[0][:], 0.0), writes=("hst0",))
            p.op("vector", lambda h: h.memset(hst[1][:], 0.0), writes=("hst1",))
        for j in range(nD):
            g = p.new_gid()
            for s in range(NSEQ):
                p.dma("gpsimd", out=xb[j][:, s * TC:(s + 1) * TC],
                      in_=xT_hbm[ts(j, 128), s * lseq + t0:s * lseq + t0 + TC],
                      writes=(f"xb{j}",), wgroup=g)
        emit_l1_pre(p, al, nc, Win_h, xb, xs, zt, xc, cwc, cbc_, wxT, dt16,
                    dtx16, dtT, dtxT, B_stk, C_stk, r_bc, dbc, stage_f, wslab,
                    ident_b, ident_f, ps_tf, ps_tb, P, C, scr1, scr2, ch, nI, nD, TC)
        emit_l1_scan(p, al, nc, A_bc, dt_rep, dtx_rep, g_rep, dtT, dtxT, dt_dram,
                     dtx_dram, B_stk, C_stk, Csel, hst, ps_y, ps_tb, yT, yv,
                     ident_b, ch, TC, TP, nI)
        emit_l1_post(p, al, nc, Wout_h, yT, xc, zt, zs, skc, xb, rTc, stage_f,
                     wslab, P, C, ln_bufs, outT_hbm, lseq, t0, nI, nD, TC)
    p.barrier_all("sync")


def emit_l1_pre(p, al, nc, Win_h, xb, xs, zt, xc, cwc, cbc_, wxT, dt16,
                dtx16, dtT, dtxT, B_stk, C_stk, r_bc, dbc, stage_f, wslab,
                ident_b, ident_f, ps_tf, ps_tb, P, C, scr1, scr2, ch, nI, nD, TC):
    T2 = 2 * TC
    if ch > 0:
        for j in range(nI):
            p.op("vector", lambda h, j=j: h.tensor_copy(
                out=xs[j][:].rearrange("p (s c) -> p s c", s=2)[:, :, 0:3],
                in_=xs[j][:].rearrange("p (s c) -> p s c", s=2)[:, :, TC:TC + 3]),
                reads=(f"xs{j}",), writes=(f"xs{j}",))
    for it in range(2 * nI):
        src = Win_h[:, ts(it, 128)].rearrange("(q p) c -> p q c", p=128)
        sl = wslab[it % 2]
        slk = f"wslab{it % 2}"
        p.dma("gpsimd", out=sl[:, 0:nD * 128].rearrange("p (q c) -> p q c", q=nD),
              in_=src, writes=(slk,))
        for q in range(nD):
            p.op("tensor", lambda h, q=q, sl=sl: h.matmul(
                out=P[0][:, 0:T2], lhsT=sl[:, ts(q, 128)], rhs=xb[q][:, 0:T2],
                start=(q == 0), stop=(q == nD - 1)),
                reads=(slk, f"xb{q}"), writes=("P0",))
        if it < nI:
            xsv = xs[it][:].rearrange("p (s c) -> p s c", s=2)[:, :, 3:3 + TC]
            psv = P[0][:, 0:T2].rearrange("p (s c) -> p s c", s=2)
            p.op("scalar", lambda h, it=it, xsv=xsv, psv=psv: h.copy(
                out=xsv, in_=psv),
                reads=("P0",), writes=(f"xs{it}",))
        else:
            p.op("scalar", lambda h, it=it: h.copy(
                out=zt[it - nI][:], in_=P[0][:, 0:T2]),
                reads=("P0",), writes=(f"zt{it - nI}",))
    for j in range(nI):
        x3 = xs[j][:].rearrange("p (s c) -> p s c", s=2)
        p.op("vector", lambda h, j=j, x3=x3: h.tensor_scalar(
            out=scr1[:].rearrange("p (s t) -> p s t", s=2),
            in0=x3[:, :, 0:TC], scalar1=cwc[:, j * DCONV:j * DCONV + 1],
            scalar2=None, op0=ALU.mult),
            reads=(f"xs{j}", "cwc"), writes=("scr1",))
        for k in range(1, DCONV):
            p.op("vector", lambda h, j=j, k=k, x3=x3: h.scalar_tensor_tensor(
                out=scr1[:].rearrange("p (s t) -> p s t", s=2),
                in0=x3[:, :, k:k + TC], scalar=cwc[:, j * DCONV + k:j * DCONV + k + 1],
                in1=scr1[:].rearrange("p (s t) -> p s t", s=2),
                op0=ALU.mult, op1=ALU.add),
                reads=(f"xs{j}", "cwc", "scr1"), writes=("scr1",))
        p.op("scalar", lambda h, j=j: h.activation(
            out=xc[j][:], in_=scr1[:], func=AF.Silu, bias=cbc_[:, j:j + 1]),
            reads=("scr1", "cbc_"), writes=(f"xc{j}",))
    for q in range(nI):
        p.op("tensor", lambda h, q=q: h.matmul(
            out=P[1][:, 0:T2], lhsT=wxT[q][:, 0:2 * DST], rhs=xc[q][:, 0:T2],
            start=(q == 0), stop=(q == nI - 1)),
            reads=(f"wxT{q}", f"xc{q}"), writes=("P1",))
    for q in range(nI):
        p.op("tensor", lambda h, q=q: h.matmul(
            out=ps_tf[0:1, 0:T2], lhsT=wxT[q][:, 2 * DST:2 * DST + 1],
            rhs=xc[q][:, 0:T2],
            start=(q == 0), stop=(q == nI - 1)),
            reads=(f"wxT{q}", f"xc{q}"), writes=("ps_tf",))
    for s in range(2):
        p.op("vector", lambda h, s=s: h.tensor_copy(
            out=B_stk[s * DST:(s + 1) * DST, :], in_=P[1][0:DST, s * TC:(s + 1) * TC]),
            reads=("P1",), writes=("B_stk",))
        p.op("vector", lambda h, s=s: h.tensor_copy(
            out=C_stk[s * DST:(s + 1) * DST, :], in_=P[1][DST:2 * DST, s * TC:(s + 1) * TC]),
            reads=("P1",), writes=("C_stk",))
    rrow = scr2
    p.op("scalar", lambda h: h.copy(out=scr2[0:1, 0:T2], in_=ps_tf[0:1, 0:T2]),
         reads=("ps_tf",), writes=("scr2",))
    p.op("tensor", lambda h: h.matmul(
        out=ps_tf[:, 0:T2], lhsT=C["ones_row"], rhs=scr2[0:1, 0:T2],
        start=True, stop=True),
        reads=("scr2", "ones_row"), writes=("ps_tf",))
    p.op("vector", lambda h: h.tensor_copy(out=r_bc[:, 0:T2], in_=ps_tf[:, 0:T2]),
         reads=("ps_tf",), writes=("r_bc",))
    for j in range(nI):
        p.op("scalar", lambda h, j=j: h.activation(
            out=scr1[:], in_=r_bc[:, 0:T2], func=AF.Abs, bias=dbc[:, j:j + 1]),
            reads=("r_bc", "dbc"), writes=("scr1",))
        p.op("scalar", lambda h: h.activation(
            out=scr1[:], in_=scr1[:], func=AF.Exp, scale=-1.0),
            reads=("scr1",), writes=("scr1",))
        p.op("scalar", lambda h: h.activation(
            out=scr1[:], in_=scr1[:], func=AF.Ln, bias=1.0),
            reads=("scr1", "cpool"), writes=("scr1",))
        p.op("scalar", lambda h, j=j: h.activation(
            out=scr2[:], in_=r_bc[:, 0:T2], func=AF.Relu, bias=dbc[:, j:j + 1]),
            reads=("r_bc", "dbc"), writes=("scr2",))
        p.op("vector", lambda h, j=j: h.tensor_add(
            out=dt16[j][:], in0=scr1[:], in1=scr2[:]),
            reads=("scr1", "scr2"), writes=(f"dt16{j}",))
        p.op("vector", lambda h, j=j: h.tensor_mul(
            out=dtx16[j][:], in0=dt16[j][:], in1=xc[j][:]),
            reads=(f"dt16{j}", f"xc{j}"), writes=(f"dtx16{j}",))
    for j in range(nI):
        for cb in range(2):
            p.op("tensor", lambda h, j=j, cb=cb: h.transpose(
                out=ps_tb[:, 0:128], in_=dt16[j][:, ts(cb, 128)], identity=ident_b[:]),
                reads=(f"dt16{j}", "ident_b"), writes=("ps_tb",))
            p.op("vector", lambda h, j=j, cb=cb: h.tensor_copy(
                out=dtT[cb][:, ts(j, 128)], in_=ps_tb[:, 0:128]),
                reads=("ps_tb",), writes=(f"dtT{cb}",))
            p.op("tensor", lambda h, j=j, cb=cb: h.transpose(
                out=ps_tb[:, 0:128], in_=dtx16[j][:, ts(cb, 128)], identity=ident_b[:]),
                reads=(f"dtx16{j}", "ident_b"), writes=("ps_tb",))
            p.op("vector", lambda h, j=j, cb=cb: h.tensor_copy(
                out=dtxT[cb][:, ts(j, 128)], in_=ps_tb[:, 0:128]),
                reads=("ps_tb",), writes=(f"dtxT{cb}",))


def emit_l1_scan(p, al, nc, A_bc, dt_rep, dtx_rep, g_rep, dtT, dtxT, dt_dram,
                 dtx_dram, B_stk, C_stk, Csel, hst, ps_y, ps_tb, yT, yv,
                 ident_b, ch, TC, TP, nI):
    DINl = DIN
    nsub = TC // TP
    g1, g2 = p.new_gid(), p.new_gid()
    for s in range(2):
        p.dma("sync", out=dt_dram[s:s + 1, :], in_=dtT[s][:],
              reads=(f"dtT{s}",), writes=("dt_dram",), wgroup=g1)
        p.dma("scalar", out=dtx_dram[s:s + 1, :], in_=dtxT[s][:],
              reads=(f"dtxT{s}",), writes=("dtx_dram",), wgroup=g2)
    for sub in range(nsub):
        tb = sub * TP
        bi = sub % 2
        p.dma("sync", out=dt_rep[bi][:],
              in_=dt_dram[:, tb * DINl:(tb + TP) * DINl]
              .unsqueeze(1).broadcast_to([2, 64, TP * DINl]),
              reads=("dt_dram",), writes=(f"dt_rep{bi}",))
        p.dma("scalar", out=dtx_rep[bi][:],
              in_=dtx_dram[:, tb * DINl:(tb + TP) * DINl]
              .unsqueeze(1).broadcast_to([2, 64, TP * DINl]),
              reads=("dtx_dram",), writes=(f"dtx_rep{bi}",))
        p.op("vector", lambda h, bi=bi: h.tensor_tensor(
            out=dt_rep[bi][:].rearrange("p (t d) -> p t d", t=TP),
            in0=A_bc[:].unsqueeze(1).broadcast_to([128, TP, DINl]),
            in1=dt_rep[bi][:].rearrange("p (t d) -> p t d", t=TP),
            op=ALU.mult),
            reads=("A_bc", f"dt_rep{bi}"), writes=(f"dt_rep{bi}",))
        p.op("scalar", lambda h, bi=bi: h.activation(
            out=dt_rep[bi][:], in_=dt_rep[bi][:], func=AF.Exp),
            reads=(f"dt_rep{bi}",), writes=(f"dt_rep{bi}",))
        for tau in range(TP):
            t = tb + tau
            tg = t % 64
            hi, ho = (ch * TC + t) % 2, (ch * TC + t + 1) % 2
            pz = (tg - 1) % 64
            zv = Csel[:].rearrange("p (a c) -> p a c", a=2)[:, :, pz:pz + 1]
            p.op("gpsimd", lambda h, zv=zv: h.memset(zv, 0.0),
                 reads=(), writes=("Csel",))
            p.op("gpsimd", lambda h, t=t, tg=tg: h.tensor_copy(
                out=Csel[0:64, tg:tg + 1], in_=C_stk[0:64, t:t + 1]),
                reads=("C_stk",), writes=("Csel",))
            p.op("gpsimd", lambda h, t=t, tg=tg: h.tensor_copy(
                out=Csel[64:128, 64 + tg:64 + tg + 1], in_=C_stk[64:128, t:t + 1]),
                reads=("C_stk",), writes=("Csel",))
            p.op("vector", lambda h, tau=tau, bi=bi, hi=hi, ho=ho: h.tensor_mul(
                out=hst[ho][:], in0=g_rep[bi][:, tau * DINl:(tau + 1) * DINl],
                in1=hst[hi][:]),
                reads=(f"dt_rep{bi}", f"hst{hi}"), writes=(f"hst{ho}",))
            p.op("vector", lambda h, tau=tau, t=t, bi=bi, ho=ho: h.scalar_tensor_tensor(
                out=hst[ho][:], in0=dtx_rep[bi][:, tau * DINl:(tau + 1) * DINl],
                scalar=B_stk[:, t:t + 1], in1=hst[ho][:],
                op0=ALU.mult, op1=ALU.add),
                reads=(f"dtx_rep{bi}", "B_stk", f"hst{ho}"), writes=(f"hst{ho}",))
            for q4 in range(4):
                p.op("tensor", lambda h, q4=q4, tg=tg, ho=ho: h.matmul(
                    out=ps_y[:, ts(q4, 512)], lhsT=Csel[:],
                    rhs=hst[ho][:, ts(q4, 512)],
                    start=(tg == 0), stop=(tg == 63), skip_group_check=True),
                    reads=("Csel", f"hst{ho}"), writes=("ps_y",))
        if (sub + 1) % (64 // TP) == 0:
            grp = sub // (64 // TP)
            p.op("scalar", lambda h: h.copy(out=yv[:], in_=ps_y[:, :]),
                 reads=("ps_y",), writes=("wslab0",))
            for jb in range(nI):
                p.op("tensor", lambda h, jb=jb: h.transpose(
                    out=ps_tb[:, 0:128], in_=yv[:, ts(jb, 128)], identity=ident_b[:]),
                    reads=("wslab0", "ident_b"), writes=("ps_tb",))
                ov = yT[jb][:].rearrange("p (s t) -> p s t", s=2)[:, :, grp * 64:grp * 64 + 64]
                iv = ps_tb[:, 0:128].rearrange("p (s t) -> p s t", s=2)
                p.op("vector", lambda h, ov=ov, iv=iv: h.tensor_copy(out=ov, in_=iv),
                     reads=("ps_tb",), writes=(f"yT{jb}",))


def emit_l1_post(p, al, nc, Wout_h, yT, xc, zt, zs, skc, xb, rTc, stage_f,
                 wslab, P, C, ln_bufs, outT_hbm, lseq, t0, nI, nD, TC):
    T2 = 2 * TC
    for j in range(nI):
        p.op("vector", lambda h, j=j: h.scalar_tensor_tensor(
            out=yT[j][:], in0=xc[j][:], scalar=skc[:, j:j + 1], in1=yT[j][:],
            op0=ALU.mult, op1=ALU.add),
            reads=(f"xc{j}", "skc", f"yT{j}"), writes=(f"yT{j}",))
        p.op("scalar", lambda h, j=j: h.activation(
            out=zs[:], in_=zt[j][:], func=AF.Silu),
            reads=(f"zt{j}",), writes=("zs",))
        p.op("vector", lambda h, j=j: h.tensor_mul(
            out=yT[j][:], in0=yT[j][:], in1=zs[:]),
            reads=(f"yT{j}", "zs"), writes=(f"yT{j}",))
    for o in range(nD):
        src = Wout_h[:, ts(o, 128)].rearrange("(q p) c -> p q c", p=128)
        sl = wslab[o % 2]
        slk = f"wslab{o % 2}"
        p.dma("gpsimd", out=sl[:, 0:nI * 128].rearrange("p (q c) -> p q c", q=nI),
              in_=src, writes=(slk,))
        for q in range(nI):
            p.op("tensor", lambda h, q=q, o=o, sl=sl: h.matmul(
                out=P[1][:, 0:T2], lhsT=sl[:, ts(q, 128)], rhs=yT[q][:],
                start=(q == 0), stop=(q == nI - 1)),
                reads=(slk, f"yT{q}"), writes=("P1",))
        p.op("vector", lambda h, o=o: h.tensor_add(
            out=rTc[o][:], in0=P[1][:, 0:T2], in1=xb[o][:, 0:T2]),
            reads=("P1", f"xb{o}"), writes=(f"rTc{o}",))
    emit_ln_chunk(p, rTc, [f"rTc{j}" for j in range(nD)],
                  rTc, [f"rTc{j}" for j in range(nD)], "P0", "P1",
                  ln_bufs, C, T2)
    for j in range(nD):
        for s in range(2):
            p.dma("sync", out=outT_hbm[ts(j, 128), s * lseq + t0:s * lseq + t0 + TC],
                  in_=rTc[j][:, s * TC:(s + 1) * TC],
                  reads=(f"rTc{j}",), writes=(f"outT{j}",))



_NC_CACHE = {}


def _get_nc(name, builder):
    if name not in _NC_CACHE:
        nc = bass.Bass("TRN2", target_bir_lowering=False, debug=False)
        builder(nc)
        _NC_CACHE[name] = nc
    return _NC_CACHE[name]


def kernel(**inp):
    n_cores = 8
    x = np.asarray(inp["L0_attn_x"])
    B_all = x.shape[0]
    per = B_all // n_cores

    xTs = [np.ascontiguousarray(
        x[c * per:(c + 1) * per].transpose(2, 0, 1).reshape(D, per * LSEQ))
        for c in range(n_cores)]

    ncA = _get_nc("A", build_l0_attn)
    w = {k: np.ascontiguousarray(np.asarray(inp[v]))
         for k, v in [("wq", "L0_attn_wq_w"), ("wk", "L0_attn_wk_w"),
                      ("wv", "L0_attn_wv_w"), ("wo", "L0_attn_wo_w"),
                      ("bq", "L0_attn_wq_b"), ("bk", "L0_attn_wk_b"),
                      ("bv", "L0_attn_wv_b"), ("bo", "L0_attn_wo_b")]}
    maps = [dict(xT=xTs[c], **w) for c in range(n_cores)]
    res = run_bass_kernel_spmd(ncA, maps, core_ids=list(range(n_cores)))
    x1 = [res.results[c]["outT"] for c in range(n_cores)]

    ncB = _get_nc("B", build_l0_ffn)
    w = {"gate_w": np.asarray(inp["L0_ffn_gate_w"]),
         "Wg": np.asarray(inp["L0_ffn_Wg"]),
         "W1": np.asarray(inp["L0_ffn_W1"]),
         "W2": np.asarray(inp["L0_ffn_W2"])}
    maps = [dict(xT=x1[c], **w) for c in range(n_cores)]
    res = run_bass_kernel_spmd(ncB, maps, core_ids=list(range(n_cores)))
    x2 = [res.results[c]["outT"] for c in range(n_cores)]

    ncC = _get_nc("C", build_l1_attn)
    w = {"W_in": np.asarray(inp["L1_attn_W_in"]),
         "conv_w": np.asarray(inp["L1_attn_conv_w"]),
         "conv_b": np.asarray(inp["L1_attn_conv_b"]),
         "W_x": np.asarray(inp["L1_attn_W_x"]),
         "A_log": np.asarray(inp["L1_attn_A_log"]),
         "dt_bias": np.asarray(inp["L1_attn_dt_bias"]),
         "D_skip": np.asarray(inp["L1_attn_D"]),
         "W_out": np.asarray(inp["L1_attn_W_out"])}
    maps = [dict(xT=x2[c], **w) for c in range(n_cores)]
    res = run_bass_kernel_spmd(ncC, maps, core_ids=list(range(n_cores)))
    x3 = [res.results[c]["outT"] for c in range(n_cores)]

    ncD = _get_nc("D", build_l1_ffn)
    w = {"router_W": np.asarray(inp["L1_ffn_router_W"]),
         "router_b": np.asarray(inp["L1_ffn_router_b"]),
         "sW1": np.asarray(inp["L1_ffn_sW1"]),
         "sb1": np.asarray(inp["L1_ffn_sb1"]),
         "sW2": np.asarray(inp["L1_ffn_sW2"]),
         "sb2": np.asarray(inp["L1_ffn_sb2"]),
         "eW1": np.asarray(inp["L1_ffn_eW1"]),
         "eb1": np.asarray(inp["L1_ffn_eb1"]),
         "eW2": np.asarray(inp["L1_ffn_eW2"]),
         "eb2": np.asarray(inp["L1_ffn_eb2"])}
    maps = [dict(xT=x3[c], **w) for c in range(n_cores)]
    res = run_bass_kernel_spmd(ncD, maps, core_ids=list(range(n_cores)))
    x4 = [res.results[c]["outT"] for c in range(n_cores)]

    out = np.concatenate(
        [x4[c].reshape(D, per, LSEQ).transpose(1, 2, 0) for c in range(n_cores)],
        axis=0)
    return out.astype(np.float32)


# version 11 (final)
# speedup vs baseline: 1.8713x; 1.0007x over previous
import numpy as np
from contextlib import ExitStack

import concourse.bass as bass
import concourse.mybir as mybir
from concourse.bass_utils import run_bass_kernel_spmd

F32 = mybir.dt.float32
BF16 = mybir.dt.bfloat16
AF = mybir.ActivationFunctionType
ALU = mybir.AluOpType
AX = mybir.AxisListType

D = 1024
NSEQ = 2
LSEQ = 2048
TOK = NSEQ * LSEQ
NH = 16
DH = 64
E0, H0 = 8, 2736
E1, H1 = 15, 1024
DIN = 2048
DST = 64
DCONV = 4

ENGS = ("sync", "vector", "scalar", "tensor", "gpsimd")


class Prog:

    def __init__(self, nc):
        self.nc = nc
        self.ops = {e: [] for e in ENGS}
        self.icount = {e: 0 for e in ENGS}
        self.dcount = {e: 0 for e in ENGS}
        self.dslot = {}
        self.writers = {}
        self.readers = {}
        self._gid = 0
        self.hiw = {e: {} for e in ENGS}
        self.sems = {}
        self._es = None

    NDMA_SEMS = 8

    def make_sems(self, es):
        for e in ENGS:
            self.sems[("c", e)] = es.enter_context(self.nc.semaphore(f"sc_{e}"))
        for e in ("sync", "scalar", "gpsimd"):
            for s in range(self.NDMA_SEMS):
                self.sems[("d", e, s)] = es.enter_context(self.nc.semaphore(f"sd_{e}{s}"))

    def new_gid(self):
        self._gid += 1
        return self._gid

    def _dep_waits(self, eng, reads, writes, wgroup=None):
        need = {}
        def add(semkey, cnt):
            if need.get(semkey, 0) < cnt:
                need[semkey] = cnt
        for k in reads:
            w = self.writers.get(k)
            if w:
                for semkey, cnt in w["members"].items():
                    add(semkey, cnt)
        for k in writes:
            w = self.writers.get(k)
            if w and (wgroup is None or w["gid"] != wgroup):
                for semkey, cnt in w["members"].items():
                    add(semkey, cnt)
            if not (w and wgroup is not None and w["gid"] == wgroup):
                for semkey, cnt in self.readers.get(k, {}).items():
                    add(semkey, cnt)
        waits = []
        hw = self.hiw[eng]
        for semkey, cnt in sorted(need.items(), key=str):
            if semkey == ("c", eng) and cnt > self.icount[eng]:
                continue
            if semkey[0] == "d":
                tgt = 16 * self.dslot[(semkey[1], semkey[2])]
            else:
                tgt = cnt
            if hw.get(semkey, 0) < tgt:
                hw[semkey] = tgt
                waits.append((self.sems[semkey], tgt))
        return waits

    def _record_access(self, semkey, cnt, reads, writes, wgroup=None):
        for k in reads:
            self.readers.setdefault(k, {})[semkey] = cnt
        for k in writes:
            w = self.writers.get(k)
            if w and wgroup is not None and w["gid"] == wgroup:
                m = w["members"]
                if m.get(semkey, 0) < cnt:
                    m[semkey] = cnt
            else:
                self.writers[k] = {"gid": wgroup if wgroup is not None else self.new_gid(),
                                   "members": {semkey: cnt}}
                self.readers[k] = {}

    def op(self, eng, fn, reads=(), writes=(), inc=True, tag=None, wgroup=None):
        waits = self._dep_waits(eng, reads, writes, wgroup)
        cnt = self.icount[eng] + 1
        if inc:
            self.icount[eng] = cnt
        sem = self.sems[("c", eng)]
        def emit(h, waits=waits, fn=fn, inc=inc, sem=sem, tag=tag):
            for s, v in waits:
                h.wait_ge(s, v)
            ins = fn(h)
            if inc:
                ins.then_inc(sem, 1)
            if tag:
                ins.annotate(tag)
        self.ops[eng].append(emit)
        self._record_access(("c", eng), cnt, reads, writes, wgroup)

    def dma(self, eng, out, in_, reads=(), writes=(), tag=None, transpose=False,
            slow=False, wgroup=None):
        waits = self._dep_waits(eng, reads, writes, wgroup)
        slot = self.dcount[eng] % self.NDMA_SEMS
        self.dcount[eng] += 1
        key = ("d", eng, slot)
        prev = self.dslot.setdefault((eng, slot), 0)
        cnt = prev + 1
        self.dslot[(eng, slot)] = cnt
        sem = self.sems[key]
        if prev:
            hw = self.hiw[eng]
            if hw.get(key, 0) < 16 * prev:
                hw[key] = 16 * prev
                waits.append((sem, 16 * prev))
        def emit(h, waits=waits, out=out, in_=in_, sem=sem, tag=tag,
                 transpose=transpose, slow=slow):
            for s, v in waits:
                h.wait_ge(s, v)
            kw = {}
            if transpose:
                kw["transpose"] = True
            if slow:
                kw["allow_slow_non_contiguous"] = True
            ins = h.dma_start(out=out, in_=in_, **kw)
            ins.then_inc(sem, 16)
            if tag:
                ins.annotate(tag)
        self.ops[eng].append(emit)
        self._record_access(key, cnt, reads, writes, wgroup)

    def barrier_all(self, eng):
        waits = []
        hw = self.hiw[eng]
        for e in ENGS:
            if self.icount[e]:
                key = ("c", e)
                tgt = self.icount[e]
                if hw.get(key, 0) < tgt:
                    hw[key] = tgt
                    waits.append((self.sems[key], tgt))
        for (e, s), cnt in self.dslot.items():
            key = ("d", e, s)
            tgt = cnt * 16
            if hw.get(key, 0) < tgt:
                hw[key] = tgt
                waits.append((self.sems[key], tgt))
        def emit(h, waits=waits):
            for s, v in waits:
                h.wait_ge(s, v)
        self.ops[eng].append(emit)

    def full_barrier(self):
        for e in ENGS:
            self.barrier_all(e)
        self.writers.clear()
        self.readers.clear()

    def finish(self, block):
        handles = {}
        def run(eng):
            def section(h):
                for fn in self.ops[eng]:
                    fn(h)
            return section
        block.sync(run("sync"))
        block.vector(run("vector"))
        block.scalar(run("scalar"))
        block.tensor(run("tensor"))
        block.gpsimd(run("gpsimd"))



def ts(i, n):
    return slice(i * n, (i + 1) * n)


def ceil_div(a, b):
    return -(-a // b)


class Alloc:

    def __init__(self, nc, es):
        self.nc = nc
        self.es = es
        self.n = 0

    def sb(self, name, shape, dtype):
        return self.es.enter_context(self.nc.sbuf_tensor(
            name, shape, dtype, allow_name_mangling=True))

    def ps(self, name, shape, dtype=F32):
        return self.es.enter_context(self.nc.psum_tensor(
            name, shape, dtype, allow_name_mangling=True))


def load_weight_T(p, al, hbm_ap, name, out_dim, in_dim, stage):
    nI, nO = in_dim // 128, out_dim // 128
    wT = [al.sb(f"{name}_T{i}", [128, out_dim], BF16) for i in range(nI)]
    stage_f, stage_b = stage
    gids = [p.new_gid() for _ in range(nI)]
    for o in range(nO):
        kf = "wstage_f"
        kb = "wstage_b"
        p.dma("sync", out=stage_f[:], in_=hbm_ap[ts(o, 128), :],
              reads=(), writes=(kf,))
        p.op("vector", lambda h, sf=stage_f, sbf=stage_b: h.tensor_copy(out=sbf[:], in_=sf[:]),
             reads=(kf,), writes=(kb,))
        for i in range(nI):
            p.dma("scalar", out=wT[i][:, ts(o, 128)], in_=stage_b[:, ts(i, 128)],
                  reads=(kb,), writes=(f"{name}_T{i}",), transpose=True, wgroup=gids[i])
    return wT


def load_weight_direct(p, al, hbm_ap, name, in_dim, out_dim):
    nI = in_dim // 128
    wT = []
    for i in range(nI):
        stage = al.sb(f"{name}_st{i}", [128, out_dim], F32)
        t = al.sb(f"{name}_d{i}", [128, out_dim], BF16)
        p.dma("sync", out=stage[:], in_=hbm_ap[ts(i, 128), :],
              reads=(), writes=(f"{name}_st{i}",))
        p.op("vector", lambda h, s=stage, t=t: h.tensor_copy(out=t[:], in_=s[:]),
             reads=(f"{name}_st{i}",), writes=(f"{name}_d{i}",))
        wT.append(t)
    return wT


def load_bias(p, al, hbm_ap, name, dim):
    nT = dim // 128
    col = al.sb(f"{name}_col", [128, nT], F32)
    p.dma("sync", out=col[:], in_=hbm_ap.rearrange("(t p) -> p t", p=128),
          reads=(), writes=(name,))
    return col


def load_bias_cols(p, al, hbm_ap, name, dim):
    nT = dim // 128
    col = al.sb(f"{name}_col", [128, nT], F32)
    g = p.new_gid()
    for j in range(nT):
        p.dma("sync", out=col[:, j:j + 1], in_=hbm_ap[ts(j, 128)].unsqueeze(1),
              reads=(), writes=(name,), wgroup=g)
    return col


def make_ones_and_masks(p, al, lseq, qc):
    ones_f = al.sb("ones_f", [128, 1], F32)
    ones_b = al.sb("ones_b", [128, 1], BF16)
    p.op("gpsimd", lambda h: h.memset(ones_f[:], 1.0), writes=("ones_f",))
    p.op("gpsimd", lambda h: h.memset(ones_b[:], 1.0), writes=("ones_b",))
    masks = {}
    for bi in range(qc // 128):
        delta = bi * 128
        m = al.sb(f"mask{bi}", [128, qc], BF16)
        p.op("gpsimd", lambda h, m=m: h.memset(m[:], 1.0), writes=(f"mask{bi}",))
        p.op("gpsimd", lambda h, m=m, delta=delta: h.affine_select(
            out=m[:], in_=m[:], compare_op=ALU.is_ge, fill=0.0,
            base=delta, pattern=[[1, qc]], channel_multiplier=-1),
            reads=(f"mask{bi}",), writes=(f"mask{bi}",))
        masks[delta] = m
    return ones_f, ones_b, masks


def emit_layernorm(p, al, r_tiles, rkeys, out_tiles, okeys, tok, tag, C,
                   cast_to=None, ckeys=None):
    nD = len(r_tiles)
    CH = 512
    ps_s0 = al.ps(f"{tag}_pss0", [1, 512], F32)
    ps_s1 = al.ps(f"{tag}_pss1", [1, 512], F32)
    row_m = al.sb(f"{tag}_rowm", [1, CH], F32)
    row_r = al.sb(f"{tag}_rowr", [1, CH], F32)
    r2 = al.sb(f"{tag}_r2", [128, CH], F32)
    bc_m = al.ps(f"{tag}_bcm", [128, 512], F32)
    bc_r = al.ps(f"{tag}_bcr", [128, 512], F32)
    inv = 1.0 / (nD * 128)
    for c in range(tok // CH):
        cs = ts(c, CH)
        for j in range(nD):
            p.op("tensor", lambda h, j=j, cs=cs: h.matmul(
                out=ps_s0[:, :], lhsT=C["ones_col"], rhs=r_tiles[j][:, cs],
                start=(j == 0), stop=(j == nD - 1)),
                reads=(rkeys[j], "ones_col"), writes=(f"{tag}_pss0",))
        for j in range(nD):
            p.op("scalar", lambda h, j=j, cs=cs: h.activation(
                out=r2[:], in_=r_tiles[j][:, cs], func=AF.Square),
                reads=(rkeys[j],), writes=(f"{tag}_r2",))
            p.op("tensor", lambda h, j=j: h.matmul(
                out=ps_s1[:, :], lhsT=C["ones_col"], rhs=r2[:],
                start=(j == 0), stop=(j == nD - 1)),
                reads=(f"{tag}_r2", "ones_col"), writes=(f"{tag}_pss1",))
        p.op("scalar", lambda h: h.activation(
            out=row_m[:], in_=ps_s0[:, :], func=AF.Copy, scale=inv),
            reads=(f"{tag}_pss0",), writes=(f"{tag}_rowm",))
        p.op("scalar", lambda h: h.activation(
            out=row_r[:], in_=row_m[:], func=AF.Square),
            reads=(f"{tag}_rowm",), writes=(f"{tag}_rowr",))
        p.op("vector", lambda h: h.scalar_tensor_tensor(
            out=row_r[:], in0=ps_s1[:, :], scalar=inv, in1=row_r[:],
            op0=ALU.mult, op1=ALU.subtract),
            reads=(f"{tag}_pss1", f"{tag}_rowr"), writes=(f"{tag}_rowr",))
        p.op("scalar", lambda h: h.activation(
            out=row_r[:], in_=row_r[:], func=AF.Ln, bias=1e-5),
            reads=(f"{tag}_rowr",), writes=(f"{tag}_rowr",))
        p.op("scalar", lambda h: h.activation(
            out=row_r[:], in_=row_r[:], func=AF.Exp, scale=-0.5),
            reads=(f"{tag}_rowr",), writes=(f"{tag}_rowr",))
        p.op("tensor", lambda h: h.matmul(
            out=bc_m[:, :CH], lhsT=C["ones_row"], rhs=row_m[:],
            start=True, stop=True),
            reads=(f"{tag}_rowm", "ones_row"), writes=(f"{tag}_bcm",))
        p.op("tensor", lambda h: h.matmul(
            out=bc_r[:, :CH], lhsT=C["ones_row"], rhs=row_r[:],
            start=True, stop=True),
            reads=(f"{tag}_rowr", "ones_row"), writes=(f"{tag}_bcr",))
        for j in range(nD):
            p.op("vector", lambda h, j=j, cs=cs: h.tensor_sub(
                out=out_tiles[j][:, cs], in0=r_tiles[j][:, cs], in1=bc_m[:, :CH]),
                reads=(rkeys[j], f"{tag}_bcm"), writes=(okeys[j],))
            p.op("vector", lambda h, j=j, cs=cs: h.tensor_mul(
                out=out_tiles[j][:, cs], in0=out_tiles[j][:, cs], in1=bc_r[:, :CH]),
                reads=(okeys[j], f"{tag}_bcr"), writes=(okeys[j],))
            if cast_to is not None:
                p.op("vector", lambda h, j=j, cs=cs: h.tensor_copy(
                    out=cast_to[j][:, cs], in_=out_tiles[j][:, cs]),
                    reads=(okeys[j],), writes=(ckeys[j],))


CONST_VALS = [1e-5, 1.0]


def make_consts(p, al):
    C = {}
    cpool = al.sb("cpool", [128, len(CONST_VALS)], F32)
    for i, v in enumerate(CONST_VALS):
        p.op("gpsimd", lambda h, i=i, v=v: h.memset(cpool[:, i:i + 1], v),
             writes=("cpool",))
        p.nc.const_aps.register(F32, v, cpool[:, i:i + 1])
    ones_col = al.sb("ones_col", [128, 1], F32)
    ones_row = al.sb("ones_row", [1, 128], F32)
    ones_row_b = al.sb("ones_row_b", [1, 128], BF16)
    p.op("gpsimd", lambda h: h.memset(ones_col[:], 1.0), writes=("ones_col",))
    p.op("gpsimd", lambda h: h.memset(ones_row[:], 1.0), writes=("ones_row",))
    p.op("gpsimd", lambda h: h.memset(ones_row_b[:], 1.0), writes=("ones_row_b",))
    C["ones_col"] = ones_col[:]
    C["ones_row"] = ones_row[:]
    C["ones_row_b"] = ones_row_b[:]
    return C


def load_xT(p, al, x_hbm, name, tok, dim=D, dtype=F32, cast_bf=True):
    nD = dim // 128
    xf = [al.sb(f"{name}_f{j}", [128, tok], F32) for j in range(nD)]
    xb = [al.sb(f"{name}_b{j}", [128, tok], BF16) for j in range(nD)] if cast_bf else None
    for j in range(nD):
        p.dma("sync", out=xf[j][:], in_=x_hbm[ts(j, 128), :], writes=(f"{name}_f{j}",))
        if cast_bf:
            p.op("vector", lambda h, xb=xb, j=j: h.tensor_copy(out=xb[j][:], in_=xf[j][:]),
                 reads=(f"{name}_f{j}",), writes=(f"{name}_b{j}",))
    return xf, xb



def build_l0_attn(nc, lseq=LSEQ):
    tok = NSEQ * lseq
    xT = nc.dram_tensor("xT", [D, tok], F32, kind="ExternalInput").ap()
    ws = [nc.dram_tensor(n, [D, D], F32, kind="ExternalInput").ap()
          for n in ("wq", "wk", "wv", "wo")]
    bs = [nc.dram_tensor(n, [D], F32, kind="ExternalInput").ap()
          for n in ("bq", "bk", "bv", "bo")]
    outT = nc.dram_tensor("outT", [D, tok], F32, kind="ExternalOutput").ap()
    es = ExitStack()
    with es:
        al = Alloc(nc, es)
        p = Prog(nc)
        p.make_sems(es)
        emit_l0_attn(nc, p, al, xT, ws, bs, outT, lseq)
        block = es.enter_context(nc.Block())
        p.finish(block)
    return nc


def emit_l0_attn(nc, p, al, xT_hbm, w_hbm, b_hbm, outT_hbm, lseq, C=None,
                 stream=None):
    nD = D // 128
    QC = 512
    if C is None:
        C = make_consts(p, al)

    P = [al.ps(f"P{i}", [128, 512], F32) for i in range(6)]

    masks = {}
    for bi in range(QC // 128):
        delta = -bi * 128
        m = al.sb(f"mask{bi}", [128, QC], BF16)
        p.op("gpsimd", lambda h, m=m: h.memset(m[:], 1.0), writes=(f"mask{bi}",))
        p.op("gpsimd", lambda h, m=m, delta=delta: h.affine_select(
            out=m[:], in_=m[:], compare_op=ALU.is_ge, fill=0.0,
            base=delta, pattern=[[1, QC]], channel_multiplier=-1),
            reads=(f"mask{bi}",), writes=(f"mask{bi}",))
        masks[delta] = m

    stage_f = al.sb("wstage_f", [128, D], F32)
    stage_b = al.sb("wstage_b", [128, D], BF16)
    wqT = load_weight_T(p, al, w_hbm[0], "wq", D, D, (stage_f, stage_b))
    wkT = load_weight_T(p, al, w_hbm[1], "wk", D, D, (stage_f, stage_b))
    wvT = load_weight_T(p, al, w_hbm[2], "wv", D, D, (stage_f, stage_b))
    woT = load_weight_T(p, al, w_hbm[3], "wo", D, D, (stage_f, stage_b))
    bq = load_bias_cols(p, al, b_hbm[0], "bq", D)
    bk = load_bias_cols(p, al, b_hbm[1], "bk", D)
    bo = load_bias_cols(p, al, b_hbm[3], "bo", D)
    bv_row_f = al.sb("bv_rowf", [1, D], F32)
    bv_row = al.sb("bv_row", [1, D], BF16)
    p.dma("sync", out=bv_row_f[:], in_=b_hbm[2].unsqueeze(0), writes=("bv_rowf",))
    p.op("vector", lambda h: h.tensor_copy(out=bv_row[:], in_=bv_row_f[:]),
         reads=("bv_rowf",), writes=("bv_row",))

    xbc = [al.sb(f"xbc{j}", [128, 512], BF16) for j in range(nD)]
    KT = [al.sb(f"KT{j}", [128, lseq], BF16) for j in range(nD)]
    Vt = [al.sb(f"Vt{i}", [128, D], BF16) for i in range(lseq // 128)]
    QTc = [al.sb(f"QTc{j}", [128, QC], BF16) for j in range(nD)]
    ATc = [al.sb(f"ATc{j}", [128, QC], BF16) for j in range(nD)]
    rTc = [al.sb(f"rTc{j}", [128, QC], F32) for j in range(nD)]
    Sb = [al.sb(f"Sbb{i}", [128, QC], BF16) for i in range(2)]

    ln_bufs = dict(
        ps_s0=P[4][0:1, :], ps_s1=P[5][0:1, :],
        bc_m=P[4], bc_r=P[5],
        row_m=al.sb("ln_rowm", [1, QC], F32),
        row_r=al.sb("ln_rowr", [1, QC], F32),
        r2=al.sb("ln_r2", [128, QC], F32),
    )

    for seq in range(NSEQ):
        s_lo = seq * lseq
        for c in range(lseq // 512):
            cs = ts(c, 512)
            for j in range(nD):
                p.dma("gpsimd", out=xbc[j][:],
                      in_=xT_hbm[ts(j, 128), s_lo + c * 512:s_lo + (c + 1) * 512],
                      writes=(f"xbc{j}",))
            for e in range(nD):
                for j in range(nD):
                    p.op("tensor", lambda h, e=e, j=j: h.matmul(
                        out=P[0][:, :], lhsT=wkT[j][:, ts(e, 128)], rhs=xbc[j][:],
                        start=(j == 0), stop=(j == nD - 1)),
                        reads=(f"wk_T{j}", f"xbc{j}"), writes=("P0",))
                p.op("vector", lambda h, e=e, cs=cs: h.tensor_scalar(
                    out=KT[e][:, cs], in0=P[0][:, :], scalar1=bk[:, e:e + 1],
                    scalar2=None, op0=ALU.add),
                    reads=("P0", "bk"), writes=(f"KT{e}",))
            for i4 in range(4):
                i = c * 4 + i4
                for hhalf in range(D // 512):
                    hs = ts(hhalf, 512)
                    for j in range(nD):
                        p.op("tensor", lambda h, i4=i4, j=j, hs=hs: h.matmul(
                            out=P[1][:, :], lhsT=xbc[j][:, ts(i4, 128)], rhs=wvT[j][:, hs],
                            start=(j == 0), stop=False),
                            reads=(f"wv_T{j}", f"xbc{j}"), writes=("P1",))
                    p.op("tensor", lambda h, hs=hs: h.matmul(
                        out=P[1][:, :], lhsT=C["ones_row_b"], rhs=bv_row[:, hs],
                        start=False, stop=True),
                        reads=("bv_row", "ones_row_b"), writes=("P1",))
                    p.op("scalar", lambda h, i=i, hs=hs: h.copy(
                        out=Vt[i][:, hs], in_=P[1][:, :]),
                        reads=("P1",), writes=(f"Vt{i}",))
        for qc in range(lseq // QC):
            q0 = qc * QC
            for j in range(nD):
                p.dma("gpsimd", out=xbc[j][:],
                      in_=xT_hbm[ts(j, 128), s_lo + q0:s_lo + q0 + QC],
                      writes=(f"xbc{j}",))
            for e in range(nD):
                for j in range(nD):
                    p.op("tensor", lambda h, e=e, j=j: h.matmul(
                        out=P[0][:, :], lhsT=wqT[j][:, ts(e, 128)],
                        rhs=xbc[j][:],
                        start=(j == 0), stop=(j == nD - 1)),
                        reads=(f"wq_T{j}", f"xbc{j}"), writes=("P0",))
                p.op("vector", lambda h, e=e: h.tensor_scalar(
                    out=QTc[e][:], in0=P[0][:, :], scalar1=bq[:, e:e + 1],
                    scalar2=None, op0=ALU.add),
                    reads=("P0", "bq"), writes=(f"QTc{e}",))
            for head in range(NH):
                et, ro = head // 2, (head % 2) * 64
                n_sblk = (q0 + QC) // 128
                for sb_ in range(n_sblk):
                    s0 = sb_ * 128
                    pp, pk = P[2 + (sb_ % 2)], f"P{2 + (sb_ % 2)}"
                    sbuf, sk = Sb[sb_ % 2], f"Sbb{sb_ % 2}"
                    p.op("tensor", lambda h, et=et, ro=ro, s0=s0, q0=q0, pp=pp: h.matmul(
                        out=pp[:, :QC], lhsT=KT[et][ro:ro + 64, s0:s0 + 128],
                        rhs=QTc[et][ro:ro + 64, :],
                        start=True, stop=True),
                        reads=(f"KT{et}", f"QTc{et}"), writes=(pk,))
                    delta = q0 - s0
                    if delta <= 0:
                        p.op("vector", lambda h, pp=pp, sbuf=sbuf, delta=delta: h.tensor_mul(
                            out=sbuf[:], in0=pp[:, :QC], in1=masks[delta][:]),
                            reads=(pk, f"mask{-delta // 128}"), writes=(sk,))
                    else:
                        p.op("scalar", lambda h, pp=pp, sbuf=sbuf: h.copy(
                            out=sbuf[:], in_=pp[:, :QC]),
                            reads=(pk,), writes=(sk,))
                    p.op("tensor", lambda h, s0=s0, ro=ro, et=et, sbuf=sbuf,
                         st=(sb_ == 0), sp=(sb_ == n_sblk - 1): h.matmul(
                        out=P[1][0:64, :QC], lhsT=Vt[s0 // 128][:, et * 128 + ro: et * 128 + ro + 64],
                        rhs=sbuf[:], start=st, stop=sp, skip_group_check=True),
                        reads=(sk, f"Vt{s0 // 128}"), writes=("P1",))
                p.op("scalar", lambda h, et=et, ro=ro: h.copy(
                    out=ATc[et][ro:ro + 64, :], in_=P[1][0:64, :QC]),
                    reads=("P1",), writes=(f"ATc{et}",))
            for e in range(nD):
                for j in range(nD):
                    p.op("tensor", lambda h, e=e, j=j: h.matmul(
                        out=P[0][:, :], lhsT=woT[j][:, ts(e, 128)], rhs=ATc[j][:],
                        start=(j == 0), stop=(j == nD - 1)),
                        reads=(f"wo_T{j}", f"ATc{j}"), writes=("P0",))
                p.op("vector", lambda h, e=e: h.scalar_tensor_tensor(
                    out=rTc[e][:], in0=P[0][:, :], scalar=bo[:, e:e + 1],
                    in1=xbc[e][:], op0=ALU.add, op1=ALU.add),
                    reads=("P0", "bo", f"xbc{e}"), writes=(f"rTc{e}",))
            emit_ln_chunk(p, rTc, [f"rTc{j}" for j in range(nD)],
                          rTc, [f"rTc{j}" for j in range(nD)], "P4", "P5",
                          ln_bufs, C, QC)
            if stream is None:
                for j in range(nD):
                    p.dma("sync", out=outT_hbm[ts(j, 128), s_lo + q0:s_lo + q0 + QC],
                          in_=rTc[j][:], reads=(f"rTc{j}",), writes=(f"outT{j}",))
            else:
                for j in range(nD):
                    p.op("vector", lambda h, j=j, q0=q0, s_lo=s_lo: h.tensor_copy(
                        out=stream[j][:, s_lo + q0:s_lo + q0 + QC], in_=rTc[j][:]),
                        reads=(f"rTc{j}",), writes=(f"ST{j}",))
    if stream is None:
        p.barrier_all("sync")


    p.barrier_all("sync")


def emit_ln_chunk(p, r_tiles, rkeys, out_tiles, okeys, pk0, pk1, B, C, CH,
                  cast_to=None, ckeys=None, out_slice=None):
    nD = len(r_tiles)
    inv = 1.0 / (nD * 128)
    osl = out_slice if out_slice is not None else slice(0, CH)
    for j in range(nD):
        p.op("tensor", lambda h, j=j: h.matmul(
            out=B["ps_s0"][:, :CH], lhsT=C["ones_col"], rhs=r_tiles[j][:],
            start=(j == 0), stop=(j == nD - 1)),
            reads=(rkeys[j], "ones_col"), writes=(pk0,))
    for j in range(nD):
        p.op("scalar", lambda h, j=j: h.activation(
            out=B["r2"][:], in_=r_tiles[j][:], func=AF.Square),
            reads=(rkeys[j],), writes=("ln_r2",))
        p.op("tensor", lambda h, j=j: h.matmul(
            out=B["ps_s1"][:, :CH], lhsT=C["ones_col"], rhs=B["r2"][:],
            start=(j == 0), stop=(j == nD - 1)),
            reads=("ln_r2", "ones_col"), writes=(pk1,))
    p.op("scalar", lambda h: h.activation(
        out=B["row_m"][:], in_=B["ps_s0"][:, :CH], func=AF.Copy, scale=inv),
        reads=(pk0,), writes=("ln_rowm",))
    p.op("scalar", lambda h: h.activation(
        out=B["row_r"][:], in_=B["row_m"][:], func=AF.Square),
        reads=("ln_rowm",), writes=("ln_rowr",))
    p.op("vector", lambda h: h.scalar_tensor_tensor(
        out=B["row_r"][:], in0=B["ps_s1"][:, :CH], scalar=inv, in1=B["row_r"][:],
        op0=ALU.mult, op1=ALU.subtract),
        reads=(pk1, "ln_rowr"), writes=("ln_rowr",))
    p.op("scalar", lambda h: h.activation(
        out=B["row_r"][:], in_=B["row_r"][:], func=AF.Ln, bias=1e-5),
        reads=("ln_rowr", "cpool"), writes=("ln_rowr",))
    p.op("scalar", lambda h: h.activation(
        out=B["row_r"][:], in_=B["row_r"][:], func=AF.Exp, scale=-0.5),
        reads=("ln_rowr",), writes=("ln_rowr",))
    p.op("tensor", lambda h: h.matmul(
        out=B["bc_m"][:, :CH], lhsT=C["ones_row"], rhs=B["row_m"][:],
        start=True, stop=True),
        reads=("ln_rowm", "ones_row"), writes=(pk0,))
    p.op("tensor", lambda h: h.matmul(
        out=B["bc_r"][:, :CH], lhsT=C["ones_row"], rhs=B["row_r"][:],
        start=True, stop=True),
        reads=("ln_rowr", "ones_row"), writes=(pk1,))
    for j in range(nD):
        p.op("vector", lambda h, j=j: h.tensor_sub(
            out=out_tiles[j][:, osl], in0=r_tiles[j][:], in1=B["bc_m"][:, :CH]),
            reads=(rkeys[j], pk0), writes=(okeys[j],))
        p.op("vector", lambda h, j=j: h.tensor_mul(
            out=out_tiles[j][:, osl], in0=out_tiles[j][:, osl], in1=B["bc_r"][:, :CH]),
            reads=(okeys[j], pk1), writes=(okeys[j],))
        if cast_to is not None:
            p.op("vector", lambda h, j=j: h.tensor_copy(
                out=cast_to[j][:, osl], in_=out_tiles[j][:, osl]),
                reads=(okeys[j],), writes=(ckeys[j],))



def stream_wT_slice_pe(p, hbm_ap, in_dim, o0, o1, dest, dkeys, col0,
                       stage_b, skey, ps_pair, pskeys, ident_b):
    ow = o1 - o0
    nI = in_dim // 128
    if ow < 128:
        ow0 = (ow // 32) * 32
        segs = [(a, b) for a, b in [(32, 64), (64, 128)] if b > ow0]
        for a, b in segs:
            a = max(a, ow0)
            p.op("vector", lambda h, a=a, b=b: h.memset(stage_b[a:b, 0:in_dim], 0.0),
                 reads=(skey,), writes=(skey,))
    p.dma("gpsimd", out=stage_b[0:ow, 0:in_dim], in_=hbm_ap[o0:o1, :],
          reads=(skey,), writes=(skey,))
    for i in range(nI):
        pt, ptk = ps_pair[i % 2], pskeys[i % 2]
        p.op("tensor", lambda h, i=i, pt=pt: h.transpose(
            out=pt[:, 0:128], in_=stage_b[0:128, ts(i, 128)], identity=ident_b),
            reads=(skey, "ident_b"), writes=(ptk,))
        g = p.new_gid()
        p.op("vector", lambda h, i=i, pt=pt: h.tensor_copy(
            out=dest[i][:, col0:col0 + 128], in_=pt[:, 0:128]),
            reads=(ptk,), writes=(dkeys[i],), wgroup=g)


def stream_wT_slice(p, al, hbm_ap, in_dim, o0, o1, dest, dkeys, col0,
                    stage_b, skey, tq="scalar"):
    ow = o1 - o0
    nI = in_dim // 128
    if ow < 128:
        ow0 = (ow // 32) * 32
        segs = [(a, b) for a, b in [(32, 64), (64, 128)] if b > ow0]
        for a, b in segs:
            a = max(a, ow0)
            p.op("vector", lambda h, a=a, b=b: h.memset(stage_b[a:b, 0:in_dim], 0.0),
                 reads=(skey,), writes=(skey,))
    p.dma("gpsimd", out=stage_b[0:ow, 0:in_dim], in_=hbm_ap[o0:o1, :],
          reads=(skey,), writes=(skey,))
    for i in range(nI):
        g = p.new_gid()
        p.dma(tq, out=dest[i][:, col0:col0 + 128],
              in_=stage_b[0:128, ts(i, 128)],
              reads=(skey,), writes=(dkeys[i],), transpose=True, wgroup=g)



def build_l0_ffn(nc, lseq=LSEQ):
    tok = NSEQ * lseq
    xT = nc.dram_tensor("xT", [D, tok], F32, kind="ExternalInput").ap()
    gate_w = nc.dram_tensor("gate_w", [E0, D], F32, kind="ExternalInput").ap()
    Wg = nc.dram_tensor("Wg", [E0, H0, D], F32, kind="ExternalInput").ap()
    W1 = nc.dram_tensor("W1", [E0, H0, D], F32, kind="ExternalInput").ap()
    W2 = nc.dram_tensor("W2", [E0, D, H0], F32, kind="ExternalInput").ap()
    outT = nc.dram_tensor("outT", [D, tok], F32, kind="ExternalOutput").ap()
    es = ExitStack()
    with es:
        al = Alloc(nc, es)
        p = Prog(nc)
        p.make_sems(es)
        emit_l0_ffn(nc, p, al, xT, gate_w, Wg, W1, W2, outT, lseq)
        block = es.enter_context(nc.Block())
        p.finish(block)
    return nc


def emit_l0_ffn(nc, p, al, xT_hbm, gate_hbm, Wg_hbm, W1_hbm, W2_hbm, outT_hbm, lseq,
                C=None, stream=None):
    tok = NSEQ * lseq
    nD = D // 128
    HT = ceil_div(H0, 128)
    TH = tok // 2
    if C is None:
        C = make_consts(p, al)

    P = [al.ps(f"P{i}", [128, 512], F32) for i in range(6)]
    ps_tb = al.ps("ps_tb", [128, 1024], BF16)
    ps_tb2 = al.ps("ps_tb2", [128, 1024], BF16)

    ident_f = al.sb("ident_f", [128, 128], F32)
    p.op("gpsimd", lambda h: h.memset(ident_f[:], 0.0), writes=("ident_f",))
    p.op("gpsimd", lambda h: h.affine_select(
        out=ident_f[:], in_=ident_f[:], compare_op=ALU.not_equal, fill=1.0,
        base=0, pattern=[[-1, 128]], channel_multiplier=1),
        reads=("ident_f",), writes=("ident_f",))
    ident_b = al.sb("ident_b", [128, 128], BF16)
    p.op("gpsimd", lambda h: h.memset(ident_b[:], 0.0), writes=("ident_b",))
    p.op("gpsimd", lambda h: h.affine_select(
        out=ident_b[:], in_=ident_b[:], compare_op=ALU.not_equal, fill=1.0,
        base=0, pattern=[[-1, 128]], channel_multiplier=1),
        reads=("ident_b",), writes=("ident_b",))
    sel = [al.sb(f"sel{e}", [8, 128], BF16) for e in range(E0)]
    for e in range(E0):
        p.op("gpsimd", lambda h, e=e: h.memset(sel[e][:], 1.0), writes=(f"sel{e}",))
        p.op("gpsimd", lambda h, e=e: h.affine_select(
            out=sel[e][:], in_=sel[e][:], compare_op=ALU.is_equal, fill=0.0,
            base=-e, pattern=[[0, 128]], channel_multiplier=1),
            reads=(f"sel{e}",), writes=(f"sel{e}",))

    stg2 = [al.sb(f"stg2_{b}", [128, 1408], BF16) for b in range(2)]
    gwT = [al.sb(f"gwT{j}", [128, 16], BF16) for j in range(nD)]
    p.op("vector", lambda h: h.memset(stg2[0][0:16, 0:D], 0.0), writes=("stg2_0",))
    p.dma("gpsimd", out=stg2[0][0:8, 0:D], in_=gate_hbm,
          reads=("stg2_0",), writes=("stg2_0",))
    for j in range(nD):
        p.dma("scalar", out=gwT[j][:], in_=stg2[0][0:16, ts(j, 128)],
              reads=("stg2_0",), writes=(f"gwT{j}",), transpose=True)

    xb = None if stream is not None else [al.sb(f"xb{j}", [128, TH], BF16)
                                          for j in range(nD)]
    nTT = TH // 128
    Ltm = al.sb("Ltm", [128, nTT * E0], F32)
    comb = al.sb("comb", [128, nTT * E0], F32)
    combE = al.sb("combE", [8, TH], BF16)
    cbc = al.sb("cbc", [128, TH], BF16)
    r_m1 = al.sb("r_m1", [128, nTT], F32)
    r_m2 = al.sb("r_m2", [128, nTT], F32)
    r_eb = al.sb("r_eb", [128, nTT * E0], F32)
    r_sc = al.sb("r_sc", [128, nTT], F32)
    r_sc2 = al.sb("r_sc2", [128, nTT], F32)

    WgTs = [[al.sb(f"WgTs{b}_{j}", [128, 128], BF16) for j in range(nD)]
            for b in range(2)]
    W1Ts = [[al.sb(f"W1Ts{b}_{j}", [128, 128], BF16) for j in range(nD)]
            for b in range(2)]
    HHT = 8
    w2T = [al.sb(f"w2T{j}", [128, D], BF16) for j in range(HHT)]
    gu = [al.sb(f"gu{j}", [128, TH], BF16) for j in range(HHT)]
    gs = al.sb("gs", [128, 512], BF16)
    gs2 = al.sb("gs2", [128, 512], BF16)
    out_acc = [al.sb(f"oacc{j}", [128, TH], BF16) for j in range(nD)]
    tmp_y = al.sb("tmp_y", [128, 512], BF16)
    rTc = [al.sb(f"rTc{j}", [128, 512], F32) for j in range(nD)]
    ln_bufs = dict(
        ps_s0=P[4][0:1, :], ps_s1=P[5][0:1, :], bc_m=P[4], bc_r=P[5],
        row_m=al.sb("ln_rowm", [1, 512], F32),
        row_r=al.sb("ln_rowr", [1, 512], F32),
        r2=al.sb("ln_r2", [128, 512], F32),
    )

    xb_half = xb
    for half in range(2):
        t0 = half * TH
        if stream is None:
            for j in range(nD):
                p.dma("gpsimd", out=xb[j][:], in_=xT_hbm[ts(j, 128), t0:t0 + TH],
                      writes=(f"xb{j}",))
            xbk = [f"xb{j}" for j in range(nD)]
        else:
            xb = [stream[j][:, t0:t0 + TH] for j in range(nD)]
            xbk = [f"ST{j}" for j in range(nD)]
        for i in range(nTT):
            for j in range(nD):
                p.op("tensor", lambda h, xb=xb, i=i, j=j: h.matmul(
                    out=P[3][:, 0:E0], lhsT=xb[j][:, ts(i, 128)], rhs=gwT[j][:, 0:E0],
                    start=(j == 0), stop=(j == nD - 1)),
                    reads=(xbk[j], f"gwT{j}"), writes=("P3",))
            p.op("scalar", lambda h, i=i: h.copy(
                out=Ltm[:, i * E0:(i + 1) * E0], in_=P[3][:, 0:E0]),
                reads=("P3",), writes=("Ltm",))
        L3 = Ltm[:].rearrange("p (t e) -> p t e", e=E0)
        if not globals().get("D_MIN"):
            p.op("vector", lambda h: h.tensor_reduce(
                out=r_m1[:], in_=L3, axis=AX.X, op=ALU.max),
                reads=("Ltm",), writes=("r_m1",))
        m1b = r_m1[:].unsqueeze(2).broadcast_to([128, nTT, E0])
        p.op("vector", lambda h: h.tensor_tensor(
            out=r_eb[:].rearrange("p (t e) -> p t e", e=E0), in0=L3, in1=m1b,
            op=ALU.subtract),
            reads=("Ltm", "r_m1"), writes=("r_eb",))
        p.op("scalar", lambda h: h.activation(
            out=r_eb[:], in_=r_eb[:], func=AF.Exp),
            reads=("r_eb",), writes=("r_eb",))
        eb3 = r_eb[:].rearrange("p (t e) -> p t e", e=E0)
        p.op("vector", lambda h: h.tensor_reduce(
            out=r_sc[:], in_=eb3, axis=AX.X, op=ALU.add),
            reads=("r_eb",), writes=("r_sc",))
        p.op("vector", lambda h: h.tensor_tensor(
            out=comb[:].rearrange("p (t e) -> p t e", e=E0), in0=L3, in1=m1b,
            op=ALU.is_ge),
            reads=("Ltm", "r_m1"), writes=("comb",))
        p.op("vector", lambda h: h.tensor_scalar(
            out=comb[:], in0=comb[:], scalar1=-1e30, scalar2=None, op0=ALU.mult),
            reads=("comb",), writes=("comb",))
        p.op("vector", lambda h: h.tensor_add(
            out=comb[:], in0=comb[:], in1=Ltm[:]),
            reads=("comb", "Ltm"), writes=("comb",))
        p.op("vector", lambda h: h.tensor_reduce(
            out=r_m2[:], in_=comb[:].rearrange("p (t e) -> p t e", e=E0),
            axis=AX.X, op=ALU.max),
            reads=("comb",), writes=("r_m2",))
        m2b = r_m2[:].unsqueeze(2).broadcast_to([128, nTT, E0])
        p.op("vector", lambda h: h.tensor_tensor(
            out=comb[:].rearrange("p (t e) -> p t e", e=E0), in0=L3, in1=m2b,
            op=ALU.is_ge),
            reads=("Ltm", "r_m2"), writes=("comb",))
        p.op("vector", lambda h: h.tensor_mul(
            out=comb[:], in0=comb[:], in1=r_eb[:]),
            reads=("comb", "r_eb"), writes=("comb",))
        p.op("vector", lambda h: h.tensor_reduce(
            out=r_sc2[:], in_=comb[:].rearrange("p (t e) -> p t e", e=E0),
            axis=AX.X, op=ALU.add),
            reads=("comb",), writes=("r_sc2",))
        p.op("vector", lambda h: h.scalar_tensor_tensor(
            out=r_sc2[:], in0=r_sc[:], scalar=1e-8, in1=r_sc2[:],
            op0=ALU.mult, op1=ALU.add),
            reads=("r_sc", "r_sc2"), writes=("r_sc2",))
        p.op("vector", lambda h: h.reciprocal(out=r_sc2[:], in_=r_sc2[:]),
             reads=("r_sc2",), writes=("r_sc2",))
        recb = r_sc2[:].unsqueeze(2).broadcast_to([128, nTT, E0])
        p.op("vector", lambda h: h.tensor_tensor(
            out=comb[:].rearrange("p (t e) -> p t e", e=E0),
            in0=comb[:].rearrange("p (t e) -> p t e", e=E0), in1=recb,
            op=ALU.mult),
            reads=("comb", "r_sc2"), writes=("comb",))
        for i in range(nTT):
            p.op("tensor", lambda h, i=i: h.transpose(
                out=P[3][0:8, 0:128], in_=comb[:, i * E0:(i + 1) * E0],
                identity=ident_f[:]),
                reads=("comb", "ident_f"), writes=("P3",))
            p.op("scalar", lambda h, i=i: h.copy(
                out=combE[:, ts(i, 128)], in_=P[3][0:8, 0:128]),
                reads=("P3",), writes=("combE",))
        for j in range(nD):
            p.op("vector", lambda h, j=j: h.memset(out_acc[j][:], 0.0),
                 writes=(f"oacc{j}",))
        for e in range(E0):
            for cc in range(TH // 512):
                p.op("tensor", lambda h, e=e, cc=cc: h.matmul(
                    out=P[3][:, :], lhsT=sel[e][:], rhs=combE[:, ts(cc, 512)],
                    start=True, stop=True),
                    reads=(f"sel{e}", "combE"), writes=("P3",))
                p.op("scalar", lambda h, cc=cc: h.copy(
                    out=cbc[:, ts(cc, 512)], in_=P[3][:, :]),
                    reads=("P3",), writes=("cbc",))
            for hh in range(3):
                jtiles = list(range(hh * HHT, min(HT, (hh + 1) * HHT)))
                hlo = jtiles[0] * 128
                hhi = min(H0, (jtiles[-1] + 1) * 128)
                wpad = len(jtiles) * 128
                w2gids = [p.new_gid() for _ in jtiles]
                for o in range(nD):
                    sb2 = stg2[o % 2]
                    sk2 = f"stg2_{o % 2}"
                    if hhi - hlo < wpad:
                        p.op("vector", lambda h, w0=hhi - hlo, w1=wpad, sb2=sb2: h.memset(
                            sb2[:, w0:w1], 0.0),
                            reads=(sk2,), writes=(sk2,))
                    p.dma("gpsimd", out=sb2[:, 0:hhi - hlo],
                          in_=W2_hbm[e, ts(o, 128), hlo:hhi],
                          reads=(sk2,), writes=(sk2,))
                    for ji, j in enumerate(jtiles):
                        pt, ptk = (ps_tb, "ps_tb") if ji % 2 == 0 else (ps_tb2, "ps_tb2")
                        p.op("tensor", lambda h, ji=ji, sb2=sb2, pt=pt: h.transpose(
                            out=pt[:, 0:128], in_=sb2[:, ts(ji, 128)],
                            identity=ident_b[:]),
                            reads=(sk2, "ident_b"), writes=(ptk,))
                        p.op("vector", lambda h, ji=ji, o=o, pt=pt: h.tensor_copy(
                            out=w2T[ji][:, ts(o, 128)], in_=pt[:, 0:128]),
                            reads=(ptk,), writes=(f"w2T{ji}",), wgroup=w2gids[ji])
                for ji, j in enumerate(jtiles):
                    jhi = min(H0, j * 128 + 128)
                    bb = ji % 2
                    stream_wT_slice_pe(p, Wg_hbm[e], D, j * 128, jhi,
                                       WgTs[bb], [f"WgTs{bb}_{q}" for q in range(nD)], 0,
                                       stg2[0], "stg2_0", (ps_tb, ps_tb2),
                                       ("ps_tb", "ps_tb2"), ident_b[:])
                    stream_wT_slice_pe(p, W1_hbm[e], D, j * 128, jhi,
                                       W1Ts[bb], [f"W1Ts{bb}_{q}" for q in range(nD)], 0,
                                       stg2[1], "stg2_1", (ps_tb, ps_tb2),
                                       ("ps_tb", "ps_tb2"), ident_b[:])
                    pg, pgk = (P[0], "P0") if ji % 2 == 0 else (P[4], "P4")
                    pu, puk = (P[1], "P1") if ji % 2 == 0 else (P[5], "P5")
                    gsb, gsk = (gs, "gs") if ji % 2 == 0 else (gs2, "gs2")
                    for cc in range(TH // 512):
                        cs = ts(cc, 512)
                        for q in range(nD):
                            p.op("tensor", lambda h, xb=xb, q=q, cs=cs, bb=bb, pg=pg: h.matmul(
                                out=pg[:, :], lhsT=WgTs[bb][q][:],
                                rhs=xb[q][:, cs], start=(q == 0), stop=(q == nD - 1)),
                                reads=(f"WgTs{bb}_{q}", xbk[q]), writes=(pgk,))
                        for q in range(nD):
                            p.op("tensor", lambda h, xb=xb, q=q, cs=cs, bb=bb, pu=pu: h.matmul(
                                out=pu[:, :], lhsT=W1Ts[bb][q][:],
                                rhs=xb[q][:, cs], start=(q == 0), stop=(q == nD - 1)),
                                reads=(f"W1Ts{bb}_{q}", xbk[q]), writes=(puk,))
                        p.op("scalar", lambda h, pg=pg, gsb=gsb: h.activation(
                            out=gsb[:, :], in_=pg[:, :], func=AF.Silu),
                            reads=(pgk,), writes=(gsk,))
                        p.op("vector", lambda h, ji=ji, cs=cs, pu=pu, gsb=gsb: h.tensor_mul(
                            out=gu[ji][:, cs], in0=gsb[:, :], in1=pu[:, :]),
                            reads=(gsk, puk), writes=(f"gu{ji}",))
                for o in range(nD):
                    for cc in range(TH // 512):
                        cs = ts(cc, 512)
                        py, pyk = (P[2], "P2") if (o * 4 + cc) % 2 == 0 else (P[3], "P3")
                        for ji, j in enumerate(jtiles):
                            p.op("tensor", lambda h, ji=ji, o=o, cs=cs, py=py,
                                 st=(ji == 0), sp=(ji == len(jtiles) - 1): h.matmul(
                                out=py[:, :], lhsT=w2T[ji][:, ts(o, 128)],
                                rhs=gu[ji][:, cs], start=st, stop=sp,
                                skip_group_check=True),
                                reads=(f"w2T{ji}", f"gu{ji}"), writes=(pyk,))
                        p.op("vector", lambda h, o=o, cs=cs, py=py: h.tensor_mul(
                            out=tmp_y[:], in0=py[:, :], in1=cbc[:, cs]),
                            reads=(pyk, "cbc"), writes=("tmp_y",))
                        p.op("vector", lambda h, o=o, cs=cs: h.tensor_add(
                            out=out_acc[o][:, cs], in0=out_acc[o][:, cs], in1=tmp_y[:]),
                            reads=(f"oacc{o}", "tmp_y"), writes=(f"oacc{o}",))
        for cc in range(TH // 512):
            cs = ts(cc, 512)
            for j in range(nD):
                if globals().get("D_MIN4"):
                    p.op("vector", lambda h, xb=xb, j=j, cs=cs: h.tensor_copy(
                        out=rTc[j][:], in_=xb[j][:, cs]),
                        reads=(xbk[j],), writes=(f"rTc{j}",))
                else:
                    p.op("vector", lambda h, xb=xb, j=j, cs=cs: h.tensor_add(
                        out=rTc[j][:], in0=xb[j][:, cs], in1=out_acc[j][:, cs]),
                        reads=(xbk[j], f"oacc{j}"), writes=(f"rTc{j}",))
            emit_ln_chunk(p, rTc, [f"rTc{j}" for j in range(nD)],
                          rTc, [f"rTc{j}" for j in range(nD)], "P4", "P5",
                          ln_bufs, C, 512)
            if outT_hbm is not None:
                for j in range(nD):
                    p.dma("sync", out=outT_hbm[ts(j, 128), t0 + cc * 512:t0 + (cc + 1) * 512],
                          in_=rTc[j][:], reads=(f"rTc{j}",), writes=(f"outT{j}",))
            else:
                for j in range(nD):
                    p.op("vector", lambda h, j=j, t0=t0, cc=cc: h.tensor_copy(
                        out=stream[j][:, t0 + cc * 512:t0 + (cc + 1) * 512],
                        in_=rTc[j][:]),
                        reads=(f"rTc{j}",), writes=(f"ST{j}",))
    if outT_hbm is not None:
        p.barrier_all("sync")



def build_l1_ffn(nc, lseq=LSEQ):
    tok = NSEQ * lseq
    xT = nc.dram_tensor("xT", [D, tok], F32, kind="ExternalInput").ap()
    rW = nc.dram_tensor("router_W", [D, E1], F32, kind="ExternalInput").ap()
    rb = nc.dram_tensor("router_b", [E1], F32, kind="ExternalInput").ap()
    sW1 = nc.dram_tensor("sW1", [D, H1], F32, kind="ExternalInput").ap()
    sb1 = nc.dram_tensor("sb1", [H1], F32, kind="ExternalInput").ap()
    sW2 = nc.dram_tensor("sW2", [H1, D], F32, kind="ExternalInput").ap()
    sb2 = nc.dram_tensor("sb2", [D], F32, kind="ExternalInput").ap()
    eW1 = nc.dram_tensor("eW1", [E1, D, H1], F32, kind="ExternalInput").ap()
    eb1 = nc.dram_tensor("eb1", [E1, H1], F32, kind="ExternalInput").ap()
    eW2 = nc.dram_tensor("eW2", [E1, H1, D], F32, kind="ExternalInput").ap()
    eb2 = nc.dram_tensor("eb2", [E1, D], F32, kind="ExternalInput").ap()
    outT = nc.dram_tensor("outT", [D, tok], F32, kind="ExternalOutput").ap()
    es = ExitStack()
    with es:
        al = Alloc(nc, es)
        p = Prog(nc)
        p.make_sems(es)
        emit_l1_ffn(nc, p, al, xT, (rW, rb, sW1, sb1, sW2, sb2, eW1, eb1, eW2, eb2),
                    outT, lseq)
        block = es.enter_context(nc.Block())
        p.finish(block)
    return nc


DBG = {}


def emit_l1_ffn(nc, p, al, xT_hbm, W, outT_hbm, lseq, C=None, stream=None):
    rW_h, rb_h, sW1_h, sb1_h, sW2_h, sb2_h, eW1_h, eb1_h, eW2_h, eb2_h = W
    tok = NSEQ * lseq
    nD = D // 128
    nH = H1 // 128
    TH = tok // 2
    nTT = TH // 128
    if C is None:
        C = make_consts(p, al)
    P = [al.ps(f"P{i}", [128, 512], F32) for i in range(6)]

    ident_f = al.sb("ident_f", [128, 128], F32)
    sel = [al.sb(f"sel{e}", [E1, 128], BF16) for e in range(E1)]
    if not globals().get("D_MIN3"):
        p.op("gpsimd", lambda h: h.memset(ident_f[:], 0.0), writes=("ident_f",))
        p.op("gpsimd", lambda h: h.affine_select(
            out=ident_f[:], in_=ident_f[:], compare_op=ALU.not_equal, fill=1.0,
            base=0, pattern=[[-1, 128]], channel_multiplier=1),
            reads=("ident_f",), writes=("ident_f",))
        for e in range(E1):
            p.op("gpsimd", lambda h, e=e: h.memset(sel[e][:], 1.0), writes=(f"sel{e}",))
            p.op("gpsimd", lambda h, e=e: h.affine_select(
                out=sel[e][:], in_=sel[e][:], compare_op=ALU.is_equal, fill=0.0,
                base=-e, pattern=[[0, 128]], channel_multiplier=1),
                reads=(f"sel{e}",), writes=(f"sel{e}",))

    D_MIN2 = globals().get("D_MIN2")
    rwT = [al.sb(f"rwT{j}", [128, E1], BF16) for j in range(nD)]
    stage_f = al.sb("wstage_f", [128, max(D, H1)], F32)
    rb_rowf = al.sb("rb_rowf", [1, E1], F32)
    rb_row = al.sb("rb_row", [1, E1], BF16)
    if not D_MIN2:
        p.dma("sync", out=rb_rowf[:], in_=rb_h.unsqueeze(0), writes=("rb_rowf",))
        p.op("vector", lambda h: h.tensor_copy(out=rb_row[:], in_=rb_rowf[:]),
             reads=("rb_rowf",), writes=("rb_row",))
        for j in range(nD):
            p.dma("sync", out=stage_f[:, 0:E1], in_=rW_h[ts(j, 128), :], writes=("wstage_f",))
            p.op("vector", lambda h, j=j: h.tensor_copy(out=rwT[j][:], in_=stage_f[:, 0:E1]),
                 reads=("wstage_f",), writes=(f"rwT{j}",))

    w1d = [[al.sb(f"w1d{b}_{j}", [128, H1], BF16) for j in range(nD)] for b in range(2)]
    w2d = [[al.sb(f"w2d{b}_{j}", [128, D], BF16) for j in range(nH)] for b in range(2)]
    b1c = [al.sb(f"b1c{b}", [128, nH], F32) for b in range(2)]
    b2c = [al.sb(f"b2c{b}", [128, nD], F32) for b in range(2)]
    hb = [al.sb(f"hb{j}", [128, TH], BF16) for j in range(nH)]
    xb_c = None if stream is not None else [al.sb(f"xb{j}", [128, TH], BF16)
                                            for j in range(nD)]
    out_acc = [al.sb(f"oacc{j}", [128, TH], BF16) for j in range(nD)]
    tmp_y = al.sb("tmp_y", [128, 512], BF16)
    cbc = al.sb("cbc", [128, TH], BF16)
    Ltm = al.sb("Ltm", [128, nTT * E1], F32)
    comb = al.sb("comb", [128, nTT * E1], F32)
    combE = al.sb("combE", [E1, TH], BF16)
    r_m1 = al.sb("r_m1", [128, nTT], F32)
    r_eb = al.sb("r_eb", [128, nTT * E1], F32)
    mskb = al.sb("mskb", [128, nTT * E1], F32)
    r_sc = al.sb("r_sc", [128, nTT], F32)
    rTc = [al.sb(f"rTc{j}", [128, 512], F32) for j in range(nD)]
    ln_bufs = dict(
        ps_s0=P[4][0:1, :], ps_s1=P[5][0:1, :], bc_m=P[4], bc_r=P[5],
        row_m=al.sb("ln_rowm", [1, 512], F32),
        row_r=al.sb("ln_rowr", [1, 512], F32),
        r2=al.sb("ln_r2", [128, 512], F32),
    )

    def load_direct(hbm, dest, dkeys, in_dim, width):
        for j in range(in_dim // 128):
            p.dma("gpsimd" if j % 2 else "gpsimd", out=dest[j][:, 0:width],
                  in_=hbm[ts(j, 128), :], writes=(dkeys[j],))

    def load_bias_cols_2d(hbm_row, name, dim, dest):
        g = p.new_gid()
        for j in range(dim // 128):
            p.dma("sync", out=dest[:, j:j + 1], in_=hbm_row[ts(j, 128)].unsqueeze(1),
                  writes=(name,), wgroup=g)

    for half in range(2):
        t0 = half * TH
        if stream is None:
            for j in range(nD):
                p.dma("gpsimd", out=xb_c[j][:], in_=xT_hbm[ts(j, 128), t0:t0 + TH],
                      writes=(f"xb{j}",))
            xb = [b[:] for b in xb_c]
            xbk = [f"xb{j}" for j in range(nD)]
        else:
            xb = [stream[j][:, t0:t0 + TH] for j in range(nD)]
            xbk = [f"ST{j}" for j in range(nD)]
        for i in range(0 if globals().get("D_MIN") else nTT):
            for j in range(nD):
                p.op("tensor", lambda h, xb=xb, i=i, j=j: h.matmul(
                    out=P[3][:, 0:E1], lhsT=xb[j][:, ts(i, 128)], rhs=rwT[j][:],
                    start=(j == 0), stop=False),
                    reads=(xbk[j], f"rwT{j}"), writes=("P3",))
            p.op("tensor", lambda h: h.matmul(
                out=P[3][:, 0:E1], lhsT=C["ones_row_b"], rhs=rb_row[:],
                start=False, stop=True),
                reads=("rb_row", "ones_row_b"), writes=("P3",))
            p.op("scalar", lambda h, i=i: h.copy(
                out=Ltm[:, i * E1:(i + 1) * E1], in_=P[3][:, 0:E1]),
                reads=("P3",), writes=("Ltm",))
        if globals().get("D_MIN"):
            L3 = None
        else:
            L3 = Ltm[:].rearrange("p (t e) -> p t e", e=E1)
        if not globals().get("D_MIN"):
            p.op("vector", lambda h: h.tensor_reduce(
                out=r_m1[:], in_=L3, axis=AX.X, op=ALU.max),
                reads=("Ltm",), writes=("r_m1",))
        if not globals().get("D_MIN"):
            m1b = r_m1[:].unsqueeze(2).broadcast_to([128, nTT, E1])
            p.op("vector", lambda h: h.tensor_tensor(
                out=r_eb[:].rearrange("p (t e) -> p t e", e=E1), in0=L3, in1=m1b,
                op=ALU.subtract),
                reads=("Ltm", "r_m1"), writes=("r_eb",))
            p.op("scalar", lambda h: h.activation(
                out=r_eb[:], in_=r_eb[:], func=AF.Exp),
                reads=("r_eb",), writes=("r_eb",))
            eb3 = r_eb[:].rearrange("p (t e) -> p t e", e=E1)
            p.op("vector", lambda h: h.tensor_reduce(
                out=r_sc[:], in_=eb3, axis=AX.X, op=ALU.add),
                reads=("r_eb",), writes=("r_sc",))
            p.op("vector", lambda h: h.reciprocal(out=r_sc[:], in_=r_sc[:]),
                 reads=("r_sc",), writes=("r_sc",))
            p.op("scalar", lambda h: h.copy(out=comb[:], in_=Ltm[:]),
                 reads=("Ltm",), writes=("comb",))
            for rnd in range(2):
                p.op("vector", lambda h: h.tensor_reduce(
                    out=r_m1[:], in_=comb[:].rearrange("p (t e) -> p t e", e=E1),
                    axis=AX.X, op=ALU.max),
                    reads=("comb",), writes=("r_m1",))
                p.op("vector", lambda h: h.tensor_tensor(
                    out=mskb[:].rearrange("p (t e) -> p t e", e=E1),
                    in0=comb[:].rearrange("p (t e) -> p t e", e=E1), in1=m1b,
                    op=ALU.is_ge),
                    reads=("comb", "r_m1"), writes=("mskb",))
                p.op("vector", lambda h: h.tensor_scalar(
                    out=mskb[:], in0=mskb[:], scalar1=-1e30, scalar2=None, op0=ALU.mult),
                    reads=("mskb",), writes=("mskb",))
                p.op("vector", lambda h: h.tensor_add(
                    out=comb[:], in0=comb[:], in1=mskb[:]),
                    reads=("comb", "mskb"), writes=("comb",))
            p.op("vector", lambda h: h.tensor_reduce(
                out=r_m1[:], in_=comb[:].rearrange("p (t e) -> p t e", e=E1),
                axis=AX.X, op=ALU.max),
                reads=("comb",), writes=("r_m1",))
            p.op("vector", lambda h: h.tensor_tensor(
                out=comb[:].rearrange("p (t e) -> p t e", e=E1), in0=L3, in1=m1b,
                op=ALU.is_ge),
                reads=("Ltm", "r_m1"), writes=("comb",))
            p.op("vector", lambda h: h.tensor_mul(
                out=comb[:], in0=comb[:], in1=r_eb[:]),
                reads=("comb", "r_eb"), writes=("comb",))
            recb = r_sc[:].unsqueeze(2).broadcast_to([128, nTT, E1])
            p.op("vector", lambda h: h.tensor_tensor(
                out=comb[:].rearrange("p (t e) -> p t e", e=E1),
                in0=comb[:].rearrange("p (t e) -> p t e", e=E1), in1=recb,
                op=ALU.mult),
                reads=("comb", "r_sc"), writes=("comb",))
            for i in range(nTT):
                p.op("tensor", lambda h, i=i: h.transpose(
                    out=P[3][0:E1, 0:128], in_=comb[:, i * E1:(i + 1) * E1],
                    identity=ident_f[:]),
                    reads=("comb", "ident_f"), writes=("P3",))
                p.op("scalar", lambda h, i=i: h.copy(
                    out=combE[:, ts(i, 128)], in_=P[3][0:E1, 0:128]),
                    reads=("P3",), writes=("combE",))
            if DBG and half == 0:
                p.dma("sync", out=DBG["combE"], in_=combE[:], reads=("combE",), writes=("dbgc",))
                p.dma("sync", out=DBG["Ltm"], in_=Ltm[:], reads=("Ltm",), writes=("dbgl",))
        if globals().get("D_NS"):
            for j in range(nD):
                p.op("vector", lambda h, j=j: h.memset(out_acc[j][:], 0.0),
                     writes=(f"oacc{j}",))
        if not D_MIN2:
            load_direct(sW1_h, w1d[0], [f"w1d0_{j}" for j in range(nD)], D, H1)
            load_direct(sW2_h, w2d[0], [f"w2d0_{j}" for j in range(nH)], H1, D)
            load_bias_cols_2d(sb1_h, "b1c0", H1, b1c[0])
            load_bias_cols_2d(sb2_h, "b2c0", D, b2c[0])
        if not globals().get("D_NS"):
            for ht in range(nH):
                for cc in range(TH // 512):
                    cs = ts(cc, 512)
                    for q in range(nD):
                        p.op("tensor", lambda h, xb=xb, q=q, ht=ht, cs=cs: h.matmul(
                            out=P[0][:, :], lhsT=w1d[0][q][:, ts(ht, 128)], rhs=xb[q][:, cs],
                            start=(q == 0), stop=(q == nD - 1)),
                            reads=(f"w1d0_{q}", xbk[q]), writes=("P0",))
                    p.op("scalar", lambda h, ht=ht, cs=cs: h.activation(
                        out=hb[ht][:, cs], in_=P[0][:, :], func=AF.Relu,
                        bias=b1c[0][:, ht:ht + 1]),
                        reads=("P0", "b1c0"), writes=(f"hb{ht}",))
            for o in range(nD):
                for cc in range(TH // 512):
                    cs = ts(cc, 512)
                    for ht in range(nH):
                        p.op("tensor", lambda h, ht=ht, o=o, cs=cs: h.matmul(
                            out=P[2][:, :], lhsT=w2d[0][ht][:, ts(o, 128)], rhs=hb[ht][:, cs],
                            start=(ht == 0), stop=(ht == nH - 1)),
                            reads=(f"w2d0_{ht}", f"hb{ht}"), writes=("P2",))
                    p.op("vector", lambda h, o=o, cs=cs: h.tensor_scalar(
                        out=out_acc[o][:, cs], in0=P[2][:, :], scalar1=b2c[0][:, o:o + 1],
                        scalar2=None, op0=ALU.add),
                        reads=("P2", "b2c0"), writes=(f"oacc{o}",))
        if DBG and half == 0:
            p.dma("sync", out=DBG["oacc0"], in_=out_acc[0][:], reads=("oacc0",), writes=("dbgo",))
            p.dma("sync", out=DBG["hb0"], in_=hb[0][:], reads=("hb0",), writes=("dbgh",))
            p.dma("gpsimd", out=DBG["st_sh"], in_=xb[0][:, 0:512], reads=(xbk[0],), writes=("dbgs1",))
        for e in range(globals().get("D_NE", E1)):
            for cc in range(TH // 512):
                p.op("tensor", lambda h, e=e, cc=cc: h.matmul(
                    out=P[3][:, :], lhsT=sel[e][:], rhs=combE[:, ts(cc, 512)],
                    start=True, stop=True),
                    reads=(f"sel{e}", "combE"), writes=("P3",))
                p.op("scalar", lambda h, cc=cc: h.copy(
                    out=cbc[:, ts(cc, 512)], in_=P[3][:, :]),
                    reads=("P3",), writes=("cbc",))
            bb = (e + 1) % 2
            load_direct(eW1_h[e], w1d[bb], [f"w1d{bb}_{j}" for j in range(nD)], D, H1)
            load_direct(eW2_h[e], w2d[bb], [f"w2d{bb}_{j}" for j in range(nH)], H1, D)
            load_bias_cols_2d(eb1_h[e], f"b1c{bb}", H1, b1c[bb])
            load_bias_cols_2d(eb2_h[e], f"b2c{bb}", D, b2c[bb])
            for ht in range(nH):
                for cc in range(TH // 512):
                    cs = ts(cc, 512)
                    for q in range(nD):
                        p.op("tensor", lambda h, xb=xb, q=q, ht=ht, cs=cs, bb=bb: h.matmul(
                            out=P[0][:, :], lhsT=w1d[bb][q][:, ts(ht, 128)], rhs=xb[q][:, cs],
                            start=(q == 0), stop=(q == nD - 1)),
                            reads=(f"w1d{bb}_{q}", xbk[q]), writes=("P0",))
                    p.op("scalar", lambda h, ht=ht, cs=cs, bb=bb: h.activation(
                        out=hb[ht][:, cs], in_=P[0][:, :], func=AF.Relu,
                        bias=b1c[bb][:, ht:ht + 1]),
                        reads=("P0", f"b1c{bb}"), writes=(f"hb{ht}",))
            if DBG and half == 0 and e in (0, 1, 3, 7, 14):
                p.dma("gpsimd", out=DBG[f"st_e{e}"], in_=xb[0][:, 0:512], reads=(xbk[0],),
                      writes=(f"dbgs2_{e}",))
            for o in range(nD):
                for cc in range(TH // 512):
                    cs = ts(cc, 512)
                    for ht in range(nH):
                        p.op("tensor", lambda h, ht=ht, o=o, cs=cs, bb=bb: h.matmul(
                            out=P[2][:, :], lhsT=w2d[bb][ht][:, ts(o, 128)], rhs=hb[ht][:, cs],
                            start=(ht == 0), stop=(ht == nH - 1)),
                            reads=(f"w2d{bb}_{ht}", f"hb{ht}"), writes=("P2",))
                    p.op("vector", lambda h, o=o, cs=cs, bb=bb: h.scalar_tensor_tensor(
                        out=tmp_y[:], in0=P[2][:, :], scalar=b2c[bb][:, o:o + 1],
                        in1=cbc[:, cs], op0=ALU.add, op1=ALU.mult),
                        reads=("P2", f"b2c{bb}", "cbc"), writes=("tmp_y",))
                    p.op("vector", lambda h, o=o, cs=cs: h.tensor_add(
                        out=out_acc[o][:, cs], in0=out_acc[o][:, cs], in1=tmp_y[:]),
                        reads=(f"oacc{o}", "tmp_y"), writes=(f"oacc{o}",))
        if DBG and half == 0:
            p.dma("sync", out=DBG["oaccF"], in_=out_acc[0][:], reads=("oacc0",), writes=("dbgo2",))
            p.dma("sync", out=DBG["cbc0"], in_=cbc[:], reads=("cbc",), writes=("dbgcb",))
        for cc in range(TH // 512):
            cs = ts(cc, 512)
            for j in range(nD):
                if globals().get("D_MIN4"):
                    p.op("vector", lambda h, xb=xb, j=j, cs=cs: h.tensor_copy(
                        out=rTc[j][:], in_=xb[j][:, cs]),
                        reads=(xbk[j],), writes=(f"rTc{j}",))
                else:
                    p.op("vector", lambda h, xb=xb, j=j, cs=cs: h.tensor_add(
                        out=rTc[j][:], in0=xb[j][:, cs], in1=out_acc[j][:, cs]),
                        reads=(xbk[j], f"oacc{j}"), writes=(f"rTc{j}",))
            emit_ln_chunk(p, rTc, [f"rTc{j}" for j in range(nD)],
                          rTc, [f"rTc{j}" for j in range(nD)], "P4", "P5",
                          ln_bufs, C, 512)
            if outT_hbm is not None:
                for j in range(nD):
                    p.dma("sync", out=outT_hbm[ts(j, 128), t0 + cc * 512:t0 + (cc + 1) * 512],
                          in_=rTc[j][:], reads=(f"rTc{j}",), writes=(f"outT{j}",))
            else:
                for j in range(nD):
                    p.op("vector", lambda h, j=j, t0=t0, cc=cc: h.tensor_copy(
                        out=stream[j][:, t0 + cc * 512:t0 + (cc + 1) * 512],
                        in_=rTc[j][:]),
                        reads=(f"rTc{j}",), writes=(f"ST{j}",))
    if outT_hbm is not None:
        p.barrier_all("sync")



def build_l1_attn(nc, lseq=LSEQ):
    tok = NSEQ * lseq
    xT = nc.dram_tensor("xT", [D, tok], F32, kind="ExternalInput").ap()
    W_in = nc.dram_tensor("W_in", [D, 2 * DIN], F32, kind="ExternalInput").ap()
    conv_w = nc.dram_tensor("conv_w", [DIN, DCONV], F32, kind="ExternalInput").ap()
    conv_b = nc.dram_tensor("conv_b", [DIN], F32, kind="ExternalInput").ap()
    W_x = nc.dram_tensor("W_x", [DIN, 2 * DST + 1], F32, kind="ExternalInput").ap()
    A_log = nc.dram_tensor("A_log", [DIN, DST], F32, kind="ExternalInput").ap()
    dt_bias = nc.dram_tensor("dt_bias", [DIN], F32, kind="ExternalInput").ap()
    D_skip = nc.dram_tensor("D_skip", [DIN], F32, kind="ExternalInput").ap()
    W_out = nc.dram_tensor("W_out", [DIN, D], F32, kind="ExternalInput").ap()
    outT = nc.dram_tensor("outT", [D, tok], F32, kind="ExternalOutput").ap()
    es = ExitStack()
    with es:
        al = Alloc(nc, es)
        p = Prog(nc)
        p.make_sems(es)
        emit_l1_attn(nc, p, al, xT,
                     (W_in, conv_w, conv_b, W_x, A_log, dt_bias, D_skip, W_out),
                     outT, lseq)
        block = es.enter_context(nc.Block())
        p.finish(block)
    return nc


def emit_l1_attn(nc, p, al, xT_hbm, W, outT_hbm, lseq, C=None, stream=None):
    (Win_h, convw_h, convb_h, Wx_h, Alog_h, dtb_h, Dskip_h, Wout_h) = W
    nD = D // 128
    nI = DIN // 128
    TC = 128
    TP = 4
    NCH = lseq // TC
    if C is None:
        C = make_consts(p, al)
    P = [al.ps(f"P{i}", [128, 512], F32) for i in range(2)]
    ps_y = al.ps("ps_y", [128, 2048], F32)
    ps_tf = al.ps("ps_tf", [128, 512], F32)
    ps_tb = al.ps("ps_tb", [128, 1024], BF16)

    ident_f = al.sb("ident_f", [128, 128], F32)
    p.op("gpsimd", lambda h: h.memset(ident_f[:], 0.0), writes=("ident_f",))
    p.op("gpsimd", lambda h: h.affine_select(
        out=ident_f[:], in_=ident_f[:], compare_op=ALU.not_equal, fill=1.0,
        base=0, pattern=[[-1, 128]], channel_multiplier=1),
        reads=("ident_f",), writes=("ident_f",))
    ident_b = al.sb("ident_b", [128, 128], BF16)
    p.op("gpsimd", lambda h: h.memset(ident_b[:], 0.0), writes=("ident_b",))
    p.op("gpsimd", lambda h: h.affine_select(
        out=ident_b[:], in_=ident_b[:], compare_op=ALU.not_equal, fill=1.0,
        base=0, pattern=[[-1, 128]], channel_multiplier=1),
        reads=("ident_b",), writes=("ident_b",))

    stage_f = al.sb("stg_f", [128, 512], F32)

    cwc = al.sb("cwc", [128, nI * DCONV], F32)
    gw_ = p.new_gid()
    for j in range(nI):
        for k in range(DCONV):
            p.dma("sync", out=cwc[:, j * DCONV + k:j * DCONV + k + 1],
                  in_=convw_h[ts(j, 128), k].unsqueeze(1), writes=("cwc",), slow=True,
                  wgroup=gw_)
    cbc_ = al.sb("cbc_", [128, nI], F32)
    dbc = al.sb("dbc", [128, nI], F32)
    skc = al.sb("skc", [128, nI], F32)
    g1_, g2_, g3_ = p.new_gid(), p.new_gid(), p.new_gid()
    for j in range(nI):
        p.dma("sync", out=cbc_[:, j:j + 1], in_=convb_h[ts(j, 128)].unsqueeze(1),
              writes=("cbc_",), wgroup=g1_)
        p.dma("sync", out=dbc[:, j:j + 1], in_=dtb_h[ts(j, 128)].unsqueeze(1),
              writes=("dbc",), wgroup=g2_)
        p.dma("sync", out=skc[:, j:j + 1], in_=Dskip_h[ts(j, 128)].unsqueeze(1),
              writes=("skc",), wgroup=g3_)

    A_bc = al.sb("A_bc", [128, DIN], BF16)
    atile = al.sb("atile", [128, DST], F32)
    for j in range(nI):
        p.dma("sync", out=stage_f[:, 0:DST], in_=Alog_h[ts(j, 128), :],
              writes=("stg_f",))
        p.op("scalar", lambda h: h.activation(
            out=atile[:], in_=stage_f[:, 0:DST], func=AF.Exp),
            reads=("stg_f",), writes=("atile",))
        p.op("vector", lambda h: h.tensor_scalar(
            out=atile[:], in0=atile[:], scalar1=-1.0, scalar2=None, op0=ALU.mult),
            reads=("atile",), writes=("atile",))
        p.op("tensor", lambda h, j=j: h.transpose(
            out=ps_tf[0:DST, 0:128], in_=atile[:], identity=ident_f[:]),
            reads=("atile", "ident_f"), writes=("ps_tf",))
        p.op("vector", lambda h, j=j: h.tensor_copy(
            out=A_bc[0:DST, ts(j, 128)], in_=ps_tf[0:DST, 0:128]),
            reads=("ps_tf",), writes=("A_bc",))
    p.op("vector", lambda h: h.tensor_copy(
        out=A_bc[64:128, :], in_=A_bc[0:64, :]),
        reads=("A_bc",), writes=("A_bc",))

    wxT = [al.sb(f"wxT{j}", [128, 2 * DST + 1], BF16) for j in range(nI)]
    for j in range(nI):
        p.dma("sync", out=stage_f[:, 0:2 * DST + 1], in_=Wx_h[ts(j, 128), :],
              writes=("stg_f",))
        p.op("vector", lambda h, j=j: h.tensor_copy(
            out=wxT[j][:], in_=stage_f[:, 0:2 * DST + 1]),
            reads=("stg_f",), writes=(f"wxT{j}",))

    T2 = 2 * TC
    xb_c = None if stream is not None else [al.sb(f"xb{j}", [128, T2], BF16)
                                            for j in range(nD)]
    xs = [al.sb(f"xs{j}", [128, 2 * (TC + 3)], BF16) for j in range(nI)]
    zt = [al.sb(f"zt{j}", [128, T2], BF16) for j in range(nI)]
    xc = [al.sb(f"xc{j}", [128, T2], BF16) for j in range(nI)]
    dt16 = [al.sb(f"dt16{j}", [128, T2], BF16) for j in range(nI)]
    dtx16 = [al.sb(f"dtx16{j}", [128, T2], BF16) for j in range(nI)]
    dtT = [al.sb(f"dtT{r}", [128, DIN], BF16) for r in range(2)]
    dtxT = [al.sb(f"dtxT{r}", [128, DIN], BF16) for r in range(2)]
    dt_dram = nc.dram_tensor("dt_dram", [2, TC * DIN], BF16, kind="Internal").ap()
    dtx_dram = nc.dram_tensor("dtx_dram", [2, TC * DIN], BF16, kind="Internal").ap()
    B_stk = al.sb("B_stk", [128, TC], F32)
    C_stk = al.sb("C_stk", [128, TC], F32)
    Csel = al.sb("Csel", [128, 128], F32)
    r_bc = al.sb("r_bc", [128, T2], F32)
    hst = [al.sb(f"hst{b}", [128, DIN], F32) for b in range(2)]
    dt_rep = [al.sb(f"dt_rep{b}", [128, TP * DIN], BF16) for b in range(2)]
    dtx_rep = [al.sb(f"dtx_rep{b}", [128, TP * DIN], BF16) for b in range(2)]
    g_rep = dt_rep
    yT = [al.sb(f"yT{j}", [128, T2], BF16) for j in range(nI)]
    scr1 = al.sb("scr1", [128, T2], F32)
    scr2 = al.sb("scr2", [128, T2], F32)
    wslab = [al.sb(f"wslab{b}", [128, 2048], BF16) for b in range(2)]
    yv = wslab[0]
    zs = al.sb("zs", [128, T2], BF16)
    rTc = [al.sb(f"rTc{j}", [128, T2], F32) for j in range(nD)]
    ln_bufs = dict(
        ps_s0=P[0][0:1, :], ps_s1=P[1][0:1, :], bc_m=P[0], bc_r=P[1],
        row_m=al.sb("ln_rowm", [1, T2], F32),
        row_r=al.sb("ln_rowr", [1, T2], F32),
        r2=al.sb("ln_r2", [128, T2], F32),
    )

    p.op("vector", lambda h: h.memset(Csel[:], 0.0), writes=("Csel",))

    for j in range(nI):
        p.op("vector", lambda h, j=j: h.memset(xs[j][:], 0.0), writes=(f"xs{j}",))

    for ch in range(NCH):
        t0 = ch * TC
        if ch == 0:
            p.op("vector", lambda h: h.memset(hst[0][:], 0.0), writes=("hst0",))
            p.op("vector", lambda h: h.memset(hst[1][:], 0.0), writes=("hst1",))
        if stream is None:
            for j in range(nD):
                g = p.new_gid()
                for s in range(NSEQ):
                    p.dma("gpsimd", out=xb_c[j][:, s * TC:(s + 1) * TC],
                          in_=xT_hbm[ts(j, 128), s * lseq + t0:s * lseq + t0 + TC],
                          writes=(f"xb{j}",), wgroup=g)
            xb, xbk = xb_c, [f"xb{j}" for j in range(nD)]
        else:
            xb = [stream[j][:].rearrange("p (s l) -> p s l", s=2)[:, :, t0:t0 + TC]
                  for j in range(nD)]
            xbk = [f"ST{j}" for j in range(nD)]
        xb2 = xb if stream is not None else [b[:, 0:T2] for b in xb]
        emit_l1_pre(p, al, nc, Win_h, xb2, xbk, xs, zt, xc, cwc, cbc_, wxT, dt16,
                    dtx16, dtT, dtxT, B_stk, C_stk, r_bc, dbc, stage_f, wslab,
                    ident_b, ident_f, ps_tf, ps_tb, P, C, scr1, scr2, ch, nI, nD, TC)
        emit_l1_scan(p, al, nc, A_bc, dt_rep, dtx_rep, g_rep, dtT, dtxT, dt_dram,
                     dtx_dram, B_stk, C_stk, Csel, hst, ps_y, ps_tb, yT, yv,
                     ident_b, ch, TC, TP, nI)
        xb3 = (xb if stream is not None else
               [b[:, 0:T2].rearrange("p (s l) -> p s l", s=2) for b in xb])
        emit_l1_post(p, al, nc, Wout_h, yT, xc, zt, zs, skc, xb3, xbk, rTc, stage_f,
                     wslab, P, C, ln_bufs, outT_hbm, lseq, t0, nI, nD, TC,
                     stream=stream)
    if stream is None:
        p.barrier_all("sync")


def emit_l1_pre(p, al, nc, Win_h, xb, xbk, xs, zt, xc, cwc, cbc_, wxT, dt16,
                dtx16, dtT, dtxT, B_stk, C_stk, r_bc, dbc, stage_f, wslab,
                ident_b, ident_f, ps_tf, ps_tb, P, C, scr1, scr2, ch, nI, nD, TC):
    T2 = 2 * TC
    if ch > 0:
        for j in range(nI):
            p.op("vector", lambda h, j=j: h.tensor_copy(
                out=xs[j][:].rearrange("p (s c) -> p s c", s=2)[:, :, 0:3],
                in_=xs[j][:].rearrange("p (s c) -> p s c", s=2)[:, :, TC:TC + 3]),
                reads=(f"xs{j}",), writes=(f"xs{j}",))
    for it in range(2 * nI):
        src = Win_h[:, ts(it, 128)].rearrange("(q p) c -> p q c", p=128)
        sl = wslab[it % 2]
        slk = f"wslab{it % 2}"
        p.dma("gpsimd", out=sl[:, 0:nD * 128].rearrange("p (q c) -> p q c", q=nD),
              in_=src, writes=(slk,))
        for q in range(nD):
            p.op("tensor", lambda h, xb=xb, q=q, sl=sl: h.matmul(
                out=P[0][:, 0:T2], lhsT=sl[:, ts(q, 128)], rhs=xb[q],
                start=(q == 0), stop=(q == nD - 1)),
                reads=(slk, xbk[q]), writes=("P0",))
        if it < nI:
            xsv = xs[it][:].rearrange("p (s c) -> p s c", s=2)[:, :, 3:3 + TC]
            psv = P[0][:, 0:T2].rearrange("p (s c) -> p s c", s=2)
            p.op("scalar", lambda h, it=it, xsv=xsv, psv=psv: h.copy(
                out=xsv, in_=psv),
                reads=("P0",), writes=(f"xs{it}",))
        else:
            p.op("scalar", lambda h, it=it: h.copy(
                out=zt[it - nI][:], in_=P[0][:, 0:T2]),
                reads=("P0",), writes=(f"zt{it - nI}",))
    for j in range(nI):
        x3 = xs[j][:].rearrange("p (s c) -> p s c", s=2)
        p.op("vector", lambda h, j=j, x3=x3: h.tensor_scalar(
            out=scr1[:].rearrange("p (s t) -> p s t", s=2),
            in0=x3[:, :, 0:TC], scalar1=cwc[:, j * DCONV:j * DCONV + 1],
            scalar2=None, op0=ALU.mult),
            reads=(f"xs{j}", "cwc"), writes=("scr1",))
        for k in range(1, DCONV):
            p.op("vector", lambda h, j=j, k=k, x3=x3: h.scalar_tensor_tensor(
                out=scr1[:].rearrange("p (s t) -> p s t", s=2),
                in0=x3[:, :, k:k + TC], scalar=cwc[:, j * DCONV + k:j * DCONV + k + 1],
                in1=scr1[:].rearrange("p (s t) -> p s t", s=2),
                op0=ALU.mult, op1=ALU.add),
                reads=(f"xs{j}", "cwc", "scr1"), writes=("scr1",))
        p.op("scalar", lambda h, j=j: h.activation(
            out=xc[j][:], in_=scr1[:], func=AF.Silu, bias=cbc_[:, j:j + 1]),
            reads=("scr1", "cbc_"), writes=(f"xc{j}",))
    for q in range(nI):
        p.op("tensor", lambda h, q=q: h.matmul(
            out=P[1][:, 0:T2], lhsT=wxT[q][:, 0:2 * DST], rhs=xc[q][:, 0:T2],
            start=(q == 0), stop=(q == nI - 1)),
            reads=(f"wxT{q}", f"xc{q}"), writes=("P1",))
    for q in range(nI):
        p.op("tensor", lambda h, q=q: h.matmul(
            out=ps_tf[0:1, 0:T2], lhsT=wxT[q][:, 2 * DST:2 * DST + 1],
            rhs=xc[q][:, 0:T2],
            start=(q == 0), stop=(q == nI - 1)),
            reads=(f"wxT{q}", f"xc{q}"), writes=("ps_tf",))
    for s in range(2):
        p.op("vector", lambda h, s=s: h.tensor_copy(
            out=B_stk[s * DST:(s + 1) * DST, :], in_=P[1][0:DST, s * TC:(s + 1) * TC]),
            reads=("P1",), writes=("B_stk",))
        p.op("vector", lambda h, s=s: h.tensor_copy(
            out=C_stk[s * DST:(s + 1) * DST, :], in_=P[1][DST:2 * DST, s * TC:(s + 1) * TC]),
            reads=("P1",), writes=("C_stk",))
    rrow = scr2
    p.op("scalar", lambda h: h.copy(out=scr2[0:1, 0:T2], in_=ps_tf[0:1, 0:T2]),
         reads=("ps_tf",), writes=("scr2",))
    p.op("tensor", lambda h: h.matmul(
        out=ps_tf[:, 0:T2], lhsT=C["ones_row"], rhs=scr2[0:1, 0:T2],
        start=True, stop=True),
        reads=("scr2", "ones_row"), writes=("ps_tf",))
    p.op("vector", lambda h: h.tensor_copy(out=r_bc[:, 0:T2], in_=ps_tf[:, 0:T2]),
         reads=("ps_tf",), writes=("r_bc",))
    for j in range(nI):
        p.op("scalar", lambda h, j=j: h.activation(
            out=scr1[:], in_=r_bc[:, 0:T2], func=AF.Abs, bias=dbc[:, j:j + 1]),
            reads=("r_bc", "dbc"), writes=("scr1",))
        p.op("scalar", lambda h: h.activation(
            out=scr1[:], in_=scr1[:], func=AF.Exp, scale=-1.0),
            reads=("scr1",), writes=("scr1",))
        p.op("scalar", lambda h: h.activation(
            out=scr1[:], in_=scr1[:], func=AF.Ln, bias=1.0),
            reads=("scr1", "cpool"), writes=("scr1",))
        p.op("scalar", lambda h, j=j: h.activation(
            out=scr2[:], in_=r_bc[:, 0:T2], func=AF.Relu, bias=dbc[:, j:j + 1]),
            reads=("r_bc", "dbc"), writes=("scr2",))
        p.op("vector", lambda h, j=j: h.tensor_add(
            out=dt16[j][:], in0=scr1[:], in1=scr2[:]),
            reads=("scr1", "scr2"), writes=(f"dt16{j}",))
        p.op("vector", lambda h, j=j: h.tensor_mul(
            out=dtx16[j][:], in0=dt16[j][:], in1=xc[j][:]),
            reads=(f"dt16{j}", f"xc{j}"), writes=(f"dtx16{j}",))
    for j in range(nI):
        for cb in range(2):
            p.op("tensor", lambda h, j=j, cb=cb: h.transpose(
                out=ps_tb[:, 0:128], in_=dt16[j][:, ts(cb, 128)], identity=ident_b[:]),
                reads=(f"dt16{j}", "ident_b"), writes=("ps_tb",))
            p.op("vector", lambda h, j=j, cb=cb: h.tensor_copy(
                out=dtT[cb][:, ts(j, 128)], in_=ps_tb[:, 0:128]),
                reads=("ps_tb",), writes=(f"dtT{cb}",))
            p.op("tensor", lambda h, j=j, cb=cb: h.transpose(
                out=ps_tb[:, 0:128], in_=dtx16[j][:, ts(cb, 128)], identity=ident_b[:]),
                reads=(f"dtx16{j}", "ident_b"), writes=("ps_tb",))
            p.op("vector", lambda h, j=j, cb=cb: h.tensor_copy(
                out=dtxT[cb][:, ts(j, 128)], in_=ps_tb[:, 0:128]),
                reads=("ps_tb",), writes=(f"dtxT{cb}",))


def emit_l1_scan(p, al, nc, A_bc, dt_rep, dtx_rep, g_rep, dtT, dtxT, dt_dram,
                 dtx_dram, B_stk, C_stk, Csel, hst, ps_y, ps_tb, yT, yv,
                 ident_b, ch, TC, TP, nI):
    DINl = DIN
    nsub = TC // TP
    g1, g2 = p.new_gid(), p.new_gid()
    for s in range(2):
        p.dma("sync", out=dt_dram[s:s + 1, :], in_=dtT[s][:],
              reads=(f"dtT{s}",), writes=("dt_dram",), wgroup=g1)
        p.dma("scalar", out=dtx_dram[s:s + 1, :], in_=dtxT[s][:],
              reads=(f"dtxT{s}",), writes=("dtx_dram",), wgroup=g2)
    for sub in range(nsub):
        tb = sub * TP
        bi = sub % 2
        p.dma("sync", out=dt_rep[bi][:],
              in_=dt_dram[:, tb * DINl:(tb + TP) * DINl]
              .unsqueeze(1).broadcast_to([2, 64, TP * DINl]),
              reads=("dt_dram",), writes=(f"dt_rep{bi}",))
        p.dma("scalar", out=dtx_rep[bi][:],
              in_=dtx_dram[:, tb * DINl:(tb + TP) * DINl]
              .unsqueeze(1).broadcast_to([2, 64, TP * DINl]),
              reads=("dtx_dram",), writes=(f"dtx_rep{bi}",))
        p.op("vector", lambda h, bi=bi: h.tensor_tensor(
            out=dt_rep[bi][:].rearrange("p (t d) -> p t d", t=TP),
            in0=A_bc[:].unsqueeze(1).broadcast_to([128, TP, DINl]),
            in1=dt_rep[bi][:].rearrange("p (t d) -> p t d", t=TP),
            op=ALU.mult),
            reads=("A_bc", f"dt_rep{bi}"), writes=(f"dt_rep{bi}",))
        p.op("scalar", lambda h, bi=bi: h.activation(
            out=dt_rep[bi][:], in_=dt_rep[bi][:], func=AF.Exp),
            reads=(f"dt_rep{bi}",), writes=(f"dt_rep{bi}",))
        for tau in range(TP):
            t = tb + tau
            tg = t % 64
            hi, ho = (ch * TC + t) % 2, (ch * TC + t + 1) % 2
            pz = (tg - 1) % 64
            zv = Csel[:].rearrange("p (a c) -> p a c", a=2)[:, :, pz:pz + 1]
            p.op("gpsimd", lambda h, zv=zv: h.memset(zv, 0.0),
                 reads=(), writes=("Csel",))
            p.op("gpsimd", lambda h, t=t, tg=tg: h.tensor_copy(
                out=Csel[0:64, tg:tg + 1], in_=C_stk[0:64, t:t + 1]),
                reads=("C_stk",), writes=("Csel",))
            p.op("gpsimd", lambda h, t=t, tg=tg: h.tensor_copy(
                out=Csel[64:128, 64 + tg:64 + tg + 1], in_=C_stk[64:128, t:t + 1]),
                reads=("C_stk",), writes=("Csel",))
            p.op("vector", lambda h, tau=tau, bi=bi, hi=hi, ho=ho: h.tensor_mul(
                out=hst[ho][:], in0=g_rep[bi][:, tau * DINl:(tau + 1) * DINl],
                in1=hst[hi][:]),
                reads=(f"dt_rep{bi}", f"hst{hi}"), writes=(f"hst{ho}",))
            p.op("vector", lambda h, tau=tau, t=t, bi=bi, ho=ho: h.scalar_tensor_tensor(
                out=hst[ho][:], in0=dtx_rep[bi][:, tau * DINl:(tau + 1) * DINl],
                scalar=B_stk[:, t:t + 1], in1=hst[ho][:],
                op0=ALU.mult, op1=ALU.add),
                reads=(f"dtx_rep{bi}", "B_stk", f"hst{ho}"), writes=(f"hst{ho}",))
            for q4 in range(4):
                p.op("tensor", lambda h, q4=q4, tg=tg, ho=ho: h.matmul(
                    out=ps_y[:, ts(q4, 512)], lhsT=Csel[:],
                    rhs=hst[ho][:, ts(q4, 512)],
                    start=(tg == 0), stop=(tg == 63), skip_group_check=True),
                    reads=("Csel", f"hst{ho}"), writes=("ps_y",))
        if (sub + 1) % (64 // TP) == 0:
            grp = sub // (64 // TP)
            p.op("scalar", lambda h: h.copy(out=yv[:], in_=ps_y[:, :]),
                 reads=("ps_y",), writes=("wslab0",))
            for jb in range(nI):
                p.op("tensor", lambda h, jb=jb: h.transpose(
                    out=ps_tb[:, 0:128], in_=yv[:, ts(jb, 128)], identity=ident_b[:]),
                    reads=("wslab0", "ident_b"), writes=("ps_tb",))
                ov = yT[jb][:].rearrange("p (s t) -> p s t", s=2)[:, :, grp * 64:grp * 64 + 64]
                iv = ps_tb[:, 0:128].rearrange("p (s t) -> p s t", s=2)
                p.op("vector", lambda h, ov=ov, iv=iv: h.tensor_copy(out=ov, in_=iv),
                     reads=("ps_tb",), writes=(f"yT{jb}",))


def emit_l1_post(p, al, nc, Wout_h, yT, xc, zt, zs, skc, xb, xbk, rTc, stage_f,
                 wslab, P, C, ln_bufs, outT_hbm, lseq, t0, nI, nD, TC, stream=None):
    T2 = 2 * TC
    for j in range(nI):
        p.op("vector", lambda h, j=j: h.scalar_tensor_tensor(
            out=yT[j][:], in0=xc[j][:], scalar=skc[:, j:j + 1], in1=yT[j][:],
            op0=ALU.mult, op1=ALU.add),
            reads=(f"xc{j}", "skc", f"yT{j}"), writes=(f"yT{j}",))
        p.op("scalar", lambda h, j=j: h.activation(
            out=zs[:], in_=zt[j][:], func=AF.Silu),
            reads=(f"zt{j}",), writes=("zs",))
        p.op("vector", lambda h, j=j: h.tensor_mul(
            out=yT[j][:], in0=yT[j][:], in1=zs[:]),
            reads=(f"yT{j}", "zs"), writes=(f"yT{j}",))
    for o in range(nD):
        src = Wout_h[:, ts(o, 128)].rearrange("(q p) c -> p q c", p=128)
        sl = wslab[o % 2]
        slk = f"wslab{o % 2}"
        p.dma("gpsimd", out=sl[:, 0:nI * 128].rearrange("p (q c) -> p q c", q=nI),
              in_=src, writes=(slk,))
        for q in range(nI):
            p.op("tensor", lambda h, q=q, o=o, sl=sl: h.matmul(
                out=P[1][:, 0:T2], lhsT=sl[:, ts(q, 128)], rhs=yT[q][:],
                start=(q == 0), stop=(q == nI - 1)),
                reads=(slk, f"yT{q}"), writes=("P1",))
        p.op("vector", lambda h, xb=xb, o=o: h.tensor_add(
            out=rTc[o][:].rearrange("p (s l) -> p s l", s=2),
            in0=P[1][:, 0:T2].rearrange("p (s l) -> p s l", s=2), in1=xb[o]),
            reads=("P1", xbk[o]), writes=(f"rTc{o}",))
    emit_ln_chunk(p, rTc, [f"rTc{j}" for j in range(nD)],
                  rTc, [f"rTc{j}" for j in range(nD)], "P0", "P1",
                  ln_bufs, C, T2)
    if stream is None:
        for j in range(nD):
            for s in range(2):
                p.dma("sync", out=outT_hbm[ts(j, 128), s * lseq + t0:s * lseq + t0 + TC],
                      in_=rTc[j][:, s * TC:(s + 1) * TC],
                      reads=(f"rTc{j}",), writes=(f"outT{j}",))
    else:
        for j in range(nD):
            ov = stream[j][:].rearrange("p (s l) -> p s l", s=2)[:, :, t0:t0 + TC]
            p.op("vector", lambda h, j=j, ov=ov: h.tensor_copy(
                out=ov, in_=rTc[j][:].rearrange("p (s l) -> p s l", s=2)),
                reads=(f"rTc{j}",), writes=(f"ST{j}",))



_NC_CACHE = {}


def _get_nc(name, builder):
    if name not in _NC_CACHE:
        nc = bass.Bass("TRN2", target_bir_lowering=False, debug=False)
        builder(nc)
        _NC_CACHE[name] = nc
    return _NC_CACHE[name]


def kernel(**inp):
    return kernel_fused(**inp)


def kernel_multilaunch(**inp):
    n_cores = 8
    x = np.asarray(inp["L0_attn_x"])
    B_all = x.shape[0]
    per = B_all // n_cores

    xTs = [np.ascontiguousarray(
        x[c * per:(c + 1) * per].transpose(2, 0, 1).reshape(D, per * LSEQ))
        for c in range(n_cores)]

    ncA = _get_nc("A", build_l0_attn)
    w = {k: np.ascontiguousarray(np.asarray(inp[v]))
         for k, v in [("wq", "L0_attn_wq_w"), ("wk", "L0_attn_wk_w"),
                      ("wv", "L0_attn_wv_w"), ("wo", "L0_attn_wo_w"),
                      ("bq", "L0_attn_wq_b"), ("bk", "L0_attn_wk_b"),
                      ("bv", "L0_attn_wv_b"), ("bo", "L0_attn_wo_b")]}
    maps = [dict(xT=xTs[c], **w) for c in range(n_cores)]
    res = run_bass_kernel_spmd(ncA, maps, core_ids=list(range(n_cores)))
    x1 = [res.results[c]["outT"] for c in range(n_cores)]

    ncB = _get_nc("B", build_l0_ffn)
    w = {"gate_w": np.asarray(inp["L0_ffn_gate_w"]),
         "Wg": np.asarray(inp["L0_ffn_Wg"]),
         "W1": np.asarray(inp["L0_ffn_W1"]),
         "W2": np.asarray(inp["L0_ffn_W2"])}
    maps = [dict(xT=x1[c], **w) for c in range(n_cores)]
    res = run_bass_kernel_spmd(ncB, maps, core_ids=list(range(n_cores)))
    x2 = [res.results[c]["outT"] for c in range(n_cores)]

    ncC = _get_nc("C", build_l1_attn)
    w = {"W_in": np.asarray(inp["L1_attn_W_in"]),
         "conv_w": np.asarray(inp["L1_attn_conv_w"]),
         "conv_b": np.asarray(inp["L1_attn_conv_b"]),
         "W_x": np.asarray(inp["L1_attn_W_x"]),
         "A_log": np.asarray(inp["L1_attn_A_log"]),
         "dt_bias": np.asarray(inp["L1_attn_dt_bias"]),
         "D_skip": np.asarray(inp["L1_attn_D"]),
         "W_out": np.asarray(inp["L1_attn_W_out"])}
    maps = [dict(xT=x2[c], **w) for c in range(n_cores)]
    res = run_bass_kernel_spmd(ncC, maps, core_ids=list(range(n_cores)))
    x3 = [res.results[c]["outT"] for c in range(n_cores)]

    ncD = _get_nc("D", build_l1_ffn)
    w = {"router_W": np.asarray(inp["L1_ffn_router_W"]),
         "router_b": np.asarray(inp["L1_ffn_router_b"]),
         "sW1": np.asarray(inp["L1_ffn_sW1"]),
         "sb1": np.asarray(inp["L1_ffn_sb1"]),
         "sW2": np.asarray(inp["L1_ffn_sW2"]),
         "sb2": np.asarray(inp["L1_ffn_sb2"]),
         "eW1": np.asarray(inp["L1_ffn_eW1"]),
         "eb1": np.asarray(inp["L1_ffn_eb1"]),
         "eW2": np.asarray(inp["L1_ffn_eW2"]),
         "eb2": np.asarray(inp["L1_ffn_eb2"])}
    maps = [dict(xT=x3[c], **w) for c in range(n_cores)]
    res = run_bass_kernel_spmd(ncD, maps, core_ids=list(range(n_cores)))
    x4 = [res.results[c]["outT"] for c in range(n_cores)]

    out = np.concatenate(
        [x4[c].reshape(D, per, LSEQ).transpose(1, 2, 0) for c in range(n_cores)],
        axis=0)
    return out.astype(np.float32)



def build_fused(nc, lseq=LSEQ, blocks='ABCD', handoff='dram'):
    tok = NSEQ * lseq
    xT = nc.dram_tensor("xT", [D, tok], F32, kind="ExternalInput").ap()
    wsA = [nc.dram_tensor(n, [D, D], F32, kind="ExternalInput").ap()
           for n in ("wq", "wk", "wv", "wo")]
    bsA = [nc.dram_tensor(n, [D], F32, kind="ExternalInput").ap()
           for n in ("bq", "bk", "bv", "bo")]
    gate_w = nc.dram_tensor("gate_w", [E0, D], F32, kind="ExternalInput").ap()
    Wg = nc.dram_tensor("Wg", [E0, H0, D], F32, kind="ExternalInput").ap()
    W1 = nc.dram_tensor("W1", [E0, H0, D], F32, kind="ExternalInput").ap()
    W2 = nc.dram_tensor("W2", [E0, D, H0], F32, kind="ExternalInput").ap()
    W_in = nc.dram_tensor("W_in", [D, 2 * DIN], F32, kind="ExternalInput").ap()
    conv_w = nc.dram_tensor("conv_w", [DIN, DCONV], F32, kind="ExternalInput").ap()
    conv_b = nc.dram_tensor("conv_b", [DIN], F32, kind="ExternalInput").ap()
    W_x = nc.dram_tensor("W_x", [DIN, 2 * DST + 1], F32, kind="ExternalInput").ap()
    A_log = nc.dram_tensor("A_log", [DIN, DST], F32, kind="ExternalInput").ap()
    dt_bias = nc.dram_tensor("dt_bias", [DIN], F32, kind="ExternalInput").ap()
    D_skip = nc.dram_tensor("D_skip", [DIN], F32, kind="ExternalInput").ap()
    W_out = nc.dram_tensor("W_out", [DIN, D], F32, kind="ExternalInput").ap()
    rW = nc.dram_tensor("router_W", [D, E1], F32, kind="ExternalInput").ap()
    rb = nc.dram_tensor("router_b", [E1], F32, kind="ExternalInput").ap()
    sW1 = nc.dram_tensor("sW1", [D, H1], F32, kind="ExternalInput").ap()
    sb1 = nc.dram_tensor("sb1", [H1], F32, kind="ExternalInput").ap()
    sW2 = nc.dram_tensor("sW2", [H1, D], F32, kind="ExternalInput").ap()
    sb2 = nc.dram_tensor("sb2", [D], F32, kind="ExternalInput").ap()
    eW1 = nc.dram_tensor("eW1", [E1, D, H1], F32, kind="ExternalInput").ap()
    eb1 = nc.dram_tensor("eb1", [E1, H1], F32, kind="ExternalInput").ap()
    eW2 = nc.dram_tensor("eW2", [E1, H1, D], F32, kind="ExternalInput").ap()
    eb2 = nc.dram_tensor("eb2", [E1, D], F32, kind="ExternalInput").ap()
    outT = nc.dram_tensor("outT", [D, tok], F32, kind="ExternalOutput").ap()
    if "G" in blocks:
        emit_l1_ffn_mod = globals()
        DBG = emit_l1_ffn_mod["DBG"]
        DBG["combE"] = nc.dram_tensor("dbg_combE", [E1, tok // 2], BF16, kind="ExternalOutput").ap()
        DBG["Ltm"] = nc.dram_tensor("dbg_Ltm", [128, (tok // 2 // 128) * E1], F32, kind="ExternalOutput").ap()
        DBG["oacc0"] = nc.dram_tensor("dbg_oacc0", [128, tok // 2], BF16, kind="ExternalOutput").ap()
        DBG["hb0"] = nc.dram_tensor("dbg_hb0", [128, tok // 2], BF16, kind="ExternalOutput").ap()
        DBG["oaccF"] = nc.dram_tensor("dbg_oaccF", [128, tok // 2], BF16, kind="ExternalOutput").ap()
        DBG["cbc0"] = nc.dram_tensor("dbg_cbc0", [128, tok // 2], BF16, kind="ExternalOutput").ap()
        DBG["rTc0a"] = nc.dram_tensor("dbg_rTc0a", [128, 512], F32, kind="ExternalOutput").ap()
        DBG["rTc0b"] = nc.dram_tensor("dbg_rTc0b", [128, 512], F32, kind="ExternalOutput").ap()
        DBG["xbv"] = nc.dram_tensor("dbg_xbv", [128, 512], BF16, kind="ExternalOutput").ap()
        DBG["st_sh"] = nc.dram_tensor("dbg_st_sh", [128, 512], BF16, kind="ExternalOutput").ap()
        DBG["std"] = nc.dram_tensor("dbg_std", [128, tok], BF16, kind="ExternalOutput").ap()
        for _e in (0, 1, 3, 7, 14):
            DBG[f"st_e{_e}"] = nc.dram_tensor(f"dbg_st_e{_e}", [128, 512], BF16, kind="ExternalOutput").ap()

    es = ExitStack()
    with es:
        al = Alloc(nc, es)
        p = Prog(nc)
        p.make_sems(es)
        C = make_consts(p, al)
        if handoff == "dram":
            a1 = nc.dram_tensor("act1", [D, tok], F32, kind="Internal").ap()
            a2 = nc.dram_tensor("act2", [D, tok], F32, kind="Internal").ap()
            a3 = nc.dram_tensor("act3", [D, tok], F32, kind="Internal").ap()
            with ExitStack() as esb:
                emit_l0_attn(nc, p, Alloc(nc, esb), xT, wsA, bsA, a1, lseq, C=C)
            p.full_barrier()
            with ExitStack() as esb:
                emit_l0_ffn(nc, p, Alloc(nc, esb), a1, gate_w, Wg, W1, W2, a2,
                            lseq, C=C)
            p.full_barrier()
            with ExitStack() as esb:
                emit_l1_attn(nc, p, Alloc(nc, esb), a2,
                             (W_in, conv_w, conv_b, W_x, A_log, dt_bias, D_skip, W_out),
                             a3, lseq, C=C)
            p.full_barrier()
            with ExitStack() as esb:
                emit_l1_ffn(nc, p, Alloc(nc, esb), a3,
                            (rW, rb, sW1, sb1, sW2, sb2, eW1, eb1, eW2, eb2),
                            outT, lseq, C=C)
            p.barrier_all("sync")
            block = es.enter_context(nc.Block())
            p.finish(block)
            return nc
        stream = [al.sb(f"ST{j}", [128, tok], BF16) for j in range(D // 128)]
        if "A" in blocks:
            with ExitStack() as esb:
                emit_l0_attn(nc, p, Alloc(nc, esb), xT, wsA, bsA, None, lseq,
                             C=C, stream=stream)
            p.full_barrier()
        else:
            for j in range(D // 128):
                p.dma("gpsimd", out=stream[j][:], in_=xT[ts(j, 128), :],
                      writes=(f"ST{j}",))
        if "B" in blocks:
            with ExitStack() as esb:
                emit_l0_ffn(nc, p, Alloc(nc, esb), None, gate_w, Wg, W1, W2, None,
                            lseq, C=C, stream=stream)
            p.full_barrier()
        if "C" in blocks:
            with ExitStack() as esb:
                emit_l1_attn(nc, p, Alloc(nc, esb), None,
                             (W_in, conv_w, conv_b, W_x, A_log, dt_bias, D_skip, W_out),
                             None, lseq, C=C, stream=stream)
            p.full_barrier()
        if "D" in blocks:
            with ExitStack() as esb:
                emit_l1_ffn(nc, p, Alloc(nc, esb), None,
                            (rW, rb, sW1, sb1, sW2, sb2, eW1, eb1, eW2, eb2),
                            outT, lseq, C=C, stream=stream)
        else:
            for j in range(D // 128):
                p.dma("gpsimd", out=outT[ts(j, 128), :], in_=stream[j][:],
                      reads=(f"ST{j}",), writes=("outT",))
        p.barrier_all("sync")
        block = es.enter_context(nc.Block())
        p.finish(block)
    return nc


def kernel_fused(**inp):
    n_cores = 8
    x = np.asarray(inp["L0_attn_x"])
    per = x.shape[0] // n_cores
    xTs = [np.ascontiguousarray(
        x[c * per:(c + 1) * per].transpose(2, 0, 1).reshape(D, per * LSEQ))
        for c in range(n_cores)]
    nc = _get_nc("F", build_fused)
    w = {"wq": np.asarray(inp["L0_attn_wq_w"]), "wk": np.asarray(inp["L0_attn_wk_w"]),
         "wv": np.asarray(inp["L0_attn_wv_w"]), "wo": np.asarray(inp["L0_attn_wo_w"]),
         "bq": np.asarray(inp["L0_attn_wq_b"]), "bk": np.asarray(inp["L0_attn_wk_b"]),
         "bv": np.asarray(inp["L0_attn_wv_b"]), "bo": np.asarray(inp["L0_attn_wo_b"]),
         "gate_w": np.asarray(inp["L0_ffn_gate_w"]), "Wg": np.asarray(inp["L0_ffn_Wg"]),
         "W1": np.asarray(inp["L0_ffn_W1"]), "W2": np.asarray(inp["L0_ffn_W2"]),
         "W_in": np.asarray(inp["L1_attn_W_in"]), "conv_w": np.asarray(inp["L1_attn_conv_w"]),
         "conv_b": np.asarray(inp["L1_attn_conv_b"]), "W_x": np.asarray(inp["L1_attn_W_x"]),
         "A_log": np.asarray(inp["L1_attn_A_log"]), "dt_bias": np.asarray(inp["L1_attn_dt_bias"]),
         "D_skip": np.asarray(inp["L1_attn_D"]), "W_out": np.asarray(inp["L1_attn_W_out"]),
         "router_W": np.asarray(inp["L1_ffn_router_W"]), "router_b": np.asarray(inp["L1_ffn_router_b"]),
         "sW1": np.asarray(inp["L1_ffn_sW1"]), "sb1": np.asarray(inp["L1_ffn_sb1"]),
         "sW2": np.asarray(inp["L1_ffn_sW2"]), "sb2": np.asarray(inp["L1_ffn_sb2"]),
         "eW1": np.asarray(inp["L1_ffn_eW1"]), "eb1": np.asarray(inp["L1_ffn_eb1"]),
         "eW2": np.asarray(inp["L1_ffn_eW2"]), "eb2": np.asarray(inp["L1_ffn_eb2"])}
    maps = [dict(xT=xTs[c], **w) for c in range(n_cores)]
    res = run_bass_kernel_spmd(nc, maps, core_ids=list(range(n_cores)))
    out = np.concatenate(
        [res.results[c]["outT"].reshape(D, per, LSEQ).transpose(1, 2, 0)
         for c in range(n_cores)], axis=0)
    return out.astype(np.float32)
